# version 14
# speedup vs baseline: 1.0633x; 1.0101x over previous
import contextlib

import numpy as np
import ml_dtypes

import concourse.bass as bass
import concourse.mybir as mybir
from concourse.bass_utils import run_bass_kernel_spmd
from concourse.bass_core_rust import IndirectOffsetOnAxis

F32 = mybir.dt.float32
BF16 = mybir.dt.bfloat16
I16 = mybir.dt.int16
I32 = mybir.dt.int32
U8 = mybir.dt.uint8
U32 = mybir.dt.uint32
AF = mybir.ActivationFunctionType
ALU = mybir.AluOpType
AX = mybir.AxisListType

NCORES = 8
B, L, D, H, HD = 4, 4096, 1024, 16, 64
T = B * L
TPC = T // NCORES
E, K4, CAP, I, G = 64, 4, 256, 768, 128
EPC = E // NCORES
NT = TPC // 128
LNEPS = 1e-5
BIGIDX = 1 << 22

BF16NP = ml_dtypes.bfloat16


class PB:

    def __init__(self, nc, ctx):
        self.nc = nc
        self.ctx = ctx
        self.ops = {k: [] for k in ("sync", "vector", "scalar", "tensor", "gpsimd")}
        self.phase = "init"
        self.sems = {}
        self.counts = {}
        self.auto = {}

    def sem(self, name):
        s = self.ctx.enter_context(self.nc.semaphore(name))
        self.sems[name] = s
        self.counts[name] = 0
        return s

    def inc(self, sem, n=1):
        self.counts[sem.name] += n
        return self.counts[sem.name]

    def cur(self, sem):
        return self.counts[sem.name]

    def emit(self, engine, fn, waits=(), incs=(), kind="compute"):
        waits = list(waits)
        incs = list(incs)
        if kind == "compute" and engine in self.auto:
            sem, cnt, last = self.auto[engine]
            if last[0] is not None:
                waits.append(last[0])
            if incs:
                s0, _ = incs[0]
                last[0] = (s0, self.counts[s0.name])
            else:
                cnt[0] += 1
                incs.append((sem, 1))
                last[0] = (sem, cnt[0])
        self.ops[engine].append((fn, waits, incs, self.phase))

    def emit_i(self, engine, fn, waits=(), inc_sem=None, inc_n=1, kind="compute"):
        incs = []
        tgt = None
        if inc_sem is not None:
            tgt = self.inc(inc_sem, inc_n)
            incs = [(inc_sem, inc_n)]
        self.emit(engine, fn, waits, incs, kind=kind)
        return tgt

    def build(self, block):
        import os
        scoped = bool(os.environ.get("BASSPROF"))
        def mk(engine):
            def run(eng):
                curph, sid = None, None
                for rec in self.ops[engine]:
                    fn, waits, incs, ph = rec
                    if scoped and ph != curph:
                        if curph is not None:
                            self.nc.leave_named_scope(engine + ":" + curph, sid, False)
                        sid, _ = self.nc.enter_named_scope(engine + ":" + ph, False)
                        curph = ph
                    for sem, val in waits:
                        eng.wait_ge(sem, val)
                    inst = fn(eng)
                    for sem, n in incs:
                        inst = inst.then_inc(sem, n)
                if scoped and curph is not None:
                    self.nc.leave_named_scope(engine + ":" + curph, sid, False)
            return run
        block.sync(mk("sync"))
        block.vector(mk("vector"))
        block.scalar(mk("scalar"))
        block.tensor(mk("tensor"))
        block.gpsimd(mk("gpsimd"))


DT = {"f32": F32, "bf16": BF16, "i16": I16, "i32": I32, "u8": U8, "u32": U32}


class Builder:
    def __init__(self, dev_stage="full"):
        self.dev_stage = dev_stage
        self.nc = bass.Bass("TRN2", target_bir_lowering=False, debug=False, detect_race_conditions=False)
        self.ctx = contextlib.ExitStack()
        self.p = PB(self.nc, self.ctx)
        self.dram = {}
        self.sb = {}
        self._mk_gu = self._mk_dn = self._mk_ebT = 0
        self._mk_gath = 0
        self._mk_scr = None
        self._deq_par = 0
        self._deq_free = [0, 0]
        self._mk_ld_extra = []
        self._mk_ld_uses = 0
        self.ps = {}

    def din(self, name, shape, dt):
        self.dram[name] = self.nc.dram_tensor(name, list(shape), DT[dt], kind="ExternalInput").ap()
        return self.dram[name]

    def dout(self, name, shape, dt):
        self.dram[name] = self.nc.dram_tensor(name, list(shape), DT[dt], kind="ExternalOutput").ap()
        return self.dram[name]

    def dint(self, name, shape, dt, **kw):
        self.dram[name] = self.nc.dram_tensor(name, list(shape), DT[dt], kind="Internal", **kw).ap()
        return self.dram[name]

    def S(self, name, shape, dt):
        self.sb[name] = self.ctx.enter_context(self.nc.sbuf_tensor(name, list(shape), DT[dt]))
        return self.sb[name]

    _ARENA_EXPERT = {
        "w_gate_lo": (0, 3072, "bf16", 128),
        "w_gate_hi": (6144, 3072, "bf16", 128),
        "w_up_lo": (12288, 3072, "bf16", 128),
        "w_up_hi": (18432, 3072, "bf16", 128),
        "w_down_lo": (24576, 3072, "bf16", 128),
        "w_down_hi": (30720, 3072, "bf16", 128),
        "pk_t": (36864, 1024, "i16", 128),
        "pk_t2": (63488, 1024, "i16", 128),
        "st_t2": (69632, 1024, "f32", 128),
        "st_t": (38912, 1024, "f32", 128),
        "ebuf": (43008, 2048, "bf16", 128),
        "ebufT": (47104, 2048, "bf16", 128),
        "gath_f32": (51200, 2048, "f32", 128),
        "hU": (59392, 1536, "bf16", 128),
        "hsil": (62464, 512, "bf16", 128),
        "h_shT": (63488, 1536, "bf16", 128),
        "retst": (66560, 1024, "bf16", 128),
        "logitsT": (68608, 2048, "f32", 64),
        "lgtile": (76800, 1024, "f32", 128),
        "ohr_c": (80896, 4096, "bf16", 128),
        "sq_scr": (89088, 1024, "f32", 128),
    }
    _ARENA_ATTN = {
        "kT": (0, 8192, "bf16", 64),
        "rT": (16384, 8192, "bf16", 64),
        "v_tm": (32768, 4096, "bf16", 128),
        "yT": (40960, 4096, "bf16", 128),
        "khat_tm": (49152, 1024, "bf16", 128),
        "at_b": (55296, 512, "bf16", 128),
        "Sbf": (53248, 1024, "bf16", 64),
        "Sst": (57344, 1024, "f32", 64),
        "Msb": (61440, 1024, "f32", 64),
        "W1p": (65536, 2048, "bf16", 64),
        "W2p": (69632, 2048, "bf16", 64),
        "Dmask": (73728, 2048, "bf16", 128),
        "rt_buf": (77824, 2048, "bf16", 64),
        "kh_buf": (81920, 2048, "bf16", 64),
        "Am": (86016, 256, "bf16", 128),
        "wkb": (86528, 2048, "bf16", 128),
        "tfb": (90624, 1024, "bf16", 128),
        "wCt": (92672, 64, "f32", 64),
        "lngb": (92928, 64, "f32", 128),
        "zmask": (93184, 128, "bf16", 128),
    }

    def carve(self, name, hi=False):
        off, n, dt, parts = (self._ARENA_EXPERT | self._ARENA_ATTN)[name]
        es = {"bf16": 2, "f32": 4, "i16": 2}[dt]
        if hi:
            ap = self.sb["arena"][64:128, :].bitcast(DT[dt])
        else:
            ap = self.sb["arena"][0:parts, :].bitcast(DT[dt])
        return ap[:, off // es: off // es + n]

    def P(self, name, shape, dt="f32"):
        self.ps[name] = self.ctx.enter_context(self.nc.psum_tensor(name, list(shape), DT[dt]))
        return self.ps[name]

    def declare_io(self):
        din, dout, dint = self.din, self.dout, self.dint
        full = self.dev_stage != "ffn1front"
        din("x0_shard", (TPC, D), "f32")
        din("gw_T", (D, E), "f32")
        if full:
            din("x0_full", (T, D), "f32")
            din("wk_T", (D, D), "f32")
            din("wv_T", (D, D), "f32")
            din("wr_T", (D, D), "f32")
            din("wo_T", (D, D), "f32")
            din("tfirst", (1, D), "f32")
            din("tdecay64", (64, 16), "f32")
            din("decay_h", (1, 16), "f32")
            din("iota1", (1, 128), "f32")
            din("iotar", (1, 128), "f32")
            din("partnerrow", (128, 1), "i32")
            din("gate_pT", (EPC, D // 2, I), "i16")
            din("up_pT", (EPC, D // 2, I), "i16")
            din("down_pT", (EPC, I // 2, D), "i16")
            din("sh_gate_pT", (D // 2, I), "i16")
            din("sh_up_pT", (D // 2, I), "i16")
            din("sh_down_pT", (I // 2, D), "i16")
            din("gate_sT", (EPC, D // G, I), "f32")
            din("up_sT", (EPC, D // G, I), "f32")
            din("down_sT", (EPC, I // G, D), "f32")
            din("sh_gate_sT", (D // G, I), "f32")
            din("sh_up_sT", (D // G, I), "f32")
            din("sh_down_sT", (I // G, D), "f32")
        din("ident", (128, 128), "bf16")
        din("identf", (128, 128), "f32")
        din("tri_excl", (128, 128), "bf16")
        din("allones", (128, 128), "bf16")
        din("onescol", (1, 128), "bf16")
        din("iota_e", (128, E), "f32")
        din("iota_ij", (128, 128), "f32")
        din("tri_ij_u8", (128, 128), "u8")
        din("tokid0", (128, 1), "f32")
        din("expbase", (128, 1), "f32")
        din("coreid", (1, 8), "i32")
        din("prevmask", (1, 8), "f32")

        dint("g_table", (E * CAP, 2), "f32")
        dint("g_table_all", (E * CAP, 2), "f32")
        dint("cnt_src", (1, E), "f32")
        dint("bown_dram", (1, E), "f32")
        dint("cnt_all", (NCORES, E), "f32")
        dint("ret_src", (EPC * CAP, D), "bf16")
        dint("ret_all", (E * CAP, D), "bf16")
        dint("hf2_src", (TPC, D), "bf16")
        dint("hf2_all", (T, D), "bf16")
        dint("st_src", (H * HD, HD), "f32")
        dint("st_all", (NCORES * H * HD, HD), "f32")

        if self.dev_stage == "ffn1x":
            dout("out", (TPC, D), "f32")
        elif self.dev_stage == "full":
            import os
            dout("out", (TPC, D), "f32")
            if os.environ.get("ATTNDBG"):
                dout("dbg_kT", (64, 8192), "f32")
                dout("dbg_rT", (64, 8192), "f32")
                dout("dbg_v", (128, 4096), "f32")
                dout("dbg_yT", (128, 4096), "f32")
                dout("dbg_S", (64, 1024), "f32")
                dout("dbg_W1", (64, 2048), "f32")
                dout("dbg_Dm", (128, 2048), "f32")
                dout("dbg_hf2", (128, 4 * D), "f32")
                dout("dbg_x4", (128, 4 * D), "f32")
                dout("dbg_v2", (128, 4096), "f32")
                dout("dbg_W2", (64, 2048), "f32")
                dout("dbg_kh", (64, 2048), "f32")
                dout("dbg_kht", (128, 1024), "f32")
                dout("dbg_Ms", (64, 1024), "f32")
        elif self.dev_stage == "ffn1front":
            dout("dbg_w", (TPC, K4), "f32")
            dout("dbg_idx", (TPC, 8), "u32")
            dout("dbg_gpos", (TPC, K4), "f32")
            dout("dbg_cnt", (NCORES, E), "f32")
            dout("dbg_hf", (TPC, D), "bf16")
            dout("dbg_stats", (128, NT * 8), "f32")
            dout("dbg_table", (E * CAP, 2), "f32")
        else:
            dout("out", (TPC, D), "f32")

    def alloc_common(self):
        S, P = self.S, self.P
        S("x", (128, NT * D), "f32")
        S("hf", (128, 4 * D), "bf16")
        S("hfT", (128, 8 * TPC), "bf16")
        S("stats", (128, NT * 8), "f32")
        S("ln_tmp", (128, 4 * NT), "f32")
        S("consts", (128, 512), "bf16")
        S("identf_sb", (128, 128), "f32")
        S("iota_e_sb", (128, E), "f32")
        S("gwT", (128, 8 * E), "bf16")

        S("m8", (128, NT * 8), "f32")
        S("idx8", (128, NT * 8), "u32")
        S("idxf", (128, NT * K4), "f32")
        S("w4", (128, NT * K4), "f32")
        S("oh", (128, NT * E), "bf16")
        S("pos4", (128, NT * K4), "f32")
        S("rtmp", (128, NT * 2 * K4), "f32")
        S("pairs", (128, NT * K4 * 2), "f32")
        S("sidx", (128, NT * K4), "i32")
        S("bownrow", (1, E), "f32")
        S("cntrows", (1, NCORES * E), "f32")
        S("bownbc", (128, E), "f32")
        S("petile", (128, E), "f32")
        S("fidx", (128, 2 * NT * K4), "f32")
        S("tokid_sb", (128, 1), "f32")
        S("coreid_sb", (1, 8), "i32")
        S("prevmask_sb", (1, 8), "f32")
        S("arena", (128, 93440), "u8")
        S("gpair", (128, 32), "f32")
        S("tokcol", (128, 16), "i32")
        S("eidx", (128, 16), "i32")
        S("iscr", (128, 8), "i32")
        S("expbase_sb", (128, 1), "f32")
        S("prow_sb", (128, 1), "i32")

        P("pb", (128, 4096), "f32")

    def load_common(self):
        p, sb, dram = self.p, self.sb, self.dram
        s_in = self.s_in
        self._bc = {}

        def mkreg(e):
            self._bc["r"] = e.to_reg(E * CAP - 1)
            return None
        p.emit("gpsimd", mkreg, kind="dma")
        for it in range(NT):
            tgt = sb["x"][:, it * D:(it + 1) * D]
            src = dram["x0_shard"].rearrange("(n p) d -> n p d", p=128)[it]
            p.emit_i("sync", lambda e, t=tgt, s=src: e.dma_start(out=t, in_=s), inc_sem=s_in, inc_n=16)
        self.x_loaded = p.cur(s_in)
        cl = [
            (sb["consts"][:, 0:128], dram["ident"]),
            (sb["consts"][:, 128:256], dram["tri_excl"]),
            (sb["consts"][:, 256:384], dram["allones"]),
            (sb["consts"][0:1, 384:512], dram["onescol"]),
            (sb["iota_e_sb"][:], dram["iota_e"]),
            (sb["identf_sb"][:], dram["identf"]),
        ]
        for tgt, src in cl:
            p.emit_i("sync", lambda e, t=tgt, s=src: e.dma_start(out=t, in_=s), inc_sem=s_in, inc_n=16)
        for kc in range(8):
            tgt = sb["gwT"][:, kc * E:(kc + 1) * E]
            src = dram["gw_T"][kc * 128:(kc + 1) * 128, :]
            p.emit_i("gpsimd", lambda e, t=tgt, s=src: e.dma_start(out=t, in_=s),
                     inc_sem=self.s_po, inc_n=16, kind="dma")
        p.emit_i("sync", lambda e: e.dma_start(out=sb["tokid_sb"][:], in_=dram["tokid0"]), inc_sem=s_in, inc_n=16)
        p.emit_i("sync", lambda e: e.dma_start(out=sb["expbase_sb"][:], in_=dram["expbase"]), inc_sem=s_in, inc_n=16)
        if "partnerrow" in dram:
            p.emit_i("sync", lambda e: e.dma_start(out=sb["prow_sb"][:], in_=dram["partnerrow"]), inc_sem=s_in, inc_n=16)
        p.emit_i("sync", lambda e: e.dma_start(out=sb["coreid_sb"][:], in_=dram["coreid"]), inc_sem=s_in, inc_n=16)
        p.emit_i("sync", lambda e: e.dma_start(out=sb["prevmask_sb"][:], in_=dram["prevmask"]), inc_sem=s_in, inc_n=16)
        p.emit_i("vector", lambda e: e.memset(sb["stats"][:], 0.0), inc_sem=self.s_ve)
        self.ve_zt = 0
        self.consts_loaded = p.cur(s_in)
        self.x_loaded = p.cur(s_in)
        self.gw_loaded = p.cur(self.s_po)
        self.identf_sb = sb["identf_sb"][:]
        self.ident = sb["consts"][:, 0:128]
        self.tri_excl = sb["consts"][:, 128:256]
        self.allones = sb["consts"][:, 256:384]
        self.onescol = sb["consts"][0:1, 384:512]

    def emit_ln(self, x_ap_fn, hf_dst_fn, stat_base, wait_first, ntiles=NT):
        p, sb = self.p, self.sb
        stats, ln_tmp, sq = sb["stats"], sb["ln_tmp"], self.carve("sq_scr")
        for it in range(ntiles):
            src = x_ap_fn(it)
            c0 = stat_base + it * 8
            p.emit("vector", lambda e, s=src, o=stats[:, c0:c0 + 1]: e.reduce_sum(out=o, in_=s, axis=AX.X),
                   waits=(wait_first if it == 0 else []))
            wsq = list(wait_first) if it == 0 else []
            p.emit_i("scalar", lambda e, s=src, o=stats[:, c0 + 1:c0 + 2]: e.activation(
                out=sq[:], in_=s, func=AF.Square, accum_out=o),
                waits=wsq, inc_sem=self.s_ac)
        ac_sq = p.cur(self.s_ac)
        st = lambda c: stats[:, stat_base + c:stat_base + ntiles * 8:8]
        allsum, allssq, allinv, allnb = st(0), st(1), st(2), st(3)
        tmpm, tmpv, tmps, tmpq = (ln_tmp[:, i * NT:i * NT + ntiles] for i in range(4))
        ve = self.s_ve
        p.emit("vector", lambda e: e.tensor_scalar(out=tmpm, in0=allsum, scalar1=1.0 / D, scalar2=None, op0=ALU.mult),
               waits=[(self.s_ac, ac_sq)])
        p.emit("vector", lambda e: e.tensor_scalar(out=tmpv, in0=allssq, scalar1=1.0 / D, scalar2=LNEPS, op0=ALU.mult, op1=ALU.add))
        p.emit("vector", lambda e: e.tensor_tensor(out=tmpq, in0=tmpm, in1=tmpm, op=ALU.mult))
        p.emit_i("vector", lambda e: e.tensor_tensor(out=tmpv, in0=tmpv, in1=tmpq, op=ALU.subtract), inc_sem=ve)
        ve_v = p.cur(ve)
        p.emit_i("scalar", lambda e: e.activation(out=tmps, in_=tmpv, func=AF.Sqrt),
                 waits=[(ve, ve_v)], inc_sem=self.s_ac)
        ac_s = p.cur(self.s_ac)
        p.emit("vector", lambda e: e.reciprocal(out=allinv, in_=tmps), waits=[(self.s_ac, ac_s)])
        p.emit_i("vector", lambda e: e.scalar_tensor_tensor(out=allnb, in0=tmpm, scalar=-1.0, in1=allinv,
                                                            op0=ALU.mult, op1=ALU.mult), inc_sem=ve)
        ve_stats = p.cur(ve)
        for it in range(ntiles):
            src = x_ap_fn(it)
            dst = hf_dst_fn(it)
            c0 = stat_base + it * 8
            p.emit_i("scalar", lambda e, s=src, o=dst, sc=stats[:, c0 + 2:c0 + 3], b=stats[:, c0 + 3:c0 + 4]:
                     e.activation(out=o, in_=s, func=AF.Identity, scale=sc, bias=b),
                     waits=([(ve, ve_stats)] if it == 0 else []), inc_sem=self.s_ac)
        return p.cur(self.s_ac)

    def emit_ln_stream(self, x_ap_fn, stat_base, wait_first, keep_hf_dbg=False, store_dram=None):
        p, sb = self.p, self.sb
        hfT = sb["hfT"]
        roll = sb["hf"]
        tile_done = []
        ac_stats = self.emit_ln_stats(x_ap_fn, stat_base, wait_first)
        stats = sb["stats"]
        for it in range(NT):
            slot = roll[:, (it % 4) * D:(it % 4 + 1) * D]
            c0 = stat_base + it * 8
            wv = []
            if it == 0:
                wv.append((self.s_ve, ac_stats))
            if it >= 4:
                wv.append((self.s_ve, tile_done[it - 4]))
            p.emit_i("scalar", lambda e, s=x_ap_fn(it), o=slot,
                     sc=stats[:, c0 + 2:c0 + 3], b=stats[:, c0 + 3:c0 + 4]:
                     e.activation(out=o, in_=s, func=AF.Identity, scale=sc, bias=b),
                     waits=wv, inc_sem=self.s_ac)
            ac_ap = p.cur(self.s_ac)
            if keep_hf_dbg and "dbg_hf" in self.dram:
                tgt = self.dram["dbg_hf"].rearrange("(n p) d -> n p d", p=128)[it]
                p.emit_i("sync", lambda e, t=tgt, s=slot: e.dma_start(out=t, in_=s),
                         waits=[(self.s_ac, ac_ap)], inc_sem=self.s_st, inc_n=16)
            if store_dram is not None:
                tgt = self.dram[store_dram].rearrange("(n p) d -> n p d", p=128)[it]
                p.emit_i("sync", lambda e, t=tgt, s=slot: e.dma_start(out=t, in_=s),
                         waits=[(self.s_ac, ac_ap)], inc_sem=self.s_st, inc_n=16)
            for kc in range(8):
                base = 256 * (kc % 4) + (kc // 4)
                srcap = slot[:, base: base + 256 - (base % 2): 2]
                dst = hfT[:, kc * TPC + it * 128: kc * TPC + (it + 1) * 128]
                self.emit_transpose(srcap, dst,
                                    [(self.s_ac, ac_ap)] if kc == 0 else [])
            tile_done.append(p.cur(self.s_ve))
        return p.cur(self.s_ve)

    def emit_ln_stats(self, x_ap_fn, stat_base, wait_first, ntiles=NT):
        p, sb = self.p, self.sb
        stats, ln_tmp, sq = sb["stats"], sb["ln_tmp"], self.carve("sq_scr")
        for it in range(ntiles):
            src = x_ap_fn(it)
            c0 = stat_base + it * 8
            p.emit("vector", lambda e, s=src, o=stats[:, c0:c0 + 1]: e.reduce_sum(out=o, in_=s, axis=AX.X),
                   waits=(list(wait_first) if it == 0 else []))
            p.emit_i("scalar", lambda e, s=src, o=stats[:, c0 + 1:c0 + 2]: e.activation(
                out=sq[:], in_=s, func=AF.Square, accum_out=o),
                waits=(list(wait_first) if it == 0 else []), inc_sem=self.s_ac)
        ac_sq = p.cur(self.s_ac)
        st = lambda c: stats[:, stat_base + c:stat_base + ntiles * 8:8]
        allsum, allssq, allinv, allnb = st(0), st(1), st(2), st(3)
        tmpm, tmpv, tmps, tmpq = (ln_tmp[:, i * NT:i * NT + ntiles] for i in range(4))
        ve = self.s_ve
        p.emit("vector", lambda e: e.tensor_scalar(out=tmpm, in0=allsum, scalar1=1.0 / D, scalar2=None, op0=ALU.mult),
               waits=[(self.s_ac, ac_sq)])
        p.emit("vector", lambda e: e.tensor_scalar(out=tmpv, in0=allssq, scalar1=1.0 / D, scalar2=LNEPS, op0=ALU.mult, op1=ALU.add))
        p.emit("vector", lambda e: e.tensor_tensor(out=tmpq, in0=tmpm, in1=tmpm, op=ALU.mult))
        p.emit_i("vector", lambda e: e.tensor_tensor(out=tmpv, in0=tmpv, in1=tmpq, op=ALU.subtract), inc_sem=ve)
        ve_v = p.cur(ve)
        p.emit_i("scalar", lambda e: e.activation(out=tmps, in_=tmpv, func=AF.Sqrt),
                 waits=[(ve, ve_v)], inc_sem=self.s_ac)
        ac_s = p.cur(self.s_ac)
        p.emit("vector", lambda e: e.reciprocal(out=allinv, in_=tmps), waits=[(self.s_ac, ac_s)])
        p.emit_i("vector", lambda e: e.scalar_tensor_tensor(out=allnb, in0=tmpm, scalar=-1.0, in1=allinv,
                                                            op0=ALU.mult, op1=ALU.mult), inc_sem=ve)
        return p.cur(ve)

    def emit_routing(self, substage=99):
        p, sb, ps = self.p, self.sb, self.ps
        s_mm, s_ve, s_ac, s_in = self.s_mm, self.s_ve, self.s_ac, self.s_in
        hfT, gwT = sb["hfT"], sb["gwT"]
        if substage <= 0:
            return None
        for ncol in range(TPC // 512):
            pst = self.bank(2, 512)[0:64, :]
            for kc in range(8):
                lhs = gwT[:, kc * E:(kc + 1) * E]
                rhs = hfT[:, kc * TPC + ncol * 512: kc * TPC + ncol * 512 + 512]
                w = []
                if kc == 0:
                    w = [(s_ve, self.ve_hfT), (s_in, self.consts_loaded), (self.s_po, self.gw_loaded)]
                    if ncol > 0:
                        w.append((s_ve, self.p.cur(s_ve)))
                p.emit_i("tensor", lambda e, o=pst, l=lhs, r=rhs, k=kc: e.matmul(
                    out=o, lhsT=l, rhs=r, start=(k == 0), stop=(k == 7)),
                    waits=w, inc_sem=(s_mm if kc == 7 else None))
            mmv = p.cur(s_mm)
            dst = self.carve("logitsT")[:, ncol * 512:(ncol + 1) * 512]
            p.emit_i("vector", lambda e, s=pst, o=dst: e.tensor_copy(out=o, in_=s),
                     waits=[(s_mm, mmv)], inc_sem=s_ve)
        ve_lg = p.cur(s_ve)
        if substage <= 1:
            return None
        for it in range(NT):
            src = self.carve("logitsT")[:, it * 128:(it + 1) * 128]
            dst = self.carve("lgtile")[:, it * E:(it + 1) * E]
            self.emit_transpose(src, dst, [(s_ve, ve_lg)] if it == 0 else [],
                                f32=True, ident=self.identf_sb[0:64, 0:64])
        if substage <= 2:
            return None
        m8, idx8, idxf, w4 = sb["m8"], sb["idx8"], sb["idxf"], sb["w4"]
        ohr, oh, rtmp = self.carve("ohr_c"), sb["oh"], sb["rtmp"]
        for it in range(NT):
            lg = self.carve("lgtile")[:, it * E:(it + 1) * E]
            m8t = m8[:, it * 8:(it + 1) * 8]
            id8t = idx8[:, it * 8:(it + 1) * 8]
            idft = idxf[:, it * K4:(it + 1) * K4]
            w4t = w4[:, it * K4:(it + 1) * K4]
            sc0 = rtmp[:, it * 8:it * 8 + 1]
            sc1 = rtmp[:, it * 8 + 1:it * 8 + 2]
            p.emit("vector", lambda e, s=lg, o=m8t: e.max(out=o, in_=s))
            p.emit("vector", lambda e, s=lg, m=m8t, o=id8t: e.max_index(out=o, in_max=m, in_values=s))
            p.emit("vector", lambda e, s=id8t, o=idft: e.tensor_copy(out=o, in_=s[:, 0:K4]))
            p.emit_i("vector", lambda e, s=m8t, o=sc0: e.tensor_scalar(
                out=o, in0=s[:, 0:1], scalar1=-1.0, scalar2=None, op0=ALU.mult), inc_sem=s_ve)
            vev = p.cur(s_ve)
            p.emit_i("scalar", lambda e, s=m8t, o=w4t, b=sc0: e.activation(
                out=o, in_=s[:, 0:K4], func=AF.Exp, bias=b, scale=1.0),
                waits=[(s_ve, vev)], inc_sem=s_ac)
            acv = p.cur(s_ac)
            p.emit("vector", lambda e, s=w4t, o=sc1: e.reduce_sum(out=o, in_=s, axis=AX.X),
                   waits=[(s_ac, acv)])
            p.emit("vector", lambda e, s=sc1, o=sc1: e.reciprocal(out=o, in_=s))
            p.emit("vector", lambda e, s=w4t, o=w4t, sc=sc1: e.tensor_scalar(
                out=o, in0=s, scalar1=sc, scalar2=None, op0=ALU.mult))
            for r in range(K4):
                ohrt = ohr[:, (it * K4 + r) * E:(it * K4 + r + 1) * E]
                p.emit("vector", lambda e, o=ohrt, idc=idft[:, r:r + 1]: e.tensor_scalar(
                    out=o, in0=sb["iota_e_sb"][:], scalar1=idc, scalar2=None, op0=ALU.is_equal))
            oht = oh[:, it * E:(it + 1) * E]
            p.emit("vector", lambda e, o=oht, a=ohr[:, (it * K4 + 0) * E:(it * K4 + 1) * E],
                   b=ohr[:, (it * K4 + 1) * E:(it * K4 + 2) * E]: e.tensor_tensor(out=o, in0=a, in1=b, op=ALU.add))
            p.emit("vector", lambda e, o=oht, a=ohr[:, (it * K4 + 2) * E:(it * K4 + 3) * E]:
                   e.tensor_tensor(out=o, in0=o, in1=a, op=ALU.add))
            p.emit_i("vector", lambda e, o=oht, a=ohr[:, (it * K4 + 3) * E:(it * K4 + 4) * E]:
                     e.tensor_tensor(out=o, in0=o, in1=a, op=ALU.add), inc_sem=s_ve)
        self.ve_oh = p.cur(s_ve)
        if substage <= 3:
            return None
        for it in range(NT):
            oht = oh[:, it * E:(it + 1) * E]
            w = [(s_ve, self.ve_oh)] if it == 0 else []
            p.emit_i("tensor", lambda e, r=oht, k=it: e.matmul(
                out=self.bank(7, 64), lhsT=self.allones, rhs=r, start=(k == 0), stop=(k == NT - 1)),
                waits=w, inc_sem=(s_mm if it == NT - 1 else None))
        mmv = p.cur(s_mm)
        cntrow = sb["bownrow"]
        p.emit_i("vector", lambda e: e.tensor_copy(out=sb["rtmp"][0:1, 0:E], in_=self.bank(7, 64)[0:1, :]),
                 waits=[(s_mm, mmv)], inc_sem=s_ve)
        ve_cnt = p.cur(s_ve)
        p.emit_i("sync", lambda e: e.dma_start(out=self.dram["cnt_src"], in_=sb["rtmp"][0:1, 0:E]),
                 waits=[(s_ve, ve_cnt)], inc_sem=self.s_st, inc_n=16)
        return p.cur(self.s_st)



    def emit_counts_exchange(self, st_cnt):
        p, sb, dram = self.p, self.sb, self.dram
        s_cc, s_ve = self.s_cc, self.s_ve
        cc = p.emit_i("gpsimd", lambda e: e.collective_compute(
            "AllGather", ALU.bypass, replica_groups=[list(range(NCORES))],
            ins=[dram["cnt_src"].opt()], outs=[dram["cnt_all"].opt()]),
            waits=[(self.s_st, st_cnt)], inc_sem=s_cc, kind="dma")
        cntrows = sb["cntrows"][0:1, :]
        p.emit_i("sync", lambda e: e.dma_start(
            out=cntrows, in_=dram["cnt_all"].rearrange("c e -> (c e)").unsqueeze(0)),
            waits=[(s_cc, cc)], inc_sem=self.s_in, inc_n=16)
        ld = p.cur(self.s_in)
        bown = sb["bownrow"][0:1, :]
        pm = sb["prevmask_sb"]
        p.emit("vector", lambda e: e.memset(bown, 0.0), waits=[(self.s_in, ld)])
        for c in range(NCORES):
            row = cntrows[:, c * E:(c + 1) * E]
            p.emit("vector", lambda e, a=row, m=pm[0:1, c:c + 1]: e.scalar_tensor_tensor(
                out=bown, in0=a, scalar=m, in1=bown, op0=ALU.mult, op1=ALU.add))
        p.emit_i("vector", lambda e: e.nop(), inc_sem=s_ve)
        return p.cur(s_ve)

    def emit_gpos_scatter(self, ve_bown):
        p, sb, ps, dram = self.p, self.sb, self.ps, self.dram
        s_mm, s_ve, s_po, s_cc = self.s_mm, self.s_ve, self.s_po, self.s_cc
        oh, ohr, idxf, w4 = sb["oh"], self.carve("ohr_c"), sb["idxf"], sb["w4"]
        zt2 = dram["g_table"].rearrange("(p n) two -> p (n two)", p=128)
        zv = p.emit_i("vector", lambda e: e.memset(self.carve("gath_f32")[:, 0:E * CAP * 2 // 128], 0.0),
                      waits=[(self.s_po, p.cur(self.s_po))], inc_sem=self.s_ve)
        p.emit_i("sync", lambda e: e.dma_start(out=zt2, in_=self.carve("gath_f32")[:, 0:E * CAP * 2 // 128]),
                 waits=[(self.s_ve, zv)], inc_sem=self.s_st, inc_n=16)
        st_zero = p.cur(self.s_st)

        bown_bc = sb["bownbc"]
        p.emit_i("sync", lambda e: e.dma_start(out=dram["bown_dram"], in_=sb["bownrow"][0:1, :]),
                 waits=[(s_ve, ve_bown)], inc_sem=self.s_st, inc_n=16)
        st_b1 = p.cur(self.s_st)
        p.emit_i("sync", lambda e: e.dma_start(out=bown_bc[:], in_=dram["bown_dram"].broadcast_to([128, E])),
                 waits=[(self.s_st, st_b1)], inc_sem=self.s_st, inc_n=16)
        st_bbc = p.cur(self.s_st)
        gpos = sb["pos4"]
        pe_sb = sb["petile"]
        ve_pe_prev = None
        for it in range(NT):
            oht = oh[:, it * E:(it + 1) * E]
            w = [] if ve_pe_prev is None else [(s_ve, ve_pe_prev)]
            p.emit_i("tensor", lambda e, r=oht: e.matmul(out=self.bank(6, 64), lhsT=self.tri_excl,
                                                         rhs=r, start=True, stop=True),
                     waits=w, inc_sem=s_mm)
            mmP = p.cur(s_mm)
            wv = [(s_mm, mmP)]
            if it == 0:
                wv.append((self.s_st, st_bbc))
            p.emit("vector", lambda e: e.tensor_tensor(out=pe_sb[:, 0:E], in0=self.bank(6, 64),
                                                       in1=bown_bc[:], op=ALU.add),
                   waits=wv)
            if it > 0:
                p.emit_i("vector", lambda e: e.tensor_tensor(out=pe_sb[:, 0:E], in0=pe_sb[:, 0:E],
                                                             in1=self.bank(7, 64), op=ALU.add), inc_sem=s_ve)
            else:
                p.emit_i("vector", lambda e: e.nop(), inc_sem=s_ve)
            ve_pe = p.cur(s_ve)
            ve_pe_prev = ve_pe
            if it < NT - 1:
                p.emit_i("tensor", lambda e, r=oht, first=(it == 0), last=(it == NT - 2): e.matmul(
                    out=self.bank(7, 64), lhsT=self.allones, rhs=r, start=first, stop=last,
                    skip_group_check=True),
                         waits=[(s_ve, ve_pe)], inc_sem=s_mm)
            for r in range(K4):
                ohrt = ohr[:, (it * K4 + r) * E:(it * K4 + r + 1) * E]
                gcol = gpos[:, it * K4 + r:it * K4 + r + 1]
                scr = self.carve("lgtile")[:, 0:E]
                p.emit("vector", lambda e, a=pe_sb[:, 0:E], b=ohrt, o=scr:
                       e.tensor_tensor(out=o, in0=a, in1=b, op=ALU.mult))
                p.emit("vector", lambda e, s=scr, o=gcol: e.reduce_sum(out=o, in_=s, axis=AX.X))
            p.emit_i("vector", lambda e: e.engine_nop(), inc_sem=s_ve)
        ve_gpos = p.cur(s_ve)

        sidx, pairs, rtmp = sb["sidx"], sb["pairs"], sb["rtmp"]
        tok0 = sb["tokid_sb"][:, 0:1]
        idx8 = sb["idx8"]
        bscr = sb["rtmp"].bitcast(U32)
        for it in range(NT):
            i0 = it * K4
            eu = idx8[:, it * 8: it * 8 + K4]
            gp = gpos[:, i0:i0 + K4]
            fidx = sb["fidx"][:, i0:i0 + K4]
            b1, b2, b3 = bscr[:, 0:K4], bscr[:, K4:2 * K4], bscr[:, 2 * K4:3 * K4]
            p.emit("vector", lambda e, o=b1, a=eu: e.tensor_scalar(
                out=o, in0=a, scalar1=3, scalar2=10, op0=ALU.logical_shift_right, op1=ALU.logical_shift_left),
                waits=([(s_ve, ve_gpos)] if it == 0 else []))
            p.emit("vector", lambda e, o=b2, a=eu: e.tensor_scalar(
                out=o, in0=a, scalar1=3, scalar2=8, op0=ALU.bitwise_and, op1=ALU.logical_shift_left))
            p.emit("vector", lambda e, o=b3, a=eu: e.tensor_scalar(
                out=o, in0=a, scalar1=4, scalar2=11, op0=ALU.bitwise_and, op1=ALU.logical_shift_left))
            p.emit("vector", lambda e, o=b1: e.tensor_tensor(out=o, in0=o, in1=b2, op=ALU.add))
            p.emit("vector", lambda e, o=b1: e.tensor_tensor(out=o, in0=o, in1=b3, op=ALU.add))
            p.emit("vector", lambda e, o=fidx, a=b1: e.tensor_copy(out=o, in_=a))
            p.emit("vector", lambda e, o=fidx, b=gp: e.tensor_tensor(out=o, in0=o, in1=b, op=ALU.add))
            mcol = sb["fidx"][:, NT * K4 + i0: NT * K4 + i0 + K4]
            p.emit("vector", lambda e, o=mcol, a=gp: e.tensor_scalar(
                out=o, in0=a, scalar1=float(CAP), scalar2=float(BIGIDX), op0=ALU.is_ge, op1=ALU.mult))
            p.emit("vector", lambda e, o=fidx, a=mcol: e.tensor_tensor(out=o, in0=o, in1=a, op=ALU.add))
            p.emit("vector", lambda e, o=sidx[:, i0:i0 + K4], a=fidx: e.tensor_copy(out=o, in_=a))
            for r in range(K4):
                pc = pairs[:, (i0 + r) * 2:(i0 + r) * 2 + 1]
                p.emit("vector", lambda e, o=pc, a=tok0, rr=r, itt=it: e.tensor_scalar(
                    out=o, in0=a, scalar1=4.0, scalar2=float((itt * 128) * 4 + rr + 1),
                    op0=ALU.mult, op1=ALU.add))
            wv = pairs[:, i0 * 2:(i0 + K4) * 2].rearrange("p (k two) -> p k two", two=2)[:, :, 1:2]
            p.emit_i("vector", lambda e, o=wv, a=w4[:, i0:i0 + K4]: e.tensor_scalar(
                out=o, in0=a.unsqueeze(2), scalar1=0.5, scalar2=None, op0=ALU.mult), inc_sem=s_ve)
        ve_pairs = p.cur(s_ve)

        po_w = [(s_ve, ve_pairs), (self.s_st, st_zero)]
        first = True
        nsc = 0
        for it in range(NT):
            for r in range(K4):
                col = it * K4 + r
                idxap = sidx[:, col:col + 1]
                datap = pairs[:, col * 2:col * 2 + 2]
                w = list(po_w) if first else []
                if nsc >= 8:
                    w.append((s_po, p.cur(s_po) - 7 * 16))
                p.emit_i("gpsimd", lambda e, ia=idxap, da=datap: e.indirect_dma_start(
                    out=dram["g_table"], out_offset=IndirectOffsetOnAxis(ia, 0),
                    in_=da, in_offset=None,
                    bounds_check=self._bc["r"], oob_is_err=False),
                    waits=w, inc_sem=s_po, inc_n=16, kind="dma")
                first = False
                nsc += 1
        po_sct = p.cur(s_po)
        cc = p.emit_i("gpsimd", lambda e: e.collective_compute(
            "AllReduce", ALU.add, replica_groups=[list(range(NCORES))],
            ins=[dram["g_table"].opt()], outs=[dram["g_table_all"].opt()]),
            waits=[(s_po, po_sct)], inc_sem=self.s_cc, kind="dma")
        return cc


    def bank(self, b, cols=512):
        return self.ps["pb"][:, b * 512:b * 512 + cols]

    def bankbf(self, b, cols=1024):
        return self.ps["pb"].bitcast(BF16)[:, b * 1024:b * 1024 + cols]

    def emit_transpose(self, srcap, dst, first_waits=(), f32=False, ident=None):
        p = self.p
        bidx = self._tb_n % 2
        ncols = srcap.shape[0]
        if f32:
            pst = self.bank(bidx, 512)[:, 0:ncols]
        else:
            pst = self.bankbf(bidx, 1024)[:, 0:ncols]
        wv = list(first_waits)
        if self._tb_evac[bidx] is not None:
            wv.append((self.s_ve, self._tb_evac[bidx]))
        kk = srcap.shape[0] if hasattr(srcap, "shape") else 128
        idt = ident if ident is not None else (self.identf_sb[0:kk, 0:kk] if f32 else self.ident[0:kk, 0:kk])
        p.emit_i("tensor", lambda e, o=pst, s=srcap, ii=idt: e.transpose(out=o, in_=s, identity=ii),
                 waits=wv, inc_sem=self.s_mm)
        mmv = p.cur(self.s_mm)
        tgt = p.emit_i("vector", lambda e, s=pst, o=dst: e.tensor_copy(out=o, in_=s),
                       waits=[(self.s_mm, mmv)], inc_sem=self.s_ve)
        self._tb_evac[bidx] = tgt
        self._tb_n += 1
        return tgt

    def emit_dequant_matrix(self, mat, j, mm_mark=None, scr_mark=None):
        p, sb, dram = self.p, self.sb, self.dram
        s_wt, s_ve = self.s_wt, self.s_ve
        width = I if mat in ("gate", "up") else D
        ntile = (D // 2 // 128) if mat in ("gate", "up") else (I // 2 // 128)
        if j < 0:
            pT, sT = dram[f"sh_{mat}_pT"], dram[f"sh_{mat}_sT"]
        else:
            pT, sT = dram[f"{mat}_pT"][j], dram[f"{mat}_sT"][j]
        wlo, whi = self.carve(f"w_{mat}_lo"), self.carve(f"w_{mat}_hi")
        if mm_mark is None:
            mm_mark = p.cur(self.s_mm)
        for dt in range(ntile):
            par = self._deq_par
            self._deq_par ^= 1
            pk_t = self.carve("pk_t2" if par else "pk_t")[:, 0:width]
            st_t = self.carve("st_t2" if par else "st_t")[:, 0:width]
            s_ld = self.s_wt2 if par else s_wt
            wv = [(s_ve, self._deq_free[par])] + ([(self.s_mm, mm_mark)] if dt == 0 else [])
            if self._mk_ld_uses > 0:
                wv = wv + self._mk_ld_extra
                self._mk_ld_uses -= 1
            p.emit_i("sync", lambda e, t=pk_t, s=pT[dt * 128:(dt + 1) * 128, :]: e.dma_start(out=t, in_=s),
                     waits=wv, inc_sem=s_ld, inc_n=16)
            p.emit_i("sync", lambda e, t=st_t, s=sT[dt * 2:dt * 2 + 2, :].unsqueeze(1).broadcast_to([2, 64, width]):
                     e.dma_start(out=t, in_=s), inc_sem=s_ld, inc_n=16)
            ld = p.cur(s_ld)
            for plane, wbuf in (("lo", wlo), ("hi", whi)):
                wt = wbuf[:, dt * width:(dt + 1) * width]
                u = self.carve("lgtile").bitcast(I16)[:, 0:width]
                g2 = pk_t.bitcast(BF16) if plane == "hi" else None
                sg = self.carve("ohr_c").bitcast(I16)[:, 0:width]
                m1, s1 = (7, 6) if plane == "lo" else (0x70, 2)
                m2, s2 = (8, 12) if plane == "lo" else (0x80, 8)
                wv = [(s_ld, ld)] if plane == "lo" else []
                if dt == 0 and plane == "lo" and scr_mark is not None:
                    wv = wv + [(self.s_mm, scr_mark)]
                    scr_mark = None
                p.emit("vector", lambda e, b=pk_t, o=sg, mm=m2, ss=s2: e.tensor_scalar(
                    out=o, in0=b, scalar1=mm, scalar2=ss, op0=ALU.bitwise_and, op1=ALU.logical_shift_left),
                    waits=wv)
                p.emit("vector", lambda e, o=sg: e.tensor_scalar(
                    out=o, in0=o, scalar1=0x3F80, scalar2=None, op0=ALU.add))
                p.emit("vector", lambda e, o=sg.bitcast(BF16), s=st_t: e.tensor_tensor(
                    out=o, in0=o, in1=s, op=ALU.mult))
                p.emit("vector", lambda e, b=pk_t, o=u, mm=m1, ss=s1: e.tensor_scalar(
                    out=o, in0=b, scalar1=mm, scalar2=ss, op0=ALU.bitwise_and, op1=ALU.logical_shift_left))
                p.emit("vector", lambda e, o=u: e.tensor_scalar(
                    out=o, in0=o, scalar1=0x3F00, scalar2=None, op0=ALU.add))
                gdst = (pk_t if plane == "hi" else self.carve("lgtile").bitcast(I16)[:, 1024:1024 + width])
                p.emit("vector", lambda e, o=gdst.bitcast(BF16), f=u.bitcast(BF16): e.tensor_scalar(
                    out=o, in0=f, scalar1=2.0, scalar2=-1.0, op0=ALU.mult, op1=ALU.add))
                p.emit("vector", lambda e, o=u.bitcast(BF16), f=gdst.bitcast(BF16): e.tensor_tensor(
                    out=o, in0=o, in1=f, op=ALU.min))
                p.emit_i("vector", lambda e, o=wt, a=u.bitcast(BF16), b2=sg.bitcast(BF16):
                         e.tensor_tensor(out=o, in0=a, in1=b2, op=ALU.mult), inc_sem=s_ve)
            self._deq_free[par] = p.cur(s_ve)
        return p.cur(s_ve)

    def emit_expert(self, call, j):
        p, sb, ps, dram = self.p, self.sb, self.ps, self.dram
        s_mm, s_ve, s_ac, s_po, s_st = self.s_mm, self.s_ve, self.s_ac, self.s_po, self.s_st
        mk_gu_prev, mk_dn_prev = self._mk_gu, self._mk_dn
        mk_ebT_prev, mk_gath_prev = self._mk_ebT, self._mk_gath
        scr0 = self._mk_scr
        self._mk_scr = None

        for h in range(2):
            icol = sb["eidx"][:, j * 2 + h:j * 2 + h + 1]
            p.emit("vector", lambda e, o=sb["fidx"][:, 0:1],
                   off=float((j & 3) * 256 + h * 128 + (8192 if j >= 4 else 0)):
                   e.tensor_scalar(out=o, in0=sb["expbase_sb"][:, 0:1], scalar1=off, scalar2=None, op0=ALU.add))
            p.emit_i("vector", lambda e, o=icol: e.tensor_copy(out=o, in_=sb["fidx"][:, 0:1]), inc_sem=s_ve)
            vei = p.cur(s_ve)
            gp = sb["gpair"][:, (j * 2 + h) * 2:(j * 2 + h) * 2 + 2]
            p.emit_i("gpsimd", lambda e, ia=icol, o=gp: e.indirect_dma_start(
                out=o, out_offset=None, in_=dram["g_table_all"],
                in_offset=IndirectOffsetOnAxis(ia, 0),
                bounds_check=self._bc["r"], oob_is_err=False),
                waits=[(s_ve, vei), (self.s_cc, p.cur(self.s_cc)), (s_ve, p.cur(s_ve))],
                inc_sem=s_po, inc_n=16, kind="dma")
            pog = p.cur(s_po)
            tk = sb["tokcol"][:, j * 2 + h:j * 2 + h + 1]
            p.emit("vector", lambda e, o=sb["iscr"][:, 0:1], a=gp: e.tensor_copy(out=o, in_=a[:, 0:1]),
                   waits=[(s_po, pog)])
            p.emit("vector", lambda e, o=sb["iscr"][:, 0:1]: e.tensor_scalar(
                out=o, in0=o, scalar1=-1, scalar2=None, op0=ALU.add))
            p.emit("vector", lambda e, o=sb["iscr"][:, 0:1]: e.tensor_scalar(
                out=o, in0=o, scalar1=2, scalar2=None, op0=ALU.arith_shift_right))
            p.emit("vector", lambda e, o=sb["iscr"][:, 1:2], a=gp: e.tensor_scalar(
                out=o, in0=a[:, 0:1], scalar1=1.0, scalar2=float(BIGIDX), op0=ALU.is_lt, op1=ALU.mult))
            p.emit_i("vector", lambda e, o=tk: e.tensor_tensor(
                out=o, in0=sb["iscr"][:, 0:1], in1=sb["iscr"][:, 1:2], op=ALU.add), inc_sem=s_ve)
            vet = p.cur(s_ve)
            gbuf = self.carve("gath_f32")[:] if call == 1 else self.carve("gath_f32")[:].bitcast(BF16)[:, 0:2048]
            gb = gbuf[:, h * D:(h + 1) * D]
            srcrows = dram["x0_full"] if call == 1 else dram["hf2_all"]
            wv = [(s_ve, vet), (s_po, p.cur(s_po)), (s_ac, mk_gath_prev), (s_mm, mk_dn_prev)]
            p.emit_i("gpsimd", lambda e, ia=tk, o=gb, s=srcrows: e.indirect_dma_start(
                out=o, out_offset=None, in_=s, in_offset=IndirectOffsetOnAxis(ia, 0),
                bounds_check=self._bc["r"], oob_is_err=False),
                waits=wv, inc_sem=s_po, inc_n=16, kind="dma")
        po_rows = p.cur(s_po)

        lnw = [(s_po, po_rows), (self.s_mm, mk_ebT_prev)]
        if call == 1:
            ac_buf = self.emit_ln(lambda h: self.carve("gath_f32")[:, h * D:(h + 1) * D],
                                  lambda h: self.carve("ebuf")[:, h * D:(h + 1) * D], 0, lnw, ntiles=2)
        else:
            for h in range(2):
                p.emit_i("scalar", lambda e, o=self.carve("ebuf")[:, h * D:(h + 1) * D],
                         s=self.carve("gath_f32")[:].bitcast(BF16)[:, h * D:(h + 1) * D]: e.copy(out=o, in_=s),
                         waits=(lnw if h == 0 else []), inc_sem=s_ac)
            ac_buf = p.cur(s_ac)

        ve_g = self.emit_dequant_matrix("gate", j, mk_gu_prev, scr_mark=scr0)
        ve_u = self.emit_dequant_matrix("up", j, mk_gu_prev)

        for h in range(2):
            for pl in range(2):
                for k in range(4):
                    sb_ = h * D + 256 * k + pl
                    srcap = self.carve("ebuf")[:, sb_: sb_ + 256 - pl: 2]
                    dst = self.carve("ebufT")[:, ((pl * 4 + k) * 2 + h) * 128: ((pl * 4 + k) * 2 + h) * 128 + 128]
                    self.emit_transpose(srcap, dst,
                                        [(s_ac, ac_buf), (s_mm, mk_dn_prev)] if (h == 0 and pl == 0 and k == 0) else [])
        ve_ebT = p.cur(s_ve)
        self._mk_ebT = p.cur(s_mm)

        for h in range(2):
            for ic in range(2):
                grp = [(s_ve, max(ve_ebT, ve_g, ve_u)), (s_ac, p.cur(s_ac)), (s_ve, p.cur(s_ve))]
                for mi, bk in ((0, 2), (1, 3)):
                    n = 0
                    for pl in range(2):
                        wsrc = (self.carve("w_gate_lo"), self.carve("w_gate_hi"), self.carve("w_up_lo"), self.carve("w_up_hi"))[mi * 2 + pl]
                        for k in range(4):
                            lhs = self.carve("ebufT")[:, ((pl * 4 + k) * 2 + h) * 128: ((pl * 4 + k) * 2 + h) * 128 + 128]
                            rhs = wsrc[:, k * I + ic * 384: k * I + ic * 384 + 384]
                            w = grp if (n == 0) else []
                            p.emit_i("tensor", lambda e, l=lhs, r=rhs, st_=(n == 0), sp=(n == 7), b=bk:
                                     e.matmul(out=self.bank(b, 384), lhsT=l, rhs=r, start=st_, stop=sp),
                                     waits=w, inc_sem=(s_mm if n == 7 else None))
                            n += 1
                mmgu = p.cur(s_mm)
                sg_t = self.carve("hsil")[:, 0:384]
                p.emit_i("scalar", lambda e, o=sg_t: e.activation(out=o, in_=self.bank(2, 384), func=AF.Silu),
                         waits=[(s_mm, mmgu)], inc_sem=s_ac)
                acs = p.cur(s_ac)
                hu = self.carve("hU")[:, h * I + ic * 384: h * I + ic * 384 + 384]
                p.emit_i("vector", lambda e, o=hu, a=sg_t: e.tensor_tensor(
                    out=o, in0=a, in1=self.bank(3, 384), op=ALU.mult),
                    waits=[(s_ac, acs)], inc_sem=s_ve)
        ve_hu = p.cur(s_ve)
        self._mk_gu = p.cur(s_mm)
        ve_d = self.emit_dequant_matrix("down", j, mk_dn_prev)

        for h in range(2):
            for pl in range(2):
                for m in range(3):
                    srcap = self.carve("hU")[:, h * I + pl * 384 + m * 128: h * I + pl * 384 + (m + 1) * 128]
                    dst = self.carve("ebufT")[:, ((pl * 3 + m) * 2 + h) * 128: ((pl * 3 + m) * 2 + h) * 128 + 128]
                    self.emit_transpose(srcap, dst, [(s_ve, ve_hu)] if (h == 0 and pl == 0 and m == 0) else [])
        ve_huT = p.cur(s_ve)

        for h in range(2):
            grp = [(s_ve, max(ve_huT, ve_d)), (s_ac, p.cur(s_ac))]
            for nc_ in range(2):
                n = 0
                for pl in range(2):
                    wsrc = self.carve("w_down_lo") if pl == 0 else self.carve("w_down_hi")
                    for m in range(3):
                        lhs = self.carve("ebufT")[:, ((pl * 3 + m) * 2 + h) * 128: ((pl * 3 + m) * 2 + h) * 128 + 128]
                        rhs = wsrc[:, m * D + nc_ * 512: m * D + nc_ * 512 + 512]
                        w = grp if (n == 0 and nc_ == 0) else []
                        p.emit_i("tensor", lambda e, l=lhs, r=rhs, st_=(n == 0), sp=(n == 5), b=4 + nc_:
                                 e.matmul(out=self.bank(b, 512), lhsT=l, rhs=r, start=st_, stop=sp),
                                 waits=w, inc_sem=(s_mm if n == 5 else None))
                        n += 1
            mmd = p.cur(s_mm)
            wcol = sb["gpair"][:, (j * 2 + h) * 2 + 1:(j * 2 + h) * 2 + 2]
            ro = self.carve("retst")[:, 0:D]
            p.emit("scalar", lambda e, o=ro[:, 0:512], w_=wcol: e.activation(
                out=o, in_=self.bank(4, 512), func=AF.Identity, scale=w_),
                waits=[(s_mm, mmd), (s_st, p.cur(s_st))])
            p.emit_i("scalar", lambda e, o=ro[:, 512:1024], w_=wcol: e.activation(
                out=o, in_=self.bank(5, 512), func=AF.Identity, scale=w_), inc_sem=s_ac)
            aco = p.cur(s_ac)
            dst = dram["ret_src"].rearrange("(n p) d -> n p d", p=128)[j * 2 + h]
            p.emit_i("scalar", lambda e, o=dst, s=ro: e.dma_start(out=o, in_=s),
                     waits=[(s_ac, aco)], inc_sem=s_st, inc_n=16, kind="dma")
        self._mk_dn = p.cur(s_mm)
        self._mk_gath = ac_buf
        return p.cur(s_st)


    def emit_shared(self, ve_hfT):
        p, sb = self.p, self.sb
        s_mm, s_ve, s_ac = self.s_mm, self.s_ve, self.s_ac
        ve_g = self.emit_dequant_matrix("gate", -1, self._mk_gu)
        ve_u = self.emit_dequant_matrix("up", -1, self._mk_gu)
        ve_d = self.emit_dequant_matrix("down", -1, self._mk_dn)
        hfT = sb["hfT"]
        hsh = self.carve("h_shT")
        last_silu = {0: p.cur(s_ac), 1: p.cur(s_ac)}
        last_hu = {0: p.cur(s_ve), 1: p.cur(s_ve)}
        for tch in range(8):
            for itile in range(6):
                par = itile % 2
                bg, bu = (2, 3) if par == 0 else (6, 7)
                grp = [(s_ve, max(ve_hfT, ve_g, ve_u, last_hu[par])), (s_ac, last_silu[par])]
                for mi, bk in ((0, bg), (1, bu)):
                    n = 0
                    for pl in range(2):
                        wsrc = (self.carve("w_gate_lo"), self.carve("w_gate_hi"), self.carve("w_up_lo"), self.carve("w_up_hi"))[mi * 2 + pl]
                        for kc in range(4):
                            lhs = wsrc[:, kc * I + itile * 128: kc * I + (itile + 1) * 128]
                            rhs = hfT[:, (pl * 4 + kc) * TPC + tch * 256: (pl * 4 + kc) * TPC + (tch + 1) * 256]
                            w = grp if n == 0 else []
                            p.emit_i("tensor", lambda e, l=lhs, r=rhs, st_=(n == 0), sp=(n == 7), b=bk:
                                     e.matmul(out=self.bank(b, 256), lhsT=l, rhs=r, start=st_, stop=sp),
                                     waits=w, inc_sem=(s_mm if n == 7 else None))
                            n += 1
                mmgu = p.cur(s_mm)
                sg_t = self.carve("hsil")[:, par * 256: par * 256 + 256]
                p.emit_i("scalar", lambda e, o=sg_t, b=bg: e.activation(out=o, in_=self.bank(b, 256), func=AF.Silu),
                         waits=[(s_mm, mmgu), (s_ve, last_hu[par])], inc_sem=s_ac)
                acs = p.cur(s_ac)
                last_silu[par] = acs
                p.emit_i("vector", lambda e, o=hsh[:, itile * 256:(itile + 1) * 256], a=sg_t, b=bu:
                         e.tensor_tensor(out=o, in0=a, in1=self.bank(b, 256), op=ALU.mult),
                         waits=[(s_ac, acs)], inc_sem=s_ve)
                last_hu[par] = p.cur(s_ve)
            ve_h = p.cur(s_ve)
            for tt4 in range(2):
                it = tch * 2 + tt4
                grp = [(s_ve, ve_h), (s_ve, p.cur(s_ve))]
                for nc_ in range(2):
                    n = 0
                    for pl in range(2):
                        wsrc = self.carve("w_down_lo") if pl == 0 else self.carve("w_down_hi")
                        for m in range(3):
                            lhs = hsh[:, (pl * 3 + m) * 256 + tt4 * 128: (pl * 3 + m) * 256 + (tt4 + 1) * 128]
                            rhs = wsrc[:, m * D + nc_ * 512: m * D + nc_ * 512 + 512]
                            w = (grp + [(s_ve, ve_d)]) if (n == 0 and nc_ == 0) else []
                            p.emit_i("tensor", lambda e, l=lhs, r=rhs, st_=(n == 0), sp=(n == 5), b=4 + nc_:
                                     e.matmul(out=self.bank(b, 512), lhsT=l, rhs=r, start=st_, stop=sp),
                                     waits=w, inc_sem=(s_mm if n == 5 else None))
                            n += 1
                mmd = p.cur(s_mm)
                for nc_ in range(2):
                    xs = sb["x"][:, it * D + nc_ * 512: it * D + nc_ * 512 + 512]
                    p.emit_i("vector", lambda e, o=xs, b=4 + nc_: e.scalar_tensor_tensor(
                        out=o, in0=self.bank(b, 512), scalar=0.5, in1=o, op0=ALU.mult, op1=ALU.add),
                        waits=([(s_mm, mmd)] if nc_ == 0 else []), inc_sem=s_ve)
        self._mk_gu = self._mk_dn = p.cur(s_mm)
        return p.cur(s_ve)

    def emit_combine(self, cc_ret, ve_mark=None, ac_mark=None, mm_mark=None):
        p, sb, dram = self.p, self.sb, self.dram
        s_ve, s_po, s_pc = self.s_ve, self.s_po, self.s_pc
        if ve_mark is None:
            ve_mark = p.cur(s_ve)
        if ac_mark is None:
            ac_mark = p.cur(self.s_ac)
        if mm_mark is None:
            mm_mark = p.cur(self.s_mm)
        ar = self.sb["arena"][0:128, :].bitcast(BF16)
        bufs = [self.carve("gath_f32")[:].bitcast(BF16),
                ar[:, 43008 // 2: 43008 // 2 + 4096]]
        last_add = {0: ve_mark, 1: ve_mark}
        first = True
        for it in range(NT):
            par = it % 2
            g4 = bufs[par]
            wvz = [(s_ve, last_add[par])]
            if first:
                wvz += [(self.s_ac, ac_mark), (s_po, p.cur(s_po)), (self.s_mm, mm_mark)]
                first = False
            pz = p.emit_i("gpsimd", lambda e, g=g4: e.memset(g[:, 0:4096], 0.0),
                          waits=wvz, inc_sem=s_pc, kind="dma")
            for r in range(K4):
                col = it * K4 + r
                idxap = sb["sidx"][:, col:col + 1]
                gb = g4[:, r * D:(r + 1) * D]
                wv = [(s_pc, pz), (self.s_cc, cc_ret)]
                p.emit_i("gpsimd", lambda e, ia=idxap, o=gb: e.indirect_dma_start(
                    out=o, out_offset=None, in_=dram["ret_all"],
                    in_offset=IndirectOffsetOnAxis(ia, 0),
                    bounds_check=self._bc["r"], oob_is_err=False),
                    waits=wv, inc_sem=s_po, inc_n=16, kind="dma")
            pog = p.cur(s_po)
            for r in range(K4):
                xs = sb["x"][:, it * D:(it + 1) * D]
                p.emit_i("vector", lambda e, o=xs, a=g4[:, r * D:(r + 1) * D]: e.tensor_tensor(
                    out=o, in0=o, in1=a, op=ALU.add),
                    waits=([(s_po, pog)] if r == 0 else []), inc_sem=s_ve)
            last_add[par] = p.cur(s_ve)
        return p.cur(s_ve)


    def emit_transpose_add(self, srcap, add_into, first_waits=()):
        p = self.p
        bidx = self._tb_n % 2
        pst = self.bankbf(bidx, 1024)[:, 0:srcap.shape[0]]
        wv = list(first_waits)
        if self._tb_evac[bidx] is not None:
            wv.append((self.s_ve, self._tb_evac[bidx]))
        p.emit_i("tensor", lambda e, o=pst, s=srcap: e.transpose(out=o, in_=s, identity=self.ident),
                 waits=wv, inc_sem=self.s_mm)
        mmv = p.cur(self.s_mm)
        tgt = p.emit_i("vector", lambda e, s=pst, o=add_into: e.tensor_tensor(out=o, in0=o, in1=s, op=ALU.add),
                       waits=[(self.s_mm, mmv)], inc_sem=self.s_ve)
        self._tb_evac[bidx] = tgt
        self._tb_n += 1
        return tgt

    def _w8(self, i):
        ap = self.sb["arena"][0:128, :].bitcast(BF16)
        if i < 2:
            off = self._ARENA_ATTN["wkb"][0]
            return ap[:, off // 2 + i * 1024: off // 2 + (i + 1) * 1024]
        if i < 4:
            off = self._ARENA_ATTN["rt_buf"][0]
            return ap[:, off // 2 + (i - 2) * 1024: off // 2 + (i - 1) * 1024]
        if i < 6:
            off = self._ARENA_ATTN["kh_buf"][0]
            return ap[:, off // 2 + (i - 4) * 1024: off // 2 + (i - 3) * 1024]
        off = self._ARENA_ATTN["khat_tm"][0]
        return ap[:, off // 2 + (i - 6) * 1024: off // 2 + (i - 5) * 1024]

    def emit_load_w8(self, wname, fold_tf):
        p, dram = self.p, self.dram
        wv = [(self.s_mm, p.cur(self.s_mm)), (self.s_ve, p.cur(self.s_ve)), (self.s_ac, p.cur(self.s_ac))]
        for kc in range(8):
            dst = self._w8(kc)
            p.emit_i("gpsimd", lambda e, o=dst, s=dram[wname][kc * 128:(kc + 1) * 128, :]:
                     e.dma_start(out=o, in_=s), waits=(wv if kc == 0 else []),
                     inc_sem=self.s_po, inc_n=16, kind="dma")
        pot = p.cur(self.s_po)
        if fold_tf:
            for kc in range(8):
                dst = self._w8(kc)
                p.emit_i("vector", lambda e, o=dst: e.tensor_tensor(out=o, in0=o, in1=self.carve("tfb"), op=ALU.mult),
                         waits=([(self.s_po, pot)] if kc == 0 else []), inc_sem=self.s_ve)
            return ("ve", p.cur(self.s_ve))
        return ("po", pot)

    def emit_attn_setup(self):
        p, sb, dram = self.p, self.sb, self.dram
        s_ve, s_ac, s_in, s_po = self.s_ve, self.s_ac, self.s_in, self.s_po
        cv = self.carve
        p.emit_i("gpsimd", lambda e: e.dma_start(out=cv("tfb"), in_=dram["tfirst"].broadcast_to([128, D])),
                 waits=[(s_ve, p.cur(s_ve)), (s_ac, p.cur(s_ac)), (self.s_mm, p.cur(self.s_mm))],
                 inc_sem=s_po, inc_n=16, kind="dma")
        lnw = cv("wCt")[:, 0:16]
        wC = cv("wCt")[:, 16:32]
        td = cv("wCt")[:, 32:48]
        wv0 = [(s_ve, p.cur(s_ve)), (s_ac, p.cur(s_ac)), (self.s_mm, p.cur(self.s_mm)),
               (self.s_st, p.cur(self.s_st))]
        p.emit_i("sync", lambda e: e.dma_start(out=td, in_=dram["tdecay64"]), waits=wv0, inc_sem=s_in, inc_n=16)
        ldt = p.cur(s_in)
        p.emit_i("scalar", lambda e: e.activation(out=lnw, in_=td, func=AF.Ln),
                 waits=[(s_in, ldt)], inc_sem=s_ac)
        ramp1 = cv("rt_buf").bitcast(F32)[:, 0:128]
        rampr = cv("rt_buf").bitcast(F32)[:, 128:256]
        p.emit_i("sync", lambda e: e.dma_start(out=ramp1, in_=dram["iota1"].broadcast_to([64, 128])), inc_sem=s_in, inc_n=16)
        p.emit_i("sync", lambda e: e.dma_start(out=rampr, in_=dram["iotar"].broadcast_to([64, 128])), inc_sem=s_in, inc_n=16)
        ldr = p.cur(s_in)
        scr = cv("kh_buf").bitcast(F32)[:, 0:128]
        acl = p.cur(s_ac)
        for h in range(H):
            p.emit("vector", lambda e, hh=h: e.tensor_scalar(
                out=scr, in0=ramp1, scalar1=lnw[:, hh:hh + 1], scalar2=None, op0=ALU.mult),
                waits=([(s_in, ldr), (s_ac, acl)] if h == 0 else [(s_ac, p.cur(s_ac))]))
            p.emit_i("vector", lambda e: e.nop(), inc_sem=s_ve)
            vv = p.cur(s_ve)
            p.emit_i("scalar", lambda e, hh=h: e.activation(out=cv("W1p")[:, hh * 128:(hh + 1) * 128], in_=scr, func=AF.Exp),
                     waits=[(s_ve, vv)], inc_sem=s_ac)
            p.emit("vector", lambda e, hh=h: e.tensor_scalar(
                out=scr, in0=rampr, scalar1=lnw[:, hh:hh + 1], scalar2=None, op0=ALU.mult),
                waits=[(s_ac, p.cur(s_ac))])
            p.emit_i("vector", lambda e: e.nop(), inc_sem=s_ve)
            vv = p.cur(s_ve)
            p.emit_i("scalar", lambda e, hh=h: e.activation(out=cv("W2p")[:, hh * 128:(hh + 1) * 128], in_=scr, func=AF.Exp),
                     waits=[(s_ve, vv)], inc_sem=s_ac)
        p.emit_i("vector", lambda e: e.tensor_scalar(out=cv("wCt")[:, 48:64], in0=lnw, scalar1=128.0, scalar2=None, op0=ALU.mult),
                 waits=[(s_ac, p.cur(s_ac))], inc_sem=s_ve)
        p.emit_i("scalar", lambda e, w=wC: e.activation(out=w, in_=cv("wCt")[:, 48:64], func=AF.Exp),
                 waits=[(s_ve, p.cur(s_ve))], inc_sem=s_ac)
        p.emit_i("sync", lambda e: e.dma_start(out=cv("lngb")[:, 0:16], in_=dram["decay_h"].broadcast_to([128, 16])), inc_sem=s_in, inc_n=16)
        ldg = p.cur(s_in)
        p.emit_i("scalar", lambda e: e.activation(out=cv("lngb")[:, 0:16], in_=cv("lngb")[:, 0:16], func=AF.Ln),
                 waits=[(s_in, ldg)], inc_sem=s_ac)
        acg = p.cur(s_ac)
        ioij = cv("wkb").bitcast(F32)[:, 0:128]
        p.emit_i("sync", lambda e: e.dma_start(out=ioij, in_=dram["iota_ij"]), inc_sem=s_in, inc_n=16)
        trit = cv("zmask").bitcast(U8)[:, 0:128]
        p.emit_i("sync", lambda e: e.dma_start(out=trit, in_=dram["tri_ij_u8"]), inc_sem=s_in, inc_n=16)
        ldij = p.cur(s_in)
        zt = cv("Am")[:, 0:128]
        p.emit("vector", lambda e: e.memset(zt, 0.0), waits=[(s_po, p.cur(s_po))])
        escr = cv("wkb").bitcast(F32)[:, 128:256]
        escb = cv("wkb")[:, 512:640]
        for h in range(H):
            p.emit("vector", lambda e, hh=h: e.tensor_scalar(
                out=escr, in0=ioij, scalar1=cv("lngb")[:, hh:hh + 1], scalar2=None, op0=ALU.mult),
                waits=([(s_in, ldij), (s_ac, acg)] if h == 0 else []))
            p.emit("vector", lambda e: e.tensor_scalar(
                out=escr, in0=escr, scalar1=0.0, scalar2=None, op0=ALU.min))
            p.emit_i("vector", lambda e: e.nop(), inc_sem=s_ve)
            vv = p.cur(s_ve)
            p.emit_i("scalar", lambda e: e.activation(out=escb, in_=escr, func=AF.Exp),
                     waits=[(s_ve, vv)], inc_sem=s_ac)
            ace = p.cur(s_ac)
            p.emit_i("vector", lambda e, hh=h: e.select(
                out=cv("Dmask")[:, hh * 128:(hh + 1) * 128], mask=trit, on_true=escb, on_false=zt),
                waits=[(s_ac, ace)], inc_sem=s_ve)
        p.emit("vector", lambda e: e.memset(cv("Sst")[:], 0.0))
        p.emit_i("vector", lambda e: e.memset(cv("Sbf")[:], 0.0), inc_sem=s_ve)
        return p.cur(s_ve)

    def emit_attn_quarter(self, q):
        p, sb = self.p, self.sb
        s_mm, s_ve, s_ac, s_po = self.s_mm, self.s_ve, self.s_ac, self.s_po
        cv = self.carve
        hfT = sb["hfT"]
        qoff = q * 512
        for wname, dstbuf, fold in (("wk_T", "kT", True), ("wr_T", "rT", False)):
            t = self.emit_load_w8(wname, fold)
            tw = [(self.s_ve, t[1])] if t[0] == "ve" else [(self.s_po, t[1])]
            for dks in range(8):
                for kc in range(8):
                    lhs = self._w8(kc)[:, dks * 128:(dks + 1) * 128]
                    rhs = hfT[:, kc * TPC + qoff: kc * TPC + qoff + 512]
                    wv = (list(tw) + [(s_ve, p.cur(s_ve)), (s_ac, p.cur(s_ac))]) if kc == 0 else []
                    p.emit_i("tensor", lambda e, l=lhs, r=rhs, st_=(kc == 0), sp=(kc == 7):
                             e.matmul(out=self.bank(2, 512), lhsT=l, rhs=r, start=st_, stop=sp),
                             waits=wv, inc_sem=(s_mm if kc == 7 else None))
                mmk = p.cur(s_mm)
                for par in range(2):
                    h = dks * 2 + par
                    dst = cv(dstbuf)[:, h * 512:(h + 1) * 512]
                    if dstbuf == "rT":
                        p.emit_i("scalar", lambda e, o=dst, pp=par: e.activation(
                            out=o, in_=self.bank(2, 512)[pp * 64:(pp + 1) * 64, :], func=AF.Sigmoid),
                            waits=[(s_mm, mmk)], inc_sem=s_ac)
                    else:
                        p.emit_i("vector", lambda e, o=dst, pp=par: e.tensor_copy(
                            out=o, in_=self.bank(2, 512)[pp * 64:(pp + 1) * 64, :]),
                            waits=[(s_mm, mmk)], inc_sem=s_ve)
        t = self.emit_load_w8("wv_T", False)
        tw = [(self.s_po, t[1])]
        for tt in range(4):
            for dvh in range(2):
                for kc in range(8):
                    lhs = hfT[:, kc * TPC + qoff + tt * 128: kc * TPC + qoff + (tt + 1) * 128]
                    rhs = self._w8(kc)[:, dvh * 512:(dvh + 1) * 512]
                    wv = (list(tw) + [(s_ve, p.cur(s_ve))]) if kc == 0 else []
                    p.emit_i("tensor", lambda e, l=lhs, r=rhs, st_=(kc == 0), sp=(kc == 7), b=4 + dvh:
                             e.matmul(out=self.bank(b, 512), lhsT=l, rhs=r, start=st_, stop=sp),
                             waits=wv, inc_sem=(s_mm if kc == 7 else None))
            mmv = p.cur(s_mm)
            for dvh in range(2):
                dst = cv("v_tm")[:, tt * 1024 + dvh * 512: tt * 1024 + (dvh + 1) * 512]
                p.emit_i("vector", lambda e, o=dst, b=4 + dvh: e.tensor_copy(out=o, in_=self.bank(b, 512)),
                         waits=[(s_mm, mmv)], inc_sem=s_ve)
        ve_proj = p.cur(s_ve)
        import os
        if q == 0 and os.environ.get("ATTNDBG"):
            self._dbg_dump([("dbg_v2", "v_tm")])
        for n in range(4):
            rview = cv("rT").rearrange("p (h t) -> p h t", h=H)[:, :, n * 128:(n + 1) * 128]
            kview = cv("kT").rearrange("p (h t) -> p h t", h=H)[:, :, n * 128:(n + 1) * 128]
            wv0 = [(s_ve, ve_proj), (s_mm, p.cur(s_mm)), (s_ac, p.cur(s_ac))]
            p.emit("vector", lambda e, i0=rview: e.tensor_tensor(
                out=cv("rt_buf").rearrange("p (h t) -> p h t", h=H), in0=i0,
                in1=cv("W1p").rearrange("p (h t) -> p h t", h=H), op=ALU.mult),
                waits=wv0)
            p.emit_i("vector", lambda e, i0=kview: e.tensor_tensor(
                out=cv("kh_buf").rearrange("p (h t) -> p h t", h=H), in0=i0,
                in1=cv("W2p").rearrange("p (h t) -> p h t", h=H), op=ALU.mult), inc_sem=s_ve)
            ve_sc = p.cur(s_ve)
            if n == 0:
                ac_proj = p.cur(s_ac)
                last_msb = {0: 0, 1: 0}
                last_yev = {0: 0, 1: 0}
                sbf_mark = 0
            for hp in range(H // 2):
                pair = (2 * hp, 2 * hp + 1)
                ba = {pair[0]: 3, pair[1]: 4}
                by = {pair[0]: 6, pair[1]: 5}
                ao = {pair[0]: 0, pair[1]: 128}
                mmA, ve_am, ve_kht, mmM, mmy = {}, {}, {}, {}, {}
                for h in pair:
                    ksx = cv("kT")[:, h * 512 + n * 128: h * 512 + (n + 1) * 128]
                    rsx = cv("rT")[:, h * 512 + n * 128: h * 512 + (n + 1) * 128]
                    if h % 2 == 0:
                        wvA = [(s_ve, max(ve_proj, last_msb[0])), (s_ac, ac_proj)]
                    else:
                        wvA = [(s_ve, ve_proj), (s_ac, max(ac_proj, last_msb[1]))]
                    p.emit_i("tensor", lambda e, l=ksx, r=rsx, b=ba[h]: e.matmul(
                        out=self.bank(b, 128), lhsT=l, rhs=r, start=True, stop=True,
                        skip_group_check=True),
                        waits=wvA, inc_sem=s_mm)
                    mmA[h] = p.cur(s_mm)
                for h in pair:
                    p.emit_i("vector", lambda e, hh=h, b=ba[h], o=ao[h]: e.tensor_tensor(
                        out=cv("Am")[:, o:o + 128], in0=self.bank(b, 128),
                        in1=cv("Dmask")[:, hh * 128:(hh + 1) * 128], op=ALU.mult),
                        waits=[(s_mm, mmA[h])], inc_sem=s_ve)
                    ve_am[h] = p.cur(s_ve)
                for h in pair:
                    khs = cv("kh_buf")[:, h * 128:(h + 1) * 128]
                    kht = cv("khat_tm")[:, h * 64:(h + 1) * 64]
                    self.emit_transpose(khs, kht, [(s_ve, ve_sc)])
                    ve_kht[h] = p.cur(s_ve)
                for h in pair:
                    kht = cv("khat_tm")[:, h * 64:(h + 1) * 64]
                    vsx = cv("v_tm")[:, n * 1024 + h * 64: n * 1024 + (h + 1) * 64]
                    p.emit_i("tensor", lambda e, l=kht, r=vsx, b=ba[h]: e.matmul(
                        out=self.bank(b, 64)[0:64, :], lhsT=l, rhs=r, start=True, stop=True,
                        skip_group_check=True),
                        waits=[(s_ve, ve_kht[h]), (s_ve, ve_am[h])], inc_sem=s_mm)
                    mmM[h] = p.cur(s_mm)
                for h in pair:
                    if h % 2 == 0:
                        p.emit_i("vector", lambda e, hh=h, b=ba[h]: e.tensor_copy(
                            out=cv("Msb")[:, hh * 64:(hh + 1) * 64], in_=self.bank(b, 64)[0:64, :]),
                            waits=[(s_mm, mmM[h])], inc_sem=s_ve)
                        last_msb[0] = p.cur(s_ve)
                    else:
                        p.emit_i("scalar", lambda e, hh=h, b=ba[h]: e.copy(
                            out=cv("Msb")[:, hh * 64:(hh + 1) * 64], in_=self.bank(b, 64)[0:64, :]),
                            waits=[(s_mm, mmM[h])], inc_sem=s_ac)
                        last_msb[1] = p.cur(s_ac)
                for h in pair:
                    wy = [(s_ve, max(last_yev[h % 2], ve_am[h], sbf_mark, ve_sc))]
                    p.emit("tensor", lambda e, hh=h, b=by[h]: e.matmul(
                        out=self.bank(b, 128)[0:64, :], lhsT=cv("Sbf")[:, hh * 64:(hh + 1) * 64],
                        rhs=cv("rt_buf")[:, hh * 128:(hh + 1) * 128], start=True, stop=False,
                        skip_group_check=True),
                        waits=wy)
                    vsx = cv("v_tm")[:, n * 1024 + h * 64: n * 1024 + (h + 1) * 64]
                    p.emit_i("tensor", lambda e, l=vsx, b=by[h], o=ao[h]: e.matmul(
                        out=self.bank(b, 128)[0:64, :], lhsT=l, rhs=cv("Am")[:, o:o + 128],
                        start=False, stop=True, skip_group_check=True),
                        waits=[], inc_sem=s_mm)
                    mmy[h] = p.cur(s_mm)
                for h in pair:
                    yd = (cv("yT")[0:64, :] if h % 2 == 0 else cv("yT", hi=True))
                    ydst = yd[:, (h // 2) * 512 + n * 128: (h // 2) * 512 + (n + 1) * 128]
                    p.emit_i("vector", lambda e, o=ydst, b=by[h]: e.tensor_copy(out=o, in_=self.bank(b, 128)[0:64, :]),
                             waits=[(s_mm, mmy[h])], inc_sem=s_ve)
                    last_yev[h % 2] = p.cur(s_ve)
            wvs = [(s_mm, p.cur(s_mm)), (s_ac, p.cur(s_ac))]
            p.emit("vector", lambda e: e.tensor_tensor(
                out=cv("Sst").rearrange("p (h e) -> p h e", h=H),
                in0=cv("Sst").rearrange("p (h e) -> p h e", h=H),
                in1=cv("wCt")[:, 16:32].unsqueeze(2).broadcast_to([64, 16, 64]),
                op=ALU.mult), waits=wvs)
            p.emit("vector", lambda e: e.tensor_tensor(
                out=cv("Sst")[:], in0=cv("Sst")[:], in1=cv("Msb")[:], op=ALU.add))
            p.emit_i("vector", lambda e: e.tensor_copy(out=cv("Sbf")[:], in_=cv("Sst")[:]), inc_sem=s_ve)
            sbf_mark = p.cur(s_ve)
        ve_y = p.cur(s_ve)
        t = self.emit_load_w8("wo_T", False)
        tw = [(self.s_po, t[1])]
        for dmt in range(8):
            for hp in range(8):
                lhs = self._w8(hp)[:, dmt * 128:(dmt + 1) * 128]
                rhs = cv("yT")[:, hp * 512:(hp + 1) * 512]
                wv = (list(tw) + [(s_ve, max(ve_y, p.cur(s_ve)))]) if hp == 0 else []
                p.emit_i("tensor", lambda e, l=lhs, r=rhs, st_=(hp == 0), sp=(hp == 7):
                         e.matmul(out=self.bank(7, 512), lhsT=l, rhs=r, start=st_, stop=sp,
                                  skip_group_check=True),
                         waits=wv, inc_sem=(s_mm if hp == 7 else None))
            mmo = p.cur(s_mm)
            atb = cv("at_b")[:, 0:512]
            p.emit_i("vector", lambda e, o=atb: e.tensor_copy(out=o, in_=self.bank(7, 512)),
                     waits=[(s_mm, mmo)], inc_sem=s_ve)
            ve_atb = p.cur(s_ve)
            for ts4 in range(4):
                it = qoff // 128 + ts4
                tx = sb["x"][:, it * D + dmt * 128: it * D + (dmt + 1) * 128]
                self.emit_transpose_add(atb[:, ts4 * 128:(ts4 + 1) * 128], tx,
                                        first_waits=([(s_ve, ve_atb)] if ts4 == 0 else []))

    def emit_attn_tail(self):
        p, sb, dram = self.p, self.sb, self.dram
        s_mm, s_ve, s_ac, s_po, s_st, s_cc = self.s_mm, self.s_ve, self.s_ac, self.s_po, self.s_st, self.s_cc
        cv = self.carve
        hfT = sb["hfT"]
        ve_now = p.cur(s_ve)
        p.emit_i("sync", lambda e: e.dma_start(out=dram["st_src"], in_=cv("Sst")[:]),
                 waits=[(s_ve, ve_now)], inc_sem=s_st, inc_n=16)
        stt = p.cur(s_st)
        cc = p.emit_i("gpsimd", lambda e: e.collective_compute(
            "AllGather", ALU.bypass, replica_groups=[list(range(NCORES))],
            ins=[dram["st_src"].opt()], outs=[dram["st_all"].opt()]),
            waits=[(s_st, stt)], inc_sem=s_cc, kind="dma")
        p.emit_i("vector", lambda e: e.memset(cv("Msb")[:], 0.0), inc_sem=s_ve)
        vz = p.cur(s_ve)
        p.emit_i("gpsimd", lambda e: e.indirect_dma_start(
            out=cv("Msb")[0:64, :], out_offset=None, in_=dram["st_all"],
            in_offset=IndirectOffsetOnAxis(sb["prow_sb"][0:64, 0:1], 0),
            bounds_check=self._bc["r"], oob_is_err=False),
            waits=[(s_cc, cc), (s_ve, vz)], inc_sem=s_po, inc_n=16, kind="dma")
        pog = p.cur(s_po)
        t = self.emit_load_w8("wr_T", False)
        tw = [(self.s_po, max(t[1], pog))]
        for dks in range(8):
            for kc in range(8):
                lhs = self._w8(kc)[:, dks * 128:(dks + 1) * 128]
                rhs = hfT[:, kc * TPC: kc * TPC + 384]
                wv = (list(tw) + [(s_ve, p.cur(s_ve)), (s_ac, p.cur(s_ac))]) if kc == 0 else []
                p.emit_i("tensor", lambda e, l=lhs, r=rhs, st_=(kc == 0), sp=(kc == 7):
                         e.matmul(out=self.bank(2, 384), lhsT=l, rhs=r, start=st_, stop=sp,
                                  skip_group_check=True),
                         waits=wv, inc_sem=(s_mm if kc == 7 else None))
            mmk = p.cur(s_mm)
            for par in range(2):
                h = dks * 2 + par
                dst = cv("rT")[:, h * 512: h * 512 + 384]
                p.emit_i("scalar", lambda e, o=dst, pp=par: e.activation(
                    out=o, in_=self.bank(2, 384)[pp * 64:(pp + 1) * 64, :], func=AF.Sigmoid),
                    waits=[(s_mm, mmk)], inc_sem=s_ac)
        p.emit_i("vector", lambda e: e.nop(), waits=[(self.s_ac, p.cur(self.s_ac))], inc_sem=s_ve)
        ve_r = p.cur(s_ve)
        p.emit_i("vector", lambda e: e.tensor_copy(out=cv("Sbf")[:], in_=cv("Msb")[:]),
                 waits=[(s_ve, ve_r)], inc_sem=s_ve)
        for n in range(3):
            if n > 0:
                p.emit("vector", lambda e: e.tensor_tensor(
                    out=cv("Msb").rearrange("p (h e) -> p h e", h=H),
                    in0=cv("Msb").rearrange("p (h e) -> p h e", h=H),
                    in1=cv("wCt")[:, 16:32].unsqueeze(2).broadcast_to([64, 16, 64]),
                    op=ALU.mult))
                p.emit_i("vector", lambda e: e.tensor_copy(out=cv("Sbf")[:], in_=cv("Msb")[:]), inc_sem=s_ve)
            rview = cv("rT").rearrange("p (h t) -> p h t", h=H)[:, :, n * 128:(n + 1) * 128]
            p.emit_i("vector", lambda e, i0=rview: e.tensor_tensor(
                out=cv("rt_buf").rearrange("p (h t) -> p h t", h=H), in0=i0,
                in1=cv("W1p").rearrange("p (h t) -> p h t", h=H), op=ALU.mult),
                waits=[(s_ve, p.cur(s_ve)), (s_mm, p.cur(s_mm)), (s_ac, p.cur(s_ac))], inc_sem=s_ve)
            ve_rt = p.cur(s_ve)
            for h in range(H):
                p.emit_i("tensor", lambda e, hh=h: e.matmul(
                    out=self.bank(6, 128)[0:64, :], lhsT=cv("Sbf")[:, hh * 64:(hh + 1) * 64],
                    rhs=cv("rt_buf")[:, hh * 128:(hh + 1) * 128], start=True, stop=True,
                    skip_group_check=True),
                    waits=[(s_ve, max(ve_rt, p.cur(s_ve)))], inc_sem=s_mm)
                mmy = p.cur(s_mm)
                yd = (cv("yT")[0:64, :] if h % 2 == 0 else cv("yT", hi=True))
                ydst = yd[:, (h // 2) * 512 + n * 128: (h // 2) * 512 + (n + 1) * 128]
                p.emit_i("vector", lambda e, o=ydst: e.tensor_copy(out=o, in_=self.bank(6, 128)[0:64, :]),
                         waits=[(s_mm, mmy)], inc_sem=s_ve)
        ve_dy = p.cur(s_ve)
        t = self.emit_load_w8("wo_T", False)
        tw = [(self.s_po, t[1])]
        for dmt in range(8):
            for hp in range(8):
                lhs = self._w8(hp)[:, dmt * 128:(dmt + 1) * 128]
                rhs = cv("yT")[:, hp * 512: hp * 512 + 384]
                wv = (list(tw) + [(s_ve, max(ve_dy, p.cur(s_ve)))]) if hp == 0 else []
                p.emit_i("tensor", lambda e, l=lhs, r=rhs, st_=(hp == 0), sp=(hp == 7):
                         e.matmul(out=self.bank(7, 384), lhsT=l, rhs=r, start=st_, stop=sp,
                                  skip_group_check=True),
                         waits=wv, inc_sem=(s_mm if hp == 7 else None))
            mmo = p.cur(s_mm)
            atb = cv("at_b")[:, 0:384]
            p.emit_i("vector", lambda e, o=atb: e.tensor_copy(out=o, in_=self.bank(7, 384)),
                     waits=[(s_mm, mmo)], inc_sem=s_ve)
            ve_atb = p.cur(s_ve)
            for ts4 in range(3):
                tx = sb["x"][:, ts4 * D + dmt * 128: ts4 * D + (dmt + 1) * 128]
                self.emit_transpose_add(atb[:, ts4 * 128:(ts4 + 1) * 128], tx,
                                        first_waits=([(s_ve, ve_atb)] if ts4 == 0 else []))
        return p.cur(s_ve)

    def _dbg_dump(self, names):
        import os
        if not os.environ.get("ATTNDBG"):
            return
        p, cv, dram = self.p, self.carve, self.dram
        wv = [(self.s_ve, p.cur(self.s_ve)), (self.s_ac, p.cur(self.s_ac)), (self.s_mm, p.cur(self.s_mm))]
        cvt = self.sb["lgtile"] if False else None
        for dname, srcname in names:
            srcap = cv(srcname) if isinstance(srcname, str) else srcname
            p.emit_i("gpsimd", lambda e, t=dram[dname], s=srcap: e.dma_start(out=t, in_=s),
                     waits=wv, inc_sem=self.s_po, inc_n=16, kind="dma")
        self._dbg_done = p.cur(self.s_po)

    def emit_attn(self):
        import os
        self.p.phase = "attn_setup"
        self.emit_attn_setup()
        if os.environ.get("ATTNDBG"):
            self._dbg_dump([("dbg_W1", "W1p"), ("dbg_Dm", "Dmask")])
            self._dbg_dump([("dbg_hf2", self.sb["hf"][:])])
        for q in range(4):
            self.p.phase = "attn_q%d" % q
            self.emit_attn_quarter(q)
            if q == 0 and os.environ.get("ATTNDBG"):
                self._dbg_dump([("dbg_kT", "kT"), ("dbg_rT", "rT"), ("dbg_v", "v_tm"),
                                ("dbg_yT", "yT"), ("dbg_S", "Sst"),
                                ("dbg_x4", self.sb["x"][:, 0:4 * D]),
                                ("dbg_W2", "W2p"), ("dbg_kh", "kh_buf"),
                                ("dbg_kht", "khat_tm"), ("dbg_Ms", "Msb")])
        self.p.phase = "attn_tail"
        return self.emit_attn_tail()


def _np_inputs_prep(inputs, core):
    x = np.ascontiguousarray(inputs["L0_attn_x"].reshape(T, D))
    out = {}
    out["x0_full"] = x
    out["x0_shard"] = np.ascontiguousarray(x[core * TPC:(core + 1) * TPC])
    kcq = np.arange(D)
    permd = 256 * ((kcq // 128) % 4) + 2 * (kcq % 128) + (kcq // 128) // 4
    out["gw_T"] = np.ascontiguousarray(inputs["L0_ffn_gate_w"].T[permd])
    out["wk_T"] = np.ascontiguousarray(inputs["L0_attn_Wk"].T[permd])
    out["wv_T"] = np.ascontiguousarray(inputs["L0_attn_Wv"].T[permd])
    out["wr_T"] = np.ascontiguousarray(inputs["L0_attn_Wr"].T[permd])
    out["wo_T"] = np.ascontiguousarray(inputs["L0_attn_Wo"].T)
    out["tfirst"] = np.ascontiguousarray(inputs["L0_attn_time_first"].reshape(1, D))

    def pack16(a):
        return np.ascontiguousarray(a.view(np.int16)[..., ::2])

    es = slice(core * EPC, (core + 1) * EPC)
    permi = np.concatenate([np.arange(0, I, 2), np.arange(1, I, 2)])
    out["gate_pT"] = np.ascontiguousarray(pack16(inputs["L0_ffn_gate_packed"][es]).transpose(0, 2, 1)[:, :, permi])
    out["up_pT"] = np.ascontiguousarray(pack16(inputs["L0_ffn_up_packed"][es]).transpose(0, 2, 1)[:, :, permi])
    out["down_pT"] = np.ascontiguousarray(pack16(inputs["L0_ffn_down_packed"][es]).transpose(0, 2, 1))
    out["sh_gate_pT"] = np.ascontiguousarray(pack16(inputs["L0_ffn_sh_gate_packed"]).T[:, permi])
    out["sh_up_pT"] = np.ascontiguousarray(pack16(inputs["L0_ffn_sh_up_packed"]).T[:, permi])
    out["sh_down_pT"] = np.ascontiguousarray(pack16(inputs["L0_ffn_sh_down_packed"]).T)
    out["gate_sT"] = np.ascontiguousarray(inputs["L0_ffn_gate_scales"][es].transpose(0, 2, 1)[:, :, permi])
    out["up_sT"] = np.ascontiguousarray(inputs["L0_ffn_up_scales"][es].transpose(0, 2, 1)[:, :, permi])
    out["down_sT"] = np.ascontiguousarray(inputs["L0_ffn_down_scales"][es].transpose(0, 2, 1))
    out["sh_gate_sT"] = np.ascontiguousarray(inputs["L0_ffn_sh_gate_scales"].T[:, permi])
    out["sh_up_sT"] = np.ascontiguousarray(inputs["L0_ffn_sh_up_scales"].T[:, permi])
    out["sh_down_sT"] = np.ascontiguousarray(inputs["L0_ffn_sh_down_scales"].T)

    out["ident"] = np.eye(128, dtype=BF16NP)
    out["identf"] = np.eye(128, dtype=np.float32)
    out["tri_excl"] = np.tril(np.ones((128, 128), BF16NP), -1)
    out["tri_excl"] = np.triu(np.ones((128, 128), np.float32), 1).astype(BF16NP)
    out["allones"] = np.ones((128, 128), BF16NP)
    out["onescol"] = np.ones((1, 128), BF16NP)
    out["iota_e"] = np.broadcast_to(np.arange(E, dtype=np.float32), (128, E)).copy()
    out["iota_ij"] = (np.arange(128, dtype=np.float32)[None, :] - np.arange(128, dtype=np.float32)[:, None])
    out["tri_ij_u8"] = (np.arange(128)[None, :] >= np.arange(128)[:, None]).astype(np.uint8)
    out["tokid0"] = (core * TPC + np.arange(128, dtype=np.float32)).reshape(128, 1)
    out["coreid"] = np.full((1, 8), core, np.int32)
    out["prevmask"] = (np.arange(8) < core).astype(np.float32).reshape(1, 8)
    out["tdecay64"] = np.ascontiguousarray(inputs["L0_attn_time_decay"].astype(np.float32).T)
    out["decay_h"] = np.ascontiguousarray(inputs["L0_attn_time_decay"][:, 0].astype(np.float32).reshape(1, 16))
    out["iota1"] = np.arange(1, 129, dtype=np.float32).reshape(1, 128)
    out["iotar"] = np.arange(127, -1, -1, dtype=np.float32).reshape(1, 128)
    if core % 2 == 1:
        pr = (core - 1) * 64 + np.arange(128)
        pr[64:] = BIGIDX
    else:
        pr = np.full(128, BIGIDX)
    out["partnerrow"] = pr.astype(np.int32).reshape(128, 1)
    out["expbase"] = (core * 1024 + np.arange(128, dtype=np.float32)).reshape(128, 1)
    return out


def build_ffn1front(stop="full"):
    b = Builder("ffn1front")
    p = b.p
    for n in ("s_in", "s_wt", "s_wt2", "s_mm", "s_ve", "s_ac", "s_po", "s_st", "s_cc", "s_pc"):
        setattr(b, n, p.sem(n))
    p.auto["vector"] = (p.sem("s_auto_ve"), [0], [None])
    p.auto["scalar"] = (p.sem("s_auto_ac"), [0], [None])
    b._tb_n = 0
    b._tb_evac = [None, None]
    b.declare_io()
    b.alloc_common()
    blk = b.ctx.enter_context(b.nc.Block())
    b.load_common()
    b.ve_hfT = b.emit_ln_stream(lambda it: b.sb["x"][:, it * D:(it + 1) * D], 0,
                                [(b.s_in, b.x_loaded)], keep_hf_dbg=True)
    stages = ["ln", "route", "cntx", "scatter", "full"]
    sidx_ = stages.index(stop)
    if sidx_ >= 1:
        import os
        st_cnt = b.emit_routing(int(os.environ.get("RSUB", "99")))
    if sidx_ >= 2:
        ve_bown = b.emit_counts_exchange(st_cnt)
    if sidx_ >= 3:
        cc_table = b.emit_gpos_scatter(ve_bown)

    sb, dram = b.sb, b.dram
    s_st = b.s_st

    def outt(dst, src_fn, waits):
        for it in range(NT):
            tgt = dram[dst].rearrange("(n p) k -> n p k", p=128)[it]
            p.emit_i("sync", lambda e, t=tgt, s=src_fn(it): e.dma_start(out=t, in_=s),
                     waits=(waits if it == 0 else []), inc_sem=s_st, inc_n=16)
    ve_all = p.cur(b.s_ve)
    ac_all = p.cur(b.s_ac)
    w0 = [(b.s_ve, ve_all), (b.s_ac, ac_all)]
    p.emit_i("sync", lambda e: e.dma_start(out=dram["dbg_stats"], in_=sb["stats"][:]),
             waits=w0, inc_sem=s_st, inc_n=16)
    import os as _os
    if sidx_ >= 1 and int(_os.environ.get("RSUB", "99")) > 3:
        outt("dbg_w", lambda it: sb["w4"][:, it * K4:(it + 1) * K4], [])
        outt("dbg_idx", lambda it: sb["idx8"][:, it * 8:(it + 1) * 8], [])
    if sidx_ >= 3:
        outt("dbg_gpos", lambda it: sb["pos4"][:, it * K4:(it + 1) * K4], [])
    if sidx_ >= 2:
        p.emit_i("sync", lambda e: e.dma_start(out=dram["dbg_cnt"], in_=dram["cnt_all"]),
                 waits=[(b.s_cc, p.cur(b.s_cc))], inc_sem=s_st, inc_n=16)
    if sidx_ >= 4:
        p.emit_i("sync", lambda e: e.dma_start(out=dram["dbg_table"], in_=dram["g_table_all"]),
                 waits=[(b.s_cc, p.cur(b.s_cc))], inc_sem=s_st, inc_n=16)
    st_all = p.cur(s_st)
    p.emit("sync", lambda e: e.nop(), waits=[(s_st, st_all)])
    p.build(blk)
    return b


def build_ffn1x():
    b = Builder("ffn1x")
    p = b.p
    for n in ("s_in", "s_wt", "s_wt2", "s_mm", "s_ve", "s_ac", "s_po", "s_st", "s_cc", "s_pc"):
        setattr(b, n, p.sem(n))
    p.auto["vector"] = (p.sem("s_auto_ve"), [0], [None])
    p.auto["scalar"] = (p.sem("s_auto_ac"), [0], [None])
    b._tb_n = 0
    b._tb_evac = [None, None]
    b.declare_io()
    b.alloc_common()
    blk = b.ctx.enter_context(b.nc.Block())
    b.load_common()
    b.ve_hfT = b.emit_ln_stream(lambda it: b.sb["x"][:, it * D:(it + 1) * D], 0,
                                [(b.s_in, b.x_loaded)])
    st_cnt = b.emit_routing()
    ve_bown = b.emit_counts_exchange(st_cnt)
    cc_table = b.emit_gpos_scatter(ve_bown)
    b.emit_shared(b.ve_hfT)
    for j in range(EPC):
        b.emit_expert(1, j)
    st_ret = p.cur(b.s_st)
    cc_ret = p.emit_i("gpsimd", lambda e: e.collective_compute(
        "AllGather", ALU.bypass, replica_groups=[list(range(NCORES))],
        ins=[b.dram["ret_src"].opt()], outs=[b.dram["ret_all"].opt()]),
        waits=[(b.s_st, st_ret)], inc_sem=b.s_cc, kind="dma")
    b.emit_combine(cc_ret)
    sb, dram = b.sb, b.dram
    wv = [(b.s_ve, p.cur(b.s_ve)), (b.s_ac, p.cur(b.s_ac))]
    for it in range(NT):
        tgt = dram["out"].rearrange("(n p) d -> n p d", p=128)[it]
        p.emit_i("sync", lambda e, t=tgt, s=sb["x"][:, it * D:(it + 1) * D]: e.dma_start(out=t, in_=s),
                 waits=(wv if it == 0 else []), inc_sem=b.s_st, inc_n=16)
    st_all = p.cur(b.s_st)
    p.emit("sync", lambda e: e.nop(), waits=[(b.s_st, st_all)])
    p.build(blk)
    return b


def build_full():
    b = Builder("full")
    p = b.p
    for n in ("s_in", "s_wt", "s_wt2", "s_mm", "s_ve", "s_ac", "s_po", "s_st", "s_cc", "s_pc"):
        setattr(b, n, p.sem(n))
    p.auto["vector"] = (p.sem("s_auto_ve"), [0], [None])
    p.auto["scalar"] = (p.sem("s_auto_ac"), [0], [None])
    b._tb_n = 0
    b._tb_evac = [None, None]
    b.declare_io()
    b.alloc_common()
    blk = b.ctx.enter_context(b.nc.Block())
    b._blk = blk
    b.load_common()
    p.phase = "ln1"
    b.ve_hfT = b.emit_ln_stream(lambda it: b.sb["x"][:, it * D:(it + 1) * D], 0,
                                [(b.s_in, b.x_loaded)])
    p.phase = "route1"
    st_cnt = b.emit_routing()
    ve_bown = b.emit_counts_exchange(st_cnt)
    cc_table = b.emit_gpos_scatter(ve_bown)
    b._mk_gu = b._mk_dn = b._mk_ebT = p.cur(b.s_mm)
    b._mk_gath = p.cur(b.s_ac)
    b._mk_scr = p.cur(b.s_mm)
    b._mk_ld_extra = [(b.s_ve, p.cur(b.s_ve)), (b.s_ac, p.cur(b.s_ac))]
    b._mk_ld_uses = 2
    for j in range(EPC):
        p.phase = "exp1_%d" % j
        b.emit_expert(1, j)
        if j == 3:
            st_a = p.cur(b.s_st)
        if j == 5:
            p.emit_i("gpsimd", lambda e: e.collective_compute(
                "AllGather", ALU.bypass, replica_groups=[list(range(NCORES))],
                ins=[b.dram["ret_src"][0:1024, :].opt()], outs=[b.dram["ret_all"][0:8192, :].opt()]),
                waits=[(b.s_st, st_a)], inc_sem=b.s_cc, kind="dma")
    st_ret = p.cur(b.s_st)
    p.phase = "retag1"
    cc_ret = p.emit_i("gpsimd", lambda e: e.collective_compute(
        "AllGather", ALU.bypass, replica_groups=[list(range(NCORES))],
        ins=[b.dram["ret_src"][1024:2048, :].opt()], outs=[b.dram["ret_all"][8192:16384, :].opt()]),
        waits=[(b.s_st, st_ret)], inc_sem=b.s_cc, kind="dma")
    ve_pre_sh = p.cur(b.s_ve)
    ac_pre_sh = p.cur(b.s_ac)
    mm_pre_sh = p.cur(b.s_mm)
    p.phase = "shared1"
    b.emit_shared(b.ve_hfT)
    p.phase = "comb1"
    b.emit_combine(cc_ret, ve_pre_sh, ac_pre_sh, mm_pre_sh)
    import os
    _fs = os.environ.get("FULLSTAGE", "x3")
    if _fs == "x1":
        _emit_out(b, p)
        return b
    p.phase = "ln2"
    ve_hf2T = b.emit_ln_stream(lambda it: b.sb["x"][:, it * D:(it + 1) * D], 0,
                               [(b.s_ve, p.cur(b.s_ve)), (b.s_po, p.cur(b.s_po))])
    b.ve_hfT = ve_hf2T
    p.phase = "attn"
    b.emit_attn()
    if _fs == "x2":
        _emit_out(b, p)
        return b
    p.phase = "ln3"
    ve_hf3T = b.emit_ln_stream(lambda it: b.sb["x"][:, it * D:(it + 1) * D], 0,
                               [(b.s_ve, p.cur(b.s_ve)), (b.s_mm, p.cur(b.s_mm))],
                               store_dram="hf2_src")
    b.ve_hfT = ve_hf3T
    st_hf2 = p.cur(b.s_st)
    p.phase = "hf2ag"
    cc_hf2 = p.emit_i("gpsimd", lambda e: e.collective_compute(
        "AllGather", ALU.bypass, replica_groups=[list(range(NCORES))],
        ins=[b.dram["hf2_src"].opt()], outs=[b.dram["hf2_all"].opt()]),
        waits=[(b.s_st, st_hf2)], inc_sem=b.s_cc, kind="dma")
    p.phase = "route2"
    st_cnt2 = b.emit_routing()
    ve_bown2 = b.emit_counts_exchange(st_cnt2)
    cc_table2 = b.emit_gpos_scatter(ve_bown2)
    b._mk_gu = b._mk_dn = b._mk_ebT = p.cur(b.s_mm)
    b._mk_gath = p.cur(b.s_ac)
    b._mk_scr = p.cur(b.s_mm)
    b._mk_ld_extra = [(b.s_ve, p.cur(b.s_ve)), (b.s_ac, p.cur(b.s_ac))]
    b._mk_ld_uses = 2
    for j in range(EPC):
        p.phase = "exp2_%d" % j
        b.emit_expert(2, j)
        if j == 3:
            st_a2 = p.cur(b.s_st)
        if j == 5:
            p.emit_i("gpsimd", lambda e: e.collective_compute(
                "AllGather", ALU.bypass, replica_groups=[list(range(NCORES))],
                ins=[b.dram["ret_src"][0:1024, :].opt()], outs=[b.dram["ret_all"][0:8192, :].opt()]),
                waits=[(b.s_st, st_a2)], inc_sem=b.s_cc, kind="dma")
    st_ret2 = p.cur(b.s_st)
    p.phase = "retag2"
    cc_ret2 = p.emit_i("gpsimd", lambda e: e.collective_compute(
        "AllGather", ALU.bypass, replica_groups=[list(range(NCORES))],
        ins=[b.dram["ret_src"][1024:2048, :].opt()], outs=[b.dram["ret_all"][8192:16384, :].opt()]),
        waits=[(b.s_st, st_ret2)], inc_sem=b.s_cc, kind="dma")
    ve_pre_sh2 = p.cur(b.s_ve)
    ac_pre_sh2 = p.cur(b.s_ac)
    mm_pre_sh2 = p.cur(b.s_mm)
    p.phase = "shared2"
    b.emit_shared(b.ve_hfT)
    p.phase = "comb2"
    b.emit_combine(cc_ret2, ve_pre_sh2, ac_pre_sh2, mm_pre_sh2)
    p.phase = "outw"
    _emit_out(b, p)
    return b


def _emit_out(b, p):
    sb, dram = b.sb, b.dram
    wv = [(b.s_ve, p.cur(b.s_ve)), (b.s_ac, p.cur(b.s_ac)), (b.s_mm, p.cur(b.s_mm))]
    for it in range(NT):
        tgt = dram["out"].rearrange("(n p) d -> n p d", p=128)[it]
        p.emit_i("sync", lambda e, t=tgt, s=sb["x"][:, it * D:(it + 1) * D]: e.dma_start(out=t, in_=s),
                 waits=(wv if it == 0 else []), inc_sem=b.s_st, inc_n=16)
    st_all = p.cur(b.s_st)
    p.emit("sync", lambda e: e.nop(), waits=[(b.s_st, st_all)])
    p.build(b._blk)


def kernel(**inputs):
    if kernel._dev_stage == "full":
        b = build_full()
        in_maps = [_np_inputs_prep(inputs, c) for c in range(NCORES)]
        res = run_bass_kernel_spmd(b.nc, in_maps, core_ids=list(range(NCORES)))
        return np.concatenate([res.results[c]["out"] for c in range(NCORES)], 0).reshape(B, L, D)
    if kernel._dev_stage == "ffn1front":
        b = build_ffn1front()
        in_maps = [_np_inputs_prep(inputs, c) for c in range(NCORES)]
        res = run_bass_kernel_spmd(b.nc, in_maps, core_ids=list(range(NCORES)))
        return res
    raise NotImplementedError("under construction")


kernel._dev_stage = "full"


# version 15
# speedup vs baseline: 1.0812x; 1.0168x over previous
import contextlib

import numpy as np
import ml_dtypes

import concourse.bass as bass
import concourse.mybir as mybir
from concourse.bass_utils import run_bass_kernel_spmd
from concourse.bass_core_rust import IndirectOffsetOnAxis

F32 = mybir.dt.float32
BF16 = mybir.dt.bfloat16
I16 = mybir.dt.int16
I32 = mybir.dt.int32
U8 = mybir.dt.uint8
U32 = mybir.dt.uint32
AF = mybir.ActivationFunctionType
ALU = mybir.AluOpType
AX = mybir.AxisListType

NCORES = 8
B, L, D, H, HD = 4, 4096, 1024, 16, 64
T = B * L
TPC = T // NCORES
E, K4, CAP, I, G = 64, 4, 256, 768, 128
EPC = E // NCORES
NT = TPC // 128
LNEPS = 1e-5
BIGIDX = 1 << 22

BF16NP = ml_dtypes.bfloat16


class PB:

    def __init__(self, nc, ctx):
        self.nc = nc
        self.ctx = ctx
        self.ops = {k: [] for k in ("sync", "vector", "scalar", "tensor", "gpsimd")}
        self.phase = "init"
        self.sems = {}
        self.counts = {}
        self.auto = {}

    def sem(self, name):
        s = self.ctx.enter_context(self.nc.semaphore(name))
        self.sems[name] = s
        self.counts[name] = 0
        return s

    def inc(self, sem, n=1):
        self.counts[sem.name] += n
        return self.counts[sem.name]

    def cur(self, sem):
        return self.counts[sem.name]

    def emit(self, engine, fn, waits=(), incs=(), kind="compute"):
        waits = list(waits)
        incs = list(incs)
        if kind == "compute" and engine in self.auto:
            sem, cnt, last = self.auto[engine]
            if last[0] is not None:
                waits.append(last[0])
            if incs:
                s0, _ = incs[0]
                last[0] = (s0, self.counts[s0.name])
            else:
                cnt[0] += 1
                incs.append((sem, 1))
                last[0] = (sem, cnt[0])
        self.ops[engine].append((fn, waits, incs, self.phase))

    def emit_i(self, engine, fn, waits=(), inc_sem=None, inc_n=1, kind="compute"):
        incs = []
        tgt = None
        if inc_sem is not None:
            tgt = self.inc(inc_sem, inc_n)
            incs = [(inc_sem, inc_n)]
        self.emit(engine, fn, waits, incs, kind=kind)
        return tgt

    def build(self, block):
        import os
        scoped = bool(os.environ.get("BASSPROF"))
        def mk(engine):
            def run(eng):
                curph, sid = None, None
                for rec in self.ops[engine]:
                    fn, waits, incs, ph = rec
                    if scoped and ph != curph:
                        if curph is not None:
                            self.nc.leave_named_scope(engine + ":" + curph, sid, False)
                        sid, _ = self.nc.enter_named_scope(engine + ":" + ph, False)
                        curph = ph
                    for sem, val in waits:
                        eng.wait_ge(sem, val)
                    inst = fn(eng)
                    for sem, n in incs:
                        inst = inst.then_inc(sem, n)
                if scoped and curph is not None:
                    self.nc.leave_named_scope(engine + ":" + curph, sid, False)
            return run
        block.sync(mk("sync"))
        block.vector(mk("vector"))
        block.scalar(mk("scalar"))
        block.tensor(mk("tensor"))
        block.gpsimd(mk("gpsimd"))


DT = {"f32": F32, "bf16": BF16, "i16": I16, "i32": I32, "u8": U8, "u32": U32}


class Builder:
    def __init__(self, dev_stage="full"):
        self.dev_stage = dev_stage
        self.nc = bass.Bass("TRN2", target_bir_lowering=False, debug=False, detect_race_conditions=False)
        self.ctx = contextlib.ExitStack()
        self.p = PB(self.nc, self.ctx)
        self.dram = {}
        self.sb = {}
        self._mk_gu = self._mk_dn = self._mk_ebT = 0
        self._mk_gath = 0
        self._mk_scr = None
        self._deq_par = 0
        self._deq_free = [0, 0]
        self._mk_ld_extra = []
        self._mk_ld_uses = 0
        self.ps = {}

    def din(self, name, shape, dt):
        self.dram[name] = self.nc.dram_tensor(name, list(shape), DT[dt], kind="ExternalInput").ap()
        return self.dram[name]

    def dout(self, name, shape, dt):
        self.dram[name] = self.nc.dram_tensor(name, list(shape), DT[dt], kind="ExternalOutput").ap()
        return self.dram[name]

    def dint(self, name, shape, dt, **kw):
        self.dram[name] = self.nc.dram_tensor(name, list(shape), DT[dt], kind="Internal", **kw).ap()
        return self.dram[name]

    def S(self, name, shape, dt):
        self.sb[name] = self.ctx.enter_context(self.nc.sbuf_tensor(name, list(shape), DT[dt]))
        return self.sb[name]

    _ARENA_EXPERT = {
        "w_gate_lo": (0, 3072, "bf16", 128),
        "w_gate_hi": (6144, 3072, "bf16", 128),
        "w_up_lo": (12288, 3072, "bf16", 128),
        "w_up_hi": (18432, 3072, "bf16", 128),
        "w_down_lo": (24576, 3072, "bf16", 128),
        "w_down_hi": (30720, 3072, "bf16", 128),
        "pk_t": (36864, 1024, "i16", 128),
        "pk_t2": (63488, 1024, "i16", 128),
        "st_t2": (69632, 1024, "f32", 128),
        "st_t": (38912, 1024, "f32", 128),
        "ebuf": (43008, 2048, "bf16", 128),
        "ebufT": (47104, 2048, "bf16", 128),
        "gath_f32": (51200, 2048, "f32", 128),
        "hU": (59392, 1536, "bf16", 128),
        "hsil": (62464, 512, "bf16", 128),
        "h_shT": (63488, 1536, "bf16", 128),
        "retst": (66560, 1024, "bf16", 128),
        "logitsT": (68608, 2048, "f32", 64),
        "lgtile": (76800, 1024, "f32", 128),
        "ohr_c": (80896, 4096, "bf16", 128),
        "sq_scr": (89088, 1024, "f32", 128),
    }
    _ARENA_ATTN = {
        "kT": (0, 8192, "bf16", 64),
        "rT": (16384, 8192, "bf16", 64),
        "v_tm": (32768, 4096, "bf16", 128),
        "yT": (40960, 4096, "bf16", 128),
        "khat_tm": (49152, 1024, "bf16", 128),
        "at_b": (55296, 512, "bf16", 128),
        "Sbf": (53248, 1024, "bf16", 64),
        "Sst": (57344, 1024, "f32", 64),
        "Msb": (61440, 1024, "f32", 64),
        "W1p": (65536, 2048, "bf16", 64),
        "W2p": (69632, 2048, "bf16", 64),
        "Dmask": (73728, 2048, "bf16", 128),
        "rt_buf": (77824, 2048, "bf16", 64),
        "kh_buf": (81920, 2048, "bf16", 64),
        "Am": (86016, 256, "bf16", 128),
        "wkb": (86528, 2048, "bf16", 128),
        "tfb": (90624, 1024, "bf16", 128),
        "wCt": (92672, 64, "f32", 64),
        "lngb": (92928, 64, "f32", 128),
        "zmask": (93184, 128, "bf16", 128),
    }

    def carve(self, name, hi=False):
        off, n, dt, parts = (self._ARENA_EXPERT | self._ARENA_ATTN)[name]
        es = {"bf16": 2, "f32": 4, "i16": 2}[dt]
        if hi:
            ap = self.sb["arena"][64:128, :].bitcast(DT[dt])
        else:
            ap = self.sb["arena"][0:parts, :].bitcast(DT[dt])
        return ap[:, off // es: off // es + n]

    def P(self, name, shape, dt="f32"):
        self.ps[name] = self.ctx.enter_context(self.nc.psum_tensor(name, list(shape), DT[dt]))
        return self.ps[name]

    def declare_io(self):
        din, dout, dint = self.din, self.dout, self.dint
        full = self.dev_stage != "ffn1front"
        din("x0_shard", (TPC, D), "f32")
        din("gw_T", (D, E), "f32")
        if full:
            din("x0_full", (T, D), "f32")
            din("wk_T", (D, D), "f32")
            din("wv_T", (D, D), "f32")
            din("wr_T", (D, D), "f32")
            din("wo_T", (D, D), "f32")
            din("tfirst", (1, D), "f32")
            din("tdecay64", (64, 16), "f32")
            din("decay_h", (1, 16), "f32")
            din("iota1", (1, 128), "f32")
            din("iotar", (1, 128), "f32")
            din("partnerrow", (128, 1), "i32")
            din("gate_pT", (EPC, D // 2, I), "i16")
            din("up_pT", (EPC, D // 2, I), "i16")
            din("down_pT", (EPC, I // 2, D), "i16")
            din("sh_gate_pT", (D // 2, I), "i16")
            din("sh_up_pT", (D // 2, I), "i16")
            din("sh_down_pT", (I // 2, D), "i16")
            din("gate_sT", (EPC, D // G, I), "f32")
            din("up_sT", (EPC, D // G, I), "f32")
            din("down_sT", (EPC, I // G, D), "f32")
            din("sh_gate_sT", (D // G, I), "f32")
            din("sh_up_sT", (D // G, I), "f32")
            din("sh_down_sT", (I // G, D), "f32")
        din("ident", (128, 128), "bf16")
        din("identf", (128, 128), "f32")
        din("tri_excl", (128, 128), "bf16")
        din("allones", (128, 128), "bf16")
        din("onescol", (1, 128), "bf16")
        din("iota_e", (128, E), "f32")
        din("iota_ij", (128, 128), "f32")
        din("tri_ij_u8", (128, 128), "u8")
        din("tokid0", (128, 1), "f32")
        din("expbase", (128, 1), "f32")
        din("coreid", (1, 8), "i32")
        din("prevmask", (1, 8), "f32")

        dint("g_table", (E * CAP, 2), "f32")
        dint("g_table_all", (E * CAP, 2), "f32")
        dint("cnt_src", (1, E), "f32")
        dint("bown_dram", (1, E), "f32")
        dint("cnt_all", (NCORES, E), "f32")
        dint("ret_src", (EPC * CAP, D), "bf16")
        dint("ret_all", (E * CAP, D), "bf16")
        dint("hf2_src", (TPC, D), "bf16")
        dint("hf2_all", (T, D), "bf16")
        dint("st_src", (H * HD, HD), "f32")
        dint("st_all", (NCORES * H * HD, HD), "f32")

        if self.dev_stage == "ffn1x":
            dout("out", (TPC, D), "f32")
        elif self.dev_stage == "full":
            import os
            dout("out", (TPC, D), "f32")
            if os.environ.get("ATTNDBG"):
                dout("dbg_kT", (64, 8192), "f32")
                dout("dbg_rT", (64, 8192), "f32")
                dout("dbg_v", (128, 4096), "f32")
                dout("dbg_yT", (128, 4096), "f32")
                dout("dbg_S", (64, 1024), "f32")
                dout("dbg_W1", (64, 2048), "f32")
                dout("dbg_Dm", (128, 2048), "f32")
                dout("dbg_hf2", (128, 4 * D), "f32")
                dout("dbg_x4", (128, 4 * D), "f32")
                dout("dbg_v2", (128, 4096), "f32")
                dout("dbg_W2", (64, 2048), "f32")
                dout("dbg_kh", (64, 2048), "f32")
                dout("dbg_kht", (128, 1024), "f32")
                dout("dbg_Ms", (64, 1024), "f32")
        elif self.dev_stage == "ffn1front":
            dout("dbg_w", (TPC, K4), "f32")
            dout("dbg_idx", (TPC, 8), "u32")
            dout("dbg_gpos", (TPC, K4), "f32")
            dout("dbg_cnt", (NCORES, E), "f32")
            dout("dbg_hf", (TPC, D), "bf16")
            dout("dbg_stats", (128, NT * 8), "f32")
            dout("dbg_table", (E * CAP, 2), "f32")
        else:
            dout("out", (TPC, D), "f32")

    def alloc_common(self):
        S, P = self.S, self.P
        S("x", (128, NT * D), "f32")
        S("hf", (128, 4 * D), "bf16")
        S("hfT", (128, 8 * TPC), "bf16")
        S("stats", (128, NT * 8), "f32")
        S("ln_tmp", (128, 4 * NT), "f32")
        S("consts", (128, 512), "bf16")
        S("identf_sb", (128, 128), "f32")
        S("iota_e_sb", (128, E), "f32")
        S("gwT", (128, 8 * E), "bf16")

        S("m8", (128, NT * 8), "f32")
        S("idx8", (128, NT * 8), "u32")
        S("idxf", (128, NT * K4), "f32")
        S("w4", (128, NT * K4), "f32")
        S("oh", (128, NT * E), "bf16")
        S("pos4", (128, NT * K4), "f32")
        S("rtmp", (128, NT * 2 * K4), "f32")
        S("pairs", (128, NT * K4 * 2), "f32")
        S("sidx", (128, NT * K4), "i32")
        S("bownrow", (1, E), "f32")
        S("cntrows", (1, NCORES * E), "f32")
        S("bownbc", (128, E), "f32")
        S("petile", (128, E), "f32")
        S("fidx", (128, 2 * NT * K4), "f32")
        S("tokid_sb", (128, 1), "f32")
        S("coreid_sb", (1, 8), "i32")
        S("prevmask_sb", (1, 8), "f32")
        S("arena", (128, 93440), "u8")
        S("gpair", (128, 32), "f32")
        S("tokcol", (128, 16), "i32")
        S("eidx", (128, 16), "i32")
        S("iscr", (128, 8), "i32")
        S("expbase_sb", (128, 1), "f32")
        S("prow_sb", (128, 1), "i32")

        P("pb", (128, 4096), "f32")

    def load_common(self):
        p, sb, dram = self.p, self.sb, self.dram
        s_in = self.s_in
        self._bc = {}

        def mkreg(e):
            self._bc["r"] = e.to_reg(E * CAP - 1)
            return None
        p.emit("gpsimd", mkreg, kind="dma")
        for it in range(NT):
            tgt = sb["x"][:, it * D:(it + 1) * D]
            src = dram["x0_shard"].rearrange("(n p) d -> n p d", p=128)[it]
            p.emit_i("sync", lambda e, t=tgt, s=src: e.dma_start(out=t, in_=s), inc_sem=s_in, inc_n=16)
        self.x_loaded = p.cur(s_in)
        cl = [
            (sb["consts"][:, 0:128], dram["ident"]),
            (sb["consts"][:, 128:256], dram["tri_excl"]),
            (sb["consts"][:, 256:384], dram["allones"]),
            (sb["consts"][0:1, 384:512], dram["onescol"]),
            (sb["iota_e_sb"][:], dram["iota_e"]),
            (sb["identf_sb"][:], dram["identf"]),
        ]
        for tgt, src in cl:
            p.emit_i("sync", lambda e, t=tgt, s=src: e.dma_start(out=t, in_=s), inc_sem=s_in, inc_n=16)
        for kc in range(8):
            tgt = sb["gwT"][:, kc * E:(kc + 1) * E]
            src = dram["gw_T"][kc * 128:(kc + 1) * 128, :]
            p.emit_i("gpsimd", lambda e, t=tgt, s=src: e.dma_start(out=t, in_=s),
                     inc_sem=self.s_po, inc_n=16, kind="dma")
        p.emit_i("sync", lambda e: e.dma_start(out=sb["tokid_sb"][:], in_=dram["tokid0"]), inc_sem=s_in, inc_n=16)
        p.emit_i("sync", lambda e: e.dma_start(out=sb["expbase_sb"][:], in_=dram["expbase"]), inc_sem=s_in, inc_n=16)
        if "partnerrow" in dram:
            p.emit_i("sync", lambda e: e.dma_start(out=sb["prow_sb"][:], in_=dram["partnerrow"]), inc_sem=s_in, inc_n=16)
        p.emit_i("sync", lambda e: e.dma_start(out=sb["coreid_sb"][:], in_=dram["coreid"]), inc_sem=s_in, inc_n=16)
        p.emit_i("sync", lambda e: e.dma_start(out=sb["prevmask_sb"][:], in_=dram["prevmask"]), inc_sem=s_in, inc_n=16)
        p.emit_i("vector", lambda e: e.memset(sb["stats"][:], 0.0), inc_sem=self.s_ve)
        self.ve_zt = 0
        self.consts_loaded = p.cur(s_in)
        self.x_loaded = p.cur(s_in)
        self.gw_loaded = p.cur(self.s_po)
        self.identf_sb = sb["identf_sb"][:]
        self.ident = sb["consts"][:, 0:128]
        self.tri_excl = sb["consts"][:, 128:256]
        self.allones = sb["consts"][:, 256:384]
        self.onescol = sb["consts"][0:1, 384:512]

    def emit_ln(self, x_ap_fn, hf_dst_fn, stat_base, wait_first, ntiles=NT):
        p, sb = self.p, self.sb
        stats, ln_tmp, sq = sb["stats"], sb["ln_tmp"], self.carve("sq_scr")
        for it in range(ntiles):
            src = x_ap_fn(it)
            c0 = stat_base + it * 8
            p.emit("vector", lambda e, s=src, o=stats[:, c0:c0 + 1]: e.reduce_sum(out=o, in_=s, axis=AX.X),
                   waits=(wait_first if it == 0 else []))
            wsq = list(wait_first) if it == 0 else []
            p.emit_i("scalar", lambda e, s=src, o=stats[:, c0 + 1:c0 + 2]: e.activation(
                out=sq[:], in_=s, func=AF.Square, accum_out=o),
                waits=wsq, inc_sem=self.s_ac)
        ac_sq = p.cur(self.s_ac)
        st = lambda c: stats[:, stat_base + c:stat_base + ntiles * 8:8]
        allsum, allssq, allinv, allnb = st(0), st(1), st(2), st(3)
        tmpm, tmpv, tmps, tmpq = (ln_tmp[:, i * NT:i * NT + ntiles] for i in range(4))
        ve = self.s_ve
        p.emit("vector", lambda e: e.tensor_scalar(out=tmpm, in0=allsum, scalar1=1.0 / D, scalar2=None, op0=ALU.mult),
               waits=[(self.s_ac, ac_sq)])
        p.emit("vector", lambda e: e.tensor_scalar(out=tmpv, in0=allssq, scalar1=1.0 / D, scalar2=LNEPS, op0=ALU.mult, op1=ALU.add))
        p.emit("vector", lambda e: e.tensor_tensor(out=tmpq, in0=tmpm, in1=tmpm, op=ALU.mult))
        p.emit_i("vector", lambda e: e.tensor_tensor(out=tmpv, in0=tmpv, in1=tmpq, op=ALU.subtract), inc_sem=ve)
        ve_v = p.cur(ve)
        p.emit_i("scalar", lambda e: e.activation(out=tmps, in_=tmpv, func=AF.Sqrt),
                 waits=[(ve, ve_v)], inc_sem=self.s_ac)
        ac_s = p.cur(self.s_ac)
        p.emit("vector", lambda e: e.reciprocal(out=allinv, in_=tmps), waits=[(self.s_ac, ac_s)])
        p.emit_i("vector", lambda e: e.scalar_tensor_tensor(out=allnb, in0=tmpm, scalar=-1.0, in1=allinv,
                                                            op0=ALU.mult, op1=ALU.mult), inc_sem=ve)
        ve_stats = p.cur(ve)
        for it in range(ntiles):
            src = x_ap_fn(it)
            dst = hf_dst_fn(it)
            c0 = stat_base + it * 8
            p.emit_i("scalar", lambda e, s=src, o=dst, sc=stats[:, c0 + 2:c0 + 3], b=stats[:, c0 + 3:c0 + 4]:
                     e.activation(out=o, in_=s, func=AF.Identity, scale=sc, bias=b),
                     waits=([(ve, ve_stats)] if it == 0 else []), inc_sem=self.s_ac)
        return p.cur(self.s_ac)

    def emit_ln_stream(self, x_ap_fn, stat_base, wait_first, keep_hf_dbg=False, store_dram=None):
        p, sb = self.p, self.sb
        hfT = sb["hfT"]
        roll = sb["hf"]
        tile_done = []
        ac_stats = self.emit_ln_stats(x_ap_fn, stat_base, wait_first)
        stats = sb["stats"]
        for it in range(NT):
            slot = roll[:, (it % 4) * D:(it % 4 + 1) * D]
            c0 = stat_base + it * 8
            wv = []
            if it == 0:
                wv.append((self.s_ve, ac_stats))
            if it >= 4:
                wv.append((self.s_ve, tile_done[it - 4]))
            p.emit_i("scalar", lambda e, s=x_ap_fn(it), o=slot,
                     sc=stats[:, c0 + 2:c0 + 3], b=stats[:, c0 + 3:c0 + 4]:
                     e.activation(out=o, in_=s, func=AF.Identity, scale=sc, bias=b),
                     waits=wv, inc_sem=self.s_ac)
            ac_ap = p.cur(self.s_ac)
            if keep_hf_dbg and "dbg_hf" in self.dram:
                tgt = self.dram["dbg_hf"].rearrange("(n p) d -> n p d", p=128)[it]
                p.emit_i("sync", lambda e, t=tgt, s=slot: e.dma_start(out=t, in_=s),
                         waits=[(self.s_ac, ac_ap)], inc_sem=self.s_st, inc_n=16)
            if store_dram is not None:
                tgt = self.dram[store_dram].rearrange("(n p) d -> n p d", p=128)[it]
                p.emit_i("sync", lambda e, t=tgt, s=slot: e.dma_start(out=t, in_=s),
                         waits=[(self.s_ac, ac_ap)], inc_sem=self.s_st, inc_n=16)
            for kc in range(8):
                base = 256 * (kc % 4) + (kc // 4)
                srcap = slot[:, base: base + 256 - (base % 2): 2]
                dst = hfT[:, kc * TPC + it * 128: kc * TPC + (it + 1) * 128]
                self.emit_transpose(srcap, dst,
                                    [(self.s_ac, ac_ap)] if kc == 0 else [])
            tile_done.append(p.cur(self.s_ve))
        return p.cur(self.s_ve)

    def emit_ln_stats(self, x_ap_fn, stat_base, wait_first, ntiles=NT):
        p, sb = self.p, self.sb
        stats, ln_tmp, sq = sb["stats"], sb["ln_tmp"], self.carve("sq_scr")
        for it in range(ntiles):
            src = x_ap_fn(it)
            c0 = stat_base + it * 8
            p.emit("vector", lambda e, s=src, o=stats[:, c0:c0 + 1]: e.reduce_sum(out=o, in_=s, axis=AX.X),
                   waits=(list(wait_first) if it == 0 else []))
            p.emit_i("scalar", lambda e, s=src, o=stats[:, c0 + 1:c0 + 2]: e.activation(
                out=sq[:], in_=s, func=AF.Square, accum_out=o),
                waits=(list(wait_first) if it == 0 else []), inc_sem=self.s_ac)
        ac_sq = p.cur(self.s_ac)
        st = lambda c: stats[:, stat_base + c:stat_base + ntiles * 8:8]
        allsum, allssq, allinv, allnb = st(0), st(1), st(2), st(3)
        tmpm, tmpv, tmps, tmpq = (ln_tmp[:, i * NT:i * NT + ntiles] for i in range(4))
        ve = self.s_ve
        p.emit("vector", lambda e: e.tensor_scalar(out=tmpm, in0=allsum, scalar1=1.0 / D, scalar2=None, op0=ALU.mult),
               waits=[(self.s_ac, ac_sq)])
        p.emit("vector", lambda e: e.tensor_scalar(out=tmpv, in0=allssq, scalar1=1.0 / D, scalar2=LNEPS, op0=ALU.mult, op1=ALU.add))
        p.emit("vector", lambda e: e.tensor_tensor(out=tmpq, in0=tmpm, in1=tmpm, op=ALU.mult))
        p.emit_i("vector", lambda e: e.tensor_tensor(out=tmpv, in0=tmpv, in1=tmpq, op=ALU.subtract), inc_sem=ve)
        ve_v = p.cur(ve)
        p.emit_i("scalar", lambda e: e.activation(out=tmps, in_=tmpv, func=AF.Sqrt),
                 waits=[(ve, ve_v)], inc_sem=self.s_ac)
        ac_s = p.cur(self.s_ac)
        p.emit("vector", lambda e: e.reciprocal(out=allinv, in_=tmps), waits=[(self.s_ac, ac_s)])
        p.emit_i("vector", lambda e: e.scalar_tensor_tensor(out=allnb, in0=tmpm, scalar=-1.0, in1=allinv,
                                                            op0=ALU.mult, op1=ALU.mult), inc_sem=ve)
        return p.cur(ve)

    def emit_routing(self, substage=99):
        p, sb, ps = self.p, self.sb, self.ps
        s_mm, s_ve, s_ac, s_in = self.s_mm, self.s_ve, self.s_ac, self.s_in
        hfT, gwT = sb["hfT"], sb["gwT"]
        if substage <= 0:
            return None
        for ncol in range(TPC // 512):
            pst = self.bank(2, 512)[0:64, :]
            for kc in range(8):
                lhs = gwT[:, kc * E:(kc + 1) * E]
                rhs = hfT[:, kc * TPC + ncol * 512: kc * TPC + ncol * 512 + 512]
                w = []
                if kc == 0:
                    w = [(s_ve, self.ve_hfT), (s_in, self.consts_loaded), (self.s_po, self.gw_loaded)]
                    if ncol > 0:
                        w.append((s_ve, self.p.cur(s_ve)))
                p.emit_i("tensor", lambda e, o=pst, l=lhs, r=rhs, k=kc: e.matmul(
                    out=o, lhsT=l, rhs=r, start=(k == 0), stop=(k == 7)),
                    waits=w, inc_sem=(s_mm if kc == 7 else None))
            mmv = p.cur(s_mm)
            dst = self.carve("logitsT")[:, ncol * 512:(ncol + 1) * 512]
            p.emit_i("vector", lambda e, s=pst, o=dst: e.tensor_copy(out=o, in_=s),
                     waits=[(s_mm, mmv)], inc_sem=s_ve)
        ve_lg = p.cur(s_ve)
        if substage <= 1:
            return None
        for it in range(NT):
            src = self.carve("logitsT")[:, it * 128:(it + 1) * 128]
            dst = self.carve("lgtile")[:, it * E:(it + 1) * E]
            self.emit_transpose(src, dst, [(s_ve, ve_lg)] if it == 0 else [],
                                f32=True, ident=self.identf_sb[0:64, 0:64])
        if substage <= 2:
            return None
        m8, idx8, idxf, w4 = sb["m8"], sb["idx8"], sb["idxf"], sb["w4"]
        ohr, oh, rtmp = self.carve("ohr_c"), sb["oh"], sb["rtmp"]
        for it in range(NT):
            lg = self.carve("lgtile")[:, it * E:(it + 1) * E]
            m8t = m8[:, it * 8:(it + 1) * 8]
            id8t = idx8[:, it * 8:(it + 1) * 8]
            idft = idxf[:, it * K4:(it + 1) * K4]
            w4t = w4[:, it * K4:(it + 1) * K4]
            sc0 = rtmp[:, it * 8:it * 8 + 1]
            sc1 = rtmp[:, it * 8 + 1:it * 8 + 2]
            p.emit("vector", lambda e, s=lg, o=m8t: e.max(out=o, in_=s))
            p.emit("vector", lambda e, s=lg, m=m8t, o=id8t: e.max_index(out=o, in_max=m, in_values=s))
            p.emit("vector", lambda e, s=id8t, o=idft: e.tensor_copy(out=o, in_=s[:, 0:K4]))
            p.emit_i("vector", lambda e, s=m8t, o=sc0: e.tensor_scalar(
                out=o, in0=s[:, 0:1], scalar1=-1.0, scalar2=None, op0=ALU.mult), inc_sem=s_ve)
            vev = p.cur(s_ve)
            p.emit_i("scalar", lambda e, s=m8t, o=w4t, b=sc0: e.activation(
                out=o, in_=s[:, 0:K4], func=AF.Exp, bias=b, scale=1.0),
                waits=[(s_ve, vev)], inc_sem=s_ac)
            acv = p.cur(s_ac)
            p.emit("vector", lambda e, s=w4t, o=sc1: e.reduce_sum(out=o, in_=s, axis=AX.X),
                   waits=[(s_ac, acv)])
            p.emit("vector", lambda e, s=sc1, o=sc1: e.reciprocal(out=o, in_=s))
            p.emit("vector", lambda e, s=w4t, o=w4t, sc=sc1: e.tensor_scalar(
                out=o, in0=s, scalar1=sc, scalar2=None, op0=ALU.mult))
            for r in range(K4):
                ohrt = ohr[:, (it * K4 + r) * E:(it * K4 + r + 1) * E]
                p.emit("vector", lambda e, o=ohrt, idc=idft[:, r:r + 1]: e.tensor_scalar(
                    out=o, in0=sb["iota_e_sb"][:], scalar1=idc, scalar2=None, op0=ALU.is_equal))
            oht = oh[:, it * E:(it + 1) * E]
            p.emit("vector", lambda e, o=oht, a=ohr[:, (it * K4 + 0) * E:(it * K4 + 1) * E],
                   b=ohr[:, (it * K4 + 1) * E:(it * K4 + 2) * E]: e.tensor_tensor(out=o, in0=a, in1=b, op=ALU.add))
            p.emit("vector", lambda e, o=oht, a=ohr[:, (it * K4 + 2) * E:(it * K4 + 3) * E]:
                   e.tensor_tensor(out=o, in0=o, in1=a, op=ALU.add))
            p.emit_i("vector", lambda e, o=oht, a=ohr[:, (it * K4 + 3) * E:(it * K4 + 4) * E]:
                     e.tensor_tensor(out=o, in0=o, in1=a, op=ALU.add), inc_sem=s_ve)
        self.ve_oh = p.cur(s_ve)
        if substage <= 3:
            return None
        for it in range(NT):
            oht = oh[:, it * E:(it + 1) * E]
            w = [(s_ve, self.ve_oh)] if it == 0 else []
            p.emit_i("tensor", lambda e, r=oht, k=it: e.matmul(
                out=self.bank(7, 64), lhsT=self.allones, rhs=r, start=(k == 0), stop=(k == NT - 1)),
                waits=w, inc_sem=(s_mm if it == NT - 1 else None))
        mmv = p.cur(s_mm)
        cntrow = sb["bownrow"]
        p.emit_i("vector", lambda e: e.tensor_copy(out=sb["rtmp"][0:1, 0:E], in_=self.bank(7, 64)[0:1, :]),
                 waits=[(s_mm, mmv)], inc_sem=s_ve)
        ve_cnt = p.cur(s_ve)
        p.emit_i("sync", lambda e: e.dma_start(out=self.dram["cnt_src"], in_=sb["rtmp"][0:1, 0:E]),
                 waits=[(s_ve, ve_cnt)], inc_sem=self.s_st, inc_n=16)
        return p.cur(self.s_st)



    def emit_counts_exchange(self, st_cnt):
        p, sb, dram = self.p, self.sb, self.dram
        s_cc, s_ve = self.s_cc, self.s_ve
        cc = p.emit_i("gpsimd", lambda e: e.collective_compute(
            "AllGather", ALU.bypass, replica_groups=[list(range(NCORES))],
            ins=[dram["cnt_src"].opt()], outs=[dram["cnt_all"].opt()]),
            waits=[(self.s_st, st_cnt)], inc_sem=s_cc, kind="dma")
        cntrows = sb["cntrows"][0:1, :]
        p.emit_i("sync", lambda e: e.dma_start(
            out=cntrows, in_=dram["cnt_all"].rearrange("c e -> (c e)").unsqueeze(0)),
            waits=[(s_cc, cc)], inc_sem=self.s_in, inc_n=16)
        ld = p.cur(self.s_in)
        bown = sb["bownrow"][0:1, :]
        pm = sb["prevmask_sb"]
        p.emit("vector", lambda e: e.memset(bown, 0.0), waits=[(self.s_in, ld)])
        for c in range(NCORES):
            row = cntrows[:, c * E:(c + 1) * E]
            p.emit("vector", lambda e, a=row, m=pm[0:1, c:c + 1]: e.scalar_tensor_tensor(
                out=bown, in0=a, scalar=m, in1=bown, op0=ALU.mult, op1=ALU.add))
        p.emit_i("vector", lambda e: e.nop(), inc_sem=s_ve)
        return p.cur(s_ve)

    def emit_gpos_scatter(self, ve_bown):
        p, sb, ps, dram = self.p, self.sb, self.ps, self.dram
        s_mm, s_ve, s_po, s_cc = self.s_mm, self.s_ve, self.s_po, self.s_cc
        oh, ohr, idxf, w4 = sb["oh"], self.carve("ohr_c"), sb["idxf"], sb["w4"]
        zt2 = dram["g_table"].rearrange("(p n) two -> p (n two)", p=128)
        zv = p.emit_i("vector", lambda e: e.memset(self.carve("gath_f32")[:, 0:E * CAP * 2 // 128], 0.0),
                      waits=[(self.s_po, p.cur(self.s_po))], inc_sem=self.s_ve)
        p.emit_i("sync", lambda e: e.dma_start(out=zt2, in_=self.carve("gath_f32")[:, 0:E * CAP * 2 // 128]),
                 waits=[(self.s_ve, zv)], inc_sem=self.s_st, inc_n=16)
        st_zero = p.cur(self.s_st)

        bown_bc = sb["bownbc"]
        p.emit_i("sync", lambda e: e.dma_start(out=dram["bown_dram"], in_=sb["bownrow"][0:1, :]),
                 waits=[(s_ve, ve_bown)], inc_sem=self.s_st, inc_n=16)
        st_b1 = p.cur(self.s_st)
        p.emit_i("sync", lambda e: e.dma_start(out=bown_bc[:], in_=dram["bown_dram"].broadcast_to([128, E])),
                 waits=[(self.s_st, st_b1)], inc_sem=self.s_st, inc_n=16)
        st_bbc = p.cur(self.s_st)
        gpos = sb["pos4"]
        pe_sb = sb["petile"]
        ve_pe_prev = None
        for it in range(NT):
            oht = oh[:, it * E:(it + 1) * E]
            w = [] if ve_pe_prev is None else [(s_ve, ve_pe_prev)]
            p.emit_i("tensor", lambda e, r=oht: e.matmul(out=self.bank(6, 64), lhsT=self.tri_excl,
                                                         rhs=r, start=True, stop=True),
                     waits=w, inc_sem=s_mm)
            mmP = p.cur(s_mm)
            wv = [(s_mm, mmP)]
            if it == 0:
                wv.append((self.s_st, st_bbc))
            p.emit("vector", lambda e: e.tensor_tensor(out=pe_sb[:, 0:E], in0=self.bank(6, 64),
                                                       in1=bown_bc[:], op=ALU.add),
                   waits=wv)
            if it > 0:
                p.emit_i("vector", lambda e: e.tensor_tensor(out=pe_sb[:, 0:E], in0=pe_sb[:, 0:E],
                                                             in1=self.bank(7, 64), op=ALU.add), inc_sem=s_ve)
            else:
                p.emit_i("vector", lambda e: e.nop(), inc_sem=s_ve)
            ve_pe = p.cur(s_ve)
            ve_pe_prev = ve_pe
            if it < NT - 1:
                p.emit_i("tensor", lambda e, r=oht, first=(it == 0), last=(it == NT - 2): e.matmul(
                    out=self.bank(7, 64), lhsT=self.allones, rhs=r, start=first, stop=last,
                    skip_group_check=True),
                         waits=[(s_ve, ve_pe)], inc_sem=s_mm)
            for r in range(K4):
                ohrt = ohr[:, (it * K4 + r) * E:(it * K4 + r + 1) * E]
                gcol = gpos[:, it * K4 + r:it * K4 + r + 1]
                scr = self.carve("lgtile")[:, 0:E]
                p.emit("vector", lambda e, a=pe_sb[:, 0:E], b=ohrt, o=scr:
                       e.tensor_tensor(out=o, in0=a, in1=b, op=ALU.mult))
                p.emit("vector", lambda e, s=scr, o=gcol: e.reduce_sum(out=o, in_=s, axis=AX.X))
            p.emit_i("vector", lambda e: e.engine_nop(), inc_sem=s_ve)
        ve_gpos = p.cur(s_ve)

        sidx, pairs, rtmp = sb["sidx"], sb["pairs"], sb["rtmp"]
        tok0 = sb["tokid_sb"][:, 0:1]
        idx8 = sb["idx8"]
        bscr = sb["rtmp"].bitcast(U32)
        for it in range(NT):
            i0 = it * K4
            eu = idx8[:, it * 8: it * 8 + K4]
            gp = gpos[:, i0:i0 + K4]
            fidx = sb["fidx"][:, i0:i0 + K4]
            b1, b2, b3 = bscr[:, 0:K4], bscr[:, K4:2 * K4], bscr[:, 2 * K4:3 * K4]
            p.emit("vector", lambda e, o=b1, a=eu: e.tensor_scalar(
                out=o, in0=a, scalar1=3, scalar2=10, op0=ALU.logical_shift_right, op1=ALU.logical_shift_left),
                waits=([(s_ve, ve_gpos)] if it == 0 else []))
            p.emit("vector", lambda e, o=b2, a=eu: e.tensor_scalar(
                out=o, in0=a, scalar1=3, scalar2=8, op0=ALU.bitwise_and, op1=ALU.logical_shift_left))
            p.emit("vector", lambda e, o=b3, a=eu: e.tensor_scalar(
                out=o, in0=a, scalar1=4, scalar2=11, op0=ALU.bitwise_and, op1=ALU.logical_shift_left))
            p.emit("vector", lambda e, o=b1: e.tensor_tensor(out=o, in0=o, in1=b2, op=ALU.add))
            p.emit("vector", lambda e, o=b1: e.tensor_tensor(out=o, in0=o, in1=b3, op=ALU.add))
            p.emit("vector", lambda e, o=fidx, a=b1: e.tensor_copy(out=o, in_=a))
            p.emit("vector", lambda e, o=fidx, b=gp: e.tensor_tensor(out=o, in0=o, in1=b, op=ALU.add))
            mcol = sb["fidx"][:, NT * K4 + i0: NT * K4 + i0 + K4]
            p.emit("vector", lambda e, o=mcol, a=gp: e.tensor_scalar(
                out=o, in0=a, scalar1=float(CAP), scalar2=float(BIGIDX), op0=ALU.is_ge, op1=ALU.mult))
            p.emit("vector", lambda e, o=fidx, a=mcol: e.tensor_tensor(out=o, in0=o, in1=a, op=ALU.add))
            p.emit("vector", lambda e, o=sidx[:, i0:i0 + K4], a=fidx: e.tensor_copy(out=o, in_=a))
            for r in range(K4):
                pc = pairs[:, (i0 + r) * 2:(i0 + r) * 2 + 1]
                p.emit("vector", lambda e, o=pc, a=tok0, rr=r, itt=it: e.tensor_scalar(
                    out=o, in0=a, scalar1=4.0, scalar2=float((itt * 128) * 4 + rr + 1),
                    op0=ALU.mult, op1=ALU.add))
            wv = pairs[:, i0 * 2:(i0 + K4) * 2].rearrange("p (k two) -> p k two", two=2)[:, :, 1:2]
            p.emit_i("vector", lambda e, o=wv, a=w4[:, i0:i0 + K4]: e.tensor_scalar(
                out=o, in0=a.unsqueeze(2), scalar1=0.5, scalar2=None, op0=ALU.mult), inc_sem=s_ve)
        ve_pairs = p.cur(s_ve)

        po_w = [(s_ve, ve_pairs), (self.s_st, st_zero)]
        first = True
        nsc = 0
        for it in range(NT):
            for r in range(K4):
                col = it * K4 + r
                idxap = sidx[:, col:col + 1]
                datap = pairs[:, col * 2:col * 2 + 2]
                w = list(po_w) if first else []
                if nsc >= 8:
                    w.append((s_po, p.cur(s_po) - 7 * 16))
                p.emit_i("gpsimd", lambda e, ia=idxap, da=datap: e.indirect_dma_start(
                    out=dram["g_table"], out_offset=IndirectOffsetOnAxis(ia, 0),
                    in_=da, in_offset=None,
                    bounds_check=self._bc["r"], oob_is_err=False),
                    waits=w, inc_sem=s_po, inc_n=16, kind="dma")
                first = False
                nsc += 1
        po_sct = p.cur(s_po)
        cc = p.emit_i("gpsimd", lambda e: e.collective_compute(
            "AllReduce", ALU.add, replica_groups=[list(range(NCORES))],
            ins=[dram["g_table"].opt()], outs=[dram["g_table_all"].opt()]),
            waits=[(s_po, po_sct)], inc_sem=self.s_cc, kind="dma")
        return cc


    def bank(self, b, cols=512):
        return self.ps["pb"][:, b * 512:b * 512 + cols]

    def bankbf(self, b, cols=1024):
        return self.ps["pb"].bitcast(BF16)[:, b * 1024:b * 1024 + cols]

    def emit_transpose(self, srcap, dst, first_waits=(), f32=False, ident=None):
        p = self.p
        bidx = self._tb_n % 2
        ncols = srcap.shape[0]
        if f32:
            pst = self.bank(bidx, 512)[:, 0:ncols]
        else:
            pst = self.bankbf(bidx, 1024)[:, 0:ncols]
        wv = list(first_waits)
        if self._tb_evac[bidx] is not None:
            wv.append((self.s_ve, self._tb_evac[bidx]))
        kk = srcap.shape[0] if hasattr(srcap, "shape") else 128
        idt = ident if ident is not None else (self.identf_sb[0:kk, 0:kk] if f32 else self.ident[0:kk, 0:kk])
        p.emit_i("tensor", lambda e, o=pst, s=srcap, ii=idt: e.transpose(out=o, in_=s, identity=ii),
                 waits=wv, inc_sem=self.s_mm)
        mmv = p.cur(self.s_mm)
        tgt = p.emit_i("vector", lambda e, s=pst, o=dst: e.tensor_copy(out=o, in_=s),
                       waits=[(self.s_mm, mmv)], inc_sem=self.s_ve)
        self._tb_evac[bidx] = tgt
        self._tb_n += 1
        return tgt

    def emit_dequant_matrix(self, mat, j, mm_mark=None, scr_mark=None):
        p, sb, dram = self.p, self.sb, self.dram
        s_wt, s_ve = self.s_wt, self.s_ve
        width = I if mat in ("gate", "up") else D
        ntile = (D // 2 // 128) if mat in ("gate", "up") else (I // 2 // 128)
        if j < 0:
            pT, sT = dram[f"sh_{mat}_pT"], dram[f"sh_{mat}_sT"]
        else:
            pT, sT = dram[f"{mat}_pT"][j], dram[f"{mat}_sT"][j]
        wlo, whi = self.carve(f"w_{mat}_lo"), self.carve(f"w_{mat}_hi")
        if mm_mark is None:
            mm_mark = p.cur(self.s_mm)
        for dt in range(ntile):
            par = self._deq_par
            self._deq_par ^= 1
            pk_t = self.carve("pk_t2" if par else "pk_t")[:, 0:width]
            st_t = self.carve("st_t2" if par else "st_t")[:, 0:width]
            s_ld = self.s_wt2 if par else s_wt
            wv = [(s_ve, self._deq_free[par])] + ([(self.s_mm, mm_mark)] if dt == 0 else [])
            if self._mk_ld_uses > 0:
                wv = wv + self._mk_ld_extra
                self._mk_ld_uses -= 1
            p.emit_i("sync", lambda e, t=pk_t, s=pT[dt * 128:(dt + 1) * 128, :]: e.dma_start(out=t, in_=s),
                     waits=wv, inc_sem=s_ld, inc_n=16)
            p.emit_i("sync", lambda e, t=st_t, s=sT[dt * 2:dt * 2 + 2, :].unsqueeze(1).broadcast_to([2, 64, width]):
                     e.dma_start(out=t, in_=s), inc_sem=s_ld, inc_n=16)
            ld = p.cur(s_ld)
            for plane, wbuf in (("lo", wlo), ("hi", whi)):
                wt = wbuf[:, dt * width:(dt + 1) * width]
                u = self.carve("lgtile").bitcast(I16)[:, 0:width]
                g2 = pk_t.bitcast(BF16) if plane == "hi" else None
                sg = self.carve("ohr_c").bitcast(I16)[:, 0:width]
                m1, s1 = (7, 6) if plane == "lo" else (0x70, 2)
                m2, s2 = (8, 12) if plane == "lo" else (0x80, 8)
                wv = [(s_ld, ld)] if plane == "lo" else []
                if dt == 0 and plane == "lo" and scr_mark is not None:
                    wv = wv + [(self.s_mm, scr_mark)]
                    scr_mark = None
                p.emit("vector", lambda e, b=pk_t, o=sg, mm=m2, ss=s2: e.tensor_scalar(
                    out=o, in0=b, scalar1=mm, scalar2=ss, op0=ALU.bitwise_and, op1=ALU.logical_shift_left),
                    waits=wv)
                p.emit("vector", lambda e, b=pk_t, o=u, mm=m1, ss=s1: e.tensor_scalar(
                    out=o, in0=b, scalar1=mm, scalar2=ss, op0=ALU.bitwise_and, op1=ALU.logical_shift_left))
                p.emit("vector", lambda e, o=u: e.tensor_scalar(
                    out=o, in0=o, scalar1=0x3F00, scalar2=None, op0=ALU.add))
                gdst = (pk_t if plane == "hi" else self.carve("lgtile").bitcast(I16)[:, 1024:1024 + width])
                p.emit("vector", lambda e, o=gdst.bitcast(BF16), f=u.bitcast(BF16): e.tensor_scalar(
                    out=o, in0=f, scalar1=2.0, scalar2=-1.0, op0=ALU.mult, op1=ALU.add))
                p.emit("vector", lambda e, o=u.bitcast(BF16), f=gdst.bitcast(BF16): e.tensor_tensor(
                    out=o, in0=o, in1=f, op=ALU.min))
                p.emit("vector", lambda e, o=u.bitcast(BF16), s=st_t: e.tensor_tensor(
                    out=o, in0=o, in1=s, op=ALU.mult))
                p.emit_i("vector", lambda e, o=wt.bitcast(I16), a=u, b2=sg:
                         e.tensor_tensor(out=o, in0=a, in1=b2, op=ALU.bitwise_xor), inc_sem=s_ve)
            self._deq_free[par] = p.cur(s_ve)
        return p.cur(s_ve)

    def emit_expert(self, call, j):
        p, sb, ps, dram = self.p, self.sb, self.ps, self.dram
        s_mm, s_ve, s_ac, s_po, s_st = self.s_mm, self.s_ve, self.s_ac, self.s_po, self.s_st
        mk_gu_prev, mk_dn_prev = self._mk_gu, self._mk_dn
        mk_ebT_prev, mk_gath_prev = self._mk_ebT, self._mk_gath
        scr0 = self._mk_scr
        self._mk_scr = None

        for h in range(2):
            icol = sb["eidx"][:, j * 2 + h:j * 2 + h + 1]
            p.emit("vector", lambda e, o=sb["fidx"][:, 0:1],
                   off=float((j & 3) * 256 + h * 128 + (8192 if j >= 4 else 0)):
                   e.tensor_scalar(out=o, in0=sb["expbase_sb"][:, 0:1], scalar1=off, scalar2=None, op0=ALU.add))
            p.emit_i("vector", lambda e, o=icol: e.tensor_copy(out=o, in_=sb["fidx"][:, 0:1]), inc_sem=s_ve)
            vei = p.cur(s_ve)
            gp = sb["gpair"][:, (j * 2 + h) * 2:(j * 2 + h) * 2 + 2]
            p.emit_i("gpsimd", lambda e, ia=icol, o=gp: e.indirect_dma_start(
                out=o, out_offset=None, in_=dram["g_table_all"],
                in_offset=IndirectOffsetOnAxis(ia, 0),
                bounds_check=self._bc["r"], oob_is_err=False),
                waits=[(s_ve, vei), (self.s_cc, p.cur(self.s_cc)), (s_ve, p.cur(s_ve))],
                inc_sem=s_po, inc_n=16, kind="dma")
            pog = p.cur(s_po)
            tk = sb["tokcol"][:, j * 2 + h:j * 2 + h + 1]
            p.emit("vector", lambda e, o=sb["iscr"][:, 0:1], a=gp: e.tensor_copy(out=o, in_=a[:, 0:1]),
                   waits=[(s_po, pog)])
            p.emit("vector", lambda e, o=sb["iscr"][:, 0:1]: e.tensor_scalar(
                out=o, in0=o, scalar1=-1, scalar2=None, op0=ALU.add))
            p.emit("vector", lambda e, o=sb["iscr"][:, 0:1]: e.tensor_scalar(
                out=o, in0=o, scalar1=2, scalar2=None, op0=ALU.arith_shift_right))
            p.emit("vector", lambda e, o=sb["iscr"][:, 1:2], a=gp: e.tensor_scalar(
                out=o, in0=a[:, 0:1], scalar1=1.0, scalar2=float(BIGIDX), op0=ALU.is_lt, op1=ALU.mult))
            p.emit_i("vector", lambda e, o=tk: e.tensor_tensor(
                out=o, in0=sb["iscr"][:, 0:1], in1=sb["iscr"][:, 1:2], op=ALU.add), inc_sem=s_ve)
            vet = p.cur(s_ve)
            gbuf = self.carve("gath_f32")[:] if call == 1 else self.carve("gath_f32")[:].bitcast(BF16)[:, 0:2048]
            gb = gbuf[:, h * D:(h + 1) * D]
            srcrows = dram["x0_full"] if call == 1 else dram["hf2_all"]
            wv = [(s_ve, vet), (s_po, p.cur(s_po)), (s_ac, mk_gath_prev), (s_mm, mk_dn_prev)]
            p.emit_i("gpsimd", lambda e, ia=tk, o=gb, s=srcrows: e.indirect_dma_start(
                out=o, out_offset=None, in_=s, in_offset=IndirectOffsetOnAxis(ia, 0),
                bounds_check=self._bc["r"], oob_is_err=False),
                waits=wv, inc_sem=s_po, inc_n=16, kind="dma")
        po_rows = p.cur(s_po)

        lnw = [(s_po, po_rows), (self.s_mm, mk_ebT_prev)]
        if call == 1:
            ac_buf = self.emit_ln(lambda h: self.carve("gath_f32")[:, h * D:(h + 1) * D],
                                  lambda h: self.carve("ebuf")[:, h * D:(h + 1) * D], 0, lnw, ntiles=2)
        else:
            for h in range(2):
                p.emit_i("scalar", lambda e, o=self.carve("ebuf")[:, h * D:(h + 1) * D],
                         s=self.carve("gath_f32")[:].bitcast(BF16)[:, h * D:(h + 1) * D]: e.copy(out=o, in_=s),
                         waits=(lnw if h == 0 else []), inc_sem=s_ac)
            ac_buf = p.cur(s_ac)

        ve_g = self.emit_dequant_matrix("gate", j, mk_gu_prev, scr_mark=scr0)
        ve_u = self.emit_dequant_matrix("up", j, mk_gu_prev)

        for h in range(2):
            for pl in range(2):
                for k in range(4):
                    sb_ = h * D + 256 * k + pl
                    srcap = self.carve("ebuf")[:, sb_: sb_ + 256 - pl: 2]
                    dst = self.carve("ebufT")[:, ((pl * 4 + k) * 2 + h) * 128: ((pl * 4 + k) * 2 + h) * 128 + 128]
                    self.emit_transpose(srcap, dst,
                                        [(s_ac, ac_buf), (s_mm, mk_dn_prev)] if (h == 0 and pl == 0 and k == 0) else [])
        ve_ebT = p.cur(s_ve)
        self._mk_ebT = p.cur(s_mm)

        for h in range(2):
            for ic in range(2):
                grp = [(s_ve, max(ve_ebT, ve_g, ve_u)), (s_ac, p.cur(s_ac)), (s_ve, p.cur(s_ve))]
                for mi, bk in ((0, 2), (1, 3)):
                    n = 0
                    for pl in range(2):
                        wsrc = (self.carve("w_gate_lo"), self.carve("w_gate_hi"), self.carve("w_up_lo"), self.carve("w_up_hi"))[mi * 2 + pl]
                        for k in range(4):
                            lhs = self.carve("ebufT")[:, ((pl * 4 + k) * 2 + h) * 128: ((pl * 4 + k) * 2 + h) * 128 + 128]
                            rhs = wsrc[:, k * I + ic * 384: k * I + ic * 384 + 384]
                            w = grp if (n == 0) else []
                            p.emit_i("tensor", lambda e, l=lhs, r=rhs, st_=(n == 0), sp=(n == 7), b=bk:
                                     e.matmul(out=self.bank(b, 384), lhsT=l, rhs=r, start=st_, stop=sp),
                                     waits=w, inc_sem=(s_mm if n == 7 else None))
                            n += 1
                mmgu = p.cur(s_mm)
                sg_t = self.carve("hsil")[:, 0:384]
                p.emit_i("scalar", lambda e, o=sg_t: e.activation(out=o, in_=self.bank(2, 384), func=AF.Silu),
                         waits=[(s_mm, mmgu)], inc_sem=s_ac)
                acs = p.cur(s_ac)
                hu = self.carve("hU")[:, h * I + ic * 384: h * I + ic * 384 + 384]
                p.emit_i("vector", lambda e, o=hu, a=sg_t: e.tensor_tensor(
                    out=o, in0=a, in1=self.bank(3, 384), op=ALU.mult),
                    waits=[(s_ac, acs)], inc_sem=s_ve)
        ve_hu = p.cur(s_ve)
        self._mk_gu = p.cur(s_mm)
        ve_d = self.emit_dequant_matrix("down", j, mk_dn_prev)

        for h in range(2):
            for pl in range(2):
                for m in range(3):
                    srcap = self.carve("hU")[:, h * I + pl * 384 + m * 128: h * I + pl * 384 + (m + 1) * 128]
                    dst = self.carve("ebufT")[:, ((pl * 3 + m) * 2 + h) * 128: ((pl * 3 + m) * 2 + h) * 128 + 128]
                    self.emit_transpose(srcap, dst, [(s_ve, ve_hu)] if (h == 0 and pl == 0 and m == 0) else [])
        ve_huT = p.cur(s_ve)

        for h in range(2):
            grp = [(s_ve, max(ve_huT, ve_d)), (s_ac, p.cur(s_ac))]
            for nc_ in range(2):
                n = 0
                for pl in range(2):
                    wsrc = self.carve("w_down_lo") if pl == 0 else self.carve("w_down_hi")
                    for m in range(3):
                        lhs = self.carve("ebufT")[:, ((pl * 3 + m) * 2 + h) * 128: ((pl * 3 + m) * 2 + h) * 128 + 128]
                        rhs = wsrc[:, m * D + nc_ * 512: m * D + nc_ * 512 + 512]
                        w = grp if (n == 0 and nc_ == 0) else []
                        p.emit_i("tensor", lambda e, l=lhs, r=rhs, st_=(n == 0), sp=(n == 5), b=4 + nc_:
                                 e.matmul(out=self.bank(b, 512), lhsT=l, rhs=r, start=st_, stop=sp),
                                 waits=w, inc_sem=(s_mm if n == 5 else None))
                        n += 1
            mmd = p.cur(s_mm)
            wcol = sb["gpair"][:, (j * 2 + h) * 2 + 1:(j * 2 + h) * 2 + 2]
            ro = self.carve("retst")[:, 0:D]
            p.emit("scalar", lambda e, o=ro[:, 0:512], w_=wcol: e.activation(
                out=o, in_=self.bank(4, 512), func=AF.Identity, scale=w_),
                waits=[(s_mm, mmd), (s_st, p.cur(s_st))])
            p.emit_i("scalar", lambda e, o=ro[:, 512:1024], w_=wcol: e.activation(
                out=o, in_=self.bank(5, 512), func=AF.Identity, scale=w_), inc_sem=s_ac)
            aco = p.cur(s_ac)
            dst = dram["ret_src"].rearrange("(n p) d -> n p d", p=128)[j * 2 + h]
            p.emit_i("scalar", lambda e, o=dst, s=ro: e.dma_start(out=o, in_=s),
                     waits=[(s_ac, aco)], inc_sem=s_st, inc_n=16, kind="dma")
        self._mk_dn = p.cur(s_mm)
        self._mk_gath = ac_buf
        return p.cur(s_st)


    def emit_shared(self, ve_hfT):
        p, sb = self.p, self.sb
        s_mm, s_ve, s_ac = self.s_mm, self.s_ve, self.s_ac
        ve_g = self.emit_dequant_matrix("gate", -1, self._mk_gu)
        ve_u = self.emit_dequant_matrix("up", -1, self._mk_gu)
        ve_d = self.emit_dequant_matrix("down", -1, self._mk_dn)
        hfT = sb["hfT"]
        hsh = self.carve("h_shT")
        last_silu = {0: p.cur(s_ac), 1: p.cur(s_ac)}
        last_hu = {0: p.cur(s_ve), 1: p.cur(s_ve)}
        for tch in range(8):
            for itile in range(6):
                par = itile % 2
                bg, bu = (2, 3) if par == 0 else (6, 7)
                grp = [(s_ve, max(ve_hfT, ve_g, ve_u, last_hu[par])), (s_ac, last_silu[par])]
                for mi, bk in ((0, bg), (1, bu)):
                    n = 0
                    for pl in range(2):
                        wsrc = (self.carve("w_gate_lo"), self.carve("w_gate_hi"), self.carve("w_up_lo"), self.carve("w_up_hi"))[mi * 2 + pl]
                        for kc in range(4):
                            lhs = wsrc[:, kc * I + itile * 128: kc * I + (itile + 1) * 128]
                            rhs = hfT[:, (pl * 4 + kc) * TPC + tch * 256: (pl * 4 + kc) * TPC + (tch + 1) * 256]
                            w = grp if n == 0 else []
                            p.emit_i("tensor", lambda e, l=lhs, r=rhs, st_=(n == 0), sp=(n == 7), b=bk:
                                     e.matmul(out=self.bank(b, 256), lhsT=l, rhs=r, start=st_, stop=sp),
                                     waits=w, inc_sem=(s_mm if n == 7 else None))
                            n += 1
                mmgu = p.cur(s_mm)
                sg_t = self.carve("hsil")[:, par * 256: par * 256 + 256]
                p.emit_i("scalar", lambda e, o=sg_t, b=bg: e.activation(out=o, in_=self.bank(b, 256), func=AF.Silu),
                         waits=[(s_mm, mmgu), (s_ve, last_hu[par])], inc_sem=s_ac)
                acs = p.cur(s_ac)
                last_silu[par] = acs
                p.emit_i("vector", lambda e, o=hsh[:, itile * 256:(itile + 1) * 256], a=sg_t, b=bu:
                         e.tensor_tensor(out=o, in0=a, in1=self.bank(b, 256), op=ALU.mult),
                         waits=[(s_ac, acs)], inc_sem=s_ve)
                last_hu[par] = p.cur(s_ve)
            ve_h = p.cur(s_ve)
            for tt4 in range(2):
                it = tch * 2 + tt4
                grp = [(s_ve, ve_h), (s_ve, p.cur(s_ve))]
                for nc_ in range(2):
                    n = 0
                    for pl in range(2):
                        wsrc = self.carve("w_down_lo") if pl == 0 else self.carve("w_down_hi")
                        for m in range(3):
                            lhs = hsh[:, (pl * 3 + m) * 256 + tt4 * 128: (pl * 3 + m) * 256 + (tt4 + 1) * 128]
                            rhs = wsrc[:, m * D + nc_ * 512: m * D + nc_ * 512 + 512]
                            w = (grp + [(s_ve, ve_d)]) if (n == 0 and nc_ == 0) else []
                            p.emit_i("tensor", lambda e, l=lhs, r=rhs, st_=(n == 0), sp=(n == 5), b=4 + nc_:
                                     e.matmul(out=self.bank(b, 512), lhsT=l, rhs=r, start=st_, stop=sp),
                                     waits=w, inc_sem=(s_mm if n == 5 else None))
                            n += 1
                mmd = p.cur(s_mm)
                for nc_ in range(2):
                    xs = sb["x"][:, it * D + nc_ * 512: it * D + nc_ * 512 + 512]
                    p.emit_i("vector", lambda e, o=xs, b=4 + nc_: e.scalar_tensor_tensor(
                        out=o, in0=self.bank(b, 512), scalar=0.5, in1=o, op0=ALU.mult, op1=ALU.add),
                        waits=([(s_mm, mmd)] if nc_ == 0 else []), inc_sem=s_ve)
        self._mk_gu = self._mk_dn = p.cur(s_mm)
        return p.cur(s_ve)

    def emit_combine(self, cc_ret, ve_mark=None, ac_mark=None, mm_mark=None):
        p, sb, dram = self.p, self.sb, self.dram
        s_ve, s_po, s_pc = self.s_ve, self.s_po, self.s_pc
        if ve_mark is None:
            ve_mark = p.cur(s_ve)
        if ac_mark is None:
            ac_mark = p.cur(self.s_ac)
        if mm_mark is None:
            mm_mark = p.cur(self.s_mm)
        ar = self.sb["arena"][0:128, :].bitcast(BF16)
        bufs = [self.carve("gath_f32")[:].bitcast(BF16),
                ar[:, 43008 // 2: 43008 // 2 + 4096]]
        last_add = {0: ve_mark, 1: ve_mark}
        first = True
        for it in range(NT):
            par = it % 2
            g4 = bufs[par]
            wvz = [(s_ve, last_add[par])]
            if first:
                wvz += [(self.s_ac, ac_mark), (s_po, p.cur(s_po)), (self.s_mm, mm_mark)]
                first = False
            pz = p.emit_i("gpsimd", lambda e, g=g4: e.memset(g[:, 0:4096], 0.0),
                          waits=wvz, inc_sem=s_pc, kind="dma")
            for r in range(K4):
                col = it * K4 + r
                idxap = sb["sidx"][:, col:col + 1]
                gb = g4[:, r * D:(r + 1) * D]
                wv = [(s_pc, pz), (self.s_cc, cc_ret)]
                p.emit_i("gpsimd", lambda e, ia=idxap, o=gb: e.indirect_dma_start(
                    out=o, out_offset=None, in_=dram["ret_all"],
                    in_offset=IndirectOffsetOnAxis(ia, 0),
                    bounds_check=self._bc["r"], oob_is_err=False),
                    waits=wv, inc_sem=s_po, inc_n=16, kind="dma")
            pog = p.cur(s_po)
            for r in range(K4):
                xs = sb["x"][:, it * D:(it + 1) * D]
                p.emit_i("vector", lambda e, o=xs, a=g4[:, r * D:(r + 1) * D]: e.tensor_tensor(
                    out=o, in0=o, in1=a, op=ALU.add),
                    waits=([(s_po, pog)] if r == 0 else []), inc_sem=s_ve)
            last_add[par] = p.cur(s_ve)
        return p.cur(s_ve)


    def emit_transpose_add(self, srcap, add_into, first_waits=()):
        p = self.p
        bidx = self._tb_n % 2
        pst = self.bankbf(bidx, 1024)[:, 0:srcap.shape[0]]
        wv = list(first_waits)
        if self._tb_evac[bidx] is not None:
            wv.append((self.s_ve, self._tb_evac[bidx]))
        p.emit_i("tensor", lambda e, o=pst, s=srcap: e.transpose(out=o, in_=s, identity=self.ident),
                 waits=wv, inc_sem=self.s_mm)
        mmv = p.cur(self.s_mm)
        tgt = p.emit_i("vector", lambda e, s=pst, o=add_into: e.tensor_tensor(out=o, in0=o, in1=s, op=ALU.add),
                       waits=[(self.s_mm, mmv)], inc_sem=self.s_ve)
        self._tb_evac[bidx] = tgt
        self._tb_n += 1
        return tgt

    def _w8(self, i):
        ap = self.sb["arena"][0:128, :].bitcast(BF16)
        if i < 2:
            off = self._ARENA_ATTN["wkb"][0]
            return ap[:, off // 2 + i * 1024: off // 2 + (i + 1) * 1024]
        if i < 4:
            off = self._ARENA_ATTN["rt_buf"][0]
            return ap[:, off // 2 + (i - 2) * 1024: off // 2 + (i - 1) * 1024]
        if i < 6:
            off = self._ARENA_ATTN["kh_buf"][0]
            return ap[:, off // 2 + (i - 4) * 1024: off // 2 + (i - 3) * 1024]
        off = self._ARENA_ATTN["khat_tm"][0]
        return ap[:, off // 2 + (i - 6) * 1024: off // 2 + (i - 5) * 1024]

    def emit_load_w8(self, wname, fold_tf):
        p, dram = self.p, self.dram
        wv = [(self.s_mm, p.cur(self.s_mm)), (self.s_ve, p.cur(self.s_ve)), (self.s_ac, p.cur(self.s_ac))]
        for kc in range(8):
            dst = self._w8(kc)
            p.emit_i("gpsimd", lambda e, o=dst, s=dram[wname][kc * 128:(kc + 1) * 128, :]:
                     e.dma_start(out=o, in_=s), waits=(wv if kc == 0 else []),
                     inc_sem=self.s_po, inc_n=16, kind="dma")
        pot = p.cur(self.s_po)
        if fold_tf:
            for kc in range(8):
                dst = self._w8(kc)
                p.emit_i("vector", lambda e, o=dst: e.tensor_tensor(out=o, in0=o, in1=self.carve("tfb"), op=ALU.mult),
                         waits=([(self.s_po, pot)] if kc == 0 else []), inc_sem=self.s_ve)
            return ("ve", p.cur(self.s_ve))
        return ("po", pot)

    def emit_attn_setup(self):
        p, sb, dram = self.p, self.sb, self.dram
        s_ve, s_ac, s_in, s_po = self.s_ve, self.s_ac, self.s_in, self.s_po
        cv = self.carve
        p.emit_i("gpsimd", lambda e: e.dma_start(out=cv("tfb"), in_=dram["tfirst"].broadcast_to([128, D])),
                 waits=[(s_ve, p.cur(s_ve)), (s_ac, p.cur(s_ac)), (self.s_mm, p.cur(self.s_mm))],
                 inc_sem=s_po, inc_n=16, kind="dma")
        lnw = cv("wCt")[:, 0:16]
        wC = cv("wCt")[:, 16:32]
        td = cv("wCt")[:, 32:48]
        wv0 = [(s_ve, p.cur(s_ve)), (s_ac, p.cur(s_ac)), (self.s_mm, p.cur(self.s_mm)),
               (self.s_st, p.cur(self.s_st))]
        p.emit_i("sync", lambda e: e.dma_start(out=td, in_=dram["tdecay64"]), waits=wv0, inc_sem=s_in, inc_n=16)
        ldt = p.cur(s_in)
        p.emit_i("scalar", lambda e: e.activation(out=lnw, in_=td, func=AF.Ln),
                 waits=[(s_in, ldt)], inc_sem=s_ac)
        ramp1 = cv("rt_buf").bitcast(F32)[:, 0:128]
        rampr = cv("rt_buf").bitcast(F32)[:, 128:256]
        p.emit_i("sync", lambda e: e.dma_start(out=ramp1, in_=dram["iota1"].broadcast_to([64, 128])), inc_sem=s_in, inc_n=16)
        p.emit_i("sync", lambda e: e.dma_start(out=rampr, in_=dram["iotar"].broadcast_to([64, 128])), inc_sem=s_in, inc_n=16)
        ldr = p.cur(s_in)
        scr = cv("kh_buf").bitcast(F32)[:, 0:128]
        acl = p.cur(s_ac)
        for h in range(H):
            p.emit("vector", lambda e, hh=h: e.tensor_scalar(
                out=scr, in0=ramp1, scalar1=lnw[:, hh:hh + 1], scalar2=None, op0=ALU.mult),
                waits=([(s_in, ldr), (s_ac, acl)] if h == 0 else [(s_ac, p.cur(s_ac))]))
            p.emit_i("vector", lambda e: e.nop(), inc_sem=s_ve)
            vv = p.cur(s_ve)
            p.emit_i("scalar", lambda e, hh=h: e.activation(out=cv("W1p")[:, hh * 128:(hh + 1) * 128], in_=scr, func=AF.Exp),
                     waits=[(s_ve, vv)], inc_sem=s_ac)
            p.emit("vector", lambda e, hh=h: e.tensor_scalar(
                out=scr, in0=rampr, scalar1=lnw[:, hh:hh + 1], scalar2=None, op0=ALU.mult),
                waits=[(s_ac, p.cur(s_ac))])
            p.emit_i("vector", lambda e: e.nop(), inc_sem=s_ve)
            vv = p.cur(s_ve)
            p.emit_i("scalar", lambda e, hh=h: e.activation(out=cv("W2p")[:, hh * 128:(hh + 1) * 128], in_=scr, func=AF.Exp),
                     waits=[(s_ve, vv)], inc_sem=s_ac)
        p.emit_i("vector", lambda e: e.tensor_scalar(out=cv("wCt")[:, 48:64], in0=lnw, scalar1=128.0, scalar2=None, op0=ALU.mult),
                 waits=[(s_ac, p.cur(s_ac))], inc_sem=s_ve)
        p.emit_i("scalar", lambda e, w=wC: e.activation(out=w, in_=cv("wCt")[:, 48:64], func=AF.Exp),
                 waits=[(s_ve, p.cur(s_ve))], inc_sem=s_ac)
        p.emit_i("sync", lambda e: e.dma_start(out=cv("lngb")[:, 0:16], in_=dram["decay_h"].broadcast_to([128, 16])), inc_sem=s_in, inc_n=16)
        ldg = p.cur(s_in)
        p.emit_i("scalar", lambda e: e.activation(out=cv("lngb")[:, 0:16], in_=cv("lngb")[:, 0:16], func=AF.Ln),
                 waits=[(s_in, ldg)], inc_sem=s_ac)
        acg = p.cur(s_ac)
        ioij = cv("wkb").bitcast(F32)[:, 0:128]
        p.emit_i("sync", lambda e: e.dma_start(out=ioij, in_=dram["iota_ij"]), inc_sem=s_in, inc_n=16)
        trit = cv("zmask").bitcast(U8)[:, 0:128]
        p.emit_i("sync", lambda e: e.dma_start(out=trit, in_=dram["tri_ij_u8"]), inc_sem=s_in, inc_n=16)
        ldij = p.cur(s_in)
        zt = cv("Am")[:, 0:128]
        p.emit("vector", lambda e: e.memset(zt, 0.0), waits=[(s_po, p.cur(s_po))])
        escr = cv("wkb").bitcast(F32)[:, 128:256]
        escb = cv("wkb")[:, 512:640]
        for h in range(H):
            p.emit("vector", lambda e, hh=h: e.tensor_scalar(
                out=escr, in0=ioij, scalar1=cv("lngb")[:, hh:hh + 1], scalar2=None, op0=ALU.mult),
                waits=([(s_in, ldij), (s_ac, acg)] if h == 0 else []))
            p.emit("vector", lambda e: e.tensor_scalar(
                out=escr, in0=escr, scalar1=0.0, scalar2=None, op0=ALU.min))
            p.emit_i("vector", lambda e: e.nop(), inc_sem=s_ve)
            vv = p.cur(s_ve)
            p.emit_i("scalar", lambda e: e.activation(out=escb, in_=escr, func=AF.Exp),
                     waits=[(s_ve, vv)], inc_sem=s_ac)
            ace = p.cur(s_ac)
            p.emit_i("vector", lambda e, hh=h: e.select(
                out=cv("Dmask")[:, hh * 128:(hh + 1) * 128], mask=trit, on_true=escb, on_false=zt),
                waits=[(s_ac, ace)], inc_sem=s_ve)
        p.emit("vector", lambda e: e.memset(cv("Sst")[:], 0.0))
        p.emit_i("vector", lambda e: e.memset(cv("Sbf")[:], 0.0), inc_sem=s_ve)
        return p.cur(s_ve)

    def emit_attn_quarter(self, q):
        p, sb = self.p, self.sb
        s_mm, s_ve, s_ac, s_po = self.s_mm, self.s_ve, self.s_ac, self.s_po
        cv = self.carve
        hfT = sb["hfT"]
        qoff = q * 512
        for wname, dstbuf, fold in (("wk_T", "kT", True), ("wr_T", "rT", False)):
            t = self.emit_load_w8(wname, fold)
            tw = [(self.s_ve, t[1])] if t[0] == "ve" else [(self.s_po, t[1])]
            for dks in range(8):
                for kc in range(8):
                    lhs = self._w8(kc)[:, dks * 128:(dks + 1) * 128]
                    rhs = hfT[:, kc * TPC + qoff: kc * TPC + qoff + 512]
                    wv = (list(tw) + [(s_ve, p.cur(s_ve)), (s_ac, p.cur(s_ac))]) if kc == 0 else []
                    p.emit_i("tensor", lambda e, l=lhs, r=rhs, st_=(kc == 0), sp=(kc == 7):
                             e.matmul(out=self.bank(2, 512), lhsT=l, rhs=r, start=st_, stop=sp),
                             waits=wv, inc_sem=(s_mm if kc == 7 else None))
                mmk = p.cur(s_mm)
                for par in range(2):
                    h = dks * 2 + par
                    dst = cv(dstbuf)[:, h * 512:(h + 1) * 512]
                    if dstbuf == "rT":
                        p.emit_i("scalar", lambda e, o=dst, pp=par: e.activation(
                            out=o, in_=self.bank(2, 512)[pp * 64:(pp + 1) * 64, :], func=AF.Sigmoid),
                            waits=[(s_mm, mmk)], inc_sem=s_ac)
                    else:
                        p.emit_i("vector", lambda e, o=dst, pp=par: e.tensor_copy(
                            out=o, in_=self.bank(2, 512)[pp * 64:(pp + 1) * 64, :]),
                            waits=[(s_mm, mmk)], inc_sem=s_ve)
        t = self.emit_load_w8("wv_T", False)
        tw = [(self.s_po, t[1])]
        for tt in range(4):
            for dvh in range(2):
                for kc in range(8):
                    lhs = hfT[:, kc * TPC + qoff + tt * 128: kc * TPC + qoff + (tt + 1) * 128]
                    rhs = self._w8(kc)[:, dvh * 512:(dvh + 1) * 512]
                    wv = (list(tw) + [(s_ve, p.cur(s_ve))]) if kc == 0 else []
                    p.emit_i("tensor", lambda e, l=lhs, r=rhs, st_=(kc == 0), sp=(kc == 7), b=4 + dvh:
                             e.matmul(out=self.bank(b, 512), lhsT=l, rhs=r, start=st_, stop=sp),
                             waits=wv, inc_sem=(s_mm if kc == 7 else None))
            mmv = p.cur(s_mm)
            for dvh in range(2):
                dst = cv("v_tm")[:, tt * 1024 + dvh * 512: tt * 1024 + (dvh + 1) * 512]
                p.emit_i("vector", lambda e, o=dst, b=4 + dvh: e.tensor_copy(out=o, in_=self.bank(b, 512)),
                         waits=[(s_mm, mmv)], inc_sem=s_ve)
        ve_proj = p.cur(s_ve)
        import os
        if q == 0 and os.environ.get("ATTNDBG"):
            self._dbg_dump([("dbg_v2", "v_tm")])
        for n in range(4):
            rview = cv("rT").rearrange("p (h t) -> p h t", h=H)[:, :, n * 128:(n + 1) * 128]
            kview = cv("kT").rearrange("p (h t) -> p h t", h=H)[:, :, n * 128:(n + 1) * 128]
            wv0 = [(s_ve, ve_proj), (s_mm, p.cur(s_mm)), (s_ac, p.cur(s_ac))]
            p.emit("vector", lambda e, i0=rview: e.tensor_tensor(
                out=cv("rt_buf").rearrange("p (h t) -> p h t", h=H), in0=i0,
                in1=cv("W1p").rearrange("p (h t) -> p h t", h=H), op=ALU.mult),
                waits=wv0)
            p.emit_i("vector", lambda e, i0=kview: e.tensor_tensor(
                out=cv("kh_buf").rearrange("p (h t) -> p h t", h=H), in0=i0,
                in1=cv("W2p").rearrange("p (h t) -> p h t", h=H), op=ALU.mult), inc_sem=s_ve)
            ve_sc = p.cur(s_ve)
            if n == 0:
                ac_proj = p.cur(s_ac)
                last_msb = {0: 0, 1: 0}
                last_yev = {0: 0, 1: 0}
                sbf_mark = 0
            for hp in range(H // 2):
                pair = (2 * hp, 2 * hp + 1)
                ba = {pair[0]: 3, pair[1]: 4}
                by = {pair[0]: 6, pair[1]: 5}
                ao = {pair[0]: 0, pair[1]: 128}
                mmA, ve_am, ve_kht, mmM, mmy = {}, {}, {}, {}, {}
                for h in pair:
                    ksx = cv("kT")[:, h * 512 + n * 128: h * 512 + (n + 1) * 128]
                    rsx = cv("rT")[:, h * 512 + n * 128: h * 512 + (n + 1) * 128]
                    if h % 2 == 0:
                        wvA = [(s_ve, max(ve_proj, last_msb[0])), (s_ac, ac_proj)]
                    else:
                        wvA = [(s_ve, ve_proj), (s_ac, max(ac_proj, last_msb[1]))]
                    p.emit_i("tensor", lambda e, l=ksx, r=rsx, b=ba[h]: e.matmul(
                        out=self.bank(b, 128), lhsT=l, rhs=r, start=True, stop=True,
                        skip_group_check=True),
                        waits=wvA, inc_sem=s_mm)
                    mmA[h] = p.cur(s_mm)
                for h in pair:
                    p.emit_i("vector", lambda e, hh=h, b=ba[h], o=ao[h]: e.tensor_tensor(
                        out=cv("Am")[:, o:o + 128], in0=self.bank(b, 128),
                        in1=cv("Dmask")[:, hh * 128:(hh + 1) * 128], op=ALU.mult),
                        waits=[(s_mm, mmA[h])], inc_sem=s_ve)
                    ve_am[h] = p.cur(s_ve)
                for h in pair:
                    khs = cv("kh_buf")[:, h * 128:(h + 1) * 128]
                    kht = cv("khat_tm")[:, h * 64:(h + 1) * 64]
                    self.emit_transpose(khs, kht, [(s_ve, ve_sc)])
                    ve_kht[h] = p.cur(s_ve)
                for h in pair:
                    kht = cv("khat_tm")[:, h * 64:(h + 1) * 64]
                    vsx = cv("v_tm")[:, n * 1024 + h * 64: n * 1024 + (h + 1) * 64]
                    p.emit_i("tensor", lambda e, l=kht, r=vsx, b=ba[h]: e.matmul(
                        out=self.bank(b, 64)[0:64, :], lhsT=l, rhs=r, start=True, stop=True,
                        skip_group_check=True),
                        waits=[(s_ve, ve_kht[h]), (s_ve, ve_am[h])], inc_sem=s_mm)
                    mmM[h] = p.cur(s_mm)
                for h in pair:
                    if h % 2 == 0:
                        p.emit_i("vector", lambda e, hh=h, b=ba[h]: e.tensor_copy(
                            out=cv("Msb")[:, hh * 64:(hh + 1) * 64], in_=self.bank(b, 64)[0:64, :]),
                            waits=[(s_mm, mmM[h])], inc_sem=s_ve)
                        last_msb[0] = p.cur(s_ve)
                    else:
                        p.emit_i("scalar", lambda e, hh=h, b=ba[h]: e.copy(
                            out=cv("Msb")[:, hh * 64:(hh + 1) * 64], in_=self.bank(b, 64)[0:64, :]),
                            waits=[(s_mm, mmM[h])], inc_sem=s_ac)
                        last_msb[1] = p.cur(s_ac)
                for h in pair:
                    wy = [(s_ve, max(last_yev[h % 2], ve_am[h], sbf_mark, ve_sc))]
                    p.emit("tensor", lambda e, hh=h, b=by[h]: e.matmul(
                        out=self.bank(b, 128)[0:64, :], lhsT=cv("Sbf")[:, hh * 64:(hh + 1) * 64],
                        rhs=cv("rt_buf")[:, hh * 128:(hh + 1) * 128], start=True, stop=False,
                        skip_group_check=True),
                        waits=wy)
                    vsx = cv("v_tm")[:, n * 1024 + h * 64: n * 1024 + (h + 1) * 64]
                    p.emit_i("tensor", lambda e, l=vsx, b=by[h], o=ao[h]: e.matmul(
                        out=self.bank(b, 128)[0:64, :], lhsT=l, rhs=cv("Am")[:, o:o + 128],
                        start=False, stop=True, skip_group_check=True),
                        waits=[], inc_sem=s_mm)
                    mmy[h] = p.cur(s_mm)
                for h in pair:
                    yd = (cv("yT")[0:64, :] if h % 2 == 0 else cv("yT", hi=True))
                    ydst = yd[:, (h // 2) * 512 + n * 128: (h // 2) * 512 + (n + 1) * 128]
                    p.emit_i("vector", lambda e, o=ydst, b=by[h]: e.tensor_copy(out=o, in_=self.bank(b, 128)[0:64, :]),
                             waits=[(s_mm, mmy[h])], inc_sem=s_ve)
                    last_yev[h % 2] = p.cur(s_ve)
            wvs = [(s_mm, p.cur(s_mm)), (s_ac, p.cur(s_ac))]
            p.emit("vector", lambda e: e.tensor_tensor(
                out=cv("Sst").rearrange("p (h e) -> p h e", h=H),
                in0=cv("Sst").rearrange("p (h e) -> p h e", h=H),
                in1=cv("wCt")[:, 16:32].unsqueeze(2).broadcast_to([64, 16, 64]),
                op=ALU.mult), waits=wvs)
            p.emit("vector", lambda e: e.tensor_tensor(
                out=cv("Sst")[:], in0=cv("Sst")[:], in1=cv("Msb")[:], op=ALU.add))
            p.emit_i("vector", lambda e: e.tensor_copy(out=cv("Sbf")[:], in_=cv("Sst")[:]), inc_sem=s_ve)
            sbf_mark = p.cur(s_ve)
        ve_y = p.cur(s_ve)
        t = self.emit_load_w8("wo_T", False)
        tw = [(self.s_po, t[1])]
        for dmt in range(8):
            for hp in range(8):
                lhs = self._w8(hp)[:, dmt * 128:(dmt + 1) * 128]
                rhs = cv("yT")[:, hp * 512:(hp + 1) * 512]
                wv = (list(tw) + [(s_ve, max(ve_y, p.cur(s_ve)))]) if hp == 0 else []
                p.emit_i("tensor", lambda e, l=lhs, r=rhs, st_=(hp == 0), sp=(hp == 7):
                         e.matmul(out=self.bank(7, 512), lhsT=l, rhs=r, start=st_, stop=sp,
                                  skip_group_check=True),
                         waits=wv, inc_sem=(s_mm if hp == 7 else None))
            mmo = p.cur(s_mm)
            atb = cv("at_b")[:, 0:512]
            p.emit_i("vector", lambda e, o=atb: e.tensor_copy(out=o, in_=self.bank(7, 512)),
                     waits=[(s_mm, mmo)], inc_sem=s_ve)
            ve_atb = p.cur(s_ve)
            for ts4 in range(4):
                it = qoff // 128 + ts4
                tx = sb["x"][:, it * D + dmt * 128: it * D + (dmt + 1) * 128]
                self.emit_transpose_add(atb[:, ts4 * 128:(ts4 + 1) * 128], tx,
                                        first_waits=([(s_ve, ve_atb)] if ts4 == 0 else []))

    def emit_attn_tail(self):
        p, sb, dram = self.p, self.sb, self.dram
        s_mm, s_ve, s_ac, s_po, s_st, s_cc = self.s_mm, self.s_ve, self.s_ac, self.s_po, self.s_st, self.s_cc
        cv = self.carve
        hfT = sb["hfT"]
        ve_now = p.cur(s_ve)
        p.emit_i("sync", lambda e: e.dma_start(out=dram["st_src"], in_=cv("Sst")[:]),
                 waits=[(s_ve, ve_now)], inc_sem=s_st, inc_n=16)
        stt = p.cur(s_st)
        cc = p.emit_i("gpsimd", lambda e: e.collective_compute(
            "AllGather", ALU.bypass, replica_groups=[list(range(NCORES))],
            ins=[dram["st_src"].opt()], outs=[dram["st_all"].opt()]),
            waits=[(s_st, stt)], inc_sem=s_cc, kind="dma")
        p.emit_i("vector", lambda e: e.memset(cv("Msb")[:], 0.0), inc_sem=s_ve)
        vz = p.cur(s_ve)
        p.emit_i("gpsimd", lambda e: e.indirect_dma_start(
            out=cv("Msb")[0:64, :], out_offset=None, in_=dram["st_all"],
            in_offset=IndirectOffsetOnAxis(sb["prow_sb"][0:64, 0:1], 0),
            bounds_check=self._bc["r"], oob_is_err=False),
            waits=[(s_cc, cc), (s_ve, vz)], inc_sem=s_po, inc_n=16, kind="dma")
        pog = p.cur(s_po)
        t = self.emit_load_w8("wr_T", False)
        tw = [(self.s_po, max(t[1], pog))]
        for dks in range(8):
            for kc in range(8):
                lhs = self._w8(kc)[:, dks * 128:(dks + 1) * 128]
                rhs = hfT[:, kc * TPC: kc * TPC + 384]
                wv = (list(tw) + [(s_ve, p.cur(s_ve)), (s_ac, p.cur(s_ac))]) if kc == 0 else []
                p.emit_i("tensor", lambda e, l=lhs, r=rhs, st_=(kc == 0), sp=(kc == 7):
                         e.matmul(out=self.bank(2, 384), lhsT=l, rhs=r, start=st_, stop=sp,
                                  skip_group_check=True),
                         waits=wv, inc_sem=(s_mm if kc == 7 else None))
            mmk = p.cur(s_mm)
            for par in range(2):
                h = dks * 2 + par
                dst = cv("rT")[:, h * 512: h * 512 + 384]
                p.emit_i("scalar", lambda e, o=dst, pp=par: e.activation(
                    out=o, in_=self.bank(2, 384)[pp * 64:(pp + 1) * 64, :], func=AF.Sigmoid),
                    waits=[(s_mm, mmk)], inc_sem=s_ac)
        p.emit_i("vector", lambda e: e.nop(), waits=[(self.s_ac, p.cur(self.s_ac))], inc_sem=s_ve)
        ve_r = p.cur(s_ve)
        p.emit_i("vector", lambda e: e.tensor_copy(out=cv("Sbf")[:], in_=cv("Msb")[:]),
                 waits=[(s_ve, ve_r)], inc_sem=s_ve)
        for n in range(3):
            if n > 0:
                p.emit("vector", lambda e: e.tensor_tensor(
                    out=cv("Msb").rearrange("p (h e) -> p h e", h=H),
                    in0=cv("Msb").rearrange("p (h e) -> p h e", h=H),
                    in1=cv("wCt")[:, 16:32].unsqueeze(2).broadcast_to([64, 16, 64]),
                    op=ALU.mult))
                p.emit_i("vector", lambda e: e.tensor_copy(out=cv("Sbf")[:], in_=cv("Msb")[:]), inc_sem=s_ve)
            rview = cv("rT").rearrange("p (h t) -> p h t", h=H)[:, :, n * 128:(n + 1) * 128]
            p.emit_i("vector", lambda e, i0=rview: e.tensor_tensor(
                out=cv("rt_buf").rearrange("p (h t) -> p h t", h=H), in0=i0,
                in1=cv("W1p").rearrange("p (h t) -> p h t", h=H), op=ALU.mult),
                waits=[(s_ve, p.cur(s_ve)), (s_mm, p.cur(s_mm)), (s_ac, p.cur(s_ac))], inc_sem=s_ve)
            ve_rt = p.cur(s_ve)
            for h in range(H):
                p.emit_i("tensor", lambda e, hh=h: e.matmul(
                    out=self.bank(6, 128)[0:64, :], lhsT=cv("Sbf")[:, hh * 64:(hh + 1) * 64],
                    rhs=cv("rt_buf")[:, hh * 128:(hh + 1) * 128], start=True, stop=True,
                    skip_group_check=True),
                    waits=[(s_ve, max(ve_rt, p.cur(s_ve)))], inc_sem=s_mm)
                mmy = p.cur(s_mm)
                yd = (cv("yT")[0:64, :] if h % 2 == 0 else cv("yT", hi=True))
                ydst = yd[:, (h // 2) * 512 + n * 128: (h // 2) * 512 + (n + 1) * 128]
                p.emit_i("vector", lambda e, o=ydst: e.tensor_copy(out=o, in_=self.bank(6, 128)[0:64, :]),
                         waits=[(s_mm, mmy)], inc_sem=s_ve)
        ve_dy = p.cur(s_ve)
        t = self.emit_load_w8("wo_T", False)
        tw = [(self.s_po, t[1])]
        for dmt in range(8):
            for hp in range(8):
                lhs = self._w8(hp)[:, dmt * 128:(dmt + 1) * 128]
                rhs = cv("yT")[:, hp * 512: hp * 512 + 384]
                wv = (list(tw) + [(s_ve, max(ve_dy, p.cur(s_ve)))]) if hp == 0 else []
                p.emit_i("tensor", lambda e, l=lhs, r=rhs, st_=(hp == 0), sp=(hp == 7):
                         e.matmul(out=self.bank(7, 384), lhsT=l, rhs=r, start=st_, stop=sp,
                                  skip_group_check=True),
                         waits=wv, inc_sem=(s_mm if hp == 7 else None))
            mmo = p.cur(s_mm)
            atb = cv("at_b")[:, 0:384]
            p.emit_i("vector", lambda e, o=atb: e.tensor_copy(out=o, in_=self.bank(7, 384)),
                     waits=[(s_mm, mmo)], inc_sem=s_ve)
            ve_atb = p.cur(s_ve)
            for ts4 in range(3):
                tx = sb["x"][:, ts4 * D + dmt * 128: ts4 * D + (dmt + 1) * 128]
                self.emit_transpose_add(atb[:, ts4 * 128:(ts4 + 1) * 128], tx,
                                        first_waits=([(s_ve, ve_atb)] if ts4 == 0 else []))
        return p.cur(s_ve)

    def _dbg_dump(self, names):
        import os
        if not os.environ.get("ATTNDBG"):
            return
        p, cv, dram = self.p, self.carve, self.dram
        wv = [(self.s_ve, p.cur(self.s_ve)), (self.s_ac, p.cur(self.s_ac)), (self.s_mm, p.cur(self.s_mm))]
        cvt = self.sb["lgtile"] if False else None
        for dname, srcname in names:
            srcap = cv(srcname) if isinstance(srcname, str) else srcname
            p.emit_i("gpsimd", lambda e, t=dram[dname], s=srcap: e.dma_start(out=t, in_=s),
                     waits=wv, inc_sem=self.s_po, inc_n=16, kind="dma")
        self._dbg_done = p.cur(self.s_po)

    def emit_attn(self):
        import os
        self.p.phase = "attn_setup"
        self.emit_attn_setup()
        if os.environ.get("ATTNDBG"):
            self._dbg_dump([("dbg_W1", "W1p"), ("dbg_Dm", "Dmask")])
            self._dbg_dump([("dbg_hf2", self.sb["hf"][:])])
        for q in range(4):
            self.p.phase = "attn_q%d" % q
            self.emit_attn_quarter(q)
            if q == 0 and os.environ.get("ATTNDBG"):
                self._dbg_dump([("dbg_kT", "kT"), ("dbg_rT", "rT"), ("dbg_v", "v_tm"),
                                ("dbg_yT", "yT"), ("dbg_S", "Sst"),
                                ("dbg_x4", self.sb["x"][:, 0:4 * D]),
                                ("dbg_W2", "W2p"), ("dbg_kh", "kh_buf"),
                                ("dbg_kht", "khat_tm"), ("dbg_Ms", "Msb")])
        self.p.phase = "attn_tail"
        return self.emit_attn_tail()


def _np_inputs_prep(inputs, core):
    x = np.ascontiguousarray(inputs["L0_attn_x"].reshape(T, D))
    out = {}
    out["x0_full"] = x
    out["x0_shard"] = np.ascontiguousarray(x[core * TPC:(core + 1) * TPC])
    kcq = np.arange(D)
    permd = 256 * ((kcq // 128) % 4) + 2 * (kcq % 128) + (kcq // 128) // 4
    out["gw_T"] = np.ascontiguousarray(inputs["L0_ffn_gate_w"].T[permd])
    out["wk_T"] = np.ascontiguousarray(inputs["L0_attn_Wk"].T[permd])
    out["wv_T"] = np.ascontiguousarray(inputs["L0_attn_Wv"].T[permd])
    out["wr_T"] = np.ascontiguousarray(inputs["L0_attn_Wr"].T[permd])
    out["wo_T"] = np.ascontiguousarray(inputs["L0_attn_Wo"].T)
    out["tfirst"] = np.ascontiguousarray(inputs["L0_attn_time_first"].reshape(1, D))

    def pack16(a):
        return np.ascontiguousarray(a.view(np.int16)[..., ::2])

    es = slice(core * EPC, (core + 1) * EPC)
    permi = np.concatenate([np.arange(0, I, 2), np.arange(1, I, 2)])
    out["gate_pT"] = np.ascontiguousarray(pack16(inputs["L0_ffn_gate_packed"][es]).transpose(0, 2, 1)[:, :, permi])
    out["up_pT"] = np.ascontiguousarray(pack16(inputs["L0_ffn_up_packed"][es]).transpose(0, 2, 1)[:, :, permi])
    out["down_pT"] = np.ascontiguousarray(pack16(inputs["L0_ffn_down_packed"][es]).transpose(0, 2, 1))
    out["sh_gate_pT"] = np.ascontiguousarray(pack16(inputs["L0_ffn_sh_gate_packed"]).T[:, permi])
    out["sh_up_pT"] = np.ascontiguousarray(pack16(inputs["L0_ffn_sh_up_packed"]).T[:, permi])
    out["sh_down_pT"] = np.ascontiguousarray(pack16(inputs["L0_ffn_sh_down_packed"]).T)
    out["gate_sT"] = np.ascontiguousarray(inputs["L0_ffn_gate_scales"][es].transpose(0, 2, 1)[:, :, permi])
    out["up_sT"] = np.ascontiguousarray(inputs["L0_ffn_up_scales"][es].transpose(0, 2, 1)[:, :, permi])
    out["down_sT"] = np.ascontiguousarray(inputs["L0_ffn_down_scales"][es].transpose(0, 2, 1))
    out["sh_gate_sT"] = np.ascontiguousarray(inputs["L0_ffn_sh_gate_scales"].T[:, permi])
    out["sh_up_sT"] = np.ascontiguousarray(inputs["L0_ffn_sh_up_scales"].T[:, permi])
    out["sh_down_sT"] = np.ascontiguousarray(inputs["L0_ffn_sh_down_scales"].T)

    out["ident"] = np.eye(128, dtype=BF16NP)
    out["identf"] = np.eye(128, dtype=np.float32)
    out["tri_excl"] = np.tril(np.ones((128, 128), BF16NP), -1)
    out["tri_excl"] = np.triu(np.ones((128, 128), np.float32), 1).astype(BF16NP)
    out["allones"] = np.ones((128, 128), BF16NP)
    out["onescol"] = np.ones((1, 128), BF16NP)
    out["iota_e"] = np.broadcast_to(np.arange(E, dtype=np.float32), (128, E)).copy()
    out["iota_ij"] = (np.arange(128, dtype=np.float32)[None, :] - np.arange(128, dtype=np.float32)[:, None])
    out["tri_ij_u8"] = (np.arange(128)[None, :] >= np.arange(128)[:, None]).astype(np.uint8)
    out["tokid0"] = (core * TPC + np.arange(128, dtype=np.float32)).reshape(128, 1)
    out["coreid"] = np.full((1, 8), core, np.int32)
    out["prevmask"] = (np.arange(8) < core).astype(np.float32).reshape(1, 8)
    out["tdecay64"] = np.ascontiguousarray(inputs["L0_attn_time_decay"].astype(np.float32).T)
    out["decay_h"] = np.ascontiguousarray(inputs["L0_attn_time_decay"][:, 0].astype(np.float32).reshape(1, 16))
    out["iota1"] = np.arange(1, 129, dtype=np.float32).reshape(1, 128)
    out["iotar"] = np.arange(127, -1, -1, dtype=np.float32).reshape(1, 128)
    if core % 2 == 1:
        pr = (core - 1) * 64 + np.arange(128)
        pr[64:] = BIGIDX
    else:
        pr = np.full(128, BIGIDX)
    out["partnerrow"] = pr.astype(np.int32).reshape(128, 1)
    out["expbase"] = (core * 1024 + np.arange(128, dtype=np.float32)).reshape(128, 1)
    return out


def build_ffn1front(stop="full"):
    b = Builder("ffn1front")
    p = b.p
    for n in ("s_in", "s_wt", "s_wt2", "s_mm", "s_ve", "s_ac", "s_po", "s_st", "s_cc", "s_pc"):
        setattr(b, n, p.sem(n))
    p.auto["vector"] = (p.sem("s_auto_ve"), [0], [None])
    p.auto["scalar"] = (p.sem("s_auto_ac"), [0], [None])
    b._tb_n = 0
    b._tb_evac = [None, None]
    b.declare_io()
    b.alloc_common()
    blk = b.ctx.enter_context(b.nc.Block())
    b.load_common()
    b.ve_hfT = b.emit_ln_stream(lambda it: b.sb["x"][:, it * D:(it + 1) * D], 0,
                                [(b.s_in, b.x_loaded)], keep_hf_dbg=True)
    stages = ["ln", "route", "cntx", "scatter", "full"]
    sidx_ = stages.index(stop)
    if sidx_ >= 1:
        import os
        st_cnt = b.emit_routing(int(os.environ.get("RSUB", "99")))
    if sidx_ >= 2:
        ve_bown = b.emit_counts_exchange(st_cnt)
    if sidx_ >= 3:
        cc_table = b.emit_gpos_scatter(ve_bown)

    sb, dram = b.sb, b.dram
    s_st = b.s_st

    def outt(dst, src_fn, waits):
        for it in range(NT):
            tgt = dram[dst].rearrange("(n p) k -> n p k", p=128)[it]
            p.emit_i("sync", lambda e, t=tgt, s=src_fn(it): e.dma_start(out=t, in_=s),
                     waits=(waits if it == 0 else []), inc_sem=s_st, inc_n=16)
    ve_all = p.cur(b.s_ve)
    ac_all = p.cur(b.s_ac)
    w0 = [(b.s_ve, ve_all), (b.s_ac, ac_all)]
    p.emit_i("sync", lambda e: e.dma_start(out=dram["dbg_stats"], in_=sb["stats"][:]),
             waits=w0, inc_sem=s_st, inc_n=16)
    import os as _os
    if sidx_ >= 1 and int(_os.environ.get("RSUB", "99")) > 3:
        outt("dbg_w", lambda it: sb["w4"][:, it * K4:(it + 1) * K4], [])
        outt("dbg_idx", lambda it: sb["idx8"][:, it * 8:(it + 1) * 8], [])
    if sidx_ >= 3:
        outt("dbg_gpos", lambda it: sb["pos4"][:, it * K4:(it + 1) * K4], [])
    if sidx_ >= 2:
        p.emit_i("sync", lambda e: e.dma_start(out=dram["dbg_cnt"], in_=dram["cnt_all"]),
                 waits=[(b.s_cc, p.cur(b.s_cc))], inc_sem=s_st, inc_n=16)
    if sidx_ >= 4:
        p.emit_i("sync", lambda e: e.dma_start(out=dram["dbg_table"], in_=dram["g_table_all"]),
                 waits=[(b.s_cc, p.cur(b.s_cc))], inc_sem=s_st, inc_n=16)
    st_all = p.cur(s_st)
    p.emit("sync", lambda e: e.nop(), waits=[(s_st, st_all)])
    p.build(blk)
    return b


def build_ffn1x():
    b = Builder("ffn1x")
    p = b.p
    for n in ("s_in", "s_wt", "s_wt2", "s_mm", "s_ve", "s_ac", "s_po", "s_st", "s_cc", "s_pc"):
        setattr(b, n, p.sem(n))
    p.auto["vector"] = (p.sem("s_auto_ve"), [0], [None])
    p.auto["scalar"] = (p.sem("s_auto_ac"), [0], [None])
    b._tb_n = 0
    b._tb_evac = [None, None]
    b.declare_io()
    b.alloc_common()
    blk = b.ctx.enter_context(b.nc.Block())
    b.load_common()
    b.ve_hfT = b.emit_ln_stream(lambda it: b.sb["x"][:, it * D:(it + 1) * D], 0,
                                [(b.s_in, b.x_loaded)])
    st_cnt = b.emit_routing()
    ve_bown = b.emit_counts_exchange(st_cnt)
    cc_table = b.emit_gpos_scatter(ve_bown)
    b.emit_shared(b.ve_hfT)
    for j in range(EPC):
        b.emit_expert(1, j)
    st_ret = p.cur(b.s_st)
    cc_ret = p.emit_i("gpsimd", lambda e: e.collective_compute(
        "AllGather", ALU.bypass, replica_groups=[list(range(NCORES))],
        ins=[b.dram["ret_src"].opt()], outs=[b.dram["ret_all"].opt()]),
        waits=[(b.s_st, st_ret)], inc_sem=b.s_cc, kind="dma")
    b.emit_combine(cc_ret)
    sb, dram = b.sb, b.dram
    wv = [(b.s_ve, p.cur(b.s_ve)), (b.s_ac, p.cur(b.s_ac))]
    for it in range(NT):
        tgt = dram["out"].rearrange("(n p) d -> n p d", p=128)[it]
        p.emit_i("sync", lambda e, t=tgt, s=sb["x"][:, it * D:(it + 1) * D]: e.dma_start(out=t, in_=s),
                 waits=(wv if it == 0 else []), inc_sem=b.s_st, inc_n=16)
    st_all = p.cur(b.s_st)
    p.emit("sync", lambda e: e.nop(), waits=[(b.s_st, st_all)])
    p.build(blk)
    return b


def build_full():
    b = Builder("full")
    p = b.p
    for n in ("s_in", "s_wt", "s_wt2", "s_mm", "s_ve", "s_ac", "s_po", "s_st", "s_cc", "s_pc"):
        setattr(b, n, p.sem(n))
    p.auto["vector"] = (p.sem("s_auto_ve"), [0], [None])
    p.auto["scalar"] = (p.sem("s_auto_ac"), [0], [None])
    b._tb_n = 0
    b._tb_evac = [None, None]
    b.declare_io()
    b.alloc_common()
    blk = b.ctx.enter_context(b.nc.Block())
    b._blk = blk
    b.load_common()
    p.phase = "ln1"
    b.ve_hfT = b.emit_ln_stream(lambda it: b.sb["x"][:, it * D:(it + 1) * D], 0,
                                [(b.s_in, b.x_loaded)])
    p.phase = "route1"
    st_cnt = b.emit_routing()
    ve_bown = b.emit_counts_exchange(st_cnt)
    cc_table = b.emit_gpos_scatter(ve_bown)
    b._mk_gu = b._mk_dn = b._mk_ebT = p.cur(b.s_mm)
    b._mk_gath = p.cur(b.s_ac)
    b._mk_scr = p.cur(b.s_mm)
    b._mk_ld_extra = [(b.s_ve, p.cur(b.s_ve)), (b.s_ac, p.cur(b.s_ac))]
    b._mk_ld_uses = 2
    for j in range(EPC):
        p.phase = "exp1_%d" % j
        b.emit_expert(1, j)
        if j == 3:
            st_a = p.cur(b.s_st)
        if j == 5:
            p.emit_i("gpsimd", lambda e: e.collective_compute(
                "AllGather", ALU.bypass, replica_groups=[list(range(NCORES))],
                ins=[b.dram["ret_src"][0:1024, :].opt()], outs=[b.dram["ret_all"][0:8192, :].opt()]),
                waits=[(b.s_st, st_a)], inc_sem=b.s_cc, kind="dma")
    st_ret = p.cur(b.s_st)
    p.phase = "retag1"
    cc_ret = p.emit_i("gpsimd", lambda e: e.collective_compute(
        "AllGather", ALU.bypass, replica_groups=[list(range(NCORES))],
        ins=[b.dram["ret_src"][1024:2048, :].opt()], outs=[b.dram["ret_all"][8192:16384, :].opt()]),
        waits=[(b.s_st, st_ret)], inc_sem=b.s_cc, kind="dma")
    ve_pre_sh = p.cur(b.s_ve)
    ac_pre_sh = p.cur(b.s_ac)
    mm_pre_sh = p.cur(b.s_mm)
    p.phase = "shared1"
    b.emit_shared(b.ve_hfT)
    p.phase = "comb1"
    b.emit_combine(cc_ret, ve_pre_sh, ac_pre_sh, mm_pre_sh)
    import os
    _fs = os.environ.get("FULLSTAGE", "x3")
    if _fs == "x1":
        _emit_out(b, p)
        return b
    p.phase = "ln2"
    ve_hf2T = b.emit_ln_stream(lambda it: b.sb["x"][:, it * D:(it + 1) * D], 0,
                               [(b.s_ve, p.cur(b.s_ve)), (b.s_po, p.cur(b.s_po))])
    b.ve_hfT = ve_hf2T
    p.phase = "attn"
    b.emit_attn()
    if _fs == "x2":
        _emit_out(b, p)
        return b
    p.phase = "ln3"
    ve_hf3T = b.emit_ln_stream(lambda it: b.sb["x"][:, it * D:(it + 1) * D], 0,
                               [(b.s_ve, p.cur(b.s_ve)), (b.s_mm, p.cur(b.s_mm))],
                               store_dram="hf2_src")
    b.ve_hfT = ve_hf3T
    st_hf2 = p.cur(b.s_st)
    p.phase = "hf2ag"
    cc_hf2 = p.emit_i("gpsimd", lambda e: e.collective_compute(
        "AllGather", ALU.bypass, replica_groups=[list(range(NCORES))],
        ins=[b.dram["hf2_src"].opt()], outs=[b.dram["hf2_all"].opt()]),
        waits=[(b.s_st, st_hf2)], inc_sem=b.s_cc, kind="dma")
    p.phase = "route2"
    st_cnt2 = b.emit_routing()
    ve_bown2 = b.emit_counts_exchange(st_cnt2)
    cc_table2 = b.emit_gpos_scatter(ve_bown2)
    b._mk_gu = b._mk_dn = b._mk_ebT = p.cur(b.s_mm)
    b._mk_gath = p.cur(b.s_ac)
    b._mk_scr = p.cur(b.s_mm)
    b._mk_ld_extra = [(b.s_ve, p.cur(b.s_ve)), (b.s_ac, p.cur(b.s_ac))]
    b._mk_ld_uses = 2
    for j in range(EPC):
        p.phase = "exp2_%d" % j
        b.emit_expert(2, j)
        if j == 3:
            st_a2 = p.cur(b.s_st)
        if j == 5:
            p.emit_i("gpsimd", lambda e: e.collective_compute(
                "AllGather", ALU.bypass, replica_groups=[list(range(NCORES))],
                ins=[b.dram["ret_src"][0:1024, :].opt()], outs=[b.dram["ret_all"][0:8192, :].opt()]),
                waits=[(b.s_st, st_a2)], inc_sem=b.s_cc, kind="dma")
    st_ret2 = p.cur(b.s_st)
    p.phase = "retag2"
    cc_ret2 = p.emit_i("gpsimd", lambda e: e.collective_compute(
        "AllGather", ALU.bypass, replica_groups=[list(range(NCORES))],
        ins=[b.dram["ret_src"][1024:2048, :].opt()], outs=[b.dram["ret_all"][8192:16384, :].opt()]),
        waits=[(b.s_st, st_ret2)], inc_sem=b.s_cc, kind="dma")
    ve_pre_sh2 = p.cur(b.s_ve)
    ac_pre_sh2 = p.cur(b.s_ac)
    mm_pre_sh2 = p.cur(b.s_mm)
    p.phase = "shared2"
    b.emit_shared(b.ve_hfT)
    p.phase = "comb2"
    b.emit_combine(cc_ret2, ve_pre_sh2, ac_pre_sh2, mm_pre_sh2)
    p.phase = "outw"
    _emit_out(b, p)
    return b


def _emit_out(b, p):
    sb, dram = b.sb, b.dram
    wv = [(b.s_ve, p.cur(b.s_ve)), (b.s_ac, p.cur(b.s_ac)), (b.s_mm, p.cur(b.s_mm))]
    for it in range(NT):
        tgt = dram["out"].rearrange("(n p) d -> n p d", p=128)[it]
        p.emit_i("sync", lambda e, t=tgt, s=sb["x"][:, it * D:(it + 1) * D]: e.dma_start(out=t, in_=s),
                 waits=(wv if it == 0 else []), inc_sem=b.s_st, inc_n=16)
    st_all = p.cur(b.s_st)
    p.emit("sync", lambda e: e.nop(), waits=[(b.s_st, st_all)])
    p.build(b._blk)


def kernel(**inputs):
    if kernel._dev_stage == "full":
        b = build_full()
        in_maps = [_np_inputs_prep(inputs, c) for c in range(NCORES)]
        res = run_bass_kernel_spmd(b.nc, in_maps, core_ids=list(range(NCORES)))
        return np.concatenate([res.results[c]["out"] for c in range(NCORES)], 0).reshape(B, L, D)
    if kernel._dev_stage == "ffn1front":
        b = build_ffn1front()
        in_maps = [_np_inputs_prep(inputs, c) for c in range(NCORES)]
        res = run_bass_kernel_spmd(b.nc, in_maps, core_ids=list(range(NCORES)))
        return res
    raise NotImplementedError("under construction")


kernel._dev_stage = "full"


# version 16
# speedup vs baseline: 1.0832x; 1.0019x over previous
import contextlib

import numpy as np
import ml_dtypes

import concourse.bass as bass
import concourse.mybir as mybir
from concourse.bass_utils import run_bass_kernel_spmd
from concourse.bass_core_rust import IndirectOffsetOnAxis

F32 = mybir.dt.float32
BF16 = mybir.dt.bfloat16
I16 = mybir.dt.int16
I32 = mybir.dt.int32
U8 = mybir.dt.uint8
U32 = mybir.dt.uint32
AF = mybir.ActivationFunctionType
ALU = mybir.AluOpType
AX = mybir.AxisListType

NCORES = 8
B, L, D, H, HD = 4, 4096, 1024, 16, 64
T = B * L
TPC = T // NCORES
E, K4, CAP, I, G = 64, 4, 256, 768, 128
EPC = E // NCORES
NT = TPC // 128
LNEPS = 1e-5
BIGIDX = 1 << 22

BF16NP = ml_dtypes.bfloat16


class PB:

    def __init__(self, nc, ctx):
        self.nc = nc
        self.ctx = ctx
        self.ops = {k: [] for k in ("sync", "vector", "scalar", "tensor", "gpsimd")}
        self.phase = "init"
        self.sems = {}
        self.counts = {}
        self.auto = {}

    def sem(self, name):
        s = self.ctx.enter_context(self.nc.semaphore(name))
        self.sems[name] = s
        self.counts[name] = 0
        return s

    def inc(self, sem, n=1):
        self.counts[sem.name] += n
        return self.counts[sem.name]

    def cur(self, sem):
        return self.counts[sem.name]

    def emit(self, engine, fn, waits=(), incs=(), kind="compute"):
        waits = list(waits)
        incs = list(incs)
        if kind == "compute" and engine in self.auto:
            sem, cnt, last = self.auto[engine]
            if last[0] is not None:
                waits.append(last[0])
            if incs:
                s0, _ = incs[0]
                last[0] = (s0, self.counts[s0.name])
            else:
                cnt[0] += 1
                incs.append((sem, 1))
                last[0] = (sem, cnt[0])
        self.ops[engine].append((fn, waits, incs, self.phase))

    def emit_i(self, engine, fn, waits=(), inc_sem=None, inc_n=1, kind="compute"):
        incs = []
        tgt = None
        if inc_sem is not None:
            tgt = self.inc(inc_sem, inc_n)
            incs = [(inc_sem, inc_n)]
        self.emit(engine, fn, waits, incs, kind=kind)
        return tgt

    def build(self, block):
        import os
        scoped = bool(os.environ.get("BASSPROF"))
        def mk(engine):
            def run(eng):
                curph, sid = None, None
                for rec in self.ops[engine]:
                    fn, waits, incs, ph = rec
                    if scoped and ph != curph:
                        if curph is not None:
                            self.nc.leave_named_scope(engine + ":" + curph, sid, False)
                        sid, _ = self.nc.enter_named_scope(engine + ":" + ph, False)
                        curph = ph
                    for sem, val in waits:
                        eng.wait_ge(sem, val)
                    inst = fn(eng)
                    for sem, n in incs:
                        inst = inst.then_inc(sem, n)
                if scoped and curph is not None:
                    self.nc.leave_named_scope(engine + ":" + curph, sid, False)
            return run
        block.sync(mk("sync"))
        block.vector(mk("vector"))
        block.scalar(mk("scalar"))
        block.tensor(mk("tensor"))
        block.gpsimd(mk("gpsimd"))


DT = {"f32": F32, "bf16": BF16, "i16": I16, "i32": I32, "u8": U8, "u32": U32}


class Builder:
    def __init__(self, dev_stage="full"):
        self.dev_stage = dev_stage
        self.nc = bass.Bass("TRN2", target_bir_lowering=False, debug=False, detect_race_conditions=False)
        self.ctx = contextlib.ExitStack()
        self.p = PB(self.nc, self.ctx)
        self.dram = {}
        self.sb = {}
        self._mk_gu = self._mk_dn = self._mk_ebT = 0
        self._mk_gath = 0
        self._mk_scr = None
        self._deq_par = 0
        self._deq_free = [0, 0]
        self._mk_ld_extra = []
        self._mk_ld_uses = 0
        self.ps = {}

    def din(self, name, shape, dt):
        self.dram[name] = self.nc.dram_tensor(name, list(shape), DT[dt], kind="ExternalInput").ap()
        return self.dram[name]

    def dout(self, name, shape, dt):
        self.dram[name] = self.nc.dram_tensor(name, list(shape), DT[dt], kind="ExternalOutput").ap()
        return self.dram[name]

    def dint(self, name, shape, dt, **kw):
        self.dram[name] = self.nc.dram_tensor(name, list(shape), DT[dt], kind="Internal", **kw).ap()
        return self.dram[name]

    def S(self, name, shape, dt):
        self.sb[name] = self.ctx.enter_context(self.nc.sbuf_tensor(name, list(shape), DT[dt]))
        return self.sb[name]

    _ARENA_EXPERT = {
        "w_gate_lo": (0, 3072, "bf16", 128),
        "w_gate_hi": (6144, 3072, "bf16", 128),
        "w_up_lo": (12288, 3072, "bf16", 128),
        "w_up_hi": (18432, 3072, "bf16", 128),
        "w_down_lo": (24576, 3072, "bf16", 128),
        "w_down_hi": (30720, 3072, "bf16", 128),
        "pk_t": (36864, 1024, "i16", 128),
        "pk_t2": (63488, 1024, "i16", 128),
        "st_t2": (69632, 1024, "f32", 128),
        "st_t": (38912, 1024, "f32", 128),
        "ebuf": (43008, 2048, "bf16", 128),
        "ebufT": (47104, 2048, "bf16", 128),
        "gath_f32": (51200, 2048, "f32", 128),
        "hU": (59392, 1536, "bf16", 128),
        "hsil": (62464, 512, "bf16", 128),
        "h_shT": (63488, 1536, "bf16", 128),
        "retst": (66560, 1024, "bf16", 128),
        "logitsT": (68608, 2048, "f32", 64),
        "lgtile": (76800, 1024, "f32", 128),
        "ohr_c": (80896, 4096, "bf16", 128),
        "sq_scr": (89088, 1024, "f32", 128),
    }
    _ARENA_ATTN = {
        "kT": (0, 8192, "bf16", 64),
        "rT": (16384, 8192, "bf16", 64),
        "v_tm": (32768, 4096, "bf16", 128),
        "yT": (40960, 4096, "bf16", 128),
        "khat_tm": (49152, 1024, "bf16", 128),
        "at_b": (55296, 512, "bf16", 128),
        "Sbf": (53248, 1024, "bf16", 64),
        "Sst": (57344, 1024, "f32", 64),
        "Msb": (61440, 1024, "f32", 64),
        "W1p": (65536, 2048, "bf16", 64),
        "W2p": (69632, 2048, "bf16", 64),
        "Dmask": (73728, 2048, "bf16", 128),
        "rt_buf": (77824, 2048, "bf16", 64),
        "kh_buf": (81920, 2048, "bf16", 64),
        "Am": (86016, 256, "bf16", 128),
        "wkb": (86528, 2048, "bf16", 128),
        "tfb": (90624, 1024, "bf16", 128),
        "wCt": (92672, 64, "f32", 64),
        "lngb": (92928, 64, "f32", 128),
        "zmask": (93184, 128, "bf16", 128),
    }

    def carve(self, name, hi=False):
        off, n, dt, parts = (self._ARENA_EXPERT | self._ARENA_ATTN)[name]
        es = {"bf16": 2, "f32": 4, "i16": 2}[dt]
        if hi:
            ap = self.sb["arena"][64:128, :].bitcast(DT[dt])
        else:
            ap = self.sb["arena"][0:parts, :].bitcast(DT[dt])
        return ap[:, off // es: off // es + n]

    def P(self, name, shape, dt="f32"):
        self.ps[name] = self.ctx.enter_context(self.nc.psum_tensor(name, list(shape), DT[dt]))
        return self.ps[name]

    def declare_io(self):
        din, dout, dint = self.din, self.dout, self.dint
        full = self.dev_stage != "ffn1front"
        din("x0_shard", (TPC, D), "f32")
        din("gw_T", (D, E), "f32")
        if full:
            din("x0_full", (T, D), "f32")
            din("wk_T", (D, D), "f32")
            din("wv_T", (D, D), "f32")
            din("wr_T", (D, D), "f32")
            din("wo_T", (D, D), "f32")
            din("tfirst", (1, D), "f32")
            din("tdecay64", (64, 16), "f32")
            din("decay_h", (1, 16), "f32")
            din("iota1", (1, 128), "f32")
            din("iotar", (1, 128), "f32")
            din("partnerrow", (128, 1), "i32")
            din("gate_pT", (EPC, D // 2, I), "i16")
            din("up_pT", (EPC, D // 2, I), "i16")
            din("down_pT", (EPC, I // 2, D), "i16")
            din("sh_gate_pT", (D // 2, I), "i16")
            din("sh_up_pT", (D // 2, I), "i16")
            din("sh_down_pT", (I // 2, D), "i16")
            din("gate_sT", (EPC, D // G, I), "f32")
            din("up_sT", (EPC, D // G, I), "f32")
            din("down_sT", (EPC, I // G, D), "f32")
            din("sh_gate_sT", (D // G, I), "f32")
            din("sh_up_sT", (D // G, I), "f32")
            din("sh_down_sT", (I // G, D), "f32")
        din("ident", (128, 128), "bf16")
        din("identf", (128, 128), "f32")
        din("tri_excl", (128, 128), "bf16")
        din("allones", (128, 128), "bf16")
        din("onescol", (1, 128), "bf16")
        din("iota_e", (128, E), "f32")
        din("iota_ij", (128, 128), "f32")
        din("tri_ij_u8", (128, 128), "u8")
        din("tokid0", (128, 1), "f32")
        din("expbase", (128, 1), "f32")
        din("coreid", (1, 8), "i32")
        din("prevmask", (1, 8), "f32")

        dint("g_table", (E * CAP, 2), "f32")
        dint("g_table_all", (E * CAP, 2), "f32")
        dint("cnt_src", (1, E), "f32")
        dint("bown_dram", (1, E), "f32")
        dint("cnt_all", (NCORES, E), "f32")
        dint("ret_src", (EPC * CAP, D), "bf16")
        dint("ret_all", (E * CAP, D), "bf16")
        dint("hf2_src", (TPC, D), "bf16")
        dint("hf2_all", (T, D), "bf16")
        dint("st_src", (H * HD, HD), "f32")
        dint("st_all", (NCORES * H * HD, HD), "f32")

        if self.dev_stage == "ffn1x":
            dout("out", (TPC, D), "f32")
        elif self.dev_stage == "full":
            import os
            dout("out", (TPC, D), "f32")
            if os.environ.get("ATTNDBG"):
                dout("dbg_kT", (64, 8192), "f32")
                dout("dbg_rT", (64, 8192), "f32")
                dout("dbg_v", (128, 4096), "f32")
                dout("dbg_yT", (128, 4096), "f32")
                dout("dbg_S", (64, 1024), "f32")
                dout("dbg_W1", (64, 2048), "f32")
                dout("dbg_Dm", (128, 2048), "f32")
                dout("dbg_hf2", (128, 4 * D), "f32")
                dout("dbg_x4", (128, 4 * D), "f32")
                dout("dbg_v2", (128, 4096), "f32")
                dout("dbg_W2", (64, 2048), "f32")
                dout("dbg_kh", (64, 2048), "f32")
                dout("dbg_kht", (128, 1024), "f32")
                dout("dbg_Ms", (64, 1024), "f32")
        elif self.dev_stage == "ffn1front":
            dout("dbg_w", (TPC, K4), "f32")
            dout("dbg_idx", (TPC, 8), "u32")
            dout("dbg_gpos", (TPC, K4), "f32")
            dout("dbg_cnt", (NCORES, E), "f32")
            dout("dbg_hf", (TPC, D), "bf16")
            dout("dbg_stats", (128, NT * 8), "f32")
            dout("dbg_table", (E * CAP, 2), "f32")
        else:
            dout("out", (TPC, D), "f32")

    def alloc_common(self):
        S, P = self.S, self.P
        S("x", (128, NT * D), "f32")
        S("hf", (128, 4 * D), "bf16")
        S("hfT", (128, 8 * TPC), "bf16")
        S("stats", (128, NT * 8), "f32")
        S("ln_tmp", (128, 4 * NT), "f32")
        S("consts", (128, 512), "bf16")
        S("identf_sb", (128, 128), "f32")
        S("iota_e_sb", (128, E), "f32")
        S("gwT", (128, 8 * E), "bf16")

        S("m8", (128, NT * 8), "f32")
        S("idx8", (128, NT * 8), "u32")
        S("idxf", (128, NT * K4), "f32")
        S("w4", (128, NT * K4), "f32")
        S("oh", (128, NT * E), "bf16")
        S("pos4", (128, NT * K4), "f32")
        S("rtmp", (128, NT * 2 * K4), "f32")
        S("pairs", (128, NT * K4 * 2), "f32")
        S("sidx", (128, NT * K4), "i32")
        S("bownrow", (1, E), "f32")
        S("cntrows", (1, NCORES * E), "f32")
        S("bownbc", (128, E), "f32")
        S("petile", (128, E), "f32")
        S("fidx", (128, 2 * NT * K4), "f32")
        S("tokid_sb", (128, 1), "f32")
        S("coreid_sb", (1, 8), "i32")
        S("prevmask_sb", (1, 8), "f32")
        S("arena", (128, 93440), "u8")
        S("gpair", (128, 32), "f32")
        S("tokcol", (128, 16), "i32")
        S("eidx", (128, 16), "i32")
        S("iscr", (128, 8), "i32")
        S("expbase_sb", (128, 1), "f32")
        S("prow_sb", (128, 1), "i32")

        P("pb", (128, 4096), "f32")

    def load_common(self):
        p, sb, dram = self.p, self.sb, self.dram
        s_in = self.s_in
        self._bc = {}

        def mkreg(e):
            self._bc["r"] = e.to_reg(E * CAP - 1)
            return None
        p.emit("gpsimd", mkreg, kind="dma")
        for it in range(NT):
            tgt = sb["x"][:, it * D:(it + 1) * D]
            src = dram["x0_shard"].rearrange("(n p) d -> n p d", p=128)[it]
            p.emit_i("sync", lambda e, t=tgt, s=src: e.dma_start(out=t, in_=s), inc_sem=s_in, inc_n=16)
        self.x_loaded = p.cur(s_in)
        cl = [
            (sb["consts"][:, 0:128], dram["ident"]),
            (sb["consts"][:, 128:256], dram["tri_excl"]),
            (sb["consts"][:, 256:384], dram["allones"]),
            (sb["consts"][0:1, 384:512], dram["onescol"]),
            (sb["iota_e_sb"][:], dram["iota_e"]),
            (sb["identf_sb"][:], dram["identf"]),
        ]
        for tgt, src in cl:
            p.emit_i("sync", lambda e, t=tgt, s=src: e.dma_start(out=t, in_=s), inc_sem=s_in, inc_n=16)
        for kc in range(8):
            tgt = sb["gwT"][:, kc * E:(kc + 1) * E]
            src = dram["gw_T"][kc * 128:(kc + 1) * 128, :]
            p.emit_i("gpsimd", lambda e, t=tgt, s=src: e.dma_start(out=t, in_=s),
                     inc_sem=self.s_po, inc_n=16, kind="dma")
        p.emit_i("sync", lambda e: e.dma_start(out=sb["tokid_sb"][:], in_=dram["tokid0"]), inc_sem=s_in, inc_n=16)
        p.emit_i("sync", lambda e: e.dma_start(out=sb["expbase_sb"][:], in_=dram["expbase"]), inc_sem=s_in, inc_n=16)
        if "partnerrow" in dram:
            p.emit_i("sync", lambda e: e.dma_start(out=sb["prow_sb"][:], in_=dram["partnerrow"]), inc_sem=s_in, inc_n=16)
        p.emit_i("sync", lambda e: e.dma_start(out=sb["coreid_sb"][:], in_=dram["coreid"]), inc_sem=s_in, inc_n=16)
        p.emit_i("sync", lambda e: e.dma_start(out=sb["prevmask_sb"][:], in_=dram["prevmask"]), inc_sem=s_in, inc_n=16)
        p.emit_i("vector", lambda e: e.memset(sb["stats"][:], 0.0), inc_sem=self.s_ve)
        self.ve_zt = 0
        self.consts_loaded = p.cur(s_in)
        self.x_loaded = p.cur(s_in)
        self.gw_loaded = p.cur(self.s_po)
        self.identf_sb = sb["identf_sb"][:]
        self.ident = sb["consts"][:, 0:128]
        self.tri_excl = sb["consts"][:, 128:256]
        self.allones = sb["consts"][:, 256:384]
        self.onescol = sb["consts"][0:1, 384:512]

    def emit_ln(self, x_ap_fn, hf_dst_fn, stat_base, wait_first, ntiles=NT):
        p, sb = self.p, self.sb
        stats, ln_tmp, sq = sb["stats"], sb["ln_tmp"], self.carve("sq_scr")
        for it in range(ntiles):
            src = x_ap_fn(it)
            c0 = stat_base + it * 8
            p.emit("vector", lambda e, s=src, o=stats[:, c0:c0 + 1]: e.reduce_sum(out=o, in_=s, axis=AX.X),
                   waits=(wait_first if it == 0 else []))
            wsq = list(wait_first) if it == 0 else []
            p.emit_i("scalar", lambda e, s=src, o=stats[:, c0 + 1:c0 + 2]: e.activation(
                out=sq[:], in_=s, func=AF.Square, accum_out=o),
                waits=wsq, inc_sem=self.s_ac)
        ac_sq = p.cur(self.s_ac)
        st = lambda c: stats[:, stat_base + c:stat_base + ntiles * 8:8]
        allsum, allssq, allinv, allnb = st(0), st(1), st(2), st(3)
        tmpm, tmpv, tmps, tmpq = (ln_tmp[:, i * NT:i * NT + ntiles] for i in range(4))
        ve = self.s_ve
        p.emit("vector", lambda e: e.tensor_scalar(out=tmpm, in0=allsum, scalar1=1.0 / D, scalar2=None, op0=ALU.mult),
               waits=[(self.s_ac, ac_sq)])
        p.emit("vector", lambda e: e.tensor_scalar(out=tmpv, in0=allssq, scalar1=1.0 / D, scalar2=LNEPS, op0=ALU.mult, op1=ALU.add))
        p.emit("vector", lambda e: e.tensor_tensor(out=tmpq, in0=tmpm, in1=tmpm, op=ALU.mult))
        p.emit_i("vector", lambda e: e.tensor_tensor(out=tmpv, in0=tmpv, in1=tmpq, op=ALU.subtract), inc_sem=ve)
        ve_v = p.cur(ve)
        p.emit_i("scalar", lambda e: e.activation(out=tmps, in_=tmpv, func=AF.Sqrt),
                 waits=[(ve, ve_v)], inc_sem=self.s_ac)
        ac_s = p.cur(self.s_ac)
        p.emit("vector", lambda e: e.reciprocal(out=allinv, in_=tmps), waits=[(self.s_ac, ac_s)])
        p.emit_i("vector", lambda e: e.scalar_tensor_tensor(out=allnb, in0=tmpm, scalar=-1.0, in1=allinv,
                                                            op0=ALU.mult, op1=ALU.mult), inc_sem=ve)
        ve_stats = p.cur(ve)
        for it in range(ntiles):
            src = x_ap_fn(it)
            dst = hf_dst_fn(it)
            c0 = stat_base + it * 8
            p.emit_i("scalar", lambda e, s=src, o=dst, sc=stats[:, c0 + 2:c0 + 3], b=stats[:, c0 + 3:c0 + 4]:
                     e.activation(out=o, in_=s, func=AF.Identity, scale=sc, bias=b),
                     waits=([(ve, ve_stats)] if it == 0 else []), inc_sem=self.s_ac)
        return p.cur(self.s_ac)

    def emit_ln_stream(self, x_ap_fn, stat_base, wait_first, keep_hf_dbg=False, store_dram=None):
        p, sb = self.p, self.sb
        hfT = sb["hfT"]
        roll = sb["hf"]
        tile_done = []
        ac_stats = self.emit_ln_stats(x_ap_fn, stat_base, wait_first)
        stats = sb["stats"]
        for it in range(NT):
            slot = roll[:, (it % 4) * D:(it % 4 + 1) * D]
            c0 = stat_base + it * 8
            wv = []
            if it == 0:
                wv.append((self.s_ve, ac_stats))
            if it >= 4:
                wv.append((self.s_ve, tile_done[it - 4]))
            p.emit_i("scalar", lambda e, s=x_ap_fn(it), o=slot,
                     sc=stats[:, c0 + 2:c0 + 3], b=stats[:, c0 + 3:c0 + 4]:
                     e.activation(out=o, in_=s, func=AF.Identity, scale=sc, bias=b),
                     waits=wv, inc_sem=self.s_ac)
            ac_ap = p.cur(self.s_ac)
            if keep_hf_dbg and "dbg_hf" in self.dram:
                tgt = self.dram["dbg_hf"].rearrange("(n p) d -> n p d", p=128)[it]
                p.emit_i("sync", lambda e, t=tgt, s=slot: e.dma_start(out=t, in_=s),
                         waits=[(self.s_ac, ac_ap)], inc_sem=self.s_st, inc_n=16)
            if store_dram is not None:
                tgt = self.dram[store_dram].rearrange("(n p) d -> n p d", p=128)[it]
                p.emit_i("sync", lambda e, t=tgt, s=slot: e.dma_start(out=t, in_=s),
                         waits=[(self.s_ac, ac_ap)], inc_sem=self.s_st, inc_n=16)
            for kc in range(8):
                base = 256 * (kc % 4) + (kc // 4)
                srcap = slot[:, base: base + 256 - (base % 2): 2]
                dst = hfT[:, kc * TPC + it * 128: kc * TPC + (it + 1) * 128]
                self.emit_transpose(srcap, dst,
                                    [(self.s_ac, ac_ap)] if kc == 0 else [])
            tile_done.append(p.cur(self.s_ve))
        return p.cur(self.s_ve)

    def emit_ln_stats(self, x_ap_fn, stat_base, wait_first, ntiles=NT):
        p, sb = self.p, self.sb
        stats, ln_tmp, sq = sb["stats"], sb["ln_tmp"], self.carve("sq_scr")
        for it in range(ntiles):
            src = x_ap_fn(it)
            c0 = stat_base + it * 8
            p.emit("vector", lambda e, s=src, o=stats[:, c0:c0 + 1]: e.reduce_sum(out=o, in_=s, axis=AX.X),
                   waits=(list(wait_first) if it == 0 else []))
            p.emit_i("scalar", lambda e, s=src, o=stats[:, c0 + 1:c0 + 2]: e.activation(
                out=sq[:], in_=s, func=AF.Square, accum_out=o),
                waits=(list(wait_first) if it == 0 else []), inc_sem=self.s_ac)
        ac_sq = p.cur(self.s_ac)
        st = lambda c: stats[:, stat_base + c:stat_base + ntiles * 8:8]
        allsum, allssq, allinv, allnb = st(0), st(1), st(2), st(3)
        tmpm, tmpv, tmps, tmpq = (ln_tmp[:, i * NT:i * NT + ntiles] for i in range(4))
        ve = self.s_ve
        p.emit("vector", lambda e: e.tensor_scalar(out=tmpm, in0=allsum, scalar1=1.0 / D, scalar2=None, op0=ALU.mult),
               waits=[(self.s_ac, ac_sq)])
        p.emit("vector", lambda e: e.tensor_scalar(out=tmpv, in0=allssq, scalar1=1.0 / D, scalar2=LNEPS, op0=ALU.mult, op1=ALU.add))
        p.emit("vector", lambda e: e.tensor_tensor(out=tmpq, in0=tmpm, in1=tmpm, op=ALU.mult))
        p.emit_i("vector", lambda e: e.tensor_tensor(out=tmpv, in0=tmpv, in1=tmpq, op=ALU.subtract), inc_sem=ve)
        ve_v = p.cur(ve)
        p.emit_i("scalar", lambda e: e.activation(out=tmps, in_=tmpv, func=AF.Sqrt),
                 waits=[(ve, ve_v)], inc_sem=self.s_ac)
        ac_s = p.cur(self.s_ac)
        p.emit("vector", lambda e: e.reciprocal(out=allinv, in_=tmps), waits=[(self.s_ac, ac_s)])
        p.emit_i("vector", lambda e: e.scalar_tensor_tensor(out=allnb, in0=tmpm, scalar=-1.0, in1=allinv,
                                                            op0=ALU.mult, op1=ALU.mult), inc_sem=ve)
        return p.cur(ve)

    def emit_routing(self, substage=99):
        p, sb, ps = self.p, self.sb, self.ps
        s_mm, s_ve, s_ac, s_in = self.s_mm, self.s_ve, self.s_ac, self.s_in
        hfT, gwT = sb["hfT"], sb["gwT"]
        if substage <= 0:
            return None
        for ncol in range(TPC // 512):
            pst = self.bank(2, 512)[0:64, :]
            for kc in range(8):
                lhs = gwT[:, kc * E:(kc + 1) * E]
                rhs = hfT[:, kc * TPC + ncol * 512: kc * TPC + ncol * 512 + 512]
                w = []
                if kc == 0:
                    w = [(s_ve, self.ve_hfT), (s_in, self.consts_loaded), (self.s_po, self.gw_loaded)]
                    if ncol > 0:
                        w.append((s_ve, self.p.cur(s_ve)))
                p.emit_i("tensor", lambda e, o=pst, l=lhs, r=rhs, k=kc: e.matmul(
                    out=o, lhsT=l, rhs=r, start=(k == 0), stop=(k == 7)),
                    waits=w, inc_sem=(s_mm if kc == 7 else None))
            mmv = p.cur(s_mm)
            dst = self.carve("logitsT")[:, ncol * 512:(ncol + 1) * 512]
            p.emit_i("vector", lambda e, s=pst, o=dst: e.tensor_copy(out=o, in_=s),
                     waits=[(s_mm, mmv)], inc_sem=s_ve)
        ve_lg = p.cur(s_ve)
        if substage <= 1:
            return None
        for it in range(NT):
            src = self.carve("logitsT")[:, it * 128:(it + 1) * 128]
            dst = self.carve("lgtile")[:, it * E:(it + 1) * E]
            self.emit_transpose(src, dst, [(s_ve, ve_lg)] if it == 0 else [],
                                f32=True, ident=self.identf_sb[0:64, 0:64])
        if substage <= 2:
            return None
        m8, idx8, idxf, w4 = sb["m8"], sb["idx8"], sb["idxf"], sb["w4"]
        ohr, oh, rtmp = self.carve("ohr_c"), sb["oh"], sb["rtmp"]
        acv = {}
        for it in range(NT):
            lg = self.carve("lgtile")[:, it * E:(it + 1) * E]
            m8t = m8[:, it * 8:(it + 1) * 8]
            id8t = idx8[:, it * 8:(it + 1) * 8]
            idft = idxf[:, it * K4:(it + 1) * K4]
            w4t = w4[:, it * K4:(it + 1) * K4]
            sc0 = rtmp[:, it * 8:it * 8 + 1]
            p.emit("vector", lambda e, s=lg, o=m8t: e.max(out=o, in_=s))
            p.emit("vector", lambda e, s=lg, m=m8t, o=id8t: e.max_index(out=o, in_max=m, in_values=s))
            p.emit("vector", lambda e, s=id8t, o=idft: e.tensor_copy(out=o, in_=s[:, 0:K4]))
            p.emit_i("vector", lambda e, s=m8t, o=sc0: e.tensor_scalar(
                out=o, in0=s[:, 0:1], scalar1=-1.0, scalar2=None, op0=ALU.mult), inc_sem=s_ve)
            vev = p.cur(s_ve)
            p.emit_i("scalar", lambda e, s=m8t, o=w4t, b=sc0: e.activation(
                out=o, in_=s[:, 0:K4], func=AF.Exp, bias=b, scale=1.0),
                waits=[(s_ve, vev)], inc_sem=s_ac)
            acv[it] = p.cur(s_ac)
            for r in range(K4):
                ohrt = ohr[:, (it * K4 + r) * E:(it * K4 + r + 1) * E]
                p.emit("vector", lambda e, o=ohrt, idc=idft[:, r:r + 1]: e.tensor_scalar(
                    out=o, in0=sb["iota_e_sb"][:], scalar1=idc, scalar2=None, op0=ALU.is_equal))
            oht = oh[:, it * E:(it + 1) * E]
            p.emit("vector", lambda e, o=oht, a=ohr[:, (it * K4 + 0) * E:(it * K4 + 1) * E],
                   b=ohr[:, (it * K4 + 1) * E:(it * K4 + 2) * E]: e.tensor_tensor(out=o, in0=a, in1=b, op=ALU.add))
            p.emit("vector", lambda e, o=oht, a=ohr[:, (it * K4 + 2) * E:(it * K4 + 3) * E]:
                   e.tensor_tensor(out=o, in0=o, in1=a, op=ALU.add))
            p.emit_i("vector", lambda e, o=oht, a=ohr[:, (it * K4 + 3) * E:(it * K4 + 4) * E]:
                     e.tensor_tensor(out=o, in0=o, in1=a, op=ALU.add), inc_sem=s_ve)
        for it in range(NT):
            w4t = w4[:, it * K4:(it + 1) * K4]
            sc1 = rtmp[:, it * 8 + 1:it * 8 + 2]
            p.emit("vector", lambda e, s=w4t, o=sc1: e.reduce_sum(out=o, in_=s, axis=AX.X),
                   waits=[(s_ac, acv[it])])
            p.emit("vector", lambda e, s=sc1, o=sc1: e.reciprocal(out=o, in_=s))
            p.emit("vector", lambda e, s=w4t, o=w4t, sc=sc1: e.tensor_scalar(
                out=o, in0=s, scalar1=sc, scalar2=None, op0=ALU.mult))
        self.ve_oh = p.cur(s_ve)
        if substage <= 3:
            return None
        for it in range(NT):
            oht = oh[:, it * E:(it + 1) * E]
            w = [(s_ve, self.ve_oh)] if it == 0 else []
            p.emit_i("tensor", lambda e, r=oht, k=it: e.matmul(
                out=self.bank(7, 64), lhsT=self.allones, rhs=r, start=(k == 0), stop=(k == NT - 1)),
                waits=w, inc_sem=(s_mm if it == NT - 1 else None))
        mmv = p.cur(s_mm)
        cntrow = sb["bownrow"]
        p.emit_i("vector", lambda e: e.tensor_copy(out=sb["rtmp"][0:1, 0:E], in_=self.bank(7, 64)[0:1, :]),
                 waits=[(s_mm, mmv)], inc_sem=s_ve)
        ve_cnt = p.cur(s_ve)
        p.emit_i("sync", lambda e: e.dma_start(out=self.dram["cnt_src"], in_=sb["rtmp"][0:1, 0:E]),
                 waits=[(s_ve, ve_cnt)], inc_sem=self.s_st, inc_n=16)
        return p.cur(self.s_st)



    def emit_counts_exchange(self, st_cnt):
        p, sb, dram = self.p, self.sb, self.dram
        s_cc, s_ve = self.s_cc, self.s_ve
        cc = p.emit_i("gpsimd", lambda e: e.collective_compute(
            "AllGather", ALU.bypass, replica_groups=[list(range(NCORES))],
            ins=[dram["cnt_src"].opt()], outs=[dram["cnt_all"].opt()]),
            waits=[(self.s_st, st_cnt)], inc_sem=s_cc, kind="dma")
        cntrows = sb["cntrows"][0:1, :]
        p.emit_i("sync", lambda e: e.dma_start(
            out=cntrows, in_=dram["cnt_all"].rearrange("c e -> (c e)").unsqueeze(0)),
            waits=[(s_cc, cc)], inc_sem=self.s_in, inc_n=16)
        ld = p.cur(self.s_in)
        bown = sb["bownrow"][0:1, :]
        pm = sb["prevmask_sb"]
        p.emit("vector", lambda e: e.memset(bown, 0.0), waits=[(self.s_in, ld)])
        for c in range(NCORES):
            row = cntrows[:, c * E:(c + 1) * E]
            p.emit("vector", lambda e, a=row, m=pm[0:1, c:c + 1]: e.scalar_tensor_tensor(
                out=bown, in0=a, scalar=m, in1=bown, op0=ALU.mult, op1=ALU.add))
        p.emit_i("vector", lambda e: e.nop(), inc_sem=s_ve)
        return p.cur(s_ve)

    def emit_gpos_scatter(self, ve_bown):
        p, sb, ps, dram = self.p, self.sb, self.ps, self.dram
        s_mm, s_ve, s_po, s_cc = self.s_mm, self.s_ve, self.s_po, self.s_cc
        oh, ohr, idxf, w4 = sb["oh"], self.carve("ohr_c"), sb["idxf"], sb["w4"]
        zt2 = dram["g_table"].rearrange("(p n) two -> p (n two)", p=128)
        zv = p.emit_i("vector", lambda e: e.memset(self.carve("gath_f32")[:, 0:E * CAP * 2 // 128], 0.0),
                      waits=[(self.s_po, p.cur(self.s_po))], inc_sem=self.s_ve)
        p.emit_i("sync", lambda e: e.dma_start(out=zt2, in_=self.carve("gath_f32")[:, 0:E * CAP * 2 // 128]),
                 waits=[(self.s_ve, zv)], inc_sem=self.s_st, inc_n=16)
        st_zero = p.cur(self.s_st)

        bown_bc = sb["bownbc"]
        p.emit_i("sync", lambda e: e.dma_start(out=dram["bown_dram"], in_=sb["bownrow"][0:1, :]),
                 waits=[(s_ve, ve_bown)], inc_sem=self.s_st, inc_n=16)
        st_b1 = p.cur(self.s_st)
        p.emit_i("sync", lambda e: e.dma_start(out=bown_bc[:], in_=dram["bown_dram"].broadcast_to([128, E])),
                 waits=[(self.s_st, st_b1)], inc_sem=self.s_st, inc_n=16)
        st_bbc = p.cur(self.s_st)
        gpos = sb["pos4"]
        pe_sb = sb["petile"]
        ve_pe_prev = None
        for it in range(NT):
            oht = oh[:, it * E:(it + 1) * E]
            w = [] if ve_pe_prev is None else [(s_ve, ve_pe_prev)]
            p.emit_i("tensor", lambda e, r=oht: e.matmul(out=self.bank(6, 64), lhsT=self.tri_excl,
                                                         rhs=r, start=True, stop=True),
                     waits=w, inc_sem=s_mm)
            mmP = p.cur(s_mm)
            wv = [(s_mm, mmP)]
            if it == 0:
                wv.append((self.s_st, st_bbc))
            p.emit("vector", lambda e: e.tensor_tensor(out=pe_sb[:, 0:E], in0=self.bank(6, 64),
                                                       in1=bown_bc[:], op=ALU.add),
                   waits=wv)
            if it > 0:
                p.emit_i("vector", lambda e: e.tensor_tensor(out=pe_sb[:, 0:E], in0=pe_sb[:, 0:E],
                                                             in1=self.bank(7, 64), op=ALU.add), inc_sem=s_ve)
            else:
                p.emit_i("vector", lambda e: e.nop(), inc_sem=s_ve)
            ve_pe = p.cur(s_ve)
            ve_pe_prev = ve_pe
            if it < NT - 1:
                p.emit_i("tensor", lambda e, r=oht, first=(it == 0), last=(it == NT - 2): e.matmul(
                    out=self.bank(7, 64), lhsT=self.allones, rhs=r, start=first, stop=last,
                    skip_group_check=True),
                         waits=[(s_ve, ve_pe)], inc_sem=s_mm)
            for r in range(K4):
                ohrt = ohr[:, (it * K4 + r) * E:(it * K4 + r + 1) * E]
                gcol = gpos[:, it * K4 + r:it * K4 + r + 1]
                scr = self.carve("lgtile")[:, 0:E]
                p.emit("vector", lambda e, a=pe_sb[:, 0:E], b=ohrt, o=scr:
                       e.tensor_tensor(out=o, in0=a, in1=b, op=ALU.mult))
                p.emit("vector", lambda e, s=scr, o=gcol: e.reduce_sum(out=o, in_=s, axis=AX.X))
            p.emit_i("vector", lambda e: e.engine_nop(), inc_sem=s_ve)
        ve_gpos = p.cur(s_ve)

        sidx, pairs, rtmp = sb["sidx"], sb["pairs"], sb["rtmp"]
        tok0 = sb["tokid_sb"][:, 0:1]
        idx8 = sb["idx8"]
        bscr = sb["rtmp"].bitcast(U32)
        for it in range(NT):
            i0 = it * K4
            eu = idx8[:, it * 8: it * 8 + K4]
            gp = gpos[:, i0:i0 + K4]
            fidx = sb["fidx"][:, i0:i0 + K4]
            b1, b2, b3 = bscr[:, 0:K4], bscr[:, K4:2 * K4], bscr[:, 2 * K4:3 * K4]
            p.emit("vector", lambda e, o=b1, a=eu: e.tensor_scalar(
                out=o, in0=a, scalar1=3, scalar2=10, op0=ALU.logical_shift_right, op1=ALU.logical_shift_left),
                waits=([(s_ve, ve_gpos)] if it == 0 else []))
            p.emit("vector", lambda e, o=b2, a=eu: e.tensor_scalar(
                out=o, in0=a, scalar1=3, scalar2=8, op0=ALU.bitwise_and, op1=ALU.logical_shift_left))
            p.emit("vector", lambda e, o=b3, a=eu: e.tensor_scalar(
                out=o, in0=a, scalar1=4, scalar2=11, op0=ALU.bitwise_and, op1=ALU.logical_shift_left))
            p.emit("vector", lambda e, o=b1: e.tensor_tensor(out=o, in0=o, in1=b2, op=ALU.add))
            p.emit("vector", lambda e, o=b1: e.tensor_tensor(out=o, in0=o, in1=b3, op=ALU.add))
            p.emit("vector", lambda e, o=fidx, a=b1: e.tensor_copy(out=o, in_=a))
            p.emit("vector", lambda e, o=fidx, b=gp: e.tensor_tensor(out=o, in0=o, in1=b, op=ALU.add))
            mcol = sb["fidx"][:, NT * K4 + i0: NT * K4 + i0 + K4]
            p.emit("vector", lambda e, o=mcol, a=gp: e.tensor_scalar(
                out=o, in0=a, scalar1=float(CAP), scalar2=float(BIGIDX), op0=ALU.is_ge, op1=ALU.mult))
            p.emit("vector", lambda e, o=fidx, a=mcol: e.tensor_tensor(out=o, in0=o, in1=a, op=ALU.add))
            p.emit("vector", lambda e, o=sidx[:, i0:i0 + K4], a=fidx: e.tensor_copy(out=o, in_=a))
            for r in range(K4):
                pc = pairs[:, (i0 + r) * 2:(i0 + r) * 2 + 1]
                p.emit("vector", lambda e, o=pc, a=tok0, rr=r, itt=it: e.tensor_scalar(
                    out=o, in0=a, scalar1=4.0, scalar2=float((itt * 128) * 4 + rr + 1),
                    op0=ALU.mult, op1=ALU.add))
            wv = pairs[:, i0 * 2:(i0 + K4) * 2].rearrange("p (k two) -> p k two", two=2)[:, :, 1:2]
            p.emit_i("vector", lambda e, o=wv, a=w4[:, i0:i0 + K4]: e.tensor_scalar(
                out=o, in0=a.unsqueeze(2), scalar1=0.5, scalar2=None, op0=ALU.mult), inc_sem=s_ve)
        ve_pairs = p.cur(s_ve)

        po_w = [(s_ve, ve_pairs), (self.s_st, st_zero)]
        first = True
        nsc = 0
        for it in range(NT):
            for r in range(K4):
                col = it * K4 + r
                idxap = sidx[:, col:col + 1]
                datap = pairs[:, col * 2:col * 2 + 2]
                w = list(po_w) if first else []
                if nsc >= 8:
                    w.append((s_po, p.cur(s_po) - 7 * 16))
                p.emit_i("gpsimd", lambda e, ia=idxap, da=datap: e.indirect_dma_start(
                    out=dram["g_table"], out_offset=IndirectOffsetOnAxis(ia, 0),
                    in_=da, in_offset=None,
                    bounds_check=self._bc["r"], oob_is_err=False),
                    waits=w, inc_sem=s_po, inc_n=16, kind="dma")
                first = False
                nsc += 1
        po_sct = p.cur(s_po)
        cc = p.emit_i("gpsimd", lambda e: e.collective_compute(
            "AllReduce", ALU.add, replica_groups=[list(range(NCORES))],
            ins=[dram["g_table"].opt()], outs=[dram["g_table_all"].opt()]),
            waits=[(s_po, po_sct)], inc_sem=self.s_cc, kind="dma")
        return cc


    def bank(self, b, cols=512):
        return self.ps["pb"][:, b * 512:b * 512 + cols]

    def bankbf(self, b, cols=1024):
        return self.ps["pb"].bitcast(BF16)[:, b * 1024:b * 1024 + cols]

    def emit_transpose(self, srcap, dst, first_waits=(), f32=False, ident=None):
        p = self.p
        bidx = self._tb_n % 2
        ncols = srcap.shape[0]
        if f32:
            pst = self.bank(bidx, 512)[:, 0:ncols]
        else:
            pst = self.bankbf(bidx, 1024)[:, 0:ncols]
        wv = list(first_waits)
        if self._tb_evac[bidx] is not None:
            wv.append((self.s_ve, self._tb_evac[bidx]))
        kk = srcap.shape[0] if hasattr(srcap, "shape") else 128
        idt = ident if ident is not None else (self.identf_sb[0:kk, 0:kk] if f32 else self.ident[0:kk, 0:kk])
        p.emit_i("tensor", lambda e, o=pst, s=srcap, ii=idt: e.transpose(out=o, in_=s, identity=ii),
                 waits=wv, inc_sem=self.s_mm)
        mmv = p.cur(self.s_mm)
        tgt = p.emit_i("vector", lambda e, s=pst, o=dst: e.tensor_copy(out=o, in_=s),
                       waits=[(self.s_mm, mmv)], inc_sem=self.s_ve)
        self._tb_evac[bidx] = tgt
        self._tb_n += 1
        return tgt

    def emit_dequant_matrix(self, mat, j, mm_mark=None, scr_mark=None):
        p, sb, dram = self.p, self.sb, self.dram
        s_wt, s_ve = self.s_wt, self.s_ve
        width = I if mat in ("gate", "up") else D
        ntile = (D // 2 // 128) if mat in ("gate", "up") else (I // 2 // 128)
        if j < 0:
            pT, sT = dram[f"sh_{mat}_pT"], dram[f"sh_{mat}_sT"]
        else:
            pT, sT = dram[f"{mat}_pT"][j], dram[f"{mat}_sT"][j]
        wlo, whi = self.carve(f"w_{mat}_lo"), self.carve(f"w_{mat}_hi")
        if mm_mark is None:
            mm_mark = p.cur(self.s_mm)
        for dt in range(ntile):
            par = self._deq_par
            self._deq_par ^= 1
            pk_t = self.carve("pk_t2" if par else "pk_t")[:, 0:width]
            st_t = self.carve("st_t2" if par else "st_t")[:, 0:width]
            s_ld = self.s_wt2 if par else s_wt
            wv = [(s_ve, self._deq_free[par])] + ([(self.s_mm, mm_mark)] if dt == 0 else [])
            if self._mk_ld_uses > 0:
                wv = wv + self._mk_ld_extra
                self._mk_ld_uses -= 1
            p.emit_i("sync", lambda e, t=pk_t, s=pT[dt * 128:(dt + 1) * 128, :]: e.dma_start(out=t, in_=s),
                     waits=wv, inc_sem=s_ld, inc_n=16)
            p.emit_i("sync", lambda e, t=st_t, s=sT[dt * 2:dt * 2 + 2, :].unsqueeze(1).broadcast_to([2, 64, width]):
                     e.dma_start(out=t, in_=s), inc_sem=s_ld, inc_n=16)
            ld = p.cur(s_ld)
            for plane, wbuf in (("lo", wlo), ("hi", whi)):
                wt = wbuf[:, dt * width:(dt + 1) * width]
                u = self.carve("lgtile").bitcast(I16)[:, 0:width]
                g2 = pk_t.bitcast(BF16) if plane == "hi" else None
                sg = self.carve("ohr_c").bitcast(I16)[:, 0:width]
                m1, s1 = (7, 6) if plane == "lo" else (0x70, 2)
                m2, s2 = (8, 12) if plane == "lo" else (0x80, 8)
                wv = [(s_ld, ld)] if plane == "lo" else []
                if dt == 0 and plane == "lo" and scr_mark is not None:
                    wv = wv + [(self.s_mm, scr_mark)]
                    scr_mark = None
                p.emit("vector", lambda e, b=pk_t, o=sg, mm=m2, ss=s2: e.tensor_scalar(
                    out=o, in0=b, scalar1=mm, scalar2=ss, op0=ALU.bitwise_and, op1=ALU.logical_shift_left),
                    waits=wv)
                p.emit("vector", lambda e, b=pk_t, o=u, mm=m1, ss=s1: e.tensor_scalar(
                    out=o, in0=b, scalar1=mm, scalar2=ss, op0=ALU.bitwise_and, op1=ALU.logical_shift_left))
                p.emit("vector", lambda e, o=u: e.tensor_scalar(
                    out=o, in0=o, scalar1=0x3F00, scalar2=None, op0=ALU.add))
                gdst = (pk_t if plane == "hi" else self.carve("lgtile").bitcast(I16)[:, 1024:1024 + width])
                p.emit("vector", lambda e, o=gdst.bitcast(BF16), f=u.bitcast(BF16): e.tensor_scalar(
                    out=o, in0=f, scalar1=2.0, scalar2=-1.0, op0=ALU.mult, op1=ALU.add))
                p.emit("vector", lambda e, o=u.bitcast(BF16), f=gdst.bitcast(BF16): e.tensor_tensor(
                    out=o, in0=o, in1=f, op=ALU.min))
                p.emit("vector", lambda e, o=u.bitcast(BF16), s=st_t: e.tensor_tensor(
                    out=o, in0=o, in1=s, op=ALU.mult))
                p.emit_i("vector", lambda e, o=wt.bitcast(I16), a=u, b2=sg:
                         e.tensor_tensor(out=o, in0=a, in1=b2, op=ALU.bitwise_xor), inc_sem=s_ve)
            self._deq_free[par] = p.cur(s_ve)
        return p.cur(s_ve)

    def emit_expert(self, call, j):
        p, sb, ps, dram = self.p, self.sb, self.ps, self.dram
        s_mm, s_ve, s_ac, s_po, s_st = self.s_mm, self.s_ve, self.s_ac, self.s_po, self.s_st
        mk_gu_prev, mk_dn_prev = self._mk_gu, self._mk_dn
        mk_ebT_prev, mk_gath_prev = self._mk_ebT, self._mk_gath
        scr0 = self._mk_scr
        self._mk_scr = None

        for h in range(2):
            icol = sb["eidx"][:, j * 2 + h:j * 2 + h + 1]
            p.emit("vector", lambda e, o=sb["fidx"][:, 0:1],
                   off=float((j & 3) * 256 + h * 128 + (8192 if j >= 4 else 0)):
                   e.tensor_scalar(out=o, in0=sb["expbase_sb"][:, 0:1], scalar1=off, scalar2=None, op0=ALU.add))
            p.emit_i("vector", lambda e, o=icol: e.tensor_copy(out=o, in_=sb["fidx"][:, 0:1]), inc_sem=s_ve)
            vei = p.cur(s_ve)
            gp = sb["gpair"][:, (j * 2 + h) * 2:(j * 2 + h) * 2 + 2]
            p.emit_i("gpsimd", lambda e, ia=icol, o=gp: e.indirect_dma_start(
                out=o, out_offset=None, in_=dram["g_table_all"],
                in_offset=IndirectOffsetOnAxis(ia, 0),
                bounds_check=self._bc["r"], oob_is_err=False),
                waits=[(s_ve, vei), (self.s_cc, p.cur(self.s_cc)), (s_ve, p.cur(s_ve))],
                inc_sem=s_po, inc_n=16, kind="dma")
            pog = p.cur(s_po)
            tk = sb["tokcol"][:, j * 2 + h:j * 2 + h + 1]
            p.emit("vector", lambda e, o=sb["iscr"][:, 0:1], a=gp: e.tensor_copy(out=o, in_=a[:, 0:1]),
                   waits=[(s_po, pog)])
            p.emit("vector", lambda e, o=sb["iscr"][:, 0:1]: e.tensor_scalar(
                out=o, in0=o, scalar1=-1, scalar2=None, op0=ALU.add))
            p.emit("vector", lambda e, o=sb["iscr"][:, 0:1]: e.tensor_scalar(
                out=o, in0=o, scalar1=2, scalar2=None, op0=ALU.arith_shift_right))
            p.emit("vector", lambda e, o=sb["iscr"][:, 1:2], a=gp: e.tensor_scalar(
                out=o, in0=a[:, 0:1], scalar1=1.0, scalar2=float(BIGIDX), op0=ALU.is_lt, op1=ALU.mult))
            p.emit_i("vector", lambda e, o=tk: e.tensor_tensor(
                out=o, in0=sb["iscr"][:, 0:1], in1=sb["iscr"][:, 1:2], op=ALU.add), inc_sem=s_ve)
            vet = p.cur(s_ve)
            gbuf = self.carve("gath_f32")[:] if call == 1 else self.carve("gath_f32")[:].bitcast(BF16)[:, 0:2048]
            gb = gbuf[:, h * D:(h + 1) * D]
            srcrows = dram["x0_full"] if call == 1 else dram["hf2_all"]
            wv = [(s_ve, vet), (s_po, p.cur(s_po)), (s_ac, mk_gath_prev), (s_mm, mk_dn_prev)]
            p.emit_i("gpsimd", lambda e, ia=tk, o=gb, s=srcrows: e.indirect_dma_start(
                out=o, out_offset=None, in_=s, in_offset=IndirectOffsetOnAxis(ia, 0),
                bounds_check=self._bc["r"], oob_is_err=False),
                waits=wv, inc_sem=s_po, inc_n=16, kind="dma")
        po_rows = p.cur(s_po)

        lnw = [(s_po, po_rows), (self.s_mm, mk_ebT_prev)]
        if call == 1:
            ac_buf = self.emit_ln(lambda h: self.carve("gath_f32")[:, h * D:(h + 1) * D],
                                  lambda h: self.carve("ebuf")[:, h * D:(h + 1) * D], 0, lnw, ntiles=2)
        else:
            for h in range(2):
                p.emit_i("scalar", lambda e, o=self.carve("ebuf")[:, h * D:(h + 1) * D],
                         s=self.carve("gath_f32")[:].bitcast(BF16)[:, h * D:(h + 1) * D]: e.copy(out=o, in_=s),
                         waits=(lnw if h == 0 else []), inc_sem=s_ac)
            ac_buf = p.cur(s_ac)

        ve_g = self.emit_dequant_matrix("gate", j, mk_gu_prev, scr_mark=scr0)
        ve_u = self.emit_dequant_matrix("up", j, mk_gu_prev)

        for h in range(2):
            for pl in range(2):
                for k in range(4):
                    sb_ = h * D + 256 * k + pl
                    srcap = self.carve("ebuf")[:, sb_: sb_ + 256 - pl: 2]
                    dst = self.carve("ebufT")[:, ((pl * 4 + k) * 2 + h) * 128: ((pl * 4 + k) * 2 + h) * 128 + 128]
                    self.emit_transpose(srcap, dst,
                                        [(s_ac, ac_buf), (s_mm, mk_dn_prev)] if (h == 0 and pl == 0 and k == 0) else [])
        ve_ebT = p.cur(s_ve)
        self._mk_ebT = p.cur(s_mm)

        for h in range(2):
            for ic in range(2):
                grp = [(s_ve, max(ve_ebT, ve_g, ve_u)), (s_ac, p.cur(s_ac)), (s_ve, p.cur(s_ve))]
                for mi, bk in ((0, 2), (1, 3)):
                    n = 0
                    for pl in range(2):
                        wsrc = (self.carve("w_gate_lo"), self.carve("w_gate_hi"), self.carve("w_up_lo"), self.carve("w_up_hi"))[mi * 2 + pl]
                        for k in range(4):
                            lhs = self.carve("ebufT")[:, ((pl * 4 + k) * 2 + h) * 128: ((pl * 4 + k) * 2 + h) * 128 + 128]
                            rhs = wsrc[:, k * I + ic * 384: k * I + ic * 384 + 384]
                            w = grp if (n == 0) else []
                            p.emit_i("tensor", lambda e, l=lhs, r=rhs, st_=(n == 0), sp=(n == 7), b=bk:
                                     e.matmul(out=self.bank(b, 384), lhsT=l, rhs=r, start=st_, stop=sp),
                                     waits=w, inc_sem=(s_mm if n == 7 else None))
                            n += 1
                mmgu = p.cur(s_mm)
                sg_t = self.carve("hsil")[:, 0:384]
                p.emit_i("scalar", lambda e, o=sg_t: e.activation(out=o, in_=self.bank(2, 384), func=AF.Silu),
                         waits=[(s_mm, mmgu)], inc_sem=s_ac)
                acs = p.cur(s_ac)
                hu = self.carve("hU")[:, h * I + ic * 384: h * I + ic * 384 + 384]
                p.emit_i("vector", lambda e, o=hu, a=sg_t: e.tensor_tensor(
                    out=o, in0=a, in1=self.bank(3, 384), op=ALU.mult),
                    waits=[(s_ac, acs)], inc_sem=s_ve)
        ve_hu = p.cur(s_ve)
        self._mk_gu = p.cur(s_mm)
        ve_d = self.emit_dequant_matrix("down", j, mk_dn_prev)

        for h in range(2):
            for pl in range(2):
                for m in range(3):
                    srcap = self.carve("hU")[:, h * I + pl * 384 + m * 128: h * I + pl * 384 + (m + 1) * 128]
                    dst = self.carve("ebufT")[:, ((pl * 3 + m) * 2 + h) * 128: ((pl * 3 + m) * 2 + h) * 128 + 128]
                    self.emit_transpose(srcap, dst, [(s_ve, ve_hu)] if (h == 0 and pl == 0 and m == 0) else [])
        ve_huT = p.cur(s_ve)

        for h in range(2):
            grp = [(s_ve, max(ve_huT, ve_d)), (s_ac, p.cur(s_ac))]
            for nc_ in range(2):
                n = 0
                for pl in range(2):
                    wsrc = self.carve("w_down_lo") if pl == 0 else self.carve("w_down_hi")
                    for m in range(3):
                        lhs = self.carve("ebufT")[:, ((pl * 3 + m) * 2 + h) * 128: ((pl * 3 + m) * 2 + h) * 128 + 128]
                        rhs = wsrc[:, m * D + nc_ * 512: m * D + nc_ * 512 + 512]
                        w = grp if (n == 0 and nc_ == 0) else []
                        p.emit_i("tensor", lambda e, l=lhs, r=rhs, st_=(n == 0), sp=(n == 5), b=4 + nc_:
                                 e.matmul(out=self.bank(b, 512), lhsT=l, rhs=r, start=st_, stop=sp),
                                 waits=w, inc_sem=(s_mm if n == 5 else None))
                        n += 1
            mmd = p.cur(s_mm)
            wcol = sb["gpair"][:, (j * 2 + h) * 2 + 1:(j * 2 + h) * 2 + 2]
            ro = self.carve("retst")[:, 0:D]
            p.emit("scalar", lambda e, o=ro[:, 0:512], w_=wcol: e.activation(
                out=o, in_=self.bank(4, 512), func=AF.Identity, scale=w_),
                waits=[(s_mm, mmd), (s_st, p.cur(s_st))])
            p.emit_i("scalar", lambda e, o=ro[:, 512:1024], w_=wcol: e.activation(
                out=o, in_=self.bank(5, 512), func=AF.Identity, scale=w_), inc_sem=s_ac)
            aco = p.cur(s_ac)
            dst = dram["ret_src"].rearrange("(n p) d -> n p d", p=128)[j * 2 + h]
            p.emit_i("scalar", lambda e, o=dst, s=ro: e.dma_start(out=o, in_=s),
                     waits=[(s_ac, aco)], inc_sem=s_st, inc_n=16, kind="dma")
        self._mk_dn = p.cur(s_mm)
        self._mk_gath = ac_buf
        return p.cur(s_st)


    def emit_shared(self, ve_hfT):
        p, sb = self.p, self.sb
        s_mm, s_ve, s_ac = self.s_mm, self.s_ve, self.s_ac
        ve_g = self.emit_dequant_matrix("gate", -1, self._mk_gu)
        ve_u = self.emit_dequant_matrix("up", -1, self._mk_gu)
        ve_d = self.emit_dequant_matrix("down", -1, self._mk_dn)
        hfT = sb["hfT"]
        hsh = self.carve("h_shT")
        last_silu = {0: p.cur(s_ac), 1: p.cur(s_ac)}
        last_hu = {0: p.cur(s_ve), 1: p.cur(s_ve)}
        for tch in range(8):
            for itile in range(6):
                par = itile % 2
                bg, bu = (2, 3) if par == 0 else (6, 7)
                grp = [(s_ve, max(ve_hfT, ve_g, ve_u, last_hu[par])), (s_ac, last_silu[par])]
                for mi, bk in ((0, bg), (1, bu)):
                    n = 0
                    for pl in range(2):
                        wsrc = (self.carve("w_gate_lo"), self.carve("w_gate_hi"), self.carve("w_up_lo"), self.carve("w_up_hi"))[mi * 2 + pl]
                        for kc in range(4):
                            lhs = wsrc[:, kc * I + itile * 128: kc * I + (itile + 1) * 128]
                            rhs = hfT[:, (pl * 4 + kc) * TPC + tch * 256: (pl * 4 + kc) * TPC + (tch + 1) * 256]
                            w = grp if n == 0 else []
                            p.emit_i("tensor", lambda e, l=lhs, r=rhs, st_=(n == 0), sp=(n == 7), b=bk:
                                     e.matmul(out=self.bank(b, 256), lhsT=l, rhs=r, start=st_, stop=sp),
                                     waits=w, inc_sem=(s_mm if n == 7 else None))
                            n += 1
                mmgu = p.cur(s_mm)
                sg_t = self.carve("hsil")[:, par * 256: par * 256 + 256]
                p.emit_i("scalar", lambda e, o=sg_t, b=bg: e.activation(out=o, in_=self.bank(b, 256), func=AF.Silu),
                         waits=[(s_mm, mmgu), (s_ve, last_hu[par])], inc_sem=s_ac)
                acs = p.cur(s_ac)
                last_silu[par] = acs
                p.emit_i("vector", lambda e, o=hsh[:, itile * 256:(itile + 1) * 256], a=sg_t, b=bu:
                         e.tensor_tensor(out=o, in0=a, in1=self.bank(b, 256), op=ALU.mult),
                         waits=[(s_ac, acs)], inc_sem=s_ve)
                last_hu[par] = p.cur(s_ve)
            ve_h = p.cur(s_ve)
            for tt4 in range(2):
                it = tch * 2 + tt4
                grp = [(s_ve, ve_h), (s_ve, p.cur(s_ve))]
                for nc_ in range(2):
                    n = 0
                    for pl in range(2):
                        wsrc = self.carve("w_down_lo") if pl == 0 else self.carve("w_down_hi")
                        for m in range(3):
                            lhs = hsh[:, (pl * 3 + m) * 256 + tt4 * 128: (pl * 3 + m) * 256 + (tt4 + 1) * 128]
                            rhs = wsrc[:, m * D + nc_ * 512: m * D + nc_ * 512 + 512]
                            w = (grp + [(s_ve, ve_d)]) if (n == 0 and nc_ == 0) else []
                            p.emit_i("tensor", lambda e, l=lhs, r=rhs, st_=(n == 0), sp=(n == 5), b=4 + nc_:
                                     e.matmul(out=self.bank(b, 512), lhsT=l, rhs=r, start=st_, stop=sp),
                                     waits=w, inc_sem=(s_mm if n == 5 else None))
                            n += 1
                mmd = p.cur(s_mm)
                for nc_ in range(2):
                    xs = sb["x"][:, it * D + nc_ * 512: it * D + nc_ * 512 + 512]
                    p.emit_i("vector", lambda e, o=xs, b=4 + nc_: e.scalar_tensor_tensor(
                        out=o, in0=self.bank(b, 512), scalar=0.5, in1=o, op0=ALU.mult, op1=ALU.add),
                        waits=([(s_mm, mmd)] if nc_ == 0 else []), inc_sem=s_ve)
        self._mk_gu = self._mk_dn = p.cur(s_mm)
        return p.cur(s_ve)

    def emit_combine(self, cc_ret, ve_mark=None, ac_mark=None, mm_mark=None):
        p, sb, dram = self.p, self.sb, self.dram
        s_ve, s_po, s_pc = self.s_ve, self.s_po, self.s_pc
        if ve_mark is None:
            ve_mark = p.cur(s_ve)
        if ac_mark is None:
            ac_mark = p.cur(self.s_ac)
        if mm_mark is None:
            mm_mark = p.cur(self.s_mm)
        ar = self.sb["arena"][0:128, :].bitcast(BF16)
        bufs = [self.carve("gath_f32")[:].bitcast(BF16),
                ar[:, 43008 // 2: 43008 // 2 + 4096]]
        last_add = {0: ve_mark, 1: ve_mark}
        first = True
        for it in range(NT):
            par = it % 2
            g4 = bufs[par]
            wvz = [(s_ve, last_add[par])]
            if first:
                wvz += [(self.s_ac, ac_mark), (s_po, p.cur(s_po)), (self.s_mm, mm_mark)]
                first = False
            pz = p.emit_i("gpsimd", lambda e, g=g4: e.memset(g[:, 0:4096], 0.0),
                          waits=wvz, inc_sem=s_pc, kind="dma")
            for r in range(K4):
                col = it * K4 + r
                idxap = sb["sidx"][:, col:col + 1]
                gb = g4[:, r * D:(r + 1) * D]
                wv = [(s_pc, pz), (self.s_cc, cc_ret)]
                p.emit_i("gpsimd", lambda e, ia=idxap, o=gb: e.indirect_dma_start(
                    out=o, out_offset=None, in_=dram["ret_all"],
                    in_offset=IndirectOffsetOnAxis(ia, 0),
                    bounds_check=self._bc["r"], oob_is_err=False),
                    waits=wv, inc_sem=s_po, inc_n=16, kind="dma")
            pog = p.cur(s_po)
            for r in range(K4):
                xs = sb["x"][:, it * D:(it + 1) * D]
                p.emit_i("vector", lambda e, o=xs, a=g4[:, r * D:(r + 1) * D]: e.tensor_tensor(
                    out=o, in0=o, in1=a, op=ALU.add),
                    waits=([(s_po, pog)] if r == 0 else []), inc_sem=s_ve)
            last_add[par] = p.cur(s_ve)
        return p.cur(s_ve)


    def emit_transpose_add(self, srcap, add_into, first_waits=()):
        p = self.p
        bidx = self._tb_n % 2
        pst = self.bankbf(bidx, 1024)[:, 0:srcap.shape[0]]
        wv = list(first_waits)
        if self._tb_evac[bidx] is not None:
            wv.append((self.s_ve, self._tb_evac[bidx]))
        p.emit_i("tensor", lambda e, o=pst, s=srcap: e.transpose(out=o, in_=s, identity=self.ident),
                 waits=wv, inc_sem=self.s_mm)
        mmv = p.cur(self.s_mm)
        tgt = p.emit_i("vector", lambda e, s=pst, o=add_into: e.tensor_tensor(out=o, in0=o, in1=s, op=ALU.add),
                       waits=[(self.s_mm, mmv)], inc_sem=self.s_ve)
        self._tb_evac[bidx] = tgt
        self._tb_n += 1
        return tgt

    def _w8(self, i):
        ap = self.sb["arena"][0:128, :].bitcast(BF16)
        if i < 2:
            off = self._ARENA_ATTN["wkb"][0]
            return ap[:, off // 2 + i * 1024: off // 2 + (i + 1) * 1024]
        if i < 4:
            off = self._ARENA_ATTN["rt_buf"][0]
            return ap[:, off // 2 + (i - 2) * 1024: off // 2 + (i - 1) * 1024]
        if i < 6:
            off = self._ARENA_ATTN["kh_buf"][0]
            return ap[:, off // 2 + (i - 4) * 1024: off // 2 + (i - 3) * 1024]
        off = self._ARENA_ATTN["khat_tm"][0]
        return ap[:, off // 2 + (i - 6) * 1024: off // 2 + (i - 5) * 1024]

    def emit_load_w8(self, wname, fold_tf):
        p, dram = self.p, self.dram
        wv = [(self.s_mm, p.cur(self.s_mm)), (self.s_ve, p.cur(self.s_ve)), (self.s_ac, p.cur(self.s_ac))]
        for kc in range(8):
            dst = self._w8(kc)
            p.emit_i("gpsimd", lambda e, o=dst, s=dram[wname][kc * 128:(kc + 1) * 128, :]:
                     e.dma_start(out=o, in_=s), waits=(wv if kc == 0 else []),
                     inc_sem=self.s_po, inc_n=16, kind="dma")
        pot = p.cur(self.s_po)
        if fold_tf:
            for kc in range(8):
                dst = self._w8(kc)
                p.emit_i("vector", lambda e, o=dst: e.tensor_tensor(out=o, in0=o, in1=self.carve("tfb"), op=ALU.mult),
                         waits=([(self.s_po, pot)] if kc == 0 else []), inc_sem=self.s_ve)
            return ("ve", p.cur(self.s_ve))
        return ("po", pot)

    def emit_attn_setup(self):
        p, sb, dram = self.p, self.sb, self.dram
        s_ve, s_ac, s_in, s_po = self.s_ve, self.s_ac, self.s_in, self.s_po
        cv = self.carve
        p.emit_i("gpsimd", lambda e: e.dma_start(out=cv("tfb"), in_=dram["tfirst"].broadcast_to([128, D])),
                 waits=[(s_ve, p.cur(s_ve)), (s_ac, p.cur(s_ac)), (self.s_mm, p.cur(self.s_mm))],
                 inc_sem=s_po, inc_n=16, kind="dma")
        lnw = cv("wCt")[:, 0:16]
        wC = cv("wCt")[:, 16:32]
        td = cv("wCt")[:, 32:48]
        wv0 = [(s_ve, p.cur(s_ve)), (s_ac, p.cur(s_ac)), (self.s_mm, p.cur(self.s_mm)),
               (self.s_st, p.cur(self.s_st))]
        p.emit_i("sync", lambda e: e.dma_start(out=td, in_=dram["tdecay64"]), waits=wv0, inc_sem=s_in, inc_n=16)
        ldt = p.cur(s_in)
        p.emit_i("scalar", lambda e: e.activation(out=lnw, in_=td, func=AF.Ln),
                 waits=[(s_in, ldt)], inc_sem=s_ac)
        ramp1 = cv("rt_buf").bitcast(F32)[:, 0:128]
        rampr = cv("rt_buf").bitcast(F32)[:, 128:256]
        p.emit_i("sync", lambda e: e.dma_start(out=ramp1, in_=dram["iota1"].broadcast_to([64, 128])), inc_sem=s_in, inc_n=16)
        p.emit_i("sync", lambda e: e.dma_start(out=rampr, in_=dram["iotar"].broadcast_to([64, 128])), inc_sem=s_in, inc_n=16)
        ldr = p.cur(s_in)
        scr = cv("kh_buf").bitcast(F32)[:, 0:128]
        acl = p.cur(s_ac)
        for h in range(H):
            p.emit("vector", lambda e, hh=h: e.tensor_scalar(
                out=scr, in0=ramp1, scalar1=lnw[:, hh:hh + 1], scalar2=None, op0=ALU.mult),
                waits=([(s_in, ldr), (s_ac, acl)] if h == 0 else [(s_ac, p.cur(s_ac))]))
            p.emit_i("vector", lambda e: e.nop(), inc_sem=s_ve)
            vv = p.cur(s_ve)
            p.emit_i("scalar", lambda e, hh=h: e.activation(out=cv("W1p")[:, hh * 128:(hh + 1) * 128], in_=scr, func=AF.Exp),
                     waits=[(s_ve, vv)], inc_sem=s_ac)
            p.emit("vector", lambda e, hh=h: e.tensor_scalar(
                out=scr, in0=rampr, scalar1=lnw[:, hh:hh + 1], scalar2=None, op0=ALU.mult),
                waits=[(s_ac, p.cur(s_ac))])
            p.emit_i("vector", lambda e: e.nop(), inc_sem=s_ve)
            vv = p.cur(s_ve)
            p.emit_i("scalar", lambda e, hh=h: e.activation(out=cv("W2p")[:, hh * 128:(hh + 1) * 128], in_=scr, func=AF.Exp),
                     waits=[(s_ve, vv)], inc_sem=s_ac)
        p.emit_i("vector", lambda e: e.tensor_scalar(out=cv("wCt")[:, 48:64], in0=lnw, scalar1=128.0, scalar2=None, op0=ALU.mult),
                 waits=[(s_ac, p.cur(s_ac))], inc_sem=s_ve)
        p.emit_i("scalar", lambda e, w=wC: e.activation(out=w, in_=cv("wCt")[:, 48:64], func=AF.Exp),
                 waits=[(s_ve, p.cur(s_ve))], inc_sem=s_ac)
        p.emit_i("sync", lambda e: e.dma_start(out=cv("lngb")[:, 0:16], in_=dram["decay_h"].broadcast_to([128, 16])), inc_sem=s_in, inc_n=16)
        ldg = p.cur(s_in)
        p.emit_i("scalar", lambda e: e.activation(out=cv("lngb")[:, 0:16], in_=cv("lngb")[:, 0:16], func=AF.Ln),
                 waits=[(s_in, ldg)], inc_sem=s_ac)
        acg = p.cur(s_ac)
        ioij = cv("wkb").bitcast(F32)[:, 0:128]
        p.emit_i("sync", lambda e: e.dma_start(out=ioij, in_=dram["iota_ij"]), inc_sem=s_in, inc_n=16)
        trit = cv("zmask").bitcast(U8)[:, 0:128]
        p.emit_i("sync", lambda e: e.dma_start(out=trit, in_=dram["tri_ij_u8"]), inc_sem=s_in, inc_n=16)
        ldij = p.cur(s_in)
        zt = cv("Am")[:, 0:128]
        p.emit("vector", lambda e: e.memset(zt, 0.0), waits=[(s_po, p.cur(s_po))])
        escr = cv("wkb").bitcast(F32)[:, 128:256]
        escb = cv("wkb")[:, 512:640]
        for h in range(H):
            p.emit("vector", lambda e, hh=h: e.tensor_scalar(
                out=escr, in0=ioij, scalar1=cv("lngb")[:, hh:hh + 1], scalar2=None, op0=ALU.mult),
                waits=([(s_in, ldij), (s_ac, acg)] if h == 0 else []))
            p.emit("vector", lambda e: e.tensor_scalar(
                out=escr, in0=escr, scalar1=0.0, scalar2=None, op0=ALU.min))
            p.emit_i("vector", lambda e: e.nop(), inc_sem=s_ve)
            vv = p.cur(s_ve)
            p.emit_i("scalar", lambda e: e.activation(out=escb, in_=escr, func=AF.Exp),
                     waits=[(s_ve, vv)], inc_sem=s_ac)
            ace = p.cur(s_ac)
            p.emit_i("vector", lambda e, hh=h: e.select(
                out=cv("Dmask")[:, hh * 128:(hh + 1) * 128], mask=trit, on_true=escb, on_false=zt),
                waits=[(s_ac, ace)], inc_sem=s_ve)
        p.emit("vector", lambda e: e.memset(cv("Sst")[:], 0.0))
        p.emit_i("vector", lambda e: e.memset(cv("Sbf")[:], 0.0), inc_sem=s_ve)
        return p.cur(s_ve)

    def emit_attn_quarter(self, q):
        p, sb = self.p, self.sb
        s_mm, s_ve, s_ac, s_po = self.s_mm, self.s_ve, self.s_ac, self.s_po
        cv = self.carve
        hfT = sb["hfT"]
        qoff = q * 512
        for wname, dstbuf, fold in (("wk_T", "kT", True), ("wr_T", "rT", False)):
            t = self.emit_load_w8(wname, fold)
            tw = [(self.s_ve, t[1])] if t[0] == "ve" else [(self.s_po, t[1])]
            for dks in range(8):
                for kc in range(8):
                    lhs = self._w8(kc)[:, dks * 128:(dks + 1) * 128]
                    rhs = hfT[:, kc * TPC + qoff: kc * TPC + qoff + 512]
                    wv = (list(tw) + [(s_ve, p.cur(s_ve)), (s_ac, p.cur(s_ac))]) if kc == 0 else []
                    p.emit_i("tensor", lambda e, l=lhs, r=rhs, st_=(kc == 0), sp=(kc == 7):
                             e.matmul(out=self.bank(2, 512), lhsT=l, rhs=r, start=st_, stop=sp),
                             waits=wv, inc_sem=(s_mm if kc == 7 else None))
                mmk = p.cur(s_mm)
                for par in range(2):
                    h = dks * 2 + par
                    dst = cv(dstbuf)[:, h * 512:(h + 1) * 512]
                    if dstbuf == "rT":
                        p.emit_i("scalar", lambda e, o=dst, pp=par: e.activation(
                            out=o, in_=self.bank(2, 512)[pp * 64:(pp + 1) * 64, :], func=AF.Sigmoid),
                            waits=[(s_mm, mmk)], inc_sem=s_ac)
                    else:
                        p.emit_i("vector", lambda e, o=dst, pp=par: e.tensor_copy(
                            out=o, in_=self.bank(2, 512)[pp * 64:(pp + 1) * 64, :]),
                            waits=[(s_mm, mmk)], inc_sem=s_ve)
        t = self.emit_load_w8("wv_T", False)
        tw = [(self.s_po, t[1])]
        for tt in range(4):
            for dvh in range(2):
                for kc in range(8):
                    lhs = hfT[:, kc * TPC + qoff + tt * 128: kc * TPC + qoff + (tt + 1) * 128]
                    rhs = self._w8(kc)[:, dvh * 512:(dvh + 1) * 512]
                    wv = (list(tw) + [(s_ve, p.cur(s_ve))]) if kc == 0 else []
                    p.emit_i("tensor", lambda e, l=lhs, r=rhs, st_=(kc == 0), sp=(kc == 7), b=4 + dvh:
                             e.matmul(out=self.bank(b, 512), lhsT=l, rhs=r, start=st_, stop=sp),
                             waits=wv, inc_sem=(s_mm if kc == 7 else None))
            mmv = p.cur(s_mm)
            for dvh in range(2):
                dst = cv("v_tm")[:, tt * 1024 + dvh * 512: tt * 1024 + (dvh + 1) * 512]
                p.emit_i("vector", lambda e, o=dst, b=4 + dvh: e.tensor_copy(out=o, in_=self.bank(b, 512)),
                         waits=[(s_mm, mmv)], inc_sem=s_ve)
        ve_proj = p.cur(s_ve)
        import os
        if q == 0 and os.environ.get("ATTNDBG"):
            self._dbg_dump([("dbg_v2", "v_tm")])
        for n in range(4):
            rview = cv("rT").rearrange("p (h t) -> p h t", h=H)[:, :, n * 128:(n + 1) * 128]
            kview = cv("kT").rearrange("p (h t) -> p h t", h=H)[:, :, n * 128:(n + 1) * 128]
            wv0 = [(s_ve, ve_proj), (s_mm, p.cur(s_mm)), (s_ac, p.cur(s_ac))]
            p.emit("vector", lambda e, i0=rview: e.tensor_tensor(
                out=cv("rt_buf").rearrange("p (h t) -> p h t", h=H), in0=i0,
                in1=cv("W1p").rearrange("p (h t) -> p h t", h=H), op=ALU.mult),
                waits=wv0)
            p.emit_i("vector", lambda e, i0=kview: e.tensor_tensor(
                out=cv("kh_buf").rearrange("p (h t) -> p h t", h=H), in0=i0,
                in1=cv("W2p").rearrange("p (h t) -> p h t", h=H), op=ALU.mult), inc_sem=s_ve)
            ve_sc = p.cur(s_ve)
            if n == 0:
                ac_proj = p.cur(s_ac)
                last_msb = {0: 0, 1: 0}
                last_yev = {0: 0, 1: 0}
                sbf_mark = 0
            for hp in range(H // 2):
                pair = (2 * hp, 2 * hp + 1)
                ba = {pair[0]: 3, pair[1]: 4}
                by = {pair[0]: 6, pair[1]: 5}
                ao = {pair[0]: 0, pair[1]: 128}
                mmA, ve_am, ve_kht, mmM, mmy = {}, {}, {}, {}, {}
                for h in pair:
                    ksx = cv("kT")[:, h * 512 + n * 128: h * 512 + (n + 1) * 128]
                    rsx = cv("rT")[:, h * 512 + n * 128: h * 512 + (n + 1) * 128]
                    if h % 2 == 0:
                        wvA = [(s_ve, max(ve_proj, last_msb[0])), (s_ac, ac_proj)]
                    else:
                        wvA = [(s_ve, ve_proj), (s_ac, max(ac_proj, last_msb[1]))]
                    p.emit_i("tensor", lambda e, l=ksx, r=rsx, b=ba[h]: e.matmul(
                        out=self.bank(b, 128), lhsT=l, rhs=r, start=True, stop=True,
                        skip_group_check=True),
                        waits=wvA, inc_sem=s_mm)
                    mmA[h] = p.cur(s_mm)
                for h in pair:
                    p.emit_i("vector", lambda e, hh=h, b=ba[h], o=ao[h]: e.tensor_tensor(
                        out=cv("Am")[:, o:o + 128], in0=self.bank(b, 128),
                        in1=cv("Dmask")[:, hh * 128:(hh + 1) * 128], op=ALU.mult),
                        waits=[(s_mm, mmA[h])], inc_sem=s_ve)
                    ve_am[h] = p.cur(s_ve)
                for h in pair:
                    khs = cv("kh_buf")[:, h * 128:(h + 1) * 128]
                    kht = cv("khat_tm")[:, h * 64:(h + 1) * 64]
                    self.emit_transpose(khs, kht, [(s_ve, ve_sc)])
                    ve_kht[h] = p.cur(s_ve)
                for h in pair:
                    kht = cv("khat_tm")[:, h * 64:(h + 1) * 64]
                    vsx = cv("v_tm")[:, n * 1024 + h * 64: n * 1024 + (h + 1) * 64]
                    p.emit_i("tensor", lambda e, l=kht, r=vsx, b=ba[h]: e.matmul(
                        out=self.bank(b, 64)[0:64, :], lhsT=l, rhs=r, start=True, stop=True,
                        skip_group_check=True),
                        waits=[(s_ve, ve_kht[h]), (s_ve, ve_am[h])], inc_sem=s_mm)
                    mmM[h] = p.cur(s_mm)
                for h in pair:
                    if h % 2 == 0:
                        p.emit_i("vector", lambda e, hh=h, b=ba[h]: e.tensor_copy(
                            out=cv("Msb")[:, hh * 64:(hh + 1) * 64], in_=self.bank(b, 64)[0:64, :]),
                            waits=[(s_mm, mmM[h])], inc_sem=s_ve)
                        last_msb[0] = p.cur(s_ve)
                    else:
                        p.emit_i("scalar", lambda e, hh=h, b=ba[h]: e.copy(
                            out=cv("Msb")[:, hh * 64:(hh + 1) * 64], in_=self.bank(b, 64)[0:64, :]),
                            waits=[(s_mm, mmM[h])], inc_sem=s_ac)
                        last_msb[1] = p.cur(s_ac)
                for h in pair:
                    wy = [(s_ve, max(last_yev[h % 2], ve_am[h], sbf_mark, ve_sc))]
                    p.emit("tensor", lambda e, hh=h, b=by[h]: e.matmul(
                        out=self.bank(b, 128)[0:64, :], lhsT=cv("Sbf")[:, hh * 64:(hh + 1) * 64],
                        rhs=cv("rt_buf")[:, hh * 128:(hh + 1) * 128], start=True, stop=False,
                        skip_group_check=True),
                        waits=wy)
                    vsx = cv("v_tm")[:, n * 1024 + h * 64: n * 1024 + (h + 1) * 64]
                    p.emit_i("tensor", lambda e, l=vsx, b=by[h], o=ao[h]: e.matmul(
                        out=self.bank(b, 128)[0:64, :], lhsT=l, rhs=cv("Am")[:, o:o + 128],
                        start=False, stop=True, skip_group_check=True),
                        waits=[], inc_sem=s_mm)
                    mmy[h] = p.cur(s_mm)
                for h in pair:
                    yd = (cv("yT")[0:64, :] if h % 2 == 0 else cv("yT", hi=True))
                    ydst = yd[:, (h // 2) * 512 + n * 128: (h // 2) * 512 + (n + 1) * 128]
                    p.emit_i("vector", lambda e, o=ydst, b=by[h]: e.tensor_copy(out=o, in_=self.bank(b, 128)[0:64, :]),
                             waits=[(s_mm, mmy[h])], inc_sem=s_ve)
                    last_yev[h % 2] = p.cur(s_ve)
            wvs = [(s_mm, p.cur(s_mm)), (s_ac, p.cur(s_ac))]
            p.emit("vector", lambda e: e.tensor_tensor(
                out=cv("Sst").rearrange("p (h e) -> p h e", h=H),
                in0=cv("Sst").rearrange("p (h e) -> p h e", h=H),
                in1=cv("wCt")[:, 16:32].unsqueeze(2).broadcast_to([64, 16, 64]),
                op=ALU.mult), waits=wvs)
            p.emit("vector", lambda e: e.tensor_tensor(
                out=cv("Sst")[:], in0=cv("Sst")[:], in1=cv("Msb")[:], op=ALU.add))
            p.emit_i("vector", lambda e: e.tensor_copy(out=cv("Sbf")[:], in_=cv("Sst")[:]), inc_sem=s_ve)
            sbf_mark = p.cur(s_ve)
        ve_y = p.cur(s_ve)
        t = self.emit_load_w8("wo_T", False)
        tw = [(self.s_po, t[1])]
        for dmt in range(8):
            for hp in range(8):
                lhs = self._w8(hp)[:, dmt * 128:(dmt + 1) * 128]
                rhs = cv("yT")[:, hp * 512:(hp + 1) * 512]
                wv = (list(tw) + [(s_ve, max(ve_y, p.cur(s_ve)))]) if hp == 0 else []
                p.emit_i("tensor", lambda e, l=lhs, r=rhs, st_=(hp == 0), sp=(hp == 7):
                         e.matmul(out=self.bank(7, 512), lhsT=l, rhs=r, start=st_, stop=sp,
                                  skip_group_check=True),
                         waits=wv, inc_sem=(s_mm if hp == 7 else None))
            mmo = p.cur(s_mm)
            atb = cv("at_b")[:, 0:512]
            p.emit_i("vector", lambda e, o=atb: e.tensor_copy(out=o, in_=self.bank(7, 512)),
                     waits=[(s_mm, mmo)], inc_sem=s_ve)
            ve_atb = p.cur(s_ve)
            for ts4 in range(4):
                it = qoff // 128 + ts4
                tx = sb["x"][:, it * D + dmt * 128: it * D + (dmt + 1) * 128]
                self.emit_transpose_add(atb[:, ts4 * 128:(ts4 + 1) * 128], tx,
                                        first_waits=([(s_ve, ve_atb)] if ts4 == 0 else []))

    def emit_attn_tail(self):
        p, sb, dram = self.p, self.sb, self.dram
        s_mm, s_ve, s_ac, s_po, s_st, s_cc = self.s_mm, self.s_ve, self.s_ac, self.s_po, self.s_st, self.s_cc
        cv = self.carve
        hfT = sb["hfT"]
        ve_now = p.cur(s_ve)
        p.emit_i("sync", lambda e: e.dma_start(out=dram["st_src"], in_=cv("Sst")[:]),
                 waits=[(s_ve, ve_now)], inc_sem=s_st, inc_n=16)
        stt = p.cur(s_st)
        cc = p.emit_i("gpsimd", lambda e: e.collective_compute(
            "AllGather", ALU.bypass, replica_groups=[list(range(NCORES))],
            ins=[dram["st_src"].opt()], outs=[dram["st_all"].opt()]),
            waits=[(s_st, stt)], inc_sem=s_cc, kind="dma")
        p.emit_i("vector", lambda e: e.memset(cv("Msb")[:], 0.0), inc_sem=s_ve)
        vz = p.cur(s_ve)
        p.emit_i("gpsimd", lambda e: e.indirect_dma_start(
            out=cv("Msb")[0:64, :], out_offset=None, in_=dram["st_all"],
            in_offset=IndirectOffsetOnAxis(sb["prow_sb"][0:64, 0:1], 0),
            bounds_check=self._bc["r"], oob_is_err=False),
            waits=[(s_cc, cc), (s_ve, vz)], inc_sem=s_po, inc_n=16, kind="dma")
        pog = p.cur(s_po)
        t = self.emit_load_w8("wr_T", False)
        tw = [(self.s_po, max(t[1], pog))]
        for dks in range(8):
            for kc in range(8):
                lhs = self._w8(kc)[:, dks * 128:(dks + 1) * 128]
                rhs = hfT[:, kc * TPC: kc * TPC + 384]
                wv = (list(tw) + [(s_ve, p.cur(s_ve)), (s_ac, p.cur(s_ac))]) if kc == 0 else []
                p.emit_i("tensor", lambda e, l=lhs, r=rhs, st_=(kc == 0), sp=(kc == 7):
                         e.matmul(out=self.bank(2, 384), lhsT=l, rhs=r, start=st_, stop=sp,
                                  skip_group_check=True),
                         waits=wv, inc_sem=(s_mm if kc == 7 else None))
            mmk = p.cur(s_mm)
            for par in range(2):
                h = dks * 2 + par
                dst = cv("rT")[:, h * 512: h * 512 + 384]
                p.emit_i("scalar", lambda e, o=dst, pp=par: e.activation(
                    out=o, in_=self.bank(2, 384)[pp * 64:(pp + 1) * 64, :], func=AF.Sigmoid),
                    waits=[(s_mm, mmk)], inc_sem=s_ac)
        p.emit_i("vector", lambda e: e.nop(), waits=[(self.s_ac, p.cur(self.s_ac))], inc_sem=s_ve)
        ve_r = p.cur(s_ve)
        p.emit_i("vector", lambda e: e.tensor_copy(out=cv("Sbf")[:], in_=cv("Msb")[:]),
                 waits=[(s_ve, ve_r)], inc_sem=s_ve)
        for n in range(3):
            if n > 0:
                p.emit("vector", lambda e: e.tensor_tensor(
                    out=cv("Msb").rearrange("p (h e) -> p h e", h=H),
                    in0=cv("Msb").rearrange("p (h e) -> p h e", h=H),
                    in1=cv("wCt")[:, 16:32].unsqueeze(2).broadcast_to([64, 16, 64]),
                    op=ALU.mult))
                p.emit_i("vector", lambda e: e.tensor_copy(out=cv("Sbf")[:], in_=cv("Msb")[:]), inc_sem=s_ve)
            rview = cv("rT").rearrange("p (h t) -> p h t", h=H)[:, :, n * 128:(n + 1) * 128]
            p.emit_i("vector", lambda e, i0=rview: e.tensor_tensor(
                out=cv("rt_buf").rearrange("p (h t) -> p h t", h=H), in0=i0,
                in1=cv("W1p").rearrange("p (h t) -> p h t", h=H), op=ALU.mult),
                waits=[(s_ve, p.cur(s_ve)), (s_mm, p.cur(s_mm)), (s_ac, p.cur(s_ac))], inc_sem=s_ve)
            ve_rt = p.cur(s_ve)
            for h in range(H):
                p.emit_i("tensor", lambda e, hh=h: e.matmul(
                    out=self.bank(6, 128)[0:64, :], lhsT=cv("Sbf")[:, hh * 64:(hh + 1) * 64],
                    rhs=cv("rt_buf")[:, hh * 128:(hh + 1) * 128], start=True, stop=True,
                    skip_group_check=True),
                    waits=[(s_ve, max(ve_rt, p.cur(s_ve)))], inc_sem=s_mm)
                mmy = p.cur(s_mm)
                yd = (cv("yT")[0:64, :] if h % 2 == 0 else cv("yT", hi=True))
                ydst = yd[:, (h // 2) * 512 + n * 128: (h // 2) * 512 + (n + 1) * 128]
                p.emit_i("vector", lambda e, o=ydst: e.tensor_copy(out=o, in_=self.bank(6, 128)[0:64, :]),
                         waits=[(s_mm, mmy)], inc_sem=s_ve)
        ve_dy = p.cur(s_ve)
        t = self.emit_load_w8("wo_T", False)
        tw = [(self.s_po, t[1])]
        for dmt in range(8):
            for hp in range(8):
                lhs = self._w8(hp)[:, dmt * 128:(dmt + 1) * 128]
                rhs = cv("yT")[:, hp * 512: hp * 512 + 384]
                wv = (list(tw) + [(s_ve, max(ve_dy, p.cur(s_ve)))]) if hp == 0 else []
                p.emit_i("tensor", lambda e, l=lhs, r=rhs, st_=(hp == 0), sp=(hp == 7):
                         e.matmul(out=self.bank(7, 384), lhsT=l, rhs=r, start=st_, stop=sp,
                                  skip_group_check=True),
                         waits=wv, inc_sem=(s_mm if hp == 7 else None))
            mmo = p.cur(s_mm)
            atb = cv("at_b")[:, 0:384]
            p.emit_i("vector", lambda e, o=atb: e.tensor_copy(out=o, in_=self.bank(7, 384)),
                     waits=[(s_mm, mmo)], inc_sem=s_ve)
            ve_atb = p.cur(s_ve)
            for ts4 in range(3):
                tx = sb["x"][:, ts4 * D + dmt * 128: ts4 * D + (dmt + 1) * 128]
                self.emit_transpose_add(atb[:, ts4 * 128:(ts4 + 1) * 128], tx,
                                        first_waits=([(s_ve, ve_atb)] if ts4 == 0 else []))
        return p.cur(s_ve)

    def _dbg_dump(self, names):
        import os
        if not os.environ.get("ATTNDBG"):
            return
        p, cv, dram = self.p, self.carve, self.dram
        wv = [(self.s_ve, p.cur(self.s_ve)), (self.s_ac, p.cur(self.s_ac)), (self.s_mm, p.cur(self.s_mm))]
        cvt = self.sb["lgtile"] if False else None
        for dname, srcname in names:
            srcap = cv(srcname) if isinstance(srcname, str) else srcname
            p.emit_i("gpsimd", lambda e, t=dram[dname], s=srcap: e.dma_start(out=t, in_=s),
                     waits=wv, inc_sem=self.s_po, inc_n=16, kind="dma")
        self._dbg_done = p.cur(self.s_po)

    def emit_attn(self):
        import os
        self.p.phase = "attn_setup"
        self.emit_attn_setup()
        if os.environ.get("ATTNDBG"):
            self._dbg_dump([("dbg_W1", "W1p"), ("dbg_Dm", "Dmask")])
            self._dbg_dump([("dbg_hf2", self.sb["hf"][:])])
        for q in range(4):
            self.p.phase = "attn_q%d" % q
            self.emit_attn_quarter(q)
            if q == 0 and os.environ.get("ATTNDBG"):
                self._dbg_dump([("dbg_kT", "kT"), ("dbg_rT", "rT"), ("dbg_v", "v_tm"),
                                ("dbg_yT", "yT"), ("dbg_S", "Sst"),
                                ("dbg_x4", self.sb["x"][:, 0:4 * D]),
                                ("dbg_W2", "W2p"), ("dbg_kh", "kh_buf"),
                                ("dbg_kht", "khat_tm"), ("dbg_Ms", "Msb")])
        self.p.phase = "attn_tail"
        return self.emit_attn_tail()


def _np_inputs_prep(inputs, core):
    x = np.ascontiguousarray(inputs["L0_attn_x"].reshape(T, D))
    out = {}
    out["x0_full"] = x
    out["x0_shard"] = np.ascontiguousarray(x[core * TPC:(core + 1) * TPC])
    kcq = np.arange(D)
    permd = 256 * ((kcq // 128) % 4) + 2 * (kcq % 128) + (kcq // 128) // 4
    out["gw_T"] = np.ascontiguousarray(inputs["L0_ffn_gate_w"].T[permd])
    out["wk_T"] = np.ascontiguousarray(inputs["L0_attn_Wk"].T[permd])
    out["wv_T"] = np.ascontiguousarray(inputs["L0_attn_Wv"].T[permd])
    out["wr_T"] = np.ascontiguousarray(inputs["L0_attn_Wr"].T[permd])
    out["wo_T"] = np.ascontiguousarray(inputs["L0_attn_Wo"].T)
    out["tfirst"] = np.ascontiguousarray(inputs["L0_attn_time_first"].reshape(1, D))

    def pack16(a):
        return np.ascontiguousarray(a.view(np.int16)[..., ::2])

    es = slice(core * EPC, (core + 1) * EPC)
    permi = np.concatenate([np.arange(0, I, 2), np.arange(1, I, 2)])
    out["gate_pT"] = np.ascontiguousarray(pack16(inputs["L0_ffn_gate_packed"][es]).transpose(0, 2, 1)[:, :, permi])
    out["up_pT"] = np.ascontiguousarray(pack16(inputs["L0_ffn_up_packed"][es]).transpose(0, 2, 1)[:, :, permi])
    out["down_pT"] = np.ascontiguousarray(pack16(inputs["L0_ffn_down_packed"][es]).transpose(0, 2, 1))
    out["sh_gate_pT"] = np.ascontiguousarray(pack16(inputs["L0_ffn_sh_gate_packed"]).T[:, permi])
    out["sh_up_pT"] = np.ascontiguousarray(pack16(inputs["L0_ffn_sh_up_packed"]).T[:, permi])
    out["sh_down_pT"] = np.ascontiguousarray(pack16(inputs["L0_ffn_sh_down_packed"]).T)
    out["gate_sT"] = np.ascontiguousarray(inputs["L0_ffn_gate_scales"][es].transpose(0, 2, 1)[:, :, permi])
    out["up_sT"] = np.ascontiguousarray(inputs["L0_ffn_up_scales"][es].transpose(0, 2, 1)[:, :, permi])
    out["down_sT"] = np.ascontiguousarray(inputs["L0_ffn_down_scales"][es].transpose(0, 2, 1))
    out["sh_gate_sT"] = np.ascontiguousarray(inputs["L0_ffn_sh_gate_scales"].T[:, permi])
    out["sh_up_sT"] = np.ascontiguousarray(inputs["L0_ffn_sh_up_scales"].T[:, permi])
    out["sh_down_sT"] = np.ascontiguousarray(inputs["L0_ffn_sh_down_scales"].T)

    out["ident"] = np.eye(128, dtype=BF16NP)
    out["identf"] = np.eye(128, dtype=np.float32)
    out["tri_excl"] = np.tril(np.ones((128, 128), BF16NP), -1)
    out["tri_excl"] = np.triu(np.ones((128, 128), np.float32), 1).astype(BF16NP)
    out["allones"] = np.ones((128, 128), BF16NP)
    out["onescol"] = np.ones((1, 128), BF16NP)
    out["iota_e"] = np.broadcast_to(np.arange(E, dtype=np.float32), (128, E)).copy()
    out["iota_ij"] = (np.arange(128, dtype=np.float32)[None, :] - np.arange(128, dtype=np.float32)[:, None])
    out["tri_ij_u8"] = (np.arange(128)[None, :] >= np.arange(128)[:, None]).astype(np.uint8)
    out["tokid0"] = (core * TPC + np.arange(128, dtype=np.float32)).reshape(128, 1)
    out["coreid"] = np.full((1, 8), core, np.int32)
    out["prevmask"] = (np.arange(8) < core).astype(np.float32).reshape(1, 8)
    out["tdecay64"] = np.ascontiguousarray(inputs["L0_attn_time_decay"].astype(np.float32).T)
    out["decay_h"] = np.ascontiguousarray(inputs["L0_attn_time_decay"][:, 0].astype(np.float32).reshape(1, 16))
    out["iota1"] = np.arange(1, 129, dtype=np.float32).reshape(1, 128)
    out["iotar"] = np.arange(127, -1, -1, dtype=np.float32).reshape(1, 128)
    if core % 2 == 1:
        pr = (core - 1) * 64 + np.arange(128)
        pr[64:] = BIGIDX
    else:
        pr = np.full(128, BIGIDX)
    out["partnerrow"] = pr.astype(np.int32).reshape(128, 1)
    out["expbase"] = (core * 1024 + np.arange(128, dtype=np.float32)).reshape(128, 1)
    return out


def build_ffn1front(stop="full"):
    b = Builder("ffn1front")
    p = b.p
    for n in ("s_in", "s_wt", "s_wt2", "s_mm", "s_ve", "s_ac", "s_po", "s_st", "s_cc", "s_pc"):
        setattr(b, n, p.sem(n))
    p.auto["vector"] = (p.sem("s_auto_ve"), [0], [None])
    p.auto["scalar"] = (p.sem("s_auto_ac"), [0], [None])
    b._tb_n = 0
    b._tb_evac = [None, None]
    b.declare_io()
    b.alloc_common()
    blk = b.ctx.enter_context(b.nc.Block())
    b.load_common()
    b.ve_hfT = b.emit_ln_stream(lambda it: b.sb["x"][:, it * D:(it + 1) * D], 0,
                                [(b.s_in, b.x_loaded)], keep_hf_dbg=True)
    stages = ["ln", "route", "cntx", "scatter", "full"]
    sidx_ = stages.index(stop)
    if sidx_ >= 1:
        import os
        st_cnt = b.emit_routing(int(os.environ.get("RSUB", "99")))
    if sidx_ >= 2:
        ve_bown = b.emit_counts_exchange(st_cnt)
    if sidx_ >= 3:
        cc_table = b.emit_gpos_scatter(ve_bown)

    sb, dram = b.sb, b.dram
    s_st = b.s_st

    def outt(dst, src_fn, waits):
        for it in range(NT):
            tgt = dram[dst].rearrange("(n p) k -> n p k", p=128)[it]
            p.emit_i("sync", lambda e, t=tgt, s=src_fn(it): e.dma_start(out=t, in_=s),
                     waits=(waits if it == 0 else []), inc_sem=s_st, inc_n=16)
    ve_all = p.cur(b.s_ve)
    ac_all = p.cur(b.s_ac)
    w0 = [(b.s_ve, ve_all), (b.s_ac, ac_all)]
    p.emit_i("sync", lambda e: e.dma_start(out=dram["dbg_stats"], in_=sb["stats"][:]),
             waits=w0, inc_sem=s_st, inc_n=16)
    import os as _os
    if sidx_ >= 1 and int(_os.environ.get("RSUB", "99")) > 3:
        outt("dbg_w", lambda it: sb["w4"][:, it * K4:(it + 1) * K4], [])
        outt("dbg_idx", lambda it: sb["idx8"][:, it * 8:(it + 1) * 8], [])
    if sidx_ >= 3:
        outt("dbg_gpos", lambda it: sb["pos4"][:, it * K4:(it + 1) * K4], [])
    if sidx_ >= 2:
        p.emit_i("sync", lambda e: e.dma_start(out=dram["dbg_cnt"], in_=dram["cnt_all"]),
                 waits=[(b.s_cc, p.cur(b.s_cc))], inc_sem=s_st, inc_n=16)
    if sidx_ >= 4:
        p.emit_i("sync", lambda e: e.dma_start(out=dram["dbg_table"], in_=dram["g_table_all"]),
                 waits=[(b.s_cc, p.cur(b.s_cc))], inc_sem=s_st, inc_n=16)
    st_all = p.cur(s_st)
    p.emit("sync", lambda e: e.nop(), waits=[(s_st, st_all)])
    p.build(blk)
    return b


def build_ffn1x():
    b = Builder("ffn1x")
    p = b.p
    for n in ("s_in", "s_wt", "s_wt2", "s_mm", "s_ve", "s_ac", "s_po", "s_st", "s_cc", "s_pc"):
        setattr(b, n, p.sem(n))
    p.auto["vector"] = (p.sem("s_auto_ve"), [0], [None])
    p.auto["scalar"] = (p.sem("s_auto_ac"), [0], [None])
    b._tb_n = 0
    b._tb_evac = [None, None]
    b.declare_io()
    b.alloc_common()
    blk = b.ctx.enter_context(b.nc.Block())
    b.load_common()
    b.ve_hfT = b.emit_ln_stream(lambda it: b.sb["x"][:, it * D:(it + 1) * D], 0,
                                [(b.s_in, b.x_loaded)])
    st_cnt = b.emit_routing()
    ve_bown = b.emit_counts_exchange(st_cnt)
    cc_table = b.emit_gpos_scatter(ve_bown)
    b.emit_shared(b.ve_hfT)
    for j in range(EPC):
        b.emit_expert(1, j)
    st_ret = p.cur(b.s_st)
    cc_ret = p.emit_i("gpsimd", lambda e: e.collective_compute(
        "AllGather", ALU.bypass, replica_groups=[list(range(NCORES))],
        ins=[b.dram["ret_src"].opt()], outs=[b.dram["ret_all"].opt()]),
        waits=[(b.s_st, st_ret)], inc_sem=b.s_cc, kind="dma")
    b.emit_combine(cc_ret)
    sb, dram = b.sb, b.dram
    wv = [(b.s_ve, p.cur(b.s_ve)), (b.s_ac, p.cur(b.s_ac))]
    for it in range(NT):
        tgt = dram["out"].rearrange("(n p) d -> n p d", p=128)[it]
        p.emit_i("sync", lambda e, t=tgt, s=sb["x"][:, it * D:(it + 1) * D]: e.dma_start(out=t, in_=s),
                 waits=(wv if it == 0 else []), inc_sem=b.s_st, inc_n=16)
    st_all = p.cur(b.s_st)
    p.emit("sync", lambda e: e.nop(), waits=[(b.s_st, st_all)])
    p.build(blk)
    return b


def build_full():
    b = Builder("full")
    p = b.p
    for n in ("s_in", "s_wt", "s_wt2", "s_mm", "s_ve", "s_ac", "s_po", "s_st", "s_cc", "s_pc"):
        setattr(b, n, p.sem(n))
    p.auto["vector"] = (p.sem("s_auto_ve"), [0], [None])
    p.auto["scalar"] = (p.sem("s_auto_ac"), [0], [None])
    b._tb_n = 0
    b._tb_evac = [None, None]
    b.declare_io()
    b.alloc_common()
    blk = b.ctx.enter_context(b.nc.Block())
    b._blk = blk
    b.load_common()
    p.phase = "ln1"
    b.ve_hfT = b.emit_ln_stream(lambda it: b.sb["x"][:, it * D:(it + 1) * D], 0,
                                [(b.s_in, b.x_loaded)])
    p.phase = "route1"
    st_cnt = b.emit_routing()
    ve_bown = b.emit_counts_exchange(st_cnt)
    cc_table = b.emit_gpos_scatter(ve_bown)
    b._mk_gu = b._mk_dn = b._mk_ebT = p.cur(b.s_mm)
    b._mk_gath = p.cur(b.s_ac)
    b._mk_scr = p.cur(b.s_mm)
    b._mk_ld_extra = [(b.s_ve, p.cur(b.s_ve)), (b.s_ac, p.cur(b.s_ac))]
    b._mk_ld_uses = 2
    for j in range(EPC):
        p.phase = "exp1_%d" % j
        b.emit_expert(1, j)
        if j == 3:
            st_a = p.cur(b.s_st)
        if j == 5:
            p.emit_i("gpsimd", lambda e: e.collective_compute(
                "AllGather", ALU.bypass, replica_groups=[list(range(NCORES))],
                ins=[b.dram["ret_src"][0:1024, :].opt()], outs=[b.dram["ret_all"][0:8192, :].opt()]),
                waits=[(b.s_st, st_a)], inc_sem=b.s_cc, kind="dma")
    st_ret = p.cur(b.s_st)
    p.phase = "retag1"
    cc_ret = p.emit_i("gpsimd", lambda e: e.collective_compute(
        "AllGather", ALU.bypass, replica_groups=[list(range(NCORES))],
        ins=[b.dram["ret_src"][1024:2048, :].opt()], outs=[b.dram["ret_all"][8192:16384, :].opt()]),
        waits=[(b.s_st, st_ret)], inc_sem=b.s_cc, kind="dma")
    ve_pre_sh = p.cur(b.s_ve)
    ac_pre_sh = p.cur(b.s_ac)
    mm_pre_sh = p.cur(b.s_mm)
    p.phase = "shared1"
    b.emit_shared(b.ve_hfT)
    p.phase = "comb1"
    b.emit_combine(cc_ret, ve_pre_sh, ac_pre_sh, mm_pre_sh)
    import os
    _fs = os.environ.get("FULLSTAGE", "x3")
    if _fs == "x1":
        _emit_out(b, p)
        return b
    p.phase = "ln2"
    ve_hf2T = b.emit_ln_stream(lambda it: b.sb["x"][:, it * D:(it + 1) * D], 0,
                               [(b.s_ve, p.cur(b.s_ve)), (b.s_po, p.cur(b.s_po))])
    b.ve_hfT = ve_hf2T
    p.phase = "attn"
    b.emit_attn()
    if _fs == "x2":
        _emit_out(b, p)
        return b
    p.phase = "ln3"
    ve_hf3T = b.emit_ln_stream(lambda it: b.sb["x"][:, it * D:(it + 1) * D], 0,
                               [(b.s_ve, p.cur(b.s_ve)), (b.s_mm, p.cur(b.s_mm))],
                               store_dram="hf2_src")
    b.ve_hfT = ve_hf3T
    st_hf2 = p.cur(b.s_st)
    p.phase = "hf2ag"
    cc_hf2 = p.emit_i("gpsimd", lambda e: e.collective_compute(
        "AllGather", ALU.bypass, replica_groups=[list(range(NCORES))],
        ins=[b.dram["hf2_src"].opt()], outs=[b.dram["hf2_all"].opt()]),
        waits=[(b.s_st, st_hf2)], inc_sem=b.s_cc, kind="dma")
    p.phase = "route2"
    st_cnt2 = b.emit_routing()
    ve_bown2 = b.emit_counts_exchange(st_cnt2)
    cc_table2 = b.emit_gpos_scatter(ve_bown2)
    b._mk_gu = b._mk_dn = b._mk_ebT = p.cur(b.s_mm)
    b._mk_gath = p.cur(b.s_ac)
    b._mk_scr = p.cur(b.s_mm)
    b._mk_ld_extra = [(b.s_ve, p.cur(b.s_ve)), (b.s_ac, p.cur(b.s_ac))]
    b._mk_ld_uses = 2
    for j in range(EPC):
        p.phase = "exp2_%d" % j
        b.emit_expert(2, j)
        if j == 3:
            st_a2 = p.cur(b.s_st)
        if j == 5:
            p.emit_i("gpsimd", lambda e: e.collective_compute(
                "AllGather", ALU.bypass, replica_groups=[list(range(NCORES))],
                ins=[b.dram["ret_src"][0:1024, :].opt()], outs=[b.dram["ret_all"][0:8192, :].opt()]),
                waits=[(b.s_st, st_a2)], inc_sem=b.s_cc, kind="dma")
    st_ret2 = p.cur(b.s_st)
    p.phase = "retag2"
    cc_ret2 = p.emit_i("gpsimd", lambda e: e.collective_compute(
        "AllGather", ALU.bypass, replica_groups=[list(range(NCORES))],
        ins=[b.dram["ret_src"][1024:2048, :].opt()], outs=[b.dram["ret_all"][8192:16384, :].opt()]),
        waits=[(b.s_st, st_ret2)], inc_sem=b.s_cc, kind="dma")
    ve_pre_sh2 = p.cur(b.s_ve)
    ac_pre_sh2 = p.cur(b.s_ac)
    mm_pre_sh2 = p.cur(b.s_mm)
    p.phase = "shared2"
    b.emit_shared(b.ve_hfT)
    p.phase = "comb2"
    b.emit_combine(cc_ret2, ve_pre_sh2, ac_pre_sh2, mm_pre_sh2)
    p.phase = "outw"
    _emit_out(b, p)
    return b


def _emit_out(b, p):
    sb, dram = b.sb, b.dram
    wv = [(b.s_ve, p.cur(b.s_ve)), (b.s_ac, p.cur(b.s_ac)), (b.s_mm, p.cur(b.s_mm))]
    for it in range(NT):
        tgt = dram["out"].rearrange("(n p) d -> n p d", p=128)[it]
        p.emit_i("sync", lambda e, t=tgt, s=sb["x"][:, it * D:(it + 1) * D]: e.dma_start(out=t, in_=s),
                 waits=(wv if it == 0 else []), inc_sem=b.s_st, inc_n=16)
    st_all = p.cur(b.s_st)
    p.emit("sync", lambda e: e.nop(), waits=[(b.s_st, st_all)])
    p.build(b._blk)


def kernel(**inputs):
    if kernel._dev_stage == "full":
        b = build_full()
        in_maps = [_np_inputs_prep(inputs, c) for c in range(NCORES)]
        res = run_bass_kernel_spmd(b.nc, in_maps, core_ids=list(range(NCORES)))
        return np.concatenate([res.results[c]["out"] for c in range(NCORES)], 0).reshape(B, L, D)
    if kernel._dev_stage == "ffn1front":
        b = build_ffn1front()
        in_maps = [_np_inputs_prep(inputs, c) for c in range(NCORES)]
        res = run_bass_kernel_spmd(b.nc, in_maps, core_ids=list(range(NCORES)))
        return res
    raise NotImplementedError("under construction")


kernel._dev_stage = "full"


# version 18 (final)
# speedup vs baseline: 1.0878x; 1.0042x over previous
import contextlib

import numpy as np
import ml_dtypes

import concourse.bass as bass
import concourse.mybir as mybir
from concourse.bass_utils import run_bass_kernel_spmd
from concourse.bass_core_rust import IndirectOffsetOnAxis

F32 = mybir.dt.float32
BF16 = mybir.dt.bfloat16
I16 = mybir.dt.int16
I32 = mybir.dt.int32
U8 = mybir.dt.uint8
U32 = mybir.dt.uint32
AF = mybir.ActivationFunctionType
ALU = mybir.AluOpType
AX = mybir.AxisListType

NCORES = 8
B, L, D, H, HD = 4, 4096, 1024, 16, 64
T = B * L
TPC = T // NCORES
E, K4, CAP, I, G = 64, 4, 256, 768, 128
EPC = E // NCORES
NT = TPC // 128
LNEPS = 1e-5
BIGIDX = 1 << 22

BF16NP = ml_dtypes.bfloat16


class PB:

    def __init__(self, nc, ctx):
        self.nc = nc
        self.ctx = ctx
        self.ops = {k: [] for k in ("sync", "vector", "scalar", "tensor", "gpsimd")}
        self.phase = "init"
        self.sems = {}
        self.counts = {}
        self.auto = {}

    def sem(self, name):
        s = self.ctx.enter_context(self.nc.semaphore(name))
        self.sems[name] = s
        self.counts[name] = 0
        return s

    def inc(self, sem, n=1):
        self.counts[sem.name] += n
        return self.counts[sem.name]

    def cur(self, sem):
        return self.counts[sem.name]

    def emit(self, engine, fn, waits=(), incs=(), kind="compute"):
        waits = list(waits)
        incs = list(incs)
        if kind == "compute" and engine in self.auto:
            sem, cnt, last = self.auto[engine]
            if last[0] is not None:
                waits.append(last[0])
            if incs:
                s0, _ = incs[0]
                last[0] = (s0, self.counts[s0.name])
            else:
                cnt[0] += 1
                incs.append((sem, 1))
                last[0] = (sem, cnt[0])
        self.ops[engine].append((fn, waits, incs, self.phase))

    def emit_i(self, engine, fn, waits=(), inc_sem=None, inc_n=1, kind="compute"):
        incs = []
        tgt = None
        if inc_sem is not None:
            tgt = self.inc(inc_sem, inc_n)
            incs = [(inc_sem, inc_n)]
        self.emit(engine, fn, waits, incs, kind=kind)
        return tgt

    def build(self, block):
        import os
        scoped = bool(os.environ.get("BASSPROF"))
        def mk(engine):
            def run(eng):
                curph, sid = None, None
                for rec in self.ops[engine]:
                    fn, waits, incs, ph = rec
                    if scoped and ph != curph:
                        if curph is not None:
                            self.nc.leave_named_scope(engine + ":" + curph, sid, False)
                        sid, _ = self.nc.enter_named_scope(engine + ":" + ph, False)
                        curph = ph
                    for sem, val in waits:
                        eng.wait_ge(sem, val)
                    inst = fn(eng)
                    for sem, n in incs:
                        inst = inst.then_inc(sem, n)
                if scoped and curph is not None:
                    self.nc.leave_named_scope(engine + ":" + curph, sid, False)
            return run
        block.sync(mk("sync"))
        block.vector(mk("vector"))
        block.scalar(mk("scalar"))
        block.tensor(mk("tensor"))
        block.gpsimd(mk("gpsimd"))


DT = {"f32": F32, "bf16": BF16, "i16": I16, "i32": I32, "u8": U8, "u32": U32}


class Builder:
    def __init__(self, dev_stage="full"):
        self.dev_stage = dev_stage
        self.nc = bass.Bass("TRN2", target_bir_lowering=False, debug=False, detect_race_conditions=False)
        self.ctx = contextlib.ExitStack()
        self.p = PB(self.nc, self.ctx)
        self.dram = {}
        self.sb = {}
        self._mk_gu = self._mk_dn = self._mk_ebT = 0
        self._mk_gath = 0
        self._mk_scr = None
        self._st_seg = [0, 0, 0]
        self._deq_par = 0
        self._deq_free = [0, 0]
        self._mk_ld_extra = []
        self._mk_ld_uses = 0
        self.ps = {}

    def din(self, name, shape, dt):
        self.dram[name] = self.nc.dram_tensor(name, list(shape), DT[dt], kind="ExternalInput").ap()
        return self.dram[name]

    def dout(self, name, shape, dt):
        self.dram[name] = self.nc.dram_tensor(name, list(shape), DT[dt], kind="ExternalOutput").ap()
        return self.dram[name]

    def dint(self, name, shape, dt, **kw):
        self.dram[name] = self.nc.dram_tensor(name, list(shape), DT[dt], kind="Internal", **kw).ap()
        return self.dram[name]

    def S(self, name, shape, dt):
        self.sb[name] = self.ctx.enter_context(self.nc.sbuf_tensor(name, list(shape), DT[dt]))
        return self.sb[name]

    _ARENA_EXPERT = {
        "w_gate_lo": (0, 3072, "bf16", 128),
        "w_gate_hi": (6144, 3072, "bf16", 128),
        "w_up_lo": (12288, 3072, "bf16", 128),
        "w_up_hi": (18432, 3072, "bf16", 128),
        "w_down_lo": (24576, 3072, "bf16", 128),
        "w_down_hi": (30720, 3072, "bf16", 128),
        "pk_t": (36864, 1024, "i16", 128),
        "pk_t2": (63488, 1024, "i16", 128),
        "st_t2": (69632, 1024, "f32", 128),
        "st_t": (38912, 1024, "f32", 128),
        "ebuf": (43008, 2048, "bf16", 128),
        "ebufT": (47104, 2048, "bf16", 128),
        "gath_f32": (51200, 2048, "f32", 128),
        "hU": (59392, 1536, "bf16", 128),
        "hsil": (62464, 512, "bf16", 128),
        "h_shT": (63488, 1536, "bf16", 128),
        "retst": (66560, 1024, "bf16", 128),
        "logitsT": (68608, 2048, "f32", 64),
        "lgtile": (76800, 1024, "f32", 128),
        "ohr_c": (80896, 4096, "bf16", 128),
        "sq_scr": (89088, 1024, "f32", 128),
    }
    _ARENA_ATTN = {
        "kT": (0, 8192, "bf16", 64),
        "rT": (16384, 8192, "bf16", 64),
        "v_tm": (32768, 4096, "bf16", 128),
        "yT": (40960, 4096, "bf16", 128),
        "khat_tm": (49152, 1024, "bf16", 128),
        "at_b": (55296, 512, "bf16", 128),
        "Sbf": (53248, 1024, "bf16", 64),
        "Sst": (57344, 1024, "f32", 64),
        "Msb": (61440, 1024, "f32", 64),
        "W1p": (65536, 2048, "bf16", 64),
        "W2p": (69632, 2048, "bf16", 64),
        "Dmask": (73728, 2048, "bf16", 128),
        "rt_buf": (77824, 2048, "bf16", 64),
        "kh_buf": (81920, 2048, "bf16", 64),
        "Am": (86016, 256, "bf16", 128),
        "wkb": (86528, 2048, "bf16", 128),
        "tfb": (90624, 1024, "bf16", 128),
        "wCt": (92672, 64, "f32", 64),
        "lngb": (92928, 64, "f32", 128),
        "zmask": (93184, 128, "bf16", 128),
    }

    def carve(self, name, hi=False):
        off, n, dt, parts = (self._ARENA_EXPERT | self._ARENA_ATTN)[name]
        es = {"bf16": 2, "f32": 4, "i16": 2}[dt]
        if hi:
            ap = self.sb["arena"][64:128, :].bitcast(DT[dt])
        else:
            ap = self.sb["arena"][0:parts, :].bitcast(DT[dt])
        return ap[:, off // es: off // es + n]

    def P(self, name, shape, dt="f32"):
        self.ps[name] = self.ctx.enter_context(self.nc.psum_tensor(name, list(shape), DT[dt]))
        return self.ps[name]

    def declare_io(self):
        din, dout, dint = self.din, self.dout, self.dint
        full = self.dev_stage != "ffn1front"
        din("x0_shard", (TPC, D), "f32")
        din("gw_T", (D, E), "f32")
        if full:
            din("x0_full", (T, D), "f32")
            din("wk_T", (D, D), "f32")
            din("wv_T", (D, D), "f32")
            din("wr_T", (D, D), "f32")
            din("wo_T", (D, D), "f32")
            din("tfirst", (1, D), "f32")
            din("tdecay64", (64, 16), "f32")
            din("decay_h", (1, 16), "f32")
            din("iota1", (1, 128), "f32")
            din("iotar", (1, 128), "f32")
            din("partnerrow", (128, 1), "i32")
            din("gate_pT", (EPC, D // 2, I), "i16")
            din("up_pT", (EPC, D // 2, I), "i16")
            din("down_pT", (EPC, I // 2, D), "i16")
            din("sh_gate_pT", (D // 2, I), "i16")
            din("sh_up_pT", (D // 2, I), "i16")
            din("sh_down_pT", (I // 2, D), "i16")
            din("gate_sT", (EPC, D // G, I), "f32")
            din("up_sT", (EPC, D // G, I), "f32")
            din("down_sT", (EPC, I // G, D), "f32")
            din("sh_gate_sT", (D // G, I), "f32")
            din("sh_up_sT", (D // G, I), "f32")
            din("sh_down_sT", (I // G, D), "f32")
        din("ident", (128, 128), "bf16")
        din("identf", (128, 128), "f32")
        din("tri_excl", (128, 128), "bf16")
        din("allones", (128, 128), "bf16")
        din("onescol", (1, 128), "bf16")
        din("iota_e", (128, E), "f32")
        din("iota_ij", (128, 128), "f32")
        din("tri_ij_u8", (128, 128), "u8")
        din("tokid0", (128, 1), "f32")
        din("expbase", (128, 1), "f32")
        din("coreid", (1, 8), "i32")
        din("prevmask", (1, 8), "f32")

        dint("g_table", (E * CAP, 2), "f32")
        dint("g_table_all", (E * CAP, 2), "f32")
        dint("cnt_src", (1, E), "f32")
        dint("bown_dram", (1, E), "f32")
        dint("cnt_all", (NCORES, E), "f32")
        dint("ret_src", (EPC * CAP, D), "bf16")
        dint("ret_all", (E * CAP, D), "bf16")
        dint("hf2_src", (TPC, D), "bf16")
        dint("hf2_all", (T, D), "bf16")
        dint("st_src", (H * HD, HD), "f32")
        dint("st_all", (NCORES * H * HD, HD), "f32")

        if self.dev_stage == "ffn1x":
            dout("out", (TPC, D), "f32")
        elif self.dev_stage == "full":
            import os
            dout("out", (TPC, D), "f32")
            if os.environ.get("ATTNDBG"):
                dout("dbg_kT", (64, 8192), "f32")
                dout("dbg_rT", (64, 8192), "f32")
                dout("dbg_v", (128, 4096), "f32")
                dout("dbg_yT", (128, 4096), "f32")
                dout("dbg_S", (64, 1024), "f32")
                dout("dbg_W1", (64, 2048), "f32")
                dout("dbg_Dm", (128, 2048), "f32")
                dout("dbg_hf2", (128, 4 * D), "f32")
                dout("dbg_x4", (128, 4 * D), "f32")
                dout("dbg_v2", (128, 4096), "f32")
                dout("dbg_W2", (64, 2048), "f32")
                dout("dbg_kh", (64, 2048), "f32")
                dout("dbg_kht", (128, 1024), "f32")
                dout("dbg_Ms", (64, 1024), "f32")
        elif self.dev_stage == "ffn1front":
            dout("dbg_w", (TPC, K4), "f32")
            dout("dbg_idx", (TPC, 8), "u32")
            dout("dbg_gpos", (TPC, K4), "f32")
            dout("dbg_cnt", (NCORES, E), "f32")
            dout("dbg_hf", (TPC, D), "bf16")
            dout("dbg_stats", (128, NT * 8), "f32")
            dout("dbg_table", (E * CAP, 2), "f32")
        else:
            dout("out", (TPC, D), "f32")

    def alloc_common(self):
        S, P = self.S, self.P
        S("x", (128, NT * D), "f32")
        S("hf", (128, 4 * D), "bf16")
        S("hfT", (128, 8 * TPC), "bf16")
        S("stats", (128, NT * 8), "f32")
        S("ln_tmp", (128, 4 * NT), "f32")
        S("consts", (128, 512), "bf16")
        S("identf_sb", (128, 128), "f32")
        S("iota_e_sb", (128, E), "f32")
        S("gwT", (128, 8 * E), "bf16")

        S("m8", (128, NT * 8), "f32")
        S("idx8", (128, NT * 8), "u32")
        S("idxf", (128, NT * K4), "f32")
        S("w4", (128, NT * K4), "f32")
        S("oh", (128, NT * E), "bf16")
        S("pos4", (128, NT * K4), "f32")
        S("rtmp", (128, NT * 2 * K4), "f32")
        S("pairs", (128, NT * K4 * 2), "f32")
        S("sidx", (128, NT * K4), "i32")
        S("bownrow", (1, E), "f32")
        S("cntrows", (1, NCORES * E), "f32")
        S("bownbc", (128, E), "f32")
        S("petile", (128, E), "f32")
        S("fidx", (128, 2 * NT * K4), "f32")
        S("tokid_sb", (128, 1), "f32")
        S("coreid_sb", (1, 8), "i32")
        S("prevmask_sb", (1, 8), "f32")
        S("arena", (128, 93440), "u8")
        S("gpair", (128, 32), "f32")
        S("tokcol", (128, 16), "i32")
        S("eidx", (128, 16), "i32")
        S("iscr", (128, 8), "i32")
        S("expbase_sb", (128, 1), "f32")
        S("prow_sb", (128, 1), "i32")

        P("pb", (128, 4096), "f32")

    def load_common(self):
        p, sb, dram = self.p, self.sb, self.dram
        s_in = self.s_in
        self._bc = {}

        def mkreg(e):
            self._bc["r"] = e.to_reg(E * CAP - 1)
            return None
        p.emit("gpsimd", mkreg, kind="dma")
        for it in range(NT):
            tgt = sb["x"][:, it * D:(it + 1) * D]
            src = dram["x0_shard"].rearrange("(n p) d -> n p d", p=128)[it]
            p.emit_i("sync", lambda e, t=tgt, s=src: e.dma_start(out=t, in_=s), inc_sem=s_in, inc_n=16)
        self.x_loaded = p.cur(s_in)
        cl = [
            (sb["consts"][:, 0:128], dram["ident"]),
            (sb["consts"][:, 128:256], dram["tri_excl"]),
            (sb["consts"][:, 256:384], dram["allones"]),
            (sb["consts"][0:1, 384:512], dram["onescol"]),
            (sb["iota_e_sb"][:], dram["iota_e"]),
            (sb["identf_sb"][:], dram["identf"]),
        ]
        for tgt, src in cl:
            p.emit_i("sync", lambda e, t=tgt, s=src: e.dma_start(out=t, in_=s), inc_sem=s_in, inc_n=16)
        for kc in range(8):
            tgt = sb["gwT"][:, kc * E:(kc + 1) * E]
            src = dram["gw_T"][kc * 128:(kc + 1) * 128, :]
            p.emit_i("gpsimd", lambda e, t=tgt, s=src: e.dma_start(out=t, in_=s),
                     inc_sem=self.s_po, inc_n=16, kind="dma")
        p.emit_i("sync", lambda e: e.dma_start(out=sb["tokid_sb"][:], in_=dram["tokid0"]), inc_sem=s_in, inc_n=16)
        p.emit_i("sync", lambda e: e.dma_start(out=sb["expbase_sb"][:], in_=dram["expbase"]), inc_sem=s_in, inc_n=16)
        if "partnerrow" in dram:
            p.emit_i("sync", lambda e: e.dma_start(out=sb["prow_sb"][:], in_=dram["partnerrow"]), inc_sem=s_in, inc_n=16)
        p.emit_i("sync", lambda e: e.dma_start(out=sb["coreid_sb"][:], in_=dram["coreid"]), inc_sem=s_in, inc_n=16)
        p.emit_i("sync", lambda e: e.dma_start(out=sb["prevmask_sb"][:], in_=dram["prevmask"]), inc_sem=s_in, inc_n=16)
        p.emit_i("vector", lambda e: e.memset(sb["stats"][:], 0.0), inc_sem=self.s_ve)
        self.ve_zt = 0
        self.consts_loaded = p.cur(s_in)
        self.x_loaded = p.cur(s_in)
        self.gw_loaded = p.cur(self.s_po)
        self.identf_sb = sb["identf_sb"][:]
        self.ident = sb["consts"][:, 0:128]
        self.tri_excl = sb["consts"][:, 128:256]
        self.allones = sb["consts"][:, 256:384]
        self.onescol = sb["consts"][0:1, 384:512]

    def emit_ln(self, x_ap_fn, hf_dst_fn, stat_base, wait_first, ntiles=NT):
        p, sb = self.p, self.sb
        stats, ln_tmp, sq = sb["stats"], sb["ln_tmp"], self.carve("sq_scr")
        for it in range(ntiles):
            src = x_ap_fn(it)
            c0 = stat_base + it * 8
            p.emit("vector", lambda e, s=src, o=stats[:, c0:c0 + 1]: e.reduce_sum(out=o, in_=s, axis=AX.X),
                   waits=(wait_first if it == 0 else []))
            wsq = list(wait_first) if it == 0 else []
            p.emit_i("scalar", lambda e, s=src, o=stats[:, c0 + 1:c0 + 2]: e.activation(
                out=sq[:], in_=s, func=AF.Square, accum_out=o),
                waits=wsq, inc_sem=self.s_ac)
        ac_sq = p.cur(self.s_ac)
        st = lambda c: stats[:, stat_base + c:stat_base + ntiles * 8:8]
        allsum, allssq, allinv, allnb = st(0), st(1), st(2), st(3)
        tmpm, tmpv, tmps, tmpq = (ln_tmp[:, i * NT:i * NT + ntiles] for i in range(4))
        ve = self.s_ve
        p.emit("vector", lambda e: e.tensor_scalar(out=tmpm, in0=allsum, scalar1=1.0 / D, scalar2=None, op0=ALU.mult),
               waits=[(self.s_ac, ac_sq)])
        p.emit("vector", lambda e: e.tensor_scalar(out=tmpv, in0=allssq, scalar1=1.0 / D, scalar2=LNEPS, op0=ALU.mult, op1=ALU.add))
        p.emit("vector", lambda e: e.tensor_tensor(out=tmpq, in0=tmpm, in1=tmpm, op=ALU.mult))
        p.emit_i("vector", lambda e: e.tensor_tensor(out=tmpv, in0=tmpv, in1=tmpq, op=ALU.subtract), inc_sem=ve)
        ve_v = p.cur(ve)
        p.emit_i("scalar", lambda e: e.activation(out=tmps, in_=tmpv, func=AF.Sqrt),
                 waits=[(ve, ve_v)], inc_sem=self.s_ac)
        ac_s = p.cur(self.s_ac)
        p.emit("vector", lambda e: e.reciprocal(out=allinv, in_=tmps), waits=[(self.s_ac, ac_s)])
        p.emit_i("vector", lambda e: e.scalar_tensor_tensor(out=allnb, in0=tmpm, scalar=-1.0, in1=allinv,
                                                            op0=ALU.mult, op1=ALU.mult), inc_sem=ve)
        ve_stats = p.cur(ve)
        for it in range(ntiles):
            src = x_ap_fn(it)
            dst = hf_dst_fn(it)
            c0 = stat_base + it * 8
            p.emit_i("scalar", lambda e, s=src, o=dst, sc=stats[:, c0 + 2:c0 + 3], b=stats[:, c0 + 3:c0 + 4]:
                     e.activation(out=o, in_=s, func=AF.Identity, scale=sc, bias=b),
                     waits=([(ve, ve_stats)] if it == 0 else []), inc_sem=self.s_ac)
        return p.cur(self.s_ac)

    def emit_ln_stream(self, x_ap_fn, stat_base, wait_first, keep_hf_dbg=False, store_dram=None):
        p, sb = self.p, self.sb
        hfT = sb["hfT"]
        roll = sb["hf"]
        tile_done = []
        ac_stats = self.emit_ln_stats(x_ap_fn, stat_base, wait_first)
        stats = sb["stats"]
        for it in range(NT):
            slot = roll[:, (it % 4) * D:(it % 4 + 1) * D]
            c0 = stat_base + it * 8
            wv = []
            if it == 0:
                wv.append((self.s_ve, ac_stats))
            if it >= 4:
                wv.append((self.s_ve, tile_done[it - 4]))
            p.emit_i("scalar", lambda e, s=x_ap_fn(it), o=slot,
                     sc=stats[:, c0 + 2:c0 + 3], b=stats[:, c0 + 3:c0 + 4]:
                     e.activation(out=o, in_=s, func=AF.Identity, scale=sc, bias=b),
                     waits=wv, inc_sem=self.s_ac)
            ac_ap = p.cur(self.s_ac)
            if keep_hf_dbg and "dbg_hf" in self.dram:
                tgt = self.dram["dbg_hf"].rearrange("(n p) d -> n p d", p=128)[it]
                p.emit_i("sync", lambda e, t=tgt, s=slot: e.dma_start(out=t, in_=s),
                         waits=[(self.s_ac, ac_ap)], inc_sem=self.s_st, inc_n=16)
            if store_dram is not None:
                tgt = self.dram[store_dram].rearrange("(n p) d -> n p d", p=128)[it]
                p.emit_i("sync", lambda e, t=tgt, s=slot: e.dma_start(out=t, in_=s),
                         waits=[(self.s_ac, ac_ap)], inc_sem=self.s_st, inc_n=16)
            for kc in range(8):
                base = 256 * (kc % 4) + (kc // 4)
                srcap = slot[:, base: base + 256 - (base % 2): 2]
                dst = hfT[:, kc * TPC + it * 128: kc * TPC + (it + 1) * 128]
                self.emit_transpose(srcap, dst,
                                    [(self.s_ac, ac_ap)] if kc == 0 else [])
            tile_done.append(p.cur(self.s_ve))
        return p.cur(self.s_ve)

    def emit_ln_stats(self, x_ap_fn, stat_base, wait_first, ntiles=NT):
        p, sb = self.p, self.sb
        stats, ln_tmp, sq = sb["stats"], sb["ln_tmp"], self.carve("sq_scr")
        for it in range(ntiles):
            src = x_ap_fn(it)
            c0 = stat_base + it * 8
            p.emit("vector", lambda e, s=src, o=stats[:, c0:c0 + 1]: e.reduce_sum(out=o, in_=s, axis=AX.X),
                   waits=(list(wait_first) if it == 0 else []))
            p.emit_i("scalar", lambda e, s=src, o=stats[:, c0 + 1:c0 + 2]: e.activation(
                out=sq[:], in_=s, func=AF.Square, accum_out=o),
                waits=(list(wait_first) if it == 0 else []), inc_sem=self.s_ac)
        ac_sq = p.cur(self.s_ac)
        st = lambda c: stats[:, stat_base + c:stat_base + ntiles * 8:8]
        allsum, allssq, allinv, allnb = st(0), st(1), st(2), st(3)
        tmpm, tmpv, tmps, tmpq = (ln_tmp[:, i * NT:i * NT + ntiles] for i in range(4))
        ve = self.s_ve
        p.emit("vector", lambda e: e.tensor_scalar(out=tmpm, in0=allsum, scalar1=1.0 / D, scalar2=None, op0=ALU.mult),
               waits=[(self.s_ac, ac_sq)])
        p.emit("vector", lambda e: e.tensor_scalar(out=tmpv, in0=allssq, scalar1=1.0 / D, scalar2=LNEPS, op0=ALU.mult, op1=ALU.add))
        p.emit("vector", lambda e: e.tensor_tensor(out=tmpq, in0=tmpm, in1=tmpm, op=ALU.mult))
        p.emit_i("vector", lambda e: e.tensor_tensor(out=tmpv, in0=tmpv, in1=tmpq, op=ALU.subtract), inc_sem=ve)
        ve_v = p.cur(ve)
        p.emit_i("scalar", lambda e: e.activation(out=tmps, in_=tmpv, func=AF.Sqrt),
                 waits=[(ve, ve_v)], inc_sem=self.s_ac)
        ac_s = p.cur(self.s_ac)
        p.emit("vector", lambda e: e.reciprocal(out=allinv, in_=tmps), waits=[(self.s_ac, ac_s)])
        p.emit_i("vector", lambda e: e.scalar_tensor_tensor(out=allnb, in0=tmpm, scalar=-1.0, in1=allinv,
                                                            op0=ALU.mult, op1=ALU.mult), inc_sem=ve)
        return p.cur(ve)

    def emit_routing(self, substage=99):
        p, sb, ps = self.p, self.sb, self.ps
        s_mm, s_ve, s_ac, s_in = self.s_mm, self.s_ve, self.s_ac, self.s_in
        hfT, gwT = sb["hfT"], sb["gwT"]
        if substage <= 0:
            return None
        for ncol in range(TPC // 512):
            pst = self.bank(2, 512)[0:64, :]
            for kc in range(8):
                lhs = gwT[:, kc * E:(kc + 1) * E]
                rhs = hfT[:, kc * TPC + ncol * 512: kc * TPC + ncol * 512 + 512]
                w = []
                if kc == 0:
                    w = [(s_ve, self.ve_hfT), (s_in, self.consts_loaded), (self.s_po, self.gw_loaded)]
                    if ncol > 0:
                        w.append((s_ve, self.p.cur(s_ve)))
                p.emit_i("tensor", lambda e, o=pst, l=lhs, r=rhs, k=kc: e.matmul(
                    out=o, lhsT=l, rhs=r, start=(k == 0), stop=(k == 7)),
                    waits=w, inc_sem=(s_mm if kc == 7 else None))
            mmv = p.cur(s_mm)
            dst = self.carve("logitsT")[:, ncol * 512:(ncol + 1) * 512]
            p.emit_i("vector", lambda e, s=pst, o=dst: e.tensor_copy(out=o, in_=s),
                     waits=[(s_mm, mmv)], inc_sem=s_ve)
        ve_lg = p.cur(s_ve)
        if substage <= 1:
            return None
        for it in range(NT):
            src = self.carve("logitsT")[:, it * 128:(it + 1) * 128]
            dst = self.carve("lgtile")[:, it * E:(it + 1) * E]
            self.emit_transpose(src, dst, [(s_ve, ve_lg)] if it == 0 else [],
                                f32=True, ident=self.identf_sb[0:64, 0:64])
        if substage <= 2:
            return None
        m8, idx8, idxf, w4 = sb["m8"], sb["idx8"], sb["idxf"], sb["w4"]
        ohr, oh, rtmp = self.carve("ohr_c"), sb["oh"], sb["rtmp"]
        acv = {}
        for it in range(NT):
            lg = self.carve("lgtile")[:, it * E:(it + 1) * E]
            m8t = m8[:, it * 8:(it + 1) * 8]
            id8t = idx8[:, it * 8:(it + 1) * 8]
            idft = idxf[:, it * K4:(it + 1) * K4]
            w4t = w4[:, it * K4:(it + 1) * K4]
            sc0 = rtmp[:, it * 8:it * 8 + 1]
            p.emit("vector", lambda e, s=lg, o=m8t: e.max(out=o, in_=s))
            p.emit("vector", lambda e, s=lg, m=m8t, o=id8t: e.max_index(out=o, in_max=m, in_values=s))
            p.emit("vector", lambda e, s=id8t, o=idft: e.tensor_copy(out=o, in_=s[:, 0:K4]))
            p.emit_i("vector", lambda e, s=m8t, o=sc0: e.tensor_scalar(
                out=o, in0=s[:, 0:1], scalar1=-1.0, scalar2=None, op0=ALU.mult), inc_sem=s_ve)
            vev = p.cur(s_ve)
            p.emit_i("scalar", lambda e, s=m8t, o=w4t, b=sc0: e.activation(
                out=o, in_=s[:, 0:K4], func=AF.Exp, bias=b, scale=1.0),
                waits=[(s_ve, vev)], inc_sem=s_ac)
            acv[it] = p.cur(s_ac)
            for r in range(K4):
                ohrt = ohr[:, (it * K4 + r) * E:(it * K4 + r + 1) * E]
                p.emit("vector", lambda e, o=ohrt, idc=idft[:, r:r + 1]: e.tensor_scalar(
                    out=o, in0=sb["iota_e_sb"][:], scalar1=idc, scalar2=None, op0=ALU.is_equal))
            oht = oh[:, it * E:(it + 1) * E]
            p.emit("vector", lambda e, o=oht, a=ohr[:, (it * K4 + 0) * E:(it * K4 + 1) * E],
                   b=ohr[:, (it * K4 + 1) * E:(it * K4 + 2) * E]: e.tensor_tensor(out=o, in0=a, in1=b, op=ALU.add))
            p.emit("vector", lambda e, o=oht, a=ohr[:, (it * K4 + 2) * E:(it * K4 + 3) * E]:
                   e.tensor_tensor(out=o, in0=o, in1=a, op=ALU.add))
            p.emit_i("vector", lambda e, o=oht, a=ohr[:, (it * K4 + 3) * E:(it * K4 + 4) * E]:
                     e.tensor_tensor(out=o, in0=o, in1=a, op=ALU.add), inc_sem=s_ve)
        for it in range(NT):
            w4t = w4[:, it * K4:(it + 1) * K4]
            sc1 = rtmp[:, it * 8 + 1:it * 8 + 2]
            p.emit("vector", lambda e, s=w4t, o=sc1: e.reduce_sum(out=o, in_=s, axis=AX.X),
                   waits=[(s_ac, acv[it])])
            p.emit("vector", lambda e, s=sc1, o=sc1: e.reciprocal(out=o, in_=s))
            p.emit("vector", lambda e, s=w4t, o=w4t, sc=sc1: e.tensor_scalar(
                out=o, in0=s, scalar1=sc, scalar2=None, op0=ALU.mult))
        self.ve_oh = p.cur(s_ve)
        if substage <= 3:
            return None
        for it in range(NT):
            oht = oh[:, it * E:(it + 1) * E]
            w = [(s_ve, self.ve_oh)] if it == 0 else []
            p.emit_i("tensor", lambda e, r=oht, k=it: e.matmul(
                out=self.bank(7, 64), lhsT=self.allones, rhs=r, start=(k == 0), stop=(k == NT - 1)),
                waits=w, inc_sem=(s_mm if it == NT - 1 else None))
        mmv = p.cur(s_mm)
        cntrow = sb["bownrow"]
        p.emit_i("vector", lambda e: e.tensor_copy(out=sb["rtmp"][0:1, 0:E], in_=self.bank(7, 64)[0:1, :]),
                 waits=[(s_mm, mmv)], inc_sem=s_ve)
        ve_cnt = p.cur(s_ve)
        p.emit_i("sync", lambda e: e.dma_start(out=self.dram["cnt_src"], in_=sb["rtmp"][0:1, 0:E]),
                 waits=[(s_ve, ve_cnt)], inc_sem=self.s_st, inc_n=16)
        return p.cur(self.s_st)



    def emit_counts_exchange(self, st_cnt):
        p, sb, dram = self.p, self.sb, self.dram
        s_cc, s_ve = self.s_cc, self.s_ve
        cc = p.emit_i("gpsimd", lambda e: e.collective_compute(
            "AllGather", ALU.bypass, replica_groups=[list(range(NCORES))],
            ins=[dram["cnt_src"].opt()], outs=[dram["cnt_all"].opt()]),
            waits=[(self.s_st, st_cnt)], inc_sem=s_cc, kind="dma")
        cntrows = sb["cntrows"][0:1, :]
        p.emit_i("sync", lambda e: e.dma_start(
            out=cntrows, in_=dram["cnt_all"].rearrange("c e -> (c e)").unsqueeze(0)),
            waits=[(s_cc, cc)], inc_sem=self.s_in, inc_n=16)
        ld = p.cur(self.s_in)
        bown = sb["bownrow"][0:1, :]
        pm = sb["prevmask_sb"]
        p.emit("vector", lambda e: e.memset(bown, 0.0), waits=[(self.s_in, ld)])
        for c in range(NCORES):
            row = cntrows[:, c * E:(c + 1) * E]
            p.emit("vector", lambda e, a=row, m=pm[0:1, c:c + 1]: e.scalar_tensor_tensor(
                out=bown, in0=a, scalar=m, in1=bown, op0=ALU.mult, op1=ALU.add))
        p.emit_i("vector", lambda e: e.nop(), inc_sem=s_ve)
        return p.cur(s_ve)

    def emit_gpos_scatter(self, ve_bown):
        p, sb, ps, dram = self.p, self.sb, self.ps, self.dram
        s_mm, s_ve, s_po, s_cc = self.s_mm, self.s_ve, self.s_po, self.s_cc
        oh, ohr, idxf, w4 = sb["oh"], self.carve("ohr_c"), sb["idxf"], sb["w4"]
        zt2 = dram["g_table"].rearrange("(p n) two -> p (n two)", p=128)
        zv = p.emit_i("vector", lambda e: e.memset(self.carve("gath_f32")[:, 0:E * CAP * 2 // 128], 0.0),
                      waits=[(self.s_po, p.cur(self.s_po))], inc_sem=self.s_ve)
        p.emit_i("sync", lambda e: e.dma_start(out=zt2, in_=self.carve("gath_f32")[:, 0:E * CAP * 2 // 128]),
                 waits=[(self.s_ve, zv)], inc_sem=self.s_st, inc_n=16)
        st_zero = p.cur(self.s_st)

        bown_bc = sb["bownbc"]
        p.emit_i("sync", lambda e: e.dma_start(out=dram["bown_dram"], in_=sb["bownrow"][0:1, :]),
                 waits=[(s_ve, ve_bown)], inc_sem=self.s_st, inc_n=16)
        st_b1 = p.cur(self.s_st)
        p.emit_i("sync", lambda e: e.dma_start(out=bown_bc[:], in_=dram["bown_dram"].broadcast_to([128, E])),
                 waits=[(self.s_st, st_b1)], inc_sem=self.s_st, inc_n=16)
        st_bbc = p.cur(self.s_st)
        gpos = sb["pos4"]
        pe_sb = sb["petile"]
        ve_pe_prev = None
        for it in range(NT):
            oht = oh[:, it * E:(it + 1) * E]
            w = [] if ve_pe_prev is None else [(s_ve, ve_pe_prev)]
            p.emit_i("tensor", lambda e, r=oht: e.matmul(out=self.bank(6, 64), lhsT=self.tri_excl,
                                                         rhs=r, start=True, stop=True),
                     waits=w, inc_sem=s_mm)
            mmP = p.cur(s_mm)
            wv = [(s_mm, mmP)]
            if it == 0:
                wv.append((self.s_st, st_bbc))
            p.emit("vector", lambda e: e.tensor_tensor(out=pe_sb[:, 0:E], in0=self.bank(6, 64),
                                                       in1=bown_bc[:], op=ALU.add),
                   waits=wv)
            if it > 0:
                p.emit_i("vector", lambda e: e.tensor_tensor(out=pe_sb[:, 0:E], in0=pe_sb[:, 0:E],
                                                             in1=self.bank(7, 64), op=ALU.add), inc_sem=s_ve)
            else:
                p.emit_i("vector", lambda e: e.nop(), inc_sem=s_ve)
            ve_pe = p.cur(s_ve)
            ve_pe_prev = ve_pe
            if it < NT - 1:
                p.emit_i("tensor", lambda e, r=oht, first=(it == 0), last=(it == NT - 2): e.matmul(
                    out=self.bank(7, 64), lhsT=self.allones, rhs=r, start=first, stop=last,
                    skip_group_check=True),
                         waits=[(s_ve, ve_pe)], inc_sem=s_mm)
            for r in range(K4):
                ohrt = ohr[:, (it * K4 + r) * E:(it * K4 + r + 1) * E]
                gcol = gpos[:, it * K4 + r:it * K4 + r + 1]
                scr = self.carve("lgtile")[:, 0:E]
                p.emit("vector", lambda e, a=pe_sb[:, 0:E], b=ohrt, o=scr:
                       e.tensor_tensor(out=o, in0=a, in1=b, op=ALU.mult))
                p.emit("vector", lambda e, s=scr, o=gcol: e.reduce_sum(out=o, in_=s, axis=AX.X))
            p.emit_i("vector", lambda e: e.engine_nop(), inc_sem=s_ve)
        ve_gpos = p.cur(s_ve)

        sidx, pairs, rtmp = sb["sidx"], sb["pairs"], sb["rtmp"]
        tok0 = sb["tokid_sb"][:, 0:1]
        idx8 = sb["idx8"]
        bscr = sb["rtmp"].bitcast(U32)
        for it in range(NT):
            i0 = it * K4
            eu = idx8[:, it * 8: it * 8 + K4]
            gp = gpos[:, i0:i0 + K4]
            fidx = sb["fidx"][:, i0:i0 + K4]
            b1, b2, b3 = bscr[:, 0:K4], bscr[:, K4:2 * K4], bscr[:, 2 * K4:3 * K4]
            p.emit("vector", lambda e, o=b1, a=eu: e.tensor_scalar(
                out=o, in0=a, scalar1=3, scalar2=9, op0=ALU.logical_shift_right, op1=ALU.logical_shift_left),
                waits=([(s_ve, ve_gpos)] if it == 0 else []))
            p.emit("vector", lambda e, o=b2, a=eu: e.tensor_scalar(
                out=o, in0=a, scalar1=1, scalar2=8, op0=ALU.bitwise_and, op1=ALU.logical_shift_left))
            p.emit("vector", lambda e, o=b3, a=eu: e.tensor_scalar(
                out=o, in0=a, scalar1=6, scalar2=11, op0=ALU.bitwise_and, op1=ALU.logical_shift_left))
            p.emit("vector", lambda e, o=b1: e.tensor_tensor(out=o, in0=o, in1=b2, op=ALU.add))
            p.emit("vector", lambda e, o=b1: e.tensor_tensor(out=o, in0=o, in1=b3, op=ALU.add))
            p.emit("vector", lambda e, o=fidx, a=b1: e.tensor_copy(out=o, in_=a))
            p.emit("vector", lambda e, o=fidx, b=gp: e.tensor_tensor(out=o, in0=o, in1=b, op=ALU.add))
            mcol = sb["fidx"][:, NT * K4 + i0: NT * K4 + i0 + K4]
            p.emit("vector", lambda e, o=mcol, a=gp: e.tensor_scalar(
                out=o, in0=a, scalar1=float(CAP), scalar2=float(BIGIDX), op0=ALU.is_ge, op1=ALU.mult))
            p.emit("vector", lambda e, o=fidx, a=mcol: e.tensor_tensor(out=o, in0=o, in1=a, op=ALU.add))
            p.emit("vector", lambda e, o=sidx[:, i0:i0 + K4], a=fidx: e.tensor_copy(out=o, in_=a))
            for r in range(K4):
                pc = pairs[:, (i0 + r) * 2:(i0 + r) * 2 + 1]
                p.emit("vector", lambda e, o=pc, a=tok0, rr=r, itt=it: e.tensor_scalar(
                    out=o, in0=a, scalar1=4.0, scalar2=float((itt * 128) * 4 + rr + 1),
                    op0=ALU.mult, op1=ALU.add))
            wv = pairs[:, i0 * 2:(i0 + K4) * 2].rearrange("p (k two) -> p k two", two=2)[:, :, 1:2]
            p.emit_i("vector", lambda e, o=wv, a=w4[:, i0:i0 + K4]: e.tensor_scalar(
                out=o, in0=a.unsqueeze(2), scalar1=0.5, scalar2=None, op0=ALU.mult), inc_sem=s_ve)
        ve_pairs = p.cur(s_ve)

        po_w = [(s_ve, ve_pairs), (self.s_st, st_zero)]
        first = True
        nsc = 0
        for it in range(NT):
            for r in range(K4):
                col = it * K4 + r
                idxap = sidx[:, col:col + 1]
                datap = pairs[:, col * 2:col * 2 + 2]
                w = list(po_w) if first else []
                if nsc >= 8:
                    w.append((s_po, p.cur(s_po) - 7 * 16))
                p.emit_i("gpsimd", lambda e, ia=idxap, da=datap: e.indirect_dma_start(
                    out=dram["g_table"], out_offset=IndirectOffsetOnAxis(ia, 0),
                    in_=da, in_offset=None,
                    bounds_check=self._bc["r"], oob_is_err=False),
                    waits=w, inc_sem=s_po, inc_n=16, kind="dma")
                first = False
                nsc += 1
        po_sct = p.cur(s_po)
        cc = p.emit_i("gpsimd", lambda e: e.collective_compute(
            "AllReduce", ALU.add, replica_groups=[list(range(NCORES))],
            ins=[dram["g_table"].opt()], outs=[dram["g_table_all"].opt()]),
            waits=[(s_po, po_sct)], inc_sem=self.s_cc, kind="dma")
        return cc


    def bank(self, b, cols=512):
        return self.ps["pb"][:, b * 512:b * 512 + cols]

    def bankbf(self, b, cols=1024):
        return self.ps["pb"].bitcast(BF16)[:, b * 1024:b * 1024 + cols]

    def emit_transpose(self, srcap, dst, first_waits=(), f32=False, ident=None):
        p = self.p
        bidx = self._tb_n % 2
        ncols = srcap.shape[0]
        if f32:
            pst = self.bank(bidx, 512)[:, 0:ncols]
        else:
            pst = self.bankbf(bidx, 1024)[:, 0:ncols]
        wv = list(first_waits)
        if self._tb_evac[bidx] is not None:
            wv.append((self.s_ve, self._tb_evac[bidx]))
        kk = srcap.shape[0] if hasattr(srcap, "shape") else 128
        idt = ident if ident is not None else (self.identf_sb[0:kk, 0:kk] if f32 else self.ident[0:kk, 0:kk])
        p.emit_i("tensor", lambda e, o=pst, s=srcap, ii=idt: e.transpose(out=o, in_=s, identity=ii),
                 waits=wv, inc_sem=self.s_mm)
        mmv = p.cur(self.s_mm)
        tgt = p.emit_i("vector", lambda e, s=pst, o=dst: e.tensor_copy(out=o, in_=s),
                       waits=[(self.s_mm, mmv)], inc_sem=self.s_ve)
        self._tb_evac[bidx] = tgt
        self._tb_n += 1
        return tgt

    def emit_dequant_matrix(self, mat, j, mm_mark=None, scr_mark=None):
        p, sb, dram = self.p, self.sb, self.dram
        s_wt, s_ve = self.s_wt, self.s_ve
        width = I if mat in ("gate", "up") else D
        ntile = (D // 2 // 128) if mat in ("gate", "up") else (I // 2 // 128)
        if j < 0:
            pT, sT = dram[f"sh_{mat}_pT"], dram[f"sh_{mat}_sT"]
        else:
            pT, sT = dram[f"{mat}_pT"][j], dram[f"{mat}_sT"][j]
        wlo, whi = self.carve(f"w_{mat}_lo"), self.carve(f"w_{mat}_hi")
        if mm_mark is None:
            mm_mark = p.cur(self.s_mm)
        for dt in range(ntile):
            par = self._deq_par
            self._deq_par ^= 1
            pk_t = self.carve("pk_t2" if par else "pk_t")[:, 0:width]
            st_t = self.carve("st_t2" if par else "st_t")[:, 0:width]
            s_ld = self.s_wt2 if par else s_wt
            wv = [(s_ve, self._deq_free[par])] + ([(self.s_mm, mm_mark)] if dt == 0 else [])
            if self._mk_ld_uses > 0:
                wv = wv + self._mk_ld_extra
                self._mk_ld_uses -= 1
            p.emit_i("sync", lambda e, t=pk_t, s=pT[dt * 128:(dt + 1) * 128, :]: e.dma_start(out=t, in_=s),
                     waits=wv, inc_sem=s_ld, inc_n=16)
            p.emit_i("sync", lambda e, t=st_t, s=sT[dt * 2:dt * 2 + 2, :].unsqueeze(1).broadcast_to([2, 64, width]):
                     e.dma_start(out=t, in_=s), inc_sem=s_ld, inc_n=16)
            ld = p.cur(s_ld)
            for plane, wbuf in (("lo", wlo), ("hi", whi)):
                wt = wbuf[:, dt * width:(dt + 1) * width]
                u = self.carve("lgtile").bitcast(I16)[:, 0:width]
                g2 = pk_t.bitcast(BF16) if plane == "hi" else None
                sg = self.carve("ohr_c").bitcast(I16)[:, 0:width]
                m1, s1 = (7, 6) if plane == "lo" else (0x70, 2)
                m2, s2 = (8, 12) if plane == "lo" else (0x80, 8)
                wv = [(s_ld, ld)] if plane == "lo" else []
                if dt == 0 and plane == "lo" and scr_mark is not None:
                    wv = wv + [(self.s_mm, scr_mark)]
                    scr_mark = None
                p.emit("vector", lambda e, b=pk_t, o=sg, mm=m2, ss=s2: e.tensor_scalar(
                    out=o, in0=b, scalar1=mm, scalar2=ss, op0=ALU.bitwise_and, op1=ALU.logical_shift_left),
                    waits=wv)
                p.emit("vector", lambda e, b=pk_t, o=u, mm=m1, ss=s1: e.tensor_scalar(
                    out=o, in0=b, scalar1=mm, scalar2=ss, op0=ALU.bitwise_and, op1=ALU.logical_shift_left))
                p.emit("vector", lambda e, o=u: e.tensor_scalar(
                    out=o, in0=o, scalar1=0x3F00, scalar2=None, op0=ALU.add))
                gdst = (pk_t if plane == "hi" else self.carve("lgtile").bitcast(I16)[:, 1024:1024 + width])
                p.emit("vector", lambda e, o=gdst.bitcast(BF16), f=u.bitcast(BF16): e.tensor_scalar(
                    out=o, in0=f, scalar1=2.0, scalar2=-1.0, op0=ALU.mult, op1=ALU.add))
                p.emit("vector", lambda e, o=u.bitcast(BF16), f=gdst.bitcast(BF16): e.tensor_tensor(
                    out=o, in0=o, in1=f, op=ALU.min))
                p.emit("vector", lambda e, o=u.bitcast(BF16), s=st_t: e.tensor_tensor(
                    out=o, in0=o, in1=s, op=ALU.mult))
                p.emit_i("vector", lambda e, o=wt.bitcast(I16), a=u, b2=sg:
                         e.tensor_tensor(out=o, in0=a, in1=b2, op=ALU.bitwise_xor), inc_sem=s_ve)
            self._deq_free[par] = p.cur(s_ve)
        return p.cur(s_ve)

    def emit_expert(self, call, j):
        p, sb, ps, dram = self.p, self.sb, self.ps, self.dram
        s_mm, s_ve, s_ac, s_po, s_st = self.s_mm, self.s_ve, self.s_ac, self.s_po, self.s_st
        mk_gu_prev, mk_dn_prev = self._mk_gu, self._mk_dn
        mk_ebT_prev, mk_gath_prev = self._mk_ebT, self._mk_gath
        scr0 = self._mk_scr
        self._mk_scr = None

        for h in range(2):
            icol = sb["eidx"][:, j * 2 + h:j * 2 + h + 1]
            p.emit("vector", lambda e, o=sb["fidx"][:, 0:1],
                   off=float((j & 1) * 256 + h * 128 + 4096 * (j >> 1)):
                   e.tensor_scalar(out=o, in0=sb["expbase_sb"][:, 0:1], scalar1=off, scalar2=None, op0=ALU.add))
            p.emit_i("vector", lambda e, o=icol: e.tensor_copy(out=o, in_=sb["fidx"][:, 0:1]), inc_sem=s_ve)
            vei = p.cur(s_ve)
            gp = sb["gpair"][:, (j * 2 + h) * 2:(j * 2 + h) * 2 + 2]
            p.emit_i("gpsimd", lambda e, ia=icol, o=gp: e.indirect_dma_start(
                out=o, out_offset=None, in_=dram["g_table_all"],
                in_offset=IndirectOffsetOnAxis(ia, 0),
                bounds_check=self._bc["r"], oob_is_err=False),
                waits=[(s_ve, vei), (self.s_cc, p.cur(self.s_cc)), (s_ve, p.cur(s_ve))],
                inc_sem=s_po, inc_n=16, kind="dma")
            pog = p.cur(s_po)
            tk = sb["tokcol"][:, j * 2 + h:j * 2 + h + 1]
            p.emit("vector", lambda e, o=sb["iscr"][:, 0:1], a=gp: e.tensor_copy(out=o, in_=a[:, 0:1]),
                   waits=[(s_po, pog)])
            p.emit("vector", lambda e, o=sb["iscr"][:, 0:1]: e.tensor_scalar(
                out=o, in0=o, scalar1=-1, scalar2=None, op0=ALU.add))
            p.emit("vector", lambda e, o=sb["iscr"][:, 0:1]: e.tensor_scalar(
                out=o, in0=o, scalar1=2, scalar2=None, op0=ALU.arith_shift_right))
            p.emit("vector", lambda e, o=sb["iscr"][:, 1:2], a=gp: e.tensor_scalar(
                out=o, in0=a[:, 0:1], scalar1=1.0, scalar2=float(BIGIDX), op0=ALU.is_lt, op1=ALU.mult))
            p.emit_i("vector", lambda e, o=tk: e.tensor_tensor(
                out=o, in0=sb["iscr"][:, 0:1], in1=sb["iscr"][:, 1:2], op=ALU.add), inc_sem=s_ve)
            vet = p.cur(s_ve)
            gbuf = self.carve("gath_f32")[:] if call == 1 else self.carve("gath_f32")[:].bitcast(BF16)[:, 0:2048]
            gb = gbuf[:, h * D:(h + 1) * D]
            srcrows = dram["x0_full"] if call == 1 else dram["hf2_all"]
            wv = [(s_ve, vet), (s_po, p.cur(s_po)), (s_ac, mk_gath_prev), (s_mm, mk_dn_prev)]
            p.emit_i("gpsimd", lambda e, ia=tk, o=gb, s=srcrows: e.indirect_dma_start(
                out=o, out_offset=None, in_=s, in_offset=IndirectOffsetOnAxis(ia, 0),
                bounds_check=self._bc["r"], oob_is_err=False),
                waits=wv, inc_sem=s_po, inc_n=16, kind="dma")
        po_rows = p.cur(s_po)

        lnw = [(s_po, po_rows), (self.s_mm, mk_ebT_prev)]
        if call == 1:
            ac_buf = self.emit_ln(lambda h: self.carve("gath_f32")[:, h * D:(h + 1) * D],
                                  lambda h: self.carve("ebuf")[:, h * D:(h + 1) * D], 0, lnw, ntiles=2)
        else:
            for h in range(2):
                p.emit_i("scalar", lambda e, o=self.carve("ebuf")[:, h * D:(h + 1) * D],
                         s=self.carve("gath_f32")[:].bitcast(BF16)[:, h * D:(h + 1) * D]: e.copy(out=o, in_=s),
                         waits=(lnw if h == 0 else []), inc_sem=s_ac)
            ac_buf = p.cur(s_ac)

        ve_g = self.emit_dequant_matrix("gate", j, mk_gu_prev, scr_mark=scr0)
        ve_u = self.emit_dequant_matrix("up", j, mk_gu_prev)

        for h in range(2):
            for pl in range(2):
                for k in range(4):
                    sb_ = h * D + 256 * k + pl
                    srcap = self.carve("ebuf")[:, sb_: sb_ + 256 - pl: 2]
                    dst = self.carve("ebufT")[:, ((pl * 4 + k) * 2 + h) * 128: ((pl * 4 + k) * 2 + h) * 128 + 128]
                    self.emit_transpose(srcap, dst,
                                        [(s_ac, ac_buf), (s_mm, mk_dn_prev)] if (h == 0 and pl == 0 and k == 0) else [])
        ve_ebT = p.cur(s_ve)
        self._mk_ebT = p.cur(s_mm)

        for h in range(2):
            for ic in range(2):
                grp = [(s_ve, max(ve_ebT, ve_g, ve_u)), (s_ac, p.cur(s_ac)), (s_ve, p.cur(s_ve))]
                for mi, bk in ((0, 2), (1, 3)):
                    n = 0
                    for pl in range(2):
                        wsrc = (self.carve("w_gate_lo"), self.carve("w_gate_hi"), self.carve("w_up_lo"), self.carve("w_up_hi"))[mi * 2 + pl]
                        for k in range(4):
                            lhs = self.carve("ebufT")[:, ((pl * 4 + k) * 2 + h) * 128: ((pl * 4 + k) * 2 + h) * 128 + 128]
                            rhs = wsrc[:, k * I + ic * 384: k * I + ic * 384 + 384]
                            w = grp if (n == 0) else []
                            p.emit_i("tensor", lambda e, l=lhs, r=rhs, st_=(n == 0), sp=(n == 7), b=bk:
                                     e.matmul(out=self.bank(b, 384), lhsT=l, rhs=r, start=st_, stop=sp),
                                     waits=w, inc_sem=(s_mm if n == 7 else None))
                            n += 1
                mmgu = p.cur(s_mm)
                sg_t = self.carve("hsil")[:, 0:384]
                p.emit_i("scalar", lambda e, o=sg_t: e.activation(out=o, in_=self.bank(2, 384), func=AF.Silu),
                         waits=[(s_mm, mmgu)], inc_sem=s_ac)
                acs = p.cur(s_ac)
                hu = self.carve("hU")[:, h * I + ic * 384: h * I + ic * 384 + 384]
                p.emit_i("vector", lambda e, o=hu, a=sg_t: e.tensor_tensor(
                    out=o, in0=a, in1=self.bank(3, 384), op=ALU.mult),
                    waits=[(s_ac, acs)], inc_sem=s_ve)
        ve_hu = p.cur(s_ve)
        self._mk_gu = p.cur(s_mm)
        ve_d = self.emit_dequant_matrix("down", j, mk_dn_prev)

        for h in range(2):
            for pl in range(2):
                for m in range(3):
                    srcap = self.carve("hU")[:, h * I + pl * 384 + m * 128: h * I + pl * 384 + (m + 1) * 128]
                    dst = self.carve("ebufT")[:, ((pl * 3 + m) * 2 + h) * 128: ((pl * 3 + m) * 2 + h) * 128 + 128]
                    self.emit_transpose(srcap, dst, [(s_ve, ve_hu)] if (h == 0 and pl == 0 and m == 0) else [])
        ve_huT = p.cur(s_ve)

        for h in range(2):
            grp = [(s_ve, max(ve_huT, ve_d)), (s_ac, p.cur(s_ac))]
            for nc_ in range(2):
                n = 0
                for pl in range(2):
                    wsrc = self.carve("w_down_lo") if pl == 0 else self.carve("w_down_hi")
                    for m in range(3):
                        lhs = self.carve("ebufT")[:, ((pl * 3 + m) * 2 + h) * 128: ((pl * 3 + m) * 2 + h) * 128 + 128]
                        rhs = wsrc[:, m * D + nc_ * 512: m * D + nc_ * 512 + 512]
                        w = grp if (n == 0 and nc_ == 0) else []
                        p.emit_i("tensor", lambda e, l=lhs, r=rhs, st_=(n == 0), sp=(n == 5), b=4 + nc_:
                                 e.matmul(out=self.bank(b, 512), lhsT=l, rhs=r, start=st_, stop=sp),
                                 waits=w, inc_sem=(s_mm if n == 5 else None))
                        n += 1
            mmd = p.cur(s_mm)
            wcol = sb["gpair"][:, (j * 2 + h) * 2 + 1:(j * 2 + h) * 2 + 2]
            ro = self.carve("retst")[:, 0:D]
            p.emit("scalar", lambda e, o=ro[:, 0:512], w_=wcol: e.activation(
                out=o, in_=self.bank(4, 512), func=AF.Identity, scale=w_),
                waits=[(s_mm, mmd), (s_st, p.cur(s_st))])
            p.emit_i("scalar", lambda e, o=ro[:, 512:1024], w_=wcol: e.activation(
                out=o, in_=self.bank(5, 512), func=AF.Identity, scale=w_), inc_sem=s_ac)
            aco = p.cur(s_ac)
            dst = dram["ret_src"].rearrange("(n p) d -> n p d", p=128)[j * 2 + h]
            p.emit_i("scalar", lambda e, o=dst, s=ro: e.dma_start(out=o, in_=s),
                     waits=[(s_ac, aco)], inc_sem=s_st, inc_n=16, kind="dma")
        self._mk_dn = p.cur(s_mm)
        self._mk_gath = ac_buf
        return p.cur(s_st)


    def emit_shared(self, ve_hfT):
        p, sb = self.p, self.sb
        s_mm, s_ve, s_ac = self.s_mm, self.s_ve, self.s_ac
        ve_g = self.emit_dequant_matrix("gate", -1, self._mk_gu)
        ve_u = self.emit_dequant_matrix("up", -1, self._mk_gu)
        ve_d = self.emit_dequant_matrix("down", -1, self._mk_dn)
        hfT = sb["hfT"]
        hsh = self.carve("h_shT")
        last_silu = {0: p.cur(s_ac), 1: p.cur(s_ac)}
        last_hu = {0: p.cur(s_ve), 1: p.cur(s_ve)}
        for tch in range(8):
            for itile in range(6):
                par = itile % 2
                bg, bu = (2, 3) if par == 0 else (6, 7)
                grp = [(s_ve, max(ve_hfT, ve_g, ve_u, last_hu[par])), (s_ac, last_silu[par])]
                for mi, bk in ((0, bg), (1, bu)):
                    n = 0
                    for pl in range(2):
                        wsrc = (self.carve("w_gate_lo"), self.carve("w_gate_hi"), self.carve("w_up_lo"), self.carve("w_up_hi"))[mi * 2 + pl]
                        for kc in range(4):
                            lhs = wsrc[:, kc * I + itile * 128: kc * I + (itile + 1) * 128]
                            rhs = hfT[:, (pl * 4 + kc) * TPC + tch * 256: (pl * 4 + kc) * TPC + (tch + 1) * 256]
                            w = grp if n == 0 else []
                            p.emit_i("tensor", lambda e, l=lhs, r=rhs, st_=(n == 0), sp=(n == 7), b=bk:
                                     e.matmul(out=self.bank(b, 256), lhsT=l, rhs=r, start=st_, stop=sp),
                                     waits=w, inc_sem=(s_mm if n == 7 else None))
                            n += 1
                mmgu = p.cur(s_mm)
                sg_t = self.carve("hsil")[:, par * 256: par * 256 + 256]
                p.emit_i("scalar", lambda e, o=sg_t, b=bg: e.activation(out=o, in_=self.bank(b, 256), func=AF.Silu),
                         waits=[(s_mm, mmgu), (s_ve, last_hu[par])], inc_sem=s_ac)
                acs = p.cur(s_ac)
                last_silu[par] = acs
                p.emit_i("vector", lambda e, o=hsh[:, itile * 256:(itile + 1) * 256], a=sg_t, b=bu:
                         e.tensor_tensor(out=o, in0=a, in1=self.bank(b, 256), op=ALU.mult),
                         waits=[(s_ac, acs)], inc_sem=s_ve)
                last_hu[par] = p.cur(s_ve)
            ve_h = p.cur(s_ve)
            for tt4 in range(2):
                it = tch * 2 + tt4
                grp = [(s_ve, ve_h), (s_ve, p.cur(s_ve))]
                for nc_ in range(2):
                    n = 0
                    for pl in range(2):
                        wsrc = self.carve("w_down_lo") if pl == 0 else self.carve("w_down_hi")
                        for m in range(3):
                            lhs = hsh[:, (pl * 3 + m) * 256 + tt4 * 128: (pl * 3 + m) * 256 + (tt4 + 1) * 128]
                            rhs = wsrc[:, m * D + nc_ * 512: m * D + nc_ * 512 + 512]
                            w = (grp + [(s_ve, ve_d)]) if (n == 0 and nc_ == 0) else []
                            p.emit_i("tensor", lambda e, l=lhs, r=rhs, st_=(n == 0), sp=(n == 5), b=4 + nc_:
                                     e.matmul(out=self.bank(b, 512), lhsT=l, rhs=r, start=st_, stop=sp),
                                     waits=w, inc_sem=(s_mm if n == 5 else None))
                            n += 1
                mmd = p.cur(s_mm)
                for nc_ in range(2):
                    xs = sb["x"][:, it * D + nc_ * 512: it * D + nc_ * 512 + 512]
                    p.emit_i("vector", lambda e, o=xs, b=4 + nc_: e.scalar_tensor_tensor(
                        out=o, in0=self.bank(b, 512), scalar=0.5, in1=o, op0=ALU.mult, op1=ALU.add),
                        waits=([(s_mm, mmd)] if nc_ == 0 else []), inc_sem=s_ve)
        self._mk_gu = self._mk_dn = p.cur(s_mm)
        return p.cur(s_ve)

    def emit_combine(self, cc_ret, ve_mark=None, ac_mark=None, mm_mark=None):
        p, sb, dram = self.p, self.sb, self.dram
        s_ve, s_po, s_pc = self.s_ve, self.s_po, self.s_pc
        if ve_mark is None:
            ve_mark = p.cur(s_ve)
        if ac_mark is None:
            ac_mark = p.cur(self.s_ac)
        if mm_mark is None:
            mm_mark = p.cur(self.s_mm)
        ar = self.sb["arena"][0:128, :].bitcast(BF16)
        bufs = [self.carve("gath_f32")[:].bitcast(BF16),
                ar[:, 43008 // 2: 43008 // 2 + 4096]]
        last_add = {0: ve_mark, 1: ve_mark}
        first = True
        for it in range(NT):
            par = it % 2
            g4 = bufs[par]
            wvz = [(s_ve, last_add[par])]
            if first:
                wvz += [(self.s_ac, ac_mark), (s_po, p.cur(s_po)), (self.s_mm, mm_mark)]
                first = False
            pz = p.emit_i("gpsimd", lambda e, g=g4: e.memset(g[:, 0:4096], 0.0),
                          waits=wvz, inc_sem=s_pc, kind="dma")
            for r in range(K4):
                col = it * K4 + r
                idxap = sb["sidx"][:, col:col + 1]
                gb = g4[:, r * D:(r + 1) * D]
                wv = [(s_pc, pz), (self.s_cc, cc_ret)]
                p.emit_i("gpsimd", lambda e, ia=idxap, o=gb: e.indirect_dma_start(
                    out=o, out_offset=None, in_=dram["ret_all"],
                    in_offset=IndirectOffsetOnAxis(ia, 0),
                    bounds_check=self._bc["r"], oob_is_err=False),
                    waits=wv, inc_sem=s_po, inc_n=16, kind="dma")
            pog = p.cur(s_po)
            for r in range(K4):
                xs = sb["x"][:, it * D:(it + 1) * D]
                p.emit_i("vector", lambda e, o=xs, a=g4[:, r * D:(r + 1) * D]: e.tensor_tensor(
                    out=o, in0=o, in1=a, op=ALU.add),
                    waits=([(s_po, pog)] if r == 0 else []), inc_sem=s_ve)
            last_add[par] = p.cur(s_ve)
        return p.cur(s_ve)


    def emit_transpose_add(self, srcap, add_into, first_waits=()):
        p = self.p
        bidx = self._tb_n % 2
        pst = self.bankbf(bidx, 1024)[:, 0:srcap.shape[0]]
        wv = list(first_waits)
        if self._tb_evac[bidx] is not None:
            wv.append((self.s_ve, self._tb_evac[bidx]))
        p.emit_i("tensor", lambda e, o=pst, s=srcap: e.transpose(out=o, in_=s, identity=self.ident),
                 waits=wv, inc_sem=self.s_mm)
        mmv = p.cur(self.s_mm)
        tgt = p.emit_i("vector", lambda e, s=pst, o=add_into: e.tensor_tensor(out=o, in0=o, in1=s, op=ALU.add),
                       waits=[(self.s_mm, mmv)], inc_sem=self.s_ve)
        self._tb_evac[bidx] = tgt
        self._tb_n += 1
        return tgt

    def _w8(self, i):
        ap = self.sb["arena"][0:128, :].bitcast(BF16)
        if i < 2:
            off = self._ARENA_ATTN["wkb"][0]
            return ap[:, off // 2 + i * 1024: off // 2 + (i + 1) * 1024]
        if i < 4:
            off = self._ARENA_ATTN["rt_buf"][0]
            return ap[:, off // 2 + (i - 2) * 1024: off // 2 + (i - 1) * 1024]
        if i < 6:
            off = self._ARENA_ATTN["kh_buf"][0]
            return ap[:, off // 2 + (i - 4) * 1024: off // 2 + (i - 3) * 1024]
        off = self._ARENA_ATTN["khat_tm"][0]
        return ap[:, off // 2 + (i - 6) * 1024: off // 2 + (i - 5) * 1024]

    def emit_load_w8(self, wname, fold_tf):
        p, dram = self.p, self.dram
        wv = [(self.s_mm, p.cur(self.s_mm)), (self.s_ve, p.cur(self.s_ve)), (self.s_ac, p.cur(self.s_ac))]
        for kc in range(8):
            dst = self._w8(kc)
            p.emit_i("gpsimd", lambda e, o=dst, s=dram[wname][kc * 128:(kc + 1) * 128, :]:
                     e.dma_start(out=o, in_=s), waits=(wv if kc == 0 else []),
                     inc_sem=self.s_po, inc_n=16, kind="dma")
        pot = p.cur(self.s_po)
        if fold_tf:
            for kc in range(8):
                dst = self._w8(kc)
                p.emit_i("vector", lambda e, o=dst: e.tensor_tensor(out=o, in0=o, in1=self.carve("tfb"), op=ALU.mult),
                         waits=([(self.s_po, pot)] if kc == 0 else []), inc_sem=self.s_ve)
            return ("ve", p.cur(self.s_ve))
        return ("po", pot)

    def emit_attn_setup(self):
        p, sb, dram = self.p, self.sb, self.dram
        s_ve, s_ac, s_in, s_po = self.s_ve, self.s_ac, self.s_in, self.s_po
        cv = self.carve
        p.emit_i("gpsimd", lambda e: e.dma_start(out=cv("tfb"), in_=dram["tfirst"].broadcast_to([128, D])),
                 waits=[(s_ve, p.cur(s_ve)), (s_ac, p.cur(s_ac)), (self.s_mm, p.cur(self.s_mm))],
                 inc_sem=s_po, inc_n=16, kind="dma")
        lnw = cv("wCt")[:, 0:16]
        wC = cv("wCt")[:, 16:32]
        td = cv("wCt")[:, 32:48]
        wv0 = [(s_ve, p.cur(s_ve)), (s_ac, p.cur(s_ac)), (self.s_mm, p.cur(self.s_mm)),
               (self.s_st, p.cur(self.s_st))]
        p.emit_i("sync", lambda e: e.dma_start(out=td, in_=dram["tdecay64"]), waits=wv0, inc_sem=s_in, inc_n=16)
        ldt = p.cur(s_in)
        p.emit_i("scalar", lambda e: e.activation(out=lnw, in_=td, func=AF.Ln),
                 waits=[(s_in, ldt)], inc_sem=s_ac)
        ramp1 = cv("rt_buf").bitcast(F32)[:, 0:128]
        rampr = cv("rt_buf").bitcast(F32)[:, 128:256]
        p.emit_i("sync", lambda e: e.dma_start(out=ramp1, in_=dram["iota1"].broadcast_to([64, 128])), inc_sem=s_in, inc_n=16)
        p.emit_i("sync", lambda e: e.dma_start(out=rampr, in_=dram["iotar"].broadcast_to([64, 128])), inc_sem=s_in, inc_n=16)
        ldr = p.cur(s_in)
        scr = cv("kh_buf").bitcast(F32)[:, 0:128]
        acl = p.cur(s_ac)
        for h in range(H):
            p.emit("vector", lambda e, hh=h: e.tensor_scalar(
                out=scr, in0=ramp1, scalar1=lnw[:, hh:hh + 1], scalar2=None, op0=ALU.mult),
                waits=([(s_in, ldr), (s_ac, acl)] if h == 0 else [(s_ac, p.cur(s_ac))]))
            p.emit_i("vector", lambda e: e.nop(), inc_sem=s_ve)
            vv = p.cur(s_ve)
            p.emit_i("scalar", lambda e, hh=h: e.activation(out=cv("W1p")[:, hh * 128:(hh + 1) * 128], in_=scr, func=AF.Exp),
                     waits=[(s_ve, vv)], inc_sem=s_ac)
            p.emit("vector", lambda e, hh=h: e.tensor_scalar(
                out=scr, in0=rampr, scalar1=lnw[:, hh:hh + 1], scalar2=None, op0=ALU.mult),
                waits=[(s_ac, p.cur(s_ac))])
            p.emit_i("vector", lambda e: e.nop(), inc_sem=s_ve)
            vv = p.cur(s_ve)
            p.emit_i("scalar", lambda e, hh=h: e.activation(out=cv("W2p")[:, hh * 128:(hh + 1) * 128], in_=scr, func=AF.Exp),
                     waits=[(s_ve, vv)], inc_sem=s_ac)
        p.emit_i("vector", lambda e: e.tensor_scalar(out=cv("wCt")[:, 48:64], in0=lnw, scalar1=128.0, scalar2=None, op0=ALU.mult),
                 waits=[(s_ac, p.cur(s_ac))], inc_sem=s_ve)
        p.emit_i("scalar", lambda e, w=wC: e.activation(out=w, in_=cv("wCt")[:, 48:64], func=AF.Exp),
                 waits=[(s_ve, p.cur(s_ve))], inc_sem=s_ac)
        p.emit_i("sync", lambda e: e.dma_start(out=cv("lngb")[:, 0:16], in_=dram["decay_h"].broadcast_to([128, 16])), inc_sem=s_in, inc_n=16)
        ldg = p.cur(s_in)
        p.emit_i("scalar", lambda e: e.activation(out=cv("lngb")[:, 0:16], in_=cv("lngb")[:, 0:16], func=AF.Ln),
                 waits=[(s_in, ldg)], inc_sem=s_ac)
        acg = p.cur(s_ac)
        ioij = cv("wkb").bitcast(F32)[:, 0:128]
        p.emit_i("sync", lambda e: e.dma_start(out=ioij, in_=dram["iota_ij"]), inc_sem=s_in, inc_n=16)
        trit = cv("zmask").bitcast(U8)[:, 0:128]
        p.emit_i("sync", lambda e: e.dma_start(out=trit, in_=dram["tri_ij_u8"]), inc_sem=s_in, inc_n=16)
        ldij = p.cur(s_in)
        zt = cv("Am")[:, 0:128]
        p.emit("vector", lambda e: e.memset(zt, 0.0), waits=[(s_po, p.cur(s_po))])
        escr = cv("wkb").bitcast(F32)[:, 128:256]
        escb = cv("wkb")[:, 512:640]
        for h in range(H):
            p.emit("vector", lambda e, hh=h: e.tensor_scalar(
                out=escr, in0=ioij, scalar1=cv("lngb")[:, hh:hh + 1], scalar2=None, op0=ALU.mult),
                waits=([(s_in, ldij), (s_ac, acg)] if h == 0 else []))
            p.emit("vector", lambda e: e.tensor_scalar(
                out=escr, in0=escr, scalar1=0.0, scalar2=None, op0=ALU.min))
            p.emit_i("vector", lambda e: e.nop(), inc_sem=s_ve)
            vv = p.cur(s_ve)
            p.emit_i("scalar", lambda e: e.activation(out=escb, in_=escr, func=AF.Exp),
                     waits=[(s_ve, vv)], inc_sem=s_ac)
            ace = p.cur(s_ac)
            p.emit_i("vector", lambda e, hh=h: e.select(
                out=cv("Dmask")[:, hh * 128:(hh + 1) * 128], mask=trit, on_true=escb, on_false=zt),
                waits=[(s_ac, ace)], inc_sem=s_ve)
        p.emit("vector", lambda e: e.memset(cv("Sst")[:], 0.0))
        p.emit_i("vector", lambda e: e.memset(cv("Sbf")[:], 0.0), inc_sem=s_ve)
        return p.cur(s_ve)

    def emit_attn_quarter(self, q):
        p, sb = self.p, self.sb
        s_mm, s_ve, s_ac, s_po = self.s_mm, self.s_ve, self.s_ac, self.s_po
        cv = self.carve
        hfT = sb["hfT"]
        qoff = q * 512
        for wname, dstbuf, fold in (("wk_T", "kT", True), ("wr_T", "rT", False)):
            t = self.emit_load_w8(wname, fold)
            tw = [(self.s_ve, t[1])] if t[0] == "ve" else [(self.s_po, t[1])]
            for dks in range(8):
                for kc in range(8):
                    lhs = self._w8(kc)[:, dks * 128:(dks + 1) * 128]
                    rhs = hfT[:, kc * TPC + qoff: kc * TPC + qoff + 512]
                    wv = (list(tw) + [(s_ve, p.cur(s_ve)), (s_ac, p.cur(s_ac))]) if kc == 0 else []
                    p.emit_i("tensor", lambda e, l=lhs, r=rhs, st_=(kc == 0), sp=(kc == 7):
                             e.matmul(out=self.bank(2, 512), lhsT=l, rhs=r, start=st_, stop=sp),
                             waits=wv, inc_sem=(s_mm if kc == 7 else None))
                mmk = p.cur(s_mm)
                for par in range(2):
                    h = dks * 2 + par
                    dst = cv(dstbuf)[:, h * 512:(h + 1) * 512]
                    if dstbuf == "rT":
                        p.emit_i("scalar", lambda e, o=dst, pp=par: e.activation(
                            out=o, in_=self.bank(2, 512)[pp * 64:(pp + 1) * 64, :], func=AF.Sigmoid),
                            waits=[(s_mm, mmk)], inc_sem=s_ac)
                    else:
                        p.emit_i("vector", lambda e, o=dst, pp=par: e.tensor_copy(
                            out=o, in_=self.bank(2, 512)[pp * 64:(pp + 1) * 64, :]),
                            waits=[(s_mm, mmk)], inc_sem=s_ve)
        t = self.emit_load_w8("wv_T", False)
        tw = [(self.s_po, t[1])]
        for tt in range(4):
            for dvh in range(2):
                for kc in range(8):
                    lhs = hfT[:, kc * TPC + qoff + tt * 128: kc * TPC + qoff + (tt + 1) * 128]
                    rhs = self._w8(kc)[:, dvh * 512:(dvh + 1) * 512]
                    wv = (list(tw) + [(s_ve, p.cur(s_ve))]) if kc == 0 else []
                    p.emit_i("tensor", lambda e, l=lhs, r=rhs, st_=(kc == 0), sp=(kc == 7), b=4 + dvh:
                             e.matmul(out=self.bank(b, 512), lhsT=l, rhs=r, start=st_, stop=sp),
                             waits=wv, inc_sem=(s_mm if kc == 7 else None))
            mmv = p.cur(s_mm)
            for dvh in range(2):
                dst = cv("v_tm")[:, tt * 1024 + dvh * 512: tt * 1024 + (dvh + 1) * 512]
                p.emit_i("vector", lambda e, o=dst, b=4 + dvh: e.tensor_copy(out=o, in_=self.bank(b, 512)),
                         waits=[(s_mm, mmv)], inc_sem=s_ve)
        ve_proj = p.cur(s_ve)
        import os
        if q == 0 and os.environ.get("ATTNDBG"):
            self._dbg_dump([("dbg_v2", "v_tm")])
        for n in range(4):
            rview = cv("rT").rearrange("p (h t) -> p h t", h=H)[:, :, n * 128:(n + 1) * 128]
            kview = cv("kT").rearrange("p (h t) -> p h t", h=H)[:, :, n * 128:(n + 1) * 128]
            wv0 = [(s_ve, ve_proj), (s_mm, p.cur(s_mm)), (s_ac, p.cur(s_ac))]
            p.emit("vector", lambda e, i0=rview: e.tensor_tensor(
                out=cv("rt_buf").rearrange("p (h t) -> p h t", h=H), in0=i0,
                in1=cv("W1p").rearrange("p (h t) -> p h t", h=H), op=ALU.mult),
                waits=wv0)
            p.emit_i("vector", lambda e, i0=kview: e.tensor_tensor(
                out=cv("kh_buf").rearrange("p (h t) -> p h t", h=H), in0=i0,
                in1=cv("W2p").rearrange("p (h t) -> p h t", h=H), op=ALU.mult), inc_sem=s_ve)
            ve_sc = p.cur(s_ve)
            if n == 0:
                ac_proj = p.cur(s_ac)
                last_msb = {0: 0, 1: 0}
                last_yev = {0: 0, 1: 0}
                sbf_mark = 0
            for hp in range(H // 2):
                pair = (2 * hp, 2 * hp + 1)
                ba = {pair[0]: 3, pair[1]: 4}
                by = {pair[0]: 6, pair[1]: 5}
                ao = {pair[0]: 0, pair[1]: 128}
                mmA, ve_am, ve_kht, mmM, mmy = {}, {}, {}, {}, {}
                for h in pair:
                    ksx = cv("kT")[:, h * 512 + n * 128: h * 512 + (n + 1) * 128]
                    rsx = cv("rT")[:, h * 512 + n * 128: h * 512 + (n + 1) * 128]
                    if h % 2 == 0:
                        wvA = [(s_ve, max(ve_proj, last_msb[0])), (s_ac, ac_proj)]
                    else:
                        wvA = [(s_ve, ve_proj), (s_ac, max(ac_proj, last_msb[1]))]
                    p.emit_i("tensor", lambda e, l=ksx, r=rsx, b=ba[h]: e.matmul(
                        out=self.bank(b, 128), lhsT=l, rhs=r, start=True, stop=True,
                        skip_group_check=True),
                        waits=wvA, inc_sem=s_mm)
                    mmA[h] = p.cur(s_mm)
                for h in pair:
                    p.emit_i("vector", lambda e, hh=h, b=ba[h], o=ao[h]: e.tensor_tensor(
                        out=cv("Am")[:, o:o + 128], in0=self.bank(b, 128),
                        in1=cv("Dmask")[:, hh * 128:(hh + 1) * 128], op=ALU.mult),
                        waits=[(s_mm, mmA[h])], inc_sem=s_ve)
                    ve_am[h] = p.cur(s_ve)
                for h in pair:
                    khs = cv("kh_buf")[:, h * 128:(h + 1) * 128]
                    kht = cv("khat_tm")[:, h * 64:(h + 1) * 64]
                    self.emit_transpose(khs, kht, [(s_ve, ve_sc)])
                    ve_kht[h] = p.cur(s_ve)
                for h in pair:
                    kht = cv("khat_tm")[:, h * 64:(h + 1) * 64]
                    vsx = cv("v_tm")[:, n * 1024 + h * 64: n * 1024 + (h + 1) * 64]
                    p.emit_i("tensor", lambda e, l=kht, r=vsx, b=ba[h]: e.matmul(
                        out=self.bank(b, 64)[0:64, :], lhsT=l, rhs=r, start=True, stop=True,
                        skip_group_check=True),
                        waits=[(s_ve, ve_kht[h]), (s_ve, ve_am[h])], inc_sem=s_mm)
                    mmM[h] = p.cur(s_mm)
                for h in pair:
                    if h % 2 == 0:
                        p.emit_i("vector", lambda e, hh=h, b=ba[h]: e.tensor_copy(
                            out=cv("Msb")[:, hh * 64:(hh + 1) * 64], in_=self.bank(b, 64)[0:64, :]),
                            waits=[(s_mm, mmM[h])], inc_sem=s_ve)
                        last_msb[0] = p.cur(s_ve)
                    else:
                        p.emit_i("scalar", lambda e, hh=h, b=ba[h]: e.copy(
                            out=cv("Msb")[:, hh * 64:(hh + 1) * 64], in_=self.bank(b, 64)[0:64, :]),
                            waits=[(s_mm, mmM[h])], inc_sem=s_ac)
                        last_msb[1] = p.cur(s_ac)
                for h in pair:
                    wy = [(s_ve, max(last_yev[h % 2], ve_am[h], sbf_mark, ve_sc))]
                    p.emit("tensor", lambda e, hh=h, b=by[h]: e.matmul(
                        out=self.bank(b, 128)[0:64, :], lhsT=cv("Sbf")[:, hh * 64:(hh + 1) * 64],
                        rhs=cv("rt_buf")[:, hh * 128:(hh + 1) * 128], start=True, stop=False,
                        skip_group_check=True),
                        waits=wy)
                    vsx = cv("v_tm")[:, n * 1024 + h * 64: n * 1024 + (h + 1) * 64]
                    p.emit_i("tensor", lambda e, l=vsx, b=by[h], o=ao[h]: e.matmul(
                        out=self.bank(b, 128)[0:64, :], lhsT=l, rhs=cv("Am")[:, o:o + 128],
                        start=False, stop=True, skip_group_check=True),
                        waits=[], inc_sem=s_mm)
                    mmy[h] = p.cur(s_mm)
                for h in pair:
                    yd = (cv("yT")[0:64, :] if h % 2 == 0 else cv("yT", hi=True))
                    ydst = yd[:, (h // 2) * 512 + n * 128: (h // 2) * 512 + (n + 1) * 128]
                    p.emit_i("vector", lambda e, o=ydst, b=by[h]: e.tensor_copy(out=o, in_=self.bank(b, 128)[0:64, :]),
                             waits=[(s_mm, mmy[h])], inc_sem=s_ve)
                    last_yev[h % 2] = p.cur(s_ve)
            wvs = [(s_mm, p.cur(s_mm)), (s_ac, p.cur(s_ac))]
            p.emit("vector", lambda e: e.tensor_tensor(
                out=cv("Sst").rearrange("p (h e) -> p h e", h=H),
                in0=cv("Sst").rearrange("p (h e) -> p h e", h=H),
                in1=cv("wCt")[:, 16:32].unsqueeze(2).broadcast_to([64, 16, 64]),
                op=ALU.mult), waits=wvs)
            p.emit("vector", lambda e: e.tensor_tensor(
                out=cv("Sst")[:], in0=cv("Sst")[:], in1=cv("Msb")[:], op=ALU.add))
            p.emit_i("vector", lambda e: e.tensor_copy(out=cv("Sbf")[:], in_=cv("Sst")[:]), inc_sem=s_ve)
            sbf_mark = p.cur(s_ve)
        ve_y = p.cur(s_ve)
        t = self.emit_load_w8("wo_T", False)
        tw = [(self.s_po, t[1])]
        for dmt in range(8):
            for hp in range(8):
                lhs = self._w8(hp)[:, dmt * 128:(dmt + 1) * 128]
                rhs = cv("yT")[:, hp * 512:(hp + 1) * 512]
                wv = (list(tw) + [(s_ve, max(ve_y, p.cur(s_ve)))]) if hp == 0 else []
                p.emit_i("tensor", lambda e, l=lhs, r=rhs, st_=(hp == 0), sp=(hp == 7):
                         e.matmul(out=self.bank(7, 512), lhsT=l, rhs=r, start=st_, stop=sp,
                                  skip_group_check=True),
                         waits=wv, inc_sem=(s_mm if hp == 7 else None))
            mmo = p.cur(s_mm)
            atb = cv("at_b")[:, 0:512]
            p.emit_i("vector", lambda e, o=atb: e.tensor_copy(out=o, in_=self.bank(7, 512)),
                     waits=[(s_mm, mmo)], inc_sem=s_ve)
            ve_atb = p.cur(s_ve)
            for ts4 in range(4):
                it = qoff // 128 + ts4
                tx = sb["x"][:, it * D + dmt * 128: it * D + (dmt + 1) * 128]
                self.emit_transpose_add(atb[:, ts4 * 128:(ts4 + 1) * 128], tx,
                                        first_waits=([(s_ve, ve_atb)] if ts4 == 0 else []))

    def emit_attn_tail(self):
        p, sb, dram = self.p, self.sb, self.dram
        s_mm, s_ve, s_ac, s_po, s_st, s_cc = self.s_mm, self.s_ve, self.s_ac, self.s_po, self.s_st, self.s_cc
        cv = self.carve
        hfT = sb["hfT"]
        ve_now = p.cur(s_ve)
        p.emit_i("sync", lambda e: e.dma_start(out=dram["st_src"], in_=cv("Sst")[:]),
                 waits=[(s_ve, ve_now)], inc_sem=s_st, inc_n=16)
        stt = p.cur(s_st)
        cc = p.emit_i("gpsimd", lambda e: e.collective_compute(
            "AllGather", ALU.bypass, replica_groups=[list(range(NCORES))],
            ins=[dram["st_src"].opt()], outs=[dram["st_all"].opt()]),
            waits=[(s_st, stt)], inc_sem=s_cc, kind="dma")
        p.emit_i("vector", lambda e: e.memset(cv("Msb")[:], 0.0), inc_sem=s_ve)
        vz = p.cur(s_ve)
        p.emit_i("gpsimd", lambda e: e.indirect_dma_start(
            out=cv("Msb")[0:64, :], out_offset=None, in_=dram["st_all"],
            in_offset=IndirectOffsetOnAxis(sb["prow_sb"][0:64, 0:1], 0),
            bounds_check=self._bc["r"], oob_is_err=False),
            waits=[(s_cc, cc), (s_ve, vz)], inc_sem=s_po, inc_n=16, kind="dma")
        pog = p.cur(s_po)
        t = self.emit_load_w8("wr_T", False)
        tw = [(self.s_po, max(t[1], pog))]
        for dks in range(8):
            for kc in range(8):
                lhs = self._w8(kc)[:, dks * 128:(dks + 1) * 128]
                rhs = hfT[:, kc * TPC: kc * TPC + 384]
                wv = (list(tw) + [(s_ve, p.cur(s_ve)), (s_ac, p.cur(s_ac))]) if kc == 0 else []
                p.emit_i("tensor", lambda e, l=lhs, r=rhs, st_=(kc == 0), sp=(kc == 7):
                         e.matmul(out=self.bank(2, 384), lhsT=l, rhs=r, start=st_, stop=sp,
                                  skip_group_check=True),
                         waits=wv, inc_sem=(s_mm if kc == 7 else None))
            mmk = p.cur(s_mm)
            for par in range(2):
                h = dks * 2 + par
                dst = cv("rT")[:, h * 512: h * 512 + 384]
                p.emit_i("scalar", lambda e, o=dst, pp=par: e.activation(
                    out=o, in_=self.bank(2, 384)[pp * 64:(pp + 1) * 64, :], func=AF.Sigmoid),
                    waits=[(s_mm, mmk)], inc_sem=s_ac)
        p.emit_i("vector", lambda e: e.nop(), waits=[(self.s_ac, p.cur(self.s_ac))], inc_sem=s_ve)
        ve_r = p.cur(s_ve)
        p.emit_i("vector", lambda e: e.tensor_copy(out=cv("Sbf")[:], in_=cv("Msb")[:]),
                 waits=[(s_ve, ve_r)], inc_sem=s_ve)
        for n in range(3):
            if n > 0:
                p.emit("vector", lambda e: e.tensor_tensor(
                    out=cv("Msb").rearrange("p (h e) -> p h e", h=H),
                    in0=cv("Msb").rearrange("p (h e) -> p h e", h=H),
                    in1=cv("wCt")[:, 16:32].unsqueeze(2).broadcast_to([64, 16, 64]),
                    op=ALU.mult))
                p.emit_i("vector", lambda e: e.tensor_copy(out=cv("Sbf")[:], in_=cv("Msb")[:]), inc_sem=s_ve)
            rview = cv("rT").rearrange("p (h t) -> p h t", h=H)[:, :, n * 128:(n + 1) * 128]
            p.emit_i("vector", lambda e, i0=rview: e.tensor_tensor(
                out=cv("rt_buf").rearrange("p (h t) -> p h t", h=H), in0=i0,
                in1=cv("W1p").rearrange("p (h t) -> p h t", h=H), op=ALU.mult),
                waits=[(s_ve, p.cur(s_ve)), (s_mm, p.cur(s_mm)), (s_ac, p.cur(s_ac))], inc_sem=s_ve)
            ve_rt = p.cur(s_ve)
            for h in range(H):
                p.emit_i("tensor", lambda e, hh=h: e.matmul(
                    out=self.bank(6, 128)[0:64, :], lhsT=cv("Sbf")[:, hh * 64:(hh + 1) * 64],
                    rhs=cv("rt_buf")[:, hh * 128:(hh + 1) * 128], start=True, stop=True,
                    skip_group_check=True),
                    waits=[(s_ve, max(ve_rt, p.cur(s_ve)))], inc_sem=s_mm)
                mmy = p.cur(s_mm)
                yd = (cv("yT")[0:64, :] if h % 2 == 0 else cv("yT", hi=True))
                ydst = yd[:, (h // 2) * 512 + n * 128: (h // 2) * 512 + (n + 1) * 128]
                p.emit_i("vector", lambda e, o=ydst: e.tensor_copy(out=o, in_=self.bank(6, 128)[0:64, :]),
                         waits=[(s_mm, mmy)], inc_sem=s_ve)
        ve_dy = p.cur(s_ve)
        t = self.emit_load_w8("wo_T", False)
        tw = [(self.s_po, t[1])]
        for dmt in range(8):
            for hp in range(8):
                lhs = self._w8(hp)[:, dmt * 128:(dmt + 1) * 128]
                rhs = cv("yT")[:, hp * 512: hp * 512 + 384]
                wv = (list(tw) + [(s_ve, max(ve_dy, p.cur(s_ve)))]) if hp == 0 else []
                p.emit_i("tensor", lambda e, l=lhs, r=rhs, st_=(hp == 0), sp=(hp == 7):
                         e.matmul(out=self.bank(7, 384), lhsT=l, rhs=r, start=st_, stop=sp,
                                  skip_group_check=True),
                         waits=wv, inc_sem=(s_mm if hp == 7 else None))
            mmo = p.cur(s_mm)
            atb = cv("at_b")[:, 0:384]
            p.emit_i("vector", lambda e, o=atb: e.tensor_copy(out=o, in_=self.bank(7, 384)),
                     waits=[(s_mm, mmo)], inc_sem=s_ve)
            ve_atb = p.cur(s_ve)
            for ts4 in range(3):
                tx = sb["x"][:, ts4 * D + dmt * 128: ts4 * D + (dmt + 1) * 128]
                self.emit_transpose_add(atb[:, ts4 * 128:(ts4 + 1) * 128], tx,
                                        first_waits=([(s_ve, ve_atb)] if ts4 == 0 else []))
        return p.cur(s_ve)

    def _dbg_dump(self, names):
        import os
        if not os.environ.get("ATTNDBG"):
            return
        p, cv, dram = self.p, self.carve, self.dram
        wv = [(self.s_ve, p.cur(self.s_ve)), (self.s_ac, p.cur(self.s_ac)), (self.s_mm, p.cur(self.s_mm))]
        cvt = self.sb["lgtile"] if False else None
        for dname, srcname in names:
            srcap = cv(srcname) if isinstance(srcname, str) else srcname
            p.emit_i("gpsimd", lambda e, t=dram[dname], s=srcap: e.dma_start(out=t, in_=s),
                     waits=wv, inc_sem=self.s_po, inc_n=16, kind="dma")
        self._dbg_done = p.cur(self.s_po)

    def emit_attn(self):
        import os
        self.p.phase = "attn_setup"
        self.emit_attn_setup()
        if os.environ.get("ATTNDBG"):
            self._dbg_dump([("dbg_W1", "W1p"), ("dbg_Dm", "Dmask")])
            self._dbg_dump([("dbg_hf2", self.sb["hf"][:])])
        for q in range(4):
            self.p.phase = "attn_q%d" % q
            self.emit_attn_quarter(q)
            if q == 0 and os.environ.get("ATTNDBG"):
                self._dbg_dump([("dbg_kT", "kT"), ("dbg_rT", "rT"), ("dbg_v", "v_tm"),
                                ("dbg_yT", "yT"), ("dbg_S", "Sst"),
                                ("dbg_x4", self.sb["x"][:, 0:4 * D]),
                                ("dbg_W2", "W2p"), ("dbg_kh", "kh_buf"),
                                ("dbg_kht", "khat_tm"), ("dbg_Ms", "Msb")])
        self.p.phase = "attn_tail"
        return self.emit_attn_tail()


def _np_inputs_prep(inputs, core):
    x = np.ascontiguousarray(inputs["L0_attn_x"].reshape(T, D))
    out = {}
    out["x0_full"] = x
    out["x0_shard"] = np.ascontiguousarray(x[core * TPC:(core + 1) * TPC])
    kcq = np.arange(D)
    permd = 256 * ((kcq // 128) % 4) + 2 * (kcq % 128) + (kcq // 128) // 4
    out["gw_T"] = np.ascontiguousarray(inputs["L0_ffn_gate_w"].T[permd])
    out["wk_T"] = np.ascontiguousarray(inputs["L0_attn_Wk"].T[permd])
    out["wv_T"] = np.ascontiguousarray(inputs["L0_attn_Wv"].T[permd])
    out["wr_T"] = np.ascontiguousarray(inputs["L0_attn_Wr"].T[permd])
    out["wo_T"] = np.ascontiguousarray(inputs["L0_attn_Wo"].T)
    out["tfirst"] = np.ascontiguousarray(inputs["L0_attn_time_first"].reshape(1, D))

    def pack16(a):
        return np.ascontiguousarray(a.view(np.int16)[..., ::2])

    es = slice(core * EPC, (core + 1) * EPC)
    permi = np.concatenate([np.arange(0, I, 2), np.arange(1, I, 2)])
    out["gate_pT"] = np.ascontiguousarray(pack16(inputs["L0_ffn_gate_packed"][es]).transpose(0, 2, 1)[:, :, permi])
    out["up_pT"] = np.ascontiguousarray(pack16(inputs["L0_ffn_up_packed"][es]).transpose(0, 2, 1)[:, :, permi])
    out["down_pT"] = np.ascontiguousarray(pack16(inputs["L0_ffn_down_packed"][es]).transpose(0, 2, 1))
    out["sh_gate_pT"] = np.ascontiguousarray(pack16(inputs["L0_ffn_sh_gate_packed"]).T[:, permi])
    out["sh_up_pT"] = np.ascontiguousarray(pack16(inputs["L0_ffn_sh_up_packed"]).T[:, permi])
    out["sh_down_pT"] = np.ascontiguousarray(pack16(inputs["L0_ffn_sh_down_packed"]).T)
    out["gate_sT"] = np.ascontiguousarray(inputs["L0_ffn_gate_scales"][es].transpose(0, 2, 1)[:, :, permi])
    out["up_sT"] = np.ascontiguousarray(inputs["L0_ffn_up_scales"][es].transpose(0, 2, 1)[:, :, permi])
    out["down_sT"] = np.ascontiguousarray(inputs["L0_ffn_down_scales"][es].transpose(0, 2, 1))
    out["sh_gate_sT"] = np.ascontiguousarray(inputs["L0_ffn_sh_gate_scales"].T[:, permi])
    out["sh_up_sT"] = np.ascontiguousarray(inputs["L0_ffn_sh_up_scales"].T[:, permi])
    out["sh_down_sT"] = np.ascontiguousarray(inputs["L0_ffn_sh_down_scales"].T)

    out["ident"] = np.eye(128, dtype=BF16NP)
    out["identf"] = np.eye(128, dtype=np.float32)
    out["tri_excl"] = np.tril(np.ones((128, 128), BF16NP), -1)
    out["tri_excl"] = np.triu(np.ones((128, 128), np.float32), 1).astype(BF16NP)
    out["allones"] = np.ones((128, 128), BF16NP)
    out["onescol"] = np.ones((1, 128), BF16NP)
    out["iota_e"] = np.broadcast_to(np.arange(E, dtype=np.float32), (128, E)).copy()
    out["iota_ij"] = (np.arange(128, dtype=np.float32)[None, :] - np.arange(128, dtype=np.float32)[:, None])
    out["tri_ij_u8"] = (np.arange(128)[None, :] >= np.arange(128)[:, None]).astype(np.uint8)
    out["tokid0"] = (core * TPC + np.arange(128, dtype=np.float32)).reshape(128, 1)
    out["coreid"] = np.full((1, 8), core, np.int32)
    out["prevmask"] = (np.arange(8) < core).astype(np.float32).reshape(1, 8)
    out["tdecay64"] = np.ascontiguousarray(inputs["L0_attn_time_decay"].astype(np.float32).T)
    out["decay_h"] = np.ascontiguousarray(inputs["L0_attn_time_decay"][:, 0].astype(np.float32).reshape(1, 16))
    out["iota1"] = np.arange(1, 129, dtype=np.float32).reshape(1, 128)
    out["iotar"] = np.arange(127, -1, -1, dtype=np.float32).reshape(1, 128)
    if core % 2 == 1:
        pr = (core - 1) * 64 + np.arange(128)
        pr[64:] = BIGIDX
    else:
        pr = np.full(128, BIGIDX)
    out["partnerrow"] = pr.astype(np.int32).reshape(128, 1)
    out["expbase"] = (core * 512 + np.arange(128, dtype=np.float32)).reshape(128, 1)
    return out


def build_ffn1front(stop="full"):
    b = Builder("ffn1front")
    p = b.p
    for n in ("s_in", "s_wt", "s_wt2", "s_mm", "s_ve", "s_ac", "s_po", "s_st", "s_cc", "s_pc"):
        setattr(b, n, p.sem(n))
    p.auto["vector"] = (p.sem("s_auto_ve"), [0], [None])
    p.auto["scalar"] = (p.sem("s_auto_ac"), [0], [None])
    b._tb_n = 0
    b._tb_evac = [None, None]
    b.declare_io()
    b.alloc_common()
    blk = b.ctx.enter_context(b.nc.Block())
    b.load_common()
    b.ve_hfT = b.emit_ln_stream(lambda it: b.sb["x"][:, it * D:(it + 1) * D], 0,
                                [(b.s_in, b.x_loaded)], keep_hf_dbg=True)
    stages = ["ln", "route", "cntx", "scatter", "full"]
    sidx_ = stages.index(stop)
    if sidx_ >= 1:
        import os
        st_cnt = b.emit_routing(int(os.environ.get("RSUB", "99")))
    if sidx_ >= 2:
        ve_bown = b.emit_counts_exchange(st_cnt)
    if sidx_ >= 3:
        cc_table = b.emit_gpos_scatter(ve_bown)

    sb, dram = b.sb, b.dram
    s_st = b.s_st

    def outt(dst, src_fn, waits):
        for it in range(NT):
            tgt = dram[dst].rearrange("(n p) k -> n p k", p=128)[it]
            p.emit_i("sync", lambda e, t=tgt, s=src_fn(it): e.dma_start(out=t, in_=s),
                     waits=(waits if it == 0 else []), inc_sem=s_st, inc_n=16)
    ve_all = p.cur(b.s_ve)
    ac_all = p.cur(b.s_ac)
    w0 = [(b.s_ve, ve_all), (b.s_ac, ac_all)]
    p.emit_i("sync", lambda e: e.dma_start(out=dram["dbg_stats"], in_=sb["stats"][:]),
             waits=w0, inc_sem=s_st, inc_n=16)
    import os as _os
    if sidx_ >= 1 and int(_os.environ.get("RSUB", "99")) > 3:
        outt("dbg_w", lambda it: sb["w4"][:, it * K4:(it + 1) * K4], [])
        outt("dbg_idx", lambda it: sb["idx8"][:, it * 8:(it + 1) * 8], [])
    if sidx_ >= 3:
        outt("dbg_gpos", lambda it: sb["pos4"][:, it * K4:(it + 1) * K4], [])
    if sidx_ >= 2:
        p.emit_i("sync", lambda e: e.dma_start(out=dram["dbg_cnt"], in_=dram["cnt_all"]),
                 waits=[(b.s_cc, p.cur(b.s_cc))], inc_sem=s_st, inc_n=16)
    if sidx_ >= 4:
        p.emit_i("sync", lambda e: e.dma_start(out=dram["dbg_table"], in_=dram["g_table_all"]),
                 waits=[(b.s_cc, p.cur(b.s_cc))], inc_sem=s_st, inc_n=16)
    st_all = p.cur(s_st)
    p.emit("sync", lambda e: e.nop(), waits=[(s_st, st_all)])
    p.build(blk)
    return b


def build_ffn1x():
    b = Builder("ffn1x")
    p = b.p
    for n in ("s_in", "s_wt", "s_wt2", "s_mm", "s_ve", "s_ac", "s_po", "s_st", "s_cc", "s_pc"):
        setattr(b, n, p.sem(n))
    p.auto["vector"] = (p.sem("s_auto_ve"), [0], [None])
    p.auto["scalar"] = (p.sem("s_auto_ac"), [0], [None])
    b._tb_n = 0
    b._tb_evac = [None, None]
    b.declare_io()
    b.alloc_common()
    blk = b.ctx.enter_context(b.nc.Block())
    b.load_common()
    b.ve_hfT = b.emit_ln_stream(lambda it: b.sb["x"][:, it * D:(it + 1) * D], 0,
                                [(b.s_in, b.x_loaded)])
    st_cnt = b.emit_routing()
    ve_bown = b.emit_counts_exchange(st_cnt)
    cc_table = b.emit_gpos_scatter(ve_bown)
    b.emit_shared(b.ve_hfT)
    for j in range(EPC):
        b.emit_expert(1, j)
    st_ret = p.cur(b.s_st)
    cc_ret = p.emit_i("gpsimd", lambda e: e.collective_compute(
        "AllGather", ALU.bypass, replica_groups=[list(range(NCORES))],
        ins=[b.dram["ret_src"].opt()], outs=[b.dram["ret_all"].opt()]),
        waits=[(b.s_st, st_ret)], inc_sem=b.s_cc, kind="dma")
    b.emit_combine(cc_ret)
    sb, dram = b.sb, b.dram
    wv = [(b.s_ve, p.cur(b.s_ve)), (b.s_ac, p.cur(b.s_ac))]
    for it in range(NT):
        tgt = dram["out"].rearrange("(n p) d -> n p d", p=128)[it]
        p.emit_i("sync", lambda e, t=tgt, s=sb["x"][:, it * D:(it + 1) * D]: e.dma_start(out=t, in_=s),
                 waits=(wv if it == 0 else []), inc_sem=b.s_st, inc_n=16)
    st_all = p.cur(b.s_st)
    p.emit("sync", lambda e: e.nop(), waits=[(b.s_st, st_all)])
    p.build(blk)
    return b


def build_full():
    b = Builder("full")
    p = b.p
    for n in ("s_in", "s_wt", "s_wt2", "s_mm", "s_ve", "s_ac", "s_po", "s_st", "s_cc", "s_pc"):
        setattr(b, n, p.sem(n))
    p.auto["vector"] = (p.sem("s_auto_ve"), [0], [None])
    p.auto["scalar"] = (p.sem("s_auto_ac"), [0], [None])
    b._tb_n = 0
    b._tb_evac = [None, None]
    b.declare_io()
    b.alloc_common()
    blk = b.ctx.enter_context(b.nc.Block())
    b._blk = blk
    b.load_common()
    p.phase = "ln1"
    b.ve_hfT = b.emit_ln_stream(lambda it: b.sb["x"][:, it * D:(it + 1) * D], 0,
                                [(b.s_in, b.x_loaded)])
    p.phase = "route1"
    st_cnt = b.emit_routing()
    ve_bown = b.emit_counts_exchange(st_cnt)
    cc_table = b.emit_gpos_scatter(ve_bown)
    b._mk_gu = b._mk_dn = b._mk_ebT = p.cur(b.s_mm)
    b._mk_gath = p.cur(b.s_ac)
    b._mk_scr = p.cur(b.s_mm)
    b._mk_ld_extra = [(b.s_ve, p.cur(b.s_ve)), (b.s_ac, p.cur(b.s_ac))]
    b._mk_ld_uses = 2
    for j in range(EPC):
        p.phase = "exp1_%d" % j
        b.emit_expert(1, j)
        if j in (1, 3, 5):
            b._st_seg[j // 2] = p.cur(b.s_st)
        if j in (3, 5, 7):
            seg = j // 2 - 1
            p.emit_i("gpsimd", lambda e, s=seg: e.collective_compute(
                "AllGather", ALU.bypass, replica_groups=[list(range(NCORES))],
                ins=[b.dram["ret_src"][s * 512:(s + 1) * 512, :].opt()],
                outs=[b.dram["ret_all"][s * 4096:(s + 1) * 4096, :].opt()]),
                waits=[(b.s_st, b._st_seg[seg])], inc_sem=b.s_cc, kind="dma")
    st_ret = p.cur(b.s_st)
    p.phase = "retag1"
    cc_ret = p.emit_i("gpsimd", lambda e: e.collective_compute(
        "AllGather", ALU.bypass, replica_groups=[list(range(NCORES))],
        ins=[b.dram["ret_src"][1536:2048, :].opt()], outs=[b.dram["ret_all"][12288:16384, :].opt()]),
        waits=[(b.s_st, st_ret)], inc_sem=b.s_cc, kind="dma")
    ve_pre_sh = p.cur(b.s_ve)
    ac_pre_sh = p.cur(b.s_ac)
    mm_pre_sh = p.cur(b.s_mm)
    p.phase = "shared1"
    b.emit_shared(b.ve_hfT)
    p.phase = "comb1"
    b.emit_combine(cc_ret, ve_pre_sh, ac_pre_sh, mm_pre_sh)
    import os
    _fs = os.environ.get("FULLSTAGE", "x3")
    if _fs == "x1":
        _emit_out(b, p)
        return b
    p.phase = "ln2"
    ve_hf2T = b.emit_ln_stream(lambda it: b.sb["x"][:, it * D:(it + 1) * D], 0,
                               [(b.s_ve, p.cur(b.s_ve)), (b.s_po, p.cur(b.s_po))])
    b.ve_hfT = ve_hf2T
    p.phase = "attn"
    b.emit_attn()
    if _fs == "x2":
        _emit_out(b, p)
        return b
    p.phase = "ln3"
    ve_hf3T = b.emit_ln_stream(lambda it: b.sb["x"][:, it * D:(it + 1) * D], 0,
                               [(b.s_ve, p.cur(b.s_ve)), (b.s_mm, p.cur(b.s_mm))],
                               store_dram="hf2_src")
    b.ve_hfT = ve_hf3T
    st_hf2 = p.cur(b.s_st)
    p.phase = "hf2ag"
    cc_hf2 = p.emit_i("gpsimd", lambda e: e.collective_compute(
        "AllGather", ALU.bypass, replica_groups=[list(range(NCORES))],
        ins=[b.dram["hf2_src"].opt()], outs=[b.dram["hf2_all"].opt()]),
        waits=[(b.s_st, st_hf2)], inc_sem=b.s_cc, kind="dma")
    p.phase = "route2"
    st_cnt2 = b.emit_routing()
    ve_bown2 = b.emit_counts_exchange(st_cnt2)
    cc_table2 = b.emit_gpos_scatter(ve_bown2)
    b._mk_gu = b._mk_dn = b._mk_ebT = p.cur(b.s_mm)
    b._mk_gath = p.cur(b.s_ac)
    b._mk_scr = p.cur(b.s_mm)
    b._mk_ld_extra = [(b.s_ve, p.cur(b.s_ve)), (b.s_ac, p.cur(b.s_ac))]
    b._mk_ld_uses = 2
    for j in range(EPC):
        p.phase = "exp2_%d" % j
        b.emit_expert(2, j)
        if j in (1, 3, 5):
            b._st_seg[j // 2] = p.cur(b.s_st)
        if j in (3, 5, 7):
            seg = j // 2 - 1
            p.emit_i("gpsimd", lambda e, s=seg: e.collective_compute(
                "AllGather", ALU.bypass, replica_groups=[list(range(NCORES))],
                ins=[b.dram["ret_src"][s * 512:(s + 1) * 512, :].opt()],
                outs=[b.dram["ret_all"][s * 4096:(s + 1) * 4096, :].opt()]),
                waits=[(b.s_st, b._st_seg[seg])], inc_sem=b.s_cc, kind="dma")
    st_ret2 = p.cur(b.s_st)
    p.phase = "retag2"
    cc_ret2 = p.emit_i("gpsimd", lambda e: e.collective_compute(
        "AllGather", ALU.bypass, replica_groups=[list(range(NCORES))],
        ins=[b.dram["ret_src"][1536:2048, :].opt()], outs=[b.dram["ret_all"][12288:16384, :].opt()]),
        waits=[(b.s_st, st_ret2)], inc_sem=b.s_cc, kind="dma")
    ve_pre_sh2 = p.cur(b.s_ve)
    ac_pre_sh2 = p.cur(b.s_ac)
    mm_pre_sh2 = p.cur(b.s_mm)
    p.phase = "shared2"
    b.emit_shared(b.ve_hfT)
    p.phase = "comb2"
    b.emit_combine(cc_ret2, ve_pre_sh2, ac_pre_sh2, mm_pre_sh2)
    p.phase = "outw"
    _emit_out(b, p)
    return b


def _emit_out(b, p):
    sb, dram = b.sb, b.dram
    wv = [(b.s_ve, p.cur(b.s_ve)), (b.s_ac, p.cur(b.s_ac)), (b.s_mm, p.cur(b.s_mm))]
    for it in range(NT):
        tgt = dram["out"].rearrange("(n p) d -> n p d", p=128)[it]
        p.emit_i("sync", lambda e, t=tgt, s=sb["x"][:, it * D:(it + 1) * D]: e.dma_start(out=t, in_=s),
                 waits=(wv if it == 0 else []), inc_sem=b.s_st, inc_n=16)
    st_all = p.cur(b.s_st)
    p.emit("sync", lambda e: e.nop(), waits=[(b.s_st, st_all)])
    p.build(b._blk)


def kernel(**inputs):
    if kernel._dev_stage == "full":
        b = build_full()
        in_maps = [_np_inputs_prep(inputs, c) for c in range(NCORES)]
        res = run_bass_kernel_spmd(b.nc, in_maps, core_ids=list(range(NCORES)))
        return np.concatenate([res.results[c]["out"] for c in range(NCORES)], 0).reshape(B, L, D)
    if kernel._dev_stage == "ffn1front":
        b = build_ffn1front()
        in_maps = [_np_inputs_prep(inputs, c) for c in range(NCORES)]
        res = run_bass_kernel_spmd(b.nc, in_maps, core_ids=list(range(NCORES)))
        return res
    raise NotImplementedError("under construction")


kernel._dev_stage = "full"
